# version 06
# baseline (speedup 1.0000x reference)
import numpy as np
import ml_dtypes
import concourse.bass as bass
import concourse.mybir as mybir
from concourse.bass_utils import run_bass_kernel_spmd
from contextlib import ExitStack

F32 = mybir.dt.float32
BF16 = mybir.dt.bfloat16
U8 = mybir.dt.uint8
AF = mybir.ActivationFunctionType
ALU = mybir.AluOpType
AX = mybir.AxisListType

NCORES = 8
B, T, C = 16, 2048, 1024
BPC = B // NCORES
NT = BPC * T
NTT = NT // 128
TPB = T // 128
NHEAD = 16
HD = 64


class Tile:
    __slots__ = ("name", "last_write", "readers", "war_deps", "dma_sem_key")

    def __init__(self, name, dma_sem_key=None):
        self.name = name
        self.last_write = None
        self.readers = []
        self.war_deps = ()
        self.dma_sem_key = dma_sem_key


class Step:
    __slots__ = ("idx", "role", "fn", "deps", "needs_inc", "ev", "inck",
                 "sem_key", "tag")

    def __init__(self, idx, role, fn, inck, sem_key, tag):
        self.idx = idx
        self.role = role
        self.fn = fn
        self.deps = ()
        self.needs_inc = inck == 16
        self.ev = None
        self.inck = inck
        self.sem_key = sem_key
        self.tag = tag


class FlushStep:
    __slots__ = ("sem_key", "ev", "needs_inc", "role")

    def __init__(self, sem_key, snap):
        self.sem_key = sem_key
        self.ev = (sem_key, snap)
        self.needs_inc = True
        self.role = "_flush"


class Prog:
    def __init__(self, nc, ctx):
        self.nc = nc
        self.ctx = ctx
        self.steps = []
        self.sem_handles = {}
        self.live_counts = {}

    def tile(self, name, dma_sem_key=None):
        return Tile(name, dma_sem_key)

    def _record(self, role, fn, reads, writes, inck, sem_key, tag):
        st = Step(len(self.steps), role, fn, inck, sem_key, tag)
        deps = set()
        for t in reads:
            if t.last_write is not None:
                deps.add(t.last_write)
        for t in writes:
            if t.last_write is not None:
                deps.add(t.last_write)
            for r in t.readers:
                deps.add(r)
            if t.war_deps:
                deps.update(t.war_deps)
                t.war_deps = ()
        deps.discard(st)
        st.deps = tuple(d for d in deps
                        if not (role == "tensor" and getattr(d, "role", "") == "tensor"))
        for t in writes:
            t.last_write = st
            t.readers = []
        for t in reads:
            t.readers.append(st)
        self.steps.append(st)
        self.live_counts[sem_key] = self.live_counts.get(sem_key, 0) + inck
        return st

    def do(self, role, fn, reads=(), writes=(), tag=""):
        return self._record(role, fn, reads, writes, 1, "e_" + role, tag)

    def dma(self, role, out_ap, in_ap, writes, reads=(), tag=""):
        key = None
        if writes and writes[0].dma_sem_key:
            key = writes[0].dma_sem_key
        elif reads and reads[0].dma_sem_key:
            key = reads[0].dma_sem_key
        else:
            key = "dma_store"
        fn = lambda eng, o=out_ap, i=in_ap: eng.dma_start(out=o, in_=i)
        return self._record(role, fn, reads, writes, 16, key, tag)

    def dram_flush(self, tiles):
        fl = FlushStep("dma_store", self.live_counts.get("dma_store", 0))
        for t in tiles:
            if t.last_write is not None:
                t.last_write = fl
                t.readers = []

    def finalize(self):
        for st in self.steps:
            for d in st.deps:
                d.needs_inc = True
        counts = {}
        for st in self.steps:
            if st.needs_inc:
                counts.setdefault(st.sem_key, 0)
                counts[st.sem_key] += st.inck
                st.ev = (st.sem_key, counts[st.sem_key])
        self.sem_totals = counts
        keys = set(counts)
        for st in self.steps:
            for d in st.deps:
                keys.add(d.ev[0] if d.ev else d.sem_key)
        for k in sorted(keys):
            self.sem_handles[k] = self.ctx.enter_context(self.nc.semaphore(k))

    def emit(self, role, eng):
        hw = {}
        for st in self.steps:
            if st.role != role:
                continue
            for d in st.deps:
                if d.ev is None:
                    continue
                key, val = d.ev
                if hw.get(key, 0) < val:
                    eng.wait_ge(self.sem_handles[key], val)
                    hw[key] = val
            ins = st.fn(eng)
            if st.needs_inc:
                ins.then_inc(self.sem_handles[st.sem_key], st.inck)

    def emit_final_wait(self, eng):
        for k, tot in sorted(self.sem_totals.items()):
            if tot and (k.startswith("dl") or k == "dma_store" or k.startswith("c_")):
                eng.wait_ge(self.sem_handles[k], tot)



class Buf:
    __slots__ = ("ap", "t", "off", "cols", "arena")

    def __getitem__(self, sl):
        return self.ap[sl]


class Arena:
    ALIGN = 32

    _sem_pool = []
    _sem_n = [0]
    _sem_last = {}

    def __init__(self, P, nc, ctx, name, cols, dtype):
        self.P = P
        self.h = ctx.enter_context(nc.sbuf_tensor(name, [128, cols], dtype))
        self.free = [(0, cols)]
        self.war_iv = []
        self.name = name
        self.live = {}

    def _collect_war(self, off, end):
        deps = set()
        keep = []
        for (a, b, dd) in self.war_iv:
            if a < end and off < b:
                deps.update(dd)
                if a < off:
                    keep.append((a, off, dd))
                if b > end:
                    keep.append((end, b, dd))
            else:
                keep.append((a, b, dd))
        self.war_iv = keep
        return deps

    def alloc(self, name, cols, dma_slot=False):
        cols = (cols + self.ALIGN - 1) & ~(self.ALIGN - 1)
        for i, (off, ln) in enumerate(self.free):
            if ln >= cols:
                if ln == cols:
                    self.free.pop(i)
                else:
                    self.free[i] = (off + cols, ln - cols)
                b = Buf()
                b.ap = self.h[:, off:off + cols]
                key = None
                sem_deps = set()
                if dma_slot:
                    if Arena._sem_pool:
                        key = Arena._sem_pool.pop()
                        old = Arena._sem_last.get(key)
                        if old is not None:
                            sem_deps.update(old.readers)
                            if old.last_write is not None and \
                                    not isinstance(old.last_write, FlushStep):
                                sem_deps.add(old.last_write)
                    else:
                        key = f"dl{Arena._sem_n[0]}"
                        Arena._sem_n[0] += 1
                b.t = self.P.tile(name, dma_sem_key=key)
                if key is not None:
                    Arena._sem_last[key] = b.t
                b.t.war_deps = tuple(self._collect_war(off, off + cols) | sem_deps)
                b.off = off
                b.cols = cols
                b.arena = self
                for o2, (n2, c2) in self.live.items():
                    assert not (off < o2 + c2 and o2 < off + cols), \
                        f"arena {self.name}: {name} [{off},{off+cols}) overlaps {n2} [{o2},{o2+c2})"
                self.live[off] = (name, cols)
                return b
        raise RuntimeError(f"arena {self.name} overflow: need {cols}, free={self.free}")

    def freeb(self, b):
        assert b.off in self.live, f"arena {self.name}: double free of {b.t.name} at {b.off}"
        del self.live[b.off]
        if b.t.dma_sem_key is not None:
            Arena._sem_pool.append(b.t.dma_sem_key)
        deps = set()
        if b.t.last_write is not None and not isinstance(b.t.last_write, FlushStep):
            deps.add(b.t.last_write)
        deps.update(b.t.readers)
        if deps:
            self.war_iv.append((b.off, b.off + b.cols, frozenset(deps)))
        self.free.append((b.off, b.cols))
        self.free.sort()
        merged = []
        for off, ln in self.free:
            if merged and merged[-1][0] + merged[-1][1] == off:
                merged[-1] = (merged[-1][0], merged[-1][1] + ln)
            else:
                merged.append([off, ln])
        self.free = [tuple(x) for x in merged]


class PsumPool:
    def __init__(self, P, nc, ctx):
        self.P = P
        self.banks = []
        self.busy = [False] * 8
        for i in range(8):
            h = ctx.enter_context(nc.psum_tensor(f"psb{i}", [128, 512], F32))
            self.banks.append((h[:], P.tile(f"psum{i}"), i))
        self.rr = 0

    def get(self):
        for _ in range(8):
            ap, t, i = self.banks[self.rr % 8]
            self.rr += 1
            if not self.busy[i]:
                return ap, t
        raise RuntimeError("all psum banks taken")

    def take(self):
        for ap, t, i in self.banks:
            if not self.busy[i]:
                self.busy[i] = True
                return ap, t, i
        raise RuntimeError("no free psum bank to take")

    def release(self, i):
        self.busy[i] = False



def _t(a):
    return np.ascontiguousarray(np.asarray(a).T)


def _rope_tables(head_dim_half, repeat):
    j = np.arange(head_dim_half, dtype=np.float32)
    theta = np.power(np.float32(10000.0),
                     np.float32(-2.0) * j / np.float32(2 * head_dim_half))
    t = np.arange(T, dtype=np.float32)
    ang = t[:, None] * theta[None, :]
    cos = np.cos(ang).astype(np.float32)
    sin = np.sin(ang).astype(np.float32)
    if repeat > 1:
        cos = np.tile(cos, (1, repeat))
        sin = np.tile(sin, (1, repeat))
    return np.ascontiguousarray(cos), np.ascontiguousarray(sin)


def _alibi_slopes(n):
    import math
    def pow2(m):
        start = 2 ** (-2 ** (-(math.log2(m) - 3)))
        return [start * start ** i for i in range(m)]
    if math.log2(n).is_integer():
        return pow2(n)
    c = 2 ** math.floor(math.log2(n))
    return pow2(c) + _alibi_slopes(2 * c)[0::2][:n - c]


def prepare_consts():
    c = {}
    c["ident"] = np.eye(128, dtype=ml_dtypes.bfloat16)
    c["ones_row"] = np.ones((1, 128), dtype=ml_dtypes.bfloat16)
    c["ones_col"] = np.ones((128, 1), dtype=ml_dtypes.bfloat16)
    c["iota8"] = np.tile(np.arange(8, dtype=np.float32), (128, 1))
    c["iota8r"] = np.tile(1000.0 - np.arange(8, dtype=np.float32), (128, 1))
    p = np.arange(128)[:, None]
    u = np.arange(896)[None, :]
    c["causmask"] = (p <= u - 384).astype(ml_dtypes.bfloat16)
    esel = np.zeros((8, 8 * 128), dtype=ml_dtypes.bfloat16)
    for e in range(8):
        esel[e, e * 128:(e + 1) * 128] = 1
    c["esel"] = esel
    c["rope0_cos"], c["rope0_sin"] = _rope_tables(HD // 2, NHEAD)
    c["rope1_cos"], c["rope1_sin"] = _rope_tables(C // 2, 1)
    slopes = np.asarray(_alibi_slopes(NHEAD), dtype=np.float64)
    idx = np.arange(T, dtype=np.float64)
    c["alibi_srow"] = np.ascontiguousarray(
        (8.0 * slopes[:, None] * idx[None, :]).reshape(1, -1).astype(np.float32))
    c["alibi_negtrow"] = np.ascontiguousarray(
        (-8.0 * slopes[:, None] * idx[None, :]).reshape(1, -1).astype(np.float32))
    band = np.minimum(p - (u - 384), 0).astype(np.float64)
    c["alibi_band"] = np.ascontiguousarray(
        (8.0 * slopes[:, None, None] * band[None]).astype(np.float32))
    pp = np.arange(128, dtype=np.float64)
    sc = np.zeros((128, 256), dtype=np.float64)
    for h in range(NHEAD):
        for kt in range(TPB):
            sc[:, h * 16 + kt] = slopes[h] * (kt * 128 + pp)
    c["alibi_scol"] = sc.astype(np.float32)
    return c


def prepare_weights(inp):
    w = {}
    for n in ("wq", "wk", "wv", "wp"):
        w[f"l0a_{n}T"] = _t(inp[f"L0_attn_{n}"])
    for n in ("bq", "bk", "bv", "bp"):
        w[f"l0a_{n}"] = np.asarray(inp[f"L0_attn_{n}"]).reshape(1, C)
    k0 = np.asarray(inp["L0_ffn_k0"]).view(np.uint8)
    w["l0f_k0t"] = np.ascontiguousarray(k0[:C])
    w["l0f_k0b"] = np.ascontiguousarray(k0[C:])
    w["l0f_k1"] = np.asarray(inp["L0_ffn_k1"]).view(np.uint8)
    w["l0f_k2"] = np.asarray(inp["L0_ffn_k2"]).view(np.uint8)
    for n in ("s0", "s1", "s2"):
        w[f"l0f_{n}"] = np.ascontiguousarray(
            np.broadcast_to(np.asarray(inp[f"L0_ffn_{n}"]).reshape(1, 1), (128, 1)))
    for n in ("wq", "wk", "wv"):
        w[f"l1a_{n}T"] = _t(inp[f"L1_attn_{n}_w"])
        w[f"l1a_b{n[1]}"] = np.asarray(inp[f"L1_attn_{n}_b"]).reshape(1, C)
    w["l1f_Wg"] = np.asarray(inp["L1_ffn_Wg"])
    w["l1f_bg"] = np.asarray(inp["L1_ffn_bg"]).reshape(1, 8)
    w["l1f_W1"] = np.asarray(inp["L1_ffn_W1"])
    w["l1f_b1"] = np.ascontiguousarray(
        np.asarray(inp["L1_ffn_b1"]).reshape(8, 8, 128).transpose(2, 0, 1).reshape(128, 64))
    w["l1f_W2"] = np.asarray(inp["L1_ffn_W2"])
    w["l1f_b2"] = np.asarray(inp["L1_ffn_b2"])
    w["l2a_wqkv"] = np.asarray(inp["L2_attn_W_qkv"])
    w["l2a_bqkv"] = np.asarray(inp["L2_attn_b_qkv"]).reshape(1, 3 * C)
    w["l2a_wp"] = np.asarray(inp["L2_attn_W_proj"])
    w["l2a_bp"] = np.asarray(inp["L2_attn_b_proj"]).reshape(1, C)
    w["l2f_gateT"] = _t(inp["L2_ffn_gate_w"])
    w["l2f_wgT"] = np.ascontiguousarray(np.transpose(inp["L2_ffn_w_gate"], (0, 2, 1)))
    w["l2f_wuT"] = np.ascontiguousarray(np.transpose(inp["L2_ffn_w_up"], (0, 2, 1)))
    w["l2f_wdT"] = np.ascontiguousarray(np.transpose(inp["L2_ffn_w_down"], (0, 2, 1)))
    w["l3a_wqkvT"] = _t(inp["L3_attn_w_qkv"])
    w["l3a_woT"] = _t(inp["L3_attn_w_out"])
    w["l3a_bo"] = np.asarray(inp["L3_attn_b_out"]).reshape(1, C)
    rft = _t(inp["L3_attn_random_features"]).astype(np.float32)
    blk = np.zeros((128, 128), dtype=np.float32)
    blk[0:64, 0:64] = rft
    blk[64:128, 64:128] = rft
    w["l3a_rfT"] = np.ascontiguousarray(blk)
    I0 = 938
    w["l3f_gateT"] = _t(inp["L3_ffn_gate_weight"])
    def pad_gu(gu):
        out = np.zeros((2048, C), dtype=np.float32)
        out[:I0] = gu[:I0]
        out[1024:1024 + I0] = gu[I0:]
        return _t(out)
    def pad_dn(dn):
        out = np.zeros((1024, C), dtype=np.float32)
        out[:I0] = np.asarray(dn).T
        return np.ascontiguousarray(out)
    w["l3f_sh_guT"] = pad_gu(np.asarray(inp["L3_ffn_shared_gate_up"]))
    w["l3f_sh_dnT"] = pad_dn(np.asarray(inp["L3_ffn_shared_down"]))
    egu = np.asarray(inp["L3_ffn_experts_gate_up"])
    edn = np.asarray(inp["L3_ffn_experts_down"])
    w["l3f_ex_guT"] = np.stack([pad_gu(egu[e]) for e in range(8)])
    w["l3f_ex_dnT"] = np.stack([pad_dn(edn[e]) for e in range(8)])
    return w



class Common:
    pass


def mm_group(P, psum_ap, psum_t, mms, reads, tag="", skip_gc=False):
    def fn(e):
        n = len(mms)
        ins = None
        for i, (out_ap, lh, rh, tp) in enumerate(mms):
            ins = e.matmul(out=out_ap, lhsT=lh, rhs=rh, start=(i == 0),
                           stop=(i == n - 1), tile_position=tp,
                           skip_group_check=skip_gc)
        return ins
    return P.do("tensor", fn, reads=reads, writes=[psum_t], tag=tag)


def ln_tile(P, cm, sl, xt_ap, xt_t, name):
    stat, sq, xn = sl["st"], sl["sq"], sl["xn"]
    mean = stat.ap[:, 0:1]
    msq = stat.ap[:, 1:2]
    var = stat.ap[:, 2:3]
    rstd = stat.ap[:, 3:4]
    P.do("vector", lambda e: e.reduce_sum(out=mean, in_=xt_ap, axis=AX.X),
         reads=[xt_t], writes=[stat.t], tag=name + "s")
    P.do("scalar", lambda e: e.activation(out=sq.ap, in_=xt_ap, func=AF.Square),
         reads=[xt_t], writes=[sq.t], tag=name + "q")
    P.do("vector", lambda e: e.reduce_sum(out=msq, in_=sq.ap, axis=AX.X),
         reads=[sq.t], writes=[stat.t], tag=name + "q2")
    P.do("vector", lambda e: e.tensor_scalar(out=stat.ap[:, 0:2], in0=stat.ap[:, 0:2],
                                             scalar1=1.0 / 1024, scalar2=None, op0=ALU.mult),
         reads=[stat.t], writes=[stat.t], tag=name + "d")
    P.do("vector", lambda e: e.tensor_scalar(out=var, in0=mean, scalar1=mean,
                                             scalar2=None, op0=ALU.mult),
         reads=[stat.t], writes=[stat.t], tag=name + "m2")
    P.do("vector", lambda e: e.scalar_tensor_tensor(out=var, in0=msq, scalar=1e-5,
                                                     in1=var, op0=ALU.add, op1=ALU.subtract),
         reads=[stat.t], writes=[stat.t], tag=name + "v")
    P.do("scalar", lambda e: e.activation(out=rstd, in_=var, func=AF.Ln),
         reads=[stat.t], writes=[stat.t], tag=name + "l")
    P.do("scalar", lambda e: e.activation(out=rstd, in_=rstd, func=AF.Exp, scale=-0.5),
         reads=[stat.t], writes=[stat.t], tag=name + "e")
    P.do("vector", lambda e: e.tensor_scalar(out=xn.ap, in0=xt_ap, scalar1=mean,
                                             scalar2=rstd, op0=ALU.subtract, op1=ALU.mult),
         reads=[xt_t, stat.t], writes=[xn.t], tag=name + "n")
    return xn


def transpose_to(P, ps, cm, src_ap, src_t, dst_ap, dst_t, ncols, tag,
                 dst_view=None):
    assert ncols <= 1024
    bank, bank_t = ps.get()
    bb = bank.bitcast(BF16)
    nb = ncols // 128
    def fn(e):
        ins = None
        for ct in range(nb):
            ins = e.transpose(out=bb[:, ct * 128:(ct + 1) * 128],
                              in_=src_ap[:, ct * 128:(ct + 1) * 128],
                              identity=cm.ident)
        return ins
    P.do("tensor", fn, reads=[src_t, cm.ident_t], writes=[bank_t], tag=tag + "t")
    if dst_view is not None:
        dv = dst_view
        sv = bb[:, 0:ncols].rearrange("p (c t) -> p c t", t=128)
    else:
        dv = dst_ap
        sv = bb[:, 0:ncols]
    P.do("scalar", lambda e: e.copy(out=dv, in_=sv),
         reads=[bank_t], writes=[dst_t], tag=tag + "e")


def load_weight_into(P, wb, stg, dram_ap, rows, cols, name, role="sync"):
    ntile = rows // 128
    for ct in range(ntile):
        P.dma(role, stg.ap[:, 0:cols], dram_ap[ct * 128:(ct + 1) * 128, :],
              writes=[stg.t], tag=name + f"L{ct}")
        dst = wb.ap[:, ct * cols:(ct + 1) * cols]
        P.do("gpsimd",
             lambda e, o=dst, i=stg.ap[:, 0:cols]: e.tensor_copy(out=o, in_=i),
             reads=[stg.t], writes=[wb.t], tag=name + f"C{ct}")


def load_weight_bf16(P, AB, AF32, dram_ap, rows, cols, name, role="sync"):
    ntile = rows // 128
    wb = AB.alloc(name, ntile * cols)
    stg = AF32.alloc(name + "_stg", cols, dma_slot=True)
    for ct in range(ntile):
        P.dma(role, stg.ap, dram_ap[ct * 128:(ct + 1) * 128, :], writes=[stg.t],
              tag=name + f"L{ct}")
        dst = wb.ap[:, ct * cols:(ct + 1) * cols]
        P.do("gpsimd",
             lambda e, o=dst, i=stg.ap: e.tensor_copy(out=o, in_=i),
             reads=[stg.t], writes=[wb.t], tag=name + f"C{ct}")
    AF32.freeb(stg)
    return wb


def load_bias_row(P, AB, AF32, dram_ap, cols, name):
    stg = AF32.alloc(name + "_stg", cols, dma_slot=True)
    bb = AB.alloc(name, cols)
    P.dma("sync", stg.ap[0:1, :], dram_ap, writes=[stg.t], tag=name + "L")
    P.do("vector", lambda e: e.tensor_copy(out=bb.ap[0:1, :], in_=stg.ap[0:1, :]),
         reads=[stg.t], writes=[bb.t], tag=name + "C")
    AF32.freeb(stg)
    return bb



def emit_l0_attn(P, AB, AF32, AU8, ps, cm, nc, io, wdr):
    for b in range(BPC):
        _softmax_attn_batch(
            P, AB, AF32, ps, cm, nc, io, b, name=f"l0a{b}",
            wspec=dict(kind="sep",
                       wq=("l0a_wqT", "l0a_bq"), wk=("l0a_wkT", "l0a_bk"),
                       wv=("l0a_wvT", "l0a_bv"), wp=("l0a_wpT", "l0a_bp")),
            wdr=wdr, rope=("rope0_cos", "rope0_sin"), causal=True,
            alibi=False, scale=0.125)


def _softmax_attn_batch(P, AB, AF32, ps, cm, nc, io, b, name, wspec, wdr,
                        rope, causal, alibi, scale):
    xin_ap, xin_tiles = io["xin"]
    xout_ap, xout_tiles = io["xout"]

    qfm = AB.alloc(name + "_qfm", 8 * T)
    kfm = AB.alloc(name + "_kfm", 8 * T)
    vtm = AB.alloc(name + "_vtm", TPB * 1024)

    if wspec["kind"] == "sep":
        wq = load_weight_bf16(P, AB, AF32, wdr[wspec["wq"][0]], C, C, name + "_wq")
        wk = load_weight_bf16(P, AB, AF32, wdr[wspec["wk"][0]], C, C, name + "_wk")
        wv = load_weight_bf16(P, AB, AF32, wdr[wspec["wv"][0]], C, C, name + "_wv")
        bq = load_bias_row(P, AB, AF32, wdr[wspec["wq"][1]], C, name + "_bq")
        bk = load_bias_row(P, AB, AF32, wdr[wspec["wk"][1]], C, name + "_bk")
        bv = load_bias_row(P, AB, AF32, wdr[wspec["wv"][1]], C, name + "_bv")
        def wslice(which, ct, oc):
            wb = {"q": wq, "k": wk, "v": wv}[which]
            return wb.ap[:, ct * C + oc * 512: ct * C + (oc + 1) * 512]
        def bslice(which, oc):
            bb = {"q": bq, "k": bk, "v": bv}[which]
            return bb.ap[0:1, oc * 512:(oc + 1) * 512]
        def wtile(which):
            return {"q": wq, "k": wk, "v": wv}[which].t
        def btile(which):
            return {"q": bq, "k": bk, "v": bv}[which].t
    else:
        wqkv = load_weight_bf16(P, AB, AF32, wdr[wspec["wqkv"][0]], C, 3 * C,
                                name + "_wqkv")
        bqkv = load_bias_row(P, AB, AF32, wdr[wspec["wqkv"][1]], 3 * C, name + "_bqkv")
        offmap = {"q": 0, "k": C, "v": 2 * C}
        def wslice(which, ct, oc):
            o0 = offmap[which] + oc * 512
            return wqkv.ap[:, ct * 3 * C + o0: ct * 3 * C + o0 + 512]
        def bslice(which, oc):
            o0 = offmap[which] + oc * 512
            return bqkv.ap[0:1, o0:o0 + 512]
        def wtile(which):
            return wqkv.t
        def btile(which):
            return bqkv.t

    sq_sh = AF32.alloc(name + "_sq", 1024)
    sl = []
    for i in range(2):
        sl.append({
            "xr": AF32.alloc(name + f"_xr{i}", 1024, dma_slot=True),
            "st": AF32.alloc(name + f"_st{i}", 32),
            "sq": sq_sh,
            "xn": AB.alloc(name + f"_xn{i}", 1024),
            "xnT": AB.alloc(name + f"_xnT{i}", 1024),
            "qb": AB.alloc(name + f"_qb{i}", 1024),
            "kb": AB.alloc(name + f"_kb{i}", 1024),
        })
    if rope is not None:
        r1 = AF32.alloc(name + "_r1", 256)
        r2 = AF32.alloc(name + "_r2", 256)
        csl = [AF32.alloc(name + f"_cs{i}", 512, dma_slot=True) for i in range(2)]
        snl = [AF32.alloc(name + f"_sn{i}", 512, dma_slot=True) for i in range(2)]
        cosd, sind = wdr[rope[0]], wdr[rope[1]]

    def proj(which, s, tt, dstb=None, ropecs=None, tagp=""):
        for oc in range(2):
            bank, bank_t = ps.get()
            mms = [(bank, cm.ones_row[0:1, :], bslice(which, oc), None)]
            for ct in range(8):
                mms.append((bank, s["xnT"].ap[:, ct * 128:(ct + 1) * 128],
                            wslice(which, ct, oc), None))
            mm_group(P, bank, bank_t, mms,
                     reads=[s["xnT"].t, wtile(which), btile(which), cm.ones_row_t],
                     tag=tagp + f"m{oc}")
            if dstb is None:
                dst = vtm.ap[:, tt * 1024 + oc * 512: tt * 1024 + (oc + 1) * 512]
                P.do("scalar", lambda e, o=dst, i=bank: e.copy(out=o, in_=i),
                     reads=[bank_t], writes=[vtm.t], tag=tagp + f"e{oc}")
            elif ropecs is None:
                dst = dstb.ap[:, oc * 512:(oc + 1) * 512]
                P.do("scalar", lambda e, o=dst, i=bank: e.copy(out=o, in_=i),
                     reads=[bank_t], writes=[dstb.t], tag=tagp + f"e{oc}")
            else:
                cs, sn = ropecs
                pe = bank[:, 0:512:2]
                po = bank[:, 1:512:2]
                css = cs.ap[:, oc * 256:(oc + 1) * 256]
                sns = sn.ap[:, oc * 256:(oc + 1) * 256]
                de = dstb.ap[:, oc * 512:(oc + 1) * 512][:, 0::2]
                do_ = dstb.ap[:, oc * 512:(oc + 1) * 512][:, 1::2]
                P.do("vector", lambda e, pe=pe, c=css: e.tensor_mul(out=r1.ap, in0=pe, in1=c),
                     reads=[bank_t, cs.t], writes=[r1.t], tag=tagp + f"ra{oc}")
                P.do("vector", lambda e, po=po, sx=sns: e.tensor_mul(out=r2.ap, in0=po, in1=sx),
                     reads=[bank_t, sn.t], writes=[r2.t], tag=tagp + f"rb{oc}")
                P.do("vector", lambda e, de=de: e.tensor_sub(out=de, in0=r1.ap, in1=r2.ap),
                     reads=[r1.t, r2.t], writes=[dstb.t], tag=tagp + f"rc{oc}")
                P.do("vector", lambda e, po=po, c=css: e.tensor_mul(out=r1.ap, in0=po, in1=c),
                     reads=[bank_t, cs.t], writes=[r1.t], tag=tagp + f"rd{oc}")
                P.do("vector", lambda e, pe=pe, sx=sns: e.tensor_mul(out=r2.ap, in0=pe, in1=sx),
                     reads=[bank_t, sn.t], writes=[r2.t], tag=tagp + f"re{oc}")
                P.do("vector", lambda e, do_=do_: e.tensor_add(out=do_, in0=r1.ap, in1=r2.ap),
                     reads=[r1.t, r2.t], writes=[dstb.t], tag=tagp + f"rf{oc}")

    for tt in range(TPB):
        gt = b * TPB + tt
        s = sl[tt % 2]
        P.dma("sync", s["xr"].ap, xin_ap(gt), writes=[s["xr"].t],
              reads=[xin_tiles[gt]], tag=name + f"xl{tt}")
        ln_tile(P, cm, s, s["xr"].ap, s["xr"].t, name + f"ln{tt}")
        transpose_to(P, ps, cm, s["xn"].ap, s["xn"].t, s["xnT"].ap, s["xnT"].t,
                     1024, name + f"xT{tt}")
        if rope is not None:
            cs, sn = csl[tt % 2], snl[tt % 2]
            P.dma("sync", cs.ap, cosd[tt * 128:(tt + 1) * 128, :], writes=[cs.t],
                  tag=name + f"cl{tt}")
            P.dma("sync", sn.ap, sind[tt * 128:(tt + 1) * 128, :], writes=[sn.t],
                  tag=name + f"sl{tt}")
        proj("v", s, tt, None, None, name + f"v{tt}")
        rc = (cs, sn) if rope is not None else None
        proj("q", s, tt, s["qb"], rc, name + f"q{tt}")
        proj("k", s, tt, s["kb"], rc, name + f"k{tt}")
        for (bb_, fm) in ((s["qb"], qfm), (s["kb"], kfm)):
            dv = fm.ap.rearrange("p (c t) -> p c t", t=T)[:, :, tt * 128:(tt + 1) * 128]
            transpose_to(P, ps, cm, bb_.ap, bb_.t, None, fm.t, 1024,
                         name + ("q" if fm is qfm else "k") + f"F{tt}", dst_view=dv)

    if wspec["kind"] == "sep":
        for wbuf in (wq, wk, wv):
            AB.freeb(wbuf)
        for bbuf in (bq, bk, bv):
            AB.freeb(bbuf)
    else:
        AB.freeb(wqkv)
        AB.freeb(bqkv)
    if rope is not None:
        for bufl in (csl, snl):
            for bf in bufl:
                AF32.freeb(bf)
        AF32.freeb(r1)
        AF32.freeb(r2)
    AF32.freeb(sq_sh)
    for s in sl:
        for k in ("xn", "xnT", "qb", "kb"):
            AB.freeb(s[k])

    wp = load_weight_bf16(P, AB, AF32, wdr[wspec["wp"][0]], C, C, name + "_wp")
    bp = load_bias_row(P, AB, AF32, wdr[wspec["wp"][1]], C, name + "_bp")

    ybuf = AB.alloc(name + "_y", 8 * 512)
    mcol = AF32.alloc(name + "_mcol", NHEAD)
    mtmp = AF32.alloc(name + "_mtmp", 16)
    mbias = AF32.alloc(name + "_mbias", NHEAD)
    mneg = AB.alloc(name + "_mneg", 16)
    mrowb = AB.alloc(name + "_mrowb", 16)
    attf = AB.alloc(name + "_attf", 2 * TPB * 512)
    attv = attf.ap.rearrange("p (z k t) -> p z k t", k=TPB, t=512)
    bcf = AF32.alloc(name + "_bcf", 512)
    srow = AB.alloc(name + "_srow", 1024)
    outs = [AF32.alloc(name + f"_o{i}", 1024, dma_slot=True) for i in range(2)]
    if alibi:
        bandb = [AF32.alloc(name + f"_band{z}", 896, dma_slot=True) for z in range(2)]
        srsl = [AF32.alloc(name + f"_sr{z}", 128, dma_slot=True) for z in range(2)]
        ntsl = [AF32.alloc(name + f"_nt{z}", 512, dma_slot=True) for z in range(2)]
        onesf = AF32.alloc(name + "_onesf", 512)
        P.do("vector", lambda e: e.memset(onesf.ap[0:1, :], 1.0),
             writes=[onesf.t], tag=name + "onf")

    qv = qfm.ap.rearrange("p (c t) -> p c t", t=T)
    kv = kfm.ap.rearrange("p (c t) -> p c t", t=T)

    for h in range(NHEAD):
        ct, half = h // 2, (h % 2) * 64
        qsl = qv[half:half + 64, ct, :]
        ksl = kv[half:half + 64, ct, :]
        first = True
        for qs in range(TPB):
            nkc = ((qs // 4) + 1) if causal else 4
            for kc in range(nkc):
                bank, bank_t = ps.get()
                mm_group(P, bank, bank_t,
                         [(bank, qsl[:, qs * 128:(qs + 1) * 128],
                           ksl[:, kc * 512:(kc + 1) * 512], None)],
                         reads=[qfm.t, kfm.t], tag=name + f"x{h}_{qs}_{kc}")
                if first:
                    P.do("vector", lambda e, o=mcol.ap[:, h:h + 1], i=bank:
                         e.reduce_max(out=o, in_=i, axis=AX.X),
                         reads=[bank_t], writes=[mcol.t], tag=name + f"xm{h}_{qs}_{kc}")
                    first = False
                else:
                    P.do("vector", lambda e, o=mtmp.ap[:, 0:1], i=bank:
                         e.reduce_max(out=o, in_=i, axis=AX.X),
                         reads=[bank_t], writes=[mtmp.t], tag=name + f"xm{h}_{qs}_{kc}")
                    P.do("vector", lambda e, o=mcol.ap[:, h:h + 1], i=mtmp.ap[:, 0:1]:
                         e.tensor_max(out=o, in0=o, in1=i),
                         reads=[mtmp.t, mcol.t], writes=[mcol.t],
                         tag=name + f"xx{h}_{qs}_{kc}")
    P.do("vector", lambda e: e.tensor_copy(out=mneg.ap[:, 0:16], in_=mcol.ap[:, 0:16]),
         reads=[mcol.t], writes=[mneg.t], tag=name + "mn")
    bankm, bankm_t = ps.get()
    bbm = bankm.bitcast(BF16)
    P.do("tensor", lambda e: e.transpose(out=bbm[0:16, 0:128], in_=mneg.ap[:, 0:16],
                                         identity=cm.ident),
         reads=[mneg.t, cm.ident_t], writes=[bankm_t], tag=name + "mt")
    P.do("vector", lambda e: e.reduce_max(out=mtmp.ap[0:16, 0:1],
                                          in_=bbm[0:16, 0:128], axis=AX.X),
         reads=[bankm_t], writes=[mtmp.t], tag=name + "mr")
    P.do("vector", lambda e, sc=-scale: e.tensor_scalar(
         out=mrowb.ap[0:16, 0:1], in0=mtmp.ap[0:16, 0:1], scalar1=sc,
         scalar2=None, op0=ALU.mult),
         reads=[mtmp.t], writes=[mrowb.t], tag=name + "ms")
    bankn, bankn_t = ps.get()
    bbn = bankn.bitcast(BF16)
    P.do("tensor", lambda e: e.transpose(out=bbn[0:16, 0:16],
                                         in_=mrowb.ap[0:16, 0:16],
                                         identity=cm.ident[0:16, 0:16]),
         reads=[mrowb.t, cm.ident_t], writes=[bankn_t], tag=name + "mt2")
    P.do("scalar", lambda e: e.copy(out=mrowb.ap[0:1, 0:16], in_=bbn[0:1, 0:16]),
         reads=[bankn_t], writes=[mrowb.t], tag=name + "mv")
    bankb, bankb_t = ps.get()
    def fnmb(e, bankb=bankb):
        ins = None
        for h in range(NHEAD):
            ins = e.matmul(out=bankb[:, h:h + 1], lhsT=cm.ones_row[0:1, :],
                           rhs=mrowb.ap[0:1, h:h + 1], start=True, stop=True)
        return ins
    P.do("tensor", fnmb, reads=[mrowb.t, cm.ones_row_t], writes=[bankb_t],
         tag=name + "mb")
    P.do("scalar", lambda e: e.copy(out=mbias.ap[:, 0:16], in_=bankb[:, 0:16]),
         reads=[bankb_t], writes=[mbias.t], tag=name + "mbe")

    for qc in range(4):
        ktmax = (4 * qc + 4) if causal else TPB
        for hp in range(8):
            if alibi:
                for z in range(2):
                    h = 2 * hp + z
                    P.dma("sync", bandb[z].ap[:, 0:896], wdr["alibi_band"][h],
                          writes=[bandb[z].t], tag=name + f"bl{qc}{hp}{z}")
                    P.dma("sync", ntsl[z].ap[0:1, 0:512],
                          wdr["alibi_negtrow"][0:1, h * T + qc * 512: h * T + (qc + 1) * 512],
                          writes=[ntsl[z].t], tag=name + f"nt{qc}{hp}{z}")
            for z in range(2):
                h = 2 * hp + z
                ct, half = h // 2, (h % 2) * 64
                ksl = kv[half:half + 64, ct, :]
                qsl = qv[half:half + 64, ct, :]
                for kt in range(ktmax):
                    bank, bank_t = ps.get()
                    mms = [(bank, ksl[:, kt * 128:(kt + 1) * 128],
                            qsl[:, qc * 512:(qc + 1) * 512], None)]
                    rd = [qfm.t, kfm.t]
                    past = kt * 128 + 127 < qc * 512
                    diag = (not past) and kt * 128 < qc * 512 + 512
                    if alibi and past:
                        P.dma("sync", srsl[z].ap[0:1, 0:128],
                              wdr["alibi_srow"][0:1, h * T + kt * 128: h * T + (kt + 1) * 128],
                              writes=[srsl[z].t], tag=name + f"sr{qc}{hp}{kt}{z}")
                        mms.append((bank, srsl[z].ap[0:1, 0:128],
                                    onesf.ap[0:1, 0:512], None))
                        mms.append((bank, onesf.ap[0:1, 0:128],
                                    ntsl[z].ap[0:1, 0:512], None))
                        rd += [srsl[z].t, ntsl[z].t, onesf.t]
                    mm_group(P, bank, bank_t, mms, reads=rd,
                             tag=name + f"s{qc}{hp}{kt}{z}", skip_gc=True)
                    att = attv[:, z, kt, :]
                    mb = mbias.ap[:, h:h + 1]
                    if alibi and diag:
                        delta = kt * 128 - qc * 512
                        off = 384 - delta
                        P.do("vector", lambda e, o=bcf.ap, i=bank,
                             m=bandb[z].ap[:, off:off + 512]:
                             e.tensor_add(out=o, in0=i, in1=m),
                             reads=[bank_t, bandb[z].t], writes=[bcf.t],
                             tag=name + f"ba{qc}{hp}{kt}{z}")
                        P.do("scalar", lambda e, o=att, i=bcf.ap, sc=scale, mb=mb:
                             e.activation(out=o, in_=i, func=AF.Exp, scale=sc, bias=mb),
                             reads=[bcf.t, mbias.t], writes=[attf.t],
                             tag=name + f"p{qc}{hp}{kt}{z}")
                    else:
                        P.do("scalar", lambda e, o=att, i=bank, sc=scale, mb=mb:
                             e.activation(out=o, in_=i, func=AF.Exp, scale=sc, bias=mb),
                             reads=[bank_t, mbias.t], writes=[attf.t],
                             tag=name + f"p{qc}{hp}{kt}{z}")
                    if causal and kt >= 4 * qc:
                        delta = kt * 128 - qc * 512
                        off = 384 - delta
                        msk = cm.causmask[:, off:off + 512]
                        P.do("vector", lambda e, o=att, m=msk:
                             e.tensor_mul(out=o, in0=o, in1=m),
                             reads=[attf.t, cm.causmask_t], writes=[attf.t],
                             tag=name + f"k{qc}{hp}{kt}{z}")
            av_ap, av_t, av_i = ps.take()
            sm = [ps.take() for _ in range(2)]
            for kt in range(ktmax):
                st_, sp_ = (kt == 0), (kt == ktmax - 1)
                def fna(e, kt=kt, st_=st_, sp_=sp_, av_ap=av_ap, hp=hp):
                    ins = None
                    for z in range(2):
                        h = 2 * hp + z
                        ins = e.matmul(out=av_ap[64 * z:64 * z + 64, :],
                                       lhsT=vtm.ap[:, kt * 1024 + h * 64: kt * 1024 + h * 64 + 64],
                                       rhs=attv[:, z, kt, :], start=st_, stop=sp_,
                                       tile_position=(0, 64 * z), skip_group_check=True)
                    return ins
                P.do("tensor", fna, reads=[vtm.t, attf.t], writes=[av_t],
                     tag=name + f"a{qc}{hp}{kt}")
                for z in range(2):
                    P.do("tensor", lambda e, o=sm[z][0][0:1, :], r=attv[:, z, kt, :],
                         s1=st_, s2=sp_:
                         e.matmul(out=o, lhsT=cm.ones_col, rhs=r, start=s1, stop=s2,
                                  skip_group_check=True),
                         reads=[attf.t, cm.ones_col_t], writes=[sm[z][1]],
                         tag=name + f"u{qc}{hp}{kt}{z}")
            for z in range(2):
                P.do("scalar", lambda e, o=srow.ap[0:1, z * 512:(z + 1) * 512],
                     i=sm[z][0][0:1, :]: e.copy(out=o, in_=i),
                     reads=[sm[z][1]], writes=[srow.t], tag=name + f"w{qc}{hp}{z}")
            bcb, bcb_t = ps.get()
            def fnb(e, bcb=bcb):
                e.matmul(out=bcb[0:64, :], lhsT=cm.ones_row[0:1, 0:64],
                         rhs=srow.ap[0:1, 0:512], start=True, stop=True,
                         tile_position=(0, 0))
                return e.matmul(out=bcb[64:128, :], lhsT=cm.ones_row[0:1, 0:64],
                                rhs=srow.ap[0:1, 512:1024], start=True, stop=True,
                                tile_position=(0, 64))
            P.do("tensor", fnb, reads=[srow.t, cm.ones_row_t], writes=[bcb_t],
                 tag=name + f"b{qc}{hp}")
            P.do("scalar", lambda e, o=bcf.ap, i=bcb: e.copy(out=o, in_=i),
                 reads=[bcb_t], writes=[bcf.t], tag=name + f"c{qc}{hp}")
            P.do("vector", lambda e, o=bcf.ap, i=bcf.ap: e.reciprocal(out=o, in_=i),
                 reads=[bcf.t], writes=[bcf.t], tag=name + f"v{qc}{hp}")
            ydst = ybuf.ap[:, hp * 512:(hp + 1) * 512]
            P.do("vector", lambda e, o=ydst, a=av_ap, d=bcf.ap:
                 e.tensor_mul(out=o, in0=a, in1=d),
                 reads=[av_t, bcf.t], writes=[ybuf.t], tag=name + f"y{qc}{hp}")
            ps.release(av_i)
            for _, smt, smi in sm:
                ps.release(smi)

        for qs in range(4):
            gt = b * TPB + qc * 4 + qs
            s = sl[qs % 2]
            P.dma("sync", s["xr"].ap, xin_ap(gt), writes=[s["xr"].t],
                  reads=[xin_tiles[gt]], tag=name + f"zl{qc}{qs}")
            ob = outs[qs % 2]
            for oc in range(2):
                bank, bank_t = ps.get()
                mms = [(bank, cm.ones_row[0:1, :], bp.ap[0:1, oc * 512:(oc + 1) * 512], None)]
                for hp in range(8):
                    mms.append((bank, ybuf.ap[:, hp * 512 + qs * 128: hp * 512 + qs * 128 + 128],
                                wp.ap[:, hp * C + oc * 512: hp * C + (oc + 1) * 512], None))
                mm_group(P, bank, bank_t, mms,
                         reads=[ybuf.t, wp.t, bp.t, cm.ones_row_t],
                         tag=name + f"j{qc}{qs}{oc}")
                dst = ob.ap[:, oc * 512:(oc + 1) * 512]
                P.do("vector", lambda e, o=dst, i=bank,
                     x=s["xr"].ap[:, oc * 512:(oc + 1) * 512]:
                     e.tensor_add(out=o, in0=i, in1=x),
                     reads=[bank_t, s["xr"].t], writes=[ob.t], tag=name + f"r{qc}{qs}{oc}")
            P.dma("scalar", xout_ap(gt), ob.ap, writes=[xout_tiles[gt]],
                  reads=[ob.t], tag=name + f"zs{qc}{qs}")

    for bb2 in (qfm, kfm, vtm, ybuf, srow, wp, bp, attf, mneg, mrowb):
        AB.freeb(bb2)
    for s in sl:
        AF32.freeb(s["xr"])
        AF32.freeb(s["st"])
    for bb2 in (mcol, mtmp, bcf, mbias):
        AF32.freeb(bb2)
    for o in outs:
        AF32.freeb(o)
    if alibi:
        for z in range(2):
            AF32.freeb(bandb[z])
            AF32.freeb(srsl[z])
            AF32.freeb(ntsl[z])
        AF32.freeb(onesf)




def emit_l0_ffn(P, AB, AF32, AU8, ps, cm, nc, io, wdr):
    name = "l0f"
    xin_ap, xin_tiles = io["xin"]
    xout_ap, xout_tiles = io["xout"]

    scol = AF32.alloc(name + "_sc", 8, dma_slot=True)
    sc2 = AF32.alloc(name + "_sc2", 8)
    for i, nsc in enumerate(("s0", "s1", "s2")):
        P.dma("sync", scol.ap[:, i:i + 1], wdr["l0f_" + nsc], writes=[scol.t],
              tag=name + f"scl{i}")
    P.do("vector", lambda e: e.tensor_scalar(out=sc2.ap[:, 0:3], in0=scol.ap[:, 0:3],
                                             scalar1=2.0, scalar2=None, op0=ALU.mult),
         reads=[scol.t], writes=[sc2.t], tag=name + "sc2")
    s0c = sc2.ap[:, 0:1]
    s1c = sc2.ap[:, 1:2]
    s2c = sc2.ap[:, 2:3]

    sq_sh = AF32.alloc(name + "_sq", 1024)
    sl = []
    for i in range(2):
        sl.append({
            "xr": AF32.alloc(name + f"_xr{i}", 1024, dma_slot=True),
            "st": AF32.alloc(name + f"_st{i}", 32),
            "sq": sq_sh,
            "xn": AB.alloc(name + f"_xn{i}", 1024),
        })
    outs = [AF32.alloc(name + f"_o{i}", 1024, dma_slot=True) for i in range(2)]

    TC = 1024
    NCH = NT // TC

    for ch in range(NCH):
        h0 = AB.alloc(name + f"_h0_{ch}", 32 * TC)
        xnT = AB.alloc(name + f"_xnT{ch}", 8 * TC)
        for lt in range(TC // 128):
            gt = ch * (TC // 128) + lt
            s = sl[lt % 2]
            P.dma("sync", s["xr"].ap, xin_ap(gt), writes=[s["xr"].t],
                  reads=[xin_tiles[gt]], tag=name + f"xl{gt}")
            ln_tile(P, cm, s, s["xr"].ap, s["xr"].t, name + f"ln{gt}")
            dv = xnT.ap.rearrange("p (c t) -> p c t", t=TC)[:, :, lt * 128:(lt + 1) * 128]
            transpose_to(P, ps, cm, s["xn"].ap, s["xn"].t, None, xnT.t, 1024,
                         name + f"xT{gt}", dst_view=dv)
        xnTv = xnT.ap.rearrange("p (c t) -> p c t", t=TC)

        dstg = AU8.alloc(name + "_dstg", 2 * 1024, dma_slot=True)
        dbf = AB.alloc(name + "_dbf", 8 * 128)
        h0v = h0.ap.rearrange("p (o t) -> p o t", t=TC)
        for o1 in range(32):
            P.dma("sync", dstg.ap[:, 0:1024].rearrange("p (c o) -> p c o", o=128),
                  wdr["l0f_k0t"].rearrange("(c p) o -> p c o", p=128)[:, :, o1 * 128:(o1 + 1) * 128],
                  writes=[dstg.t], tag=name + f"k0t{ch}_{o1}")
            P.dma("sync", dstg.ap[:, 1024:2048].rearrange("p (c o) -> p c o", o=128),
                  wdr["l0f_k0b"].rearrange("(c p) o -> p c o", p=128)[:, :, o1 * 128:(o1 + 1) * 128],
                  writes=[dstg.t], tag=name + f"k0b{ch}_{o1}")
            P.do("gpsimd", lambda e, o=dbf.ap, a=dstg.ap[:, 0:1024], b=dstg.ap[:, 1024:2048]:
                 e.tensor_sub(out=o, in0=a, in1=b),
                 reads=[dstg.t], writes=[dbf.t], tag=name + f"dcv{ch}_{o1}")
            for ts in range(TC // 512):
                bank, bank_t = ps.get()
                mms = []
                for ct in range(8):
                    mms.append((bank, dbf.ap[:, ct * 128:(ct + 1) * 128],
                                xnTv[:, ct, ts * 512:(ts + 1) * 512], None))
                mm_group(P, bank, bank_t, mms, reads=[dbf.t, xnT.t],
                         tag=name + f"m1_{ch}_{o1}_{ts}")
                dst = h0v[:, o1, ts * 512:(ts + 1) * 512]
                P.do("vector", lambda e, o=dst, i=bank:
                     e.tensor_scalar(out=o, in0=i, scalar1=s0c, scalar2=0.0,
                                     op0=ALU.mult, op1=ALU.max),
                     reads=[bank_t, sc2.t], writes=[h0.t], tag=name + f"e1_{ch}_{o1}_{ts}")
        AU8.freeb(dstg)
        AB.freeb(dbf)
        AB.freeb(xnT)
        h1 = AB.alloc(name + f"_h1_{ch}", 32 * TC)

        kstg = AU8.alloc(name + "_kstg", 4 * 512, dma_slot=True)
        kbf = AB.alloc(name + "_kbf", 4 * 512)
        h1v = h1.ap.rearrange("p (o t) -> p o t", t=TC)
        k1r = wdr["l0f_k1"].rearrange("(c p) o -> p c o", p=128)
        for o2 in range(8):
            bank2 = [ps.get() for _ in range(8)]
            for cg in range(8):
                P.dma("sync", kstg.ap.rearrange("p (c o) -> p c o", o=512),
                      k1r[:, cg * 4:(cg + 1) * 4, o2 * 512:(o2 + 1) * 512],
                      writes=[kstg.t], tag=name + f"k1{ch}_{o2}_{cg}")
                P.do("gpsimd", lambda e, o=kbf.ap, i=kstg.ap:
                     e.tensor_scalar(out=o, in0=i, scalar1=2.0, scalar2=1.0,
                                     op0=ALU.mult, op1=ALU.subtract),
                     reads=[kstg.t], writes=[kbf.t], tag=name + f"kc{ch}_{o2}_{cg}")
                for ci in range(4):
                    c = cg * 4 + ci
                    for ts in range(TC // 512):
                        for ot in range(4):
                            bank, bank_t = bank2[ts * 4 + ot]
                            P.do("tensor", lambda e, o=bank,
                                 l=kbf.ap[:, ci * 512 + ot * 128: ci * 512 + ot * 128 + 128],
                                 r=h0v[:, c, ts * 512:(ts + 1) * 512],
                                 s1_=(c == 0), s2_=(c == 31):
                                 e.matmul(out=o, lhsT=l, rhs=r, start=s1_, stop=s2_,
                                          skip_group_check=True),
                                 reads=[kbf.t, h0.t], writes=[bank_t],
                                 tag=name + f"m2_{ch}_{o2}_{ts}_{c}_{ot}")
            for ts in range(TC // 512):
                for ot in range(4):
                    bank, bank_t = bank2[ts * 4 + ot]
                    dst = h1v[:, o2 * 4 + ot, ts * 512:(ts + 1) * 512]
                    P.do("vector", lambda e, o=dst, i=bank:
                         e.tensor_scalar(out=o, in0=i, scalar1=s1c, scalar2=0.0,
                                         op0=ALU.mult, op1=ALU.max),
                         reads=[bank_t, sc2.t], writes=[h1.t],
                         tag=name + f"e2_{ch}_{o2}_{ts}_{ot}")
        AU8.freeb(kstg)
        AB.freeb(kbf)
        AB.freeb(h0)

        k2stg = AU8.alloc(name + "_k2stg", 8 * 1024, dma_slot=True)
        k2bf = AB.alloc(name + "_k2bf", 32 * 1024)
        k2r = wdr["l0f_k2"].rearrange("(c p) o -> p c o", p=128)
        for qtr in range(4):
            P.dma("sync", k2stg.ap.rearrange("p (c o) -> p c o", o=1024),
                  k2r[:, qtr * 8:(qtr + 1) * 8, :],
                  writes=[k2stg.t], tag=name + f"k2l{ch}_{qtr}")
            P.do("gpsimd", lambda e, o=k2bf.ap[:, qtr * 8192:(qtr + 1) * 8192],
                 i=k2stg.ap:
                 e.tensor_scalar(out=o, in0=i, scalar1=2.0, scalar2=1.0,
                                 op0=ALU.mult, op1=ALU.subtract),
                 reads=[k2stg.t], writes=[k2bf.t], tag=name + f"k2c{ch}_{qtr}")
        k2v = k2bf.ap.rearrange("p (c o) -> p c o", o=1024)
        for lt in range(TC // 128):
            gt = ch * (TC // 128) + lt
            s = sl[lt % 2]
            P.dma("sync", s["xr"].ap, xin_ap(gt), writes=[s["xr"].t],
                  reads=[xin_tiles[gt]], tag=name + f"x3{gt}")
            ob = outs[lt % 2]
            for oc in range(2):
                bank, bank_t = ps.get()
                mms = []
                for c in range(32):
                    mms.append((bank, h1v[:, c, lt * 128:(lt + 1) * 128],
                                k2v[:, c, oc * 512:(oc + 1) * 512], None))
                mm_group(P, bank, bank_t, mms, reads=[h1.t, k2bf.t],
                         tag=name + f"m3_{gt}_{oc}")
                dst = ob.ap[:, oc * 512:(oc + 1) * 512]
                P.do("vector", lambda e, o=dst, i=bank,
                     x=s["xr"].ap[:, oc * 512:(oc + 1) * 512]:
                     e.scalar_tensor_tensor(out=o, in0=i, scalar=s2c, in1=x,
                                            op0=ALU.mult, op1=ALU.add),
                     reads=[bank_t, sc2.t, s["xr"].t], writes=[ob.t],
                     tag=name + f"e3_{gt}_{oc}")
            P.dma("scalar", xout_ap(gt), ob.ap, writes=[xout_tiles[gt]],
                  reads=[ob.t], tag=name + f"zs{gt}")
        AU8.freeb(k2stg)
        AB.freeb(k2bf)
        AB.freeb(h1)

    AF32.freeb(sq_sh)
    for s in sl:
        AF32.freeb(s["xr"])
        AF32.freeb(s["st"])
        AB.freeb(s["xn"])
    for o in outs:
        AF32.freeb(o)
    AF32.freeb(scol)
    AF32.freeb(sc2)




def emit_l1_attn(P, AB, AF32, AU8, ps, cm, nc, io, wdr):
    for b in range(BPC):
        _l1_attn_batch(P, AB, AF32, ps, cm, nc, io, b, wdr, name=f"l1a{b}")


def _l1_attn_batch(P, AB, AF32, ps, cm, nc, io, b, wdr, name):
    xin_ap, xin_tiles = io["xin"]
    xout_ap, xout_tiles = io["xout"]

    qfm = AB.alloc(name + "_qfm", 8 * T)
    kfm = AB.alloc(name + "_kfm", 8 * T)
    vtm = AB.alloc(name + "_vtm", TPB * 1024)

    wq = load_weight_bf16(P, AB, AF32, wdr["l1a_wqT"], C, C, name + "_wq")
    wk = load_weight_bf16(P, AB, AF32, wdr["l1a_wkT"], C, C, name + "_wk")
    wv = load_weight_bf16(P, AB, AF32, wdr["l1a_wvT"], C, C, name + "_wv")
    bq = load_bias_row(P, AB, AF32, wdr["l1a_bq"], C, name + "_bq")
    bk = load_bias_row(P, AB, AF32, wdr["l1a_bk"], C, name + "_bk")
    bv = load_bias_row(P, AB, AF32, wdr["l1a_bv"], C, name + "_bv")
    wmap = {"q": (wq, bq), "k": (wk, bk), "v": (wv, bv)}

    sq_sh = AF32.alloc(name + "_sq", 1024)
    sl = []
    for i in range(2):
        sl.append({
            "xr": AF32.alloc(name + f"_xr{i}", 1024, dma_slot=True),
            "st": AF32.alloc(name + f"_st{i}", 32),
            "sq": sq_sh,
            "xn": AB.alloc(name + f"_xn{i}", 1024),
            "xnT": AB.alloc(name + f"_xnT{i}", 1024),
            "qb": AB.alloc(name + f"_qb{i}", 1024),
            "kb": AB.alloc(name + f"_kb{i}", 1024),
        })
    r1 = AF32.alloc(name + "_r1", 256)
    r2 = AF32.alloc(name + "_r2", 256)
    csl = [AF32.alloc(name + f"_cs{i}", 512, dma_slot=True) for i in range(2)]
    snl = [AF32.alloc(name + f"_sn{i}", 512, dma_slot=True) for i in range(2)]
    cosd, sind = wdr["rope1_cos"], wdr["rope1_sin"]

    def proj(which, s, tt, dstb, ropecs, tagp):
        wb, bb = wmap[which]
        for oc in range(2):
            bank, bank_t = ps.get()
            mms = [(bank, cm.ones_row[0:1, :], bb.ap[0:1, oc * 512:(oc + 1) * 512], None)]
            for ct in range(8):
                mms.append((bank, s["xnT"].ap[:, ct * 128:(ct + 1) * 128],
                            wb.ap[:, ct * C + oc * 512: ct * C + (oc + 1) * 512], None))
            mm_group(P, bank, bank_t, mms,
                     reads=[s["xnT"].t, wb.t, bb.t, cm.ones_row_t], tag=tagp + f"m{oc}")
            if dstb is None:
                dst = vtm.ap[:, tt * 1024 + oc * 512: tt * 1024 + (oc + 1) * 512]
                P.do("scalar", lambda e, o=dst, i=bank: e.copy(out=o, in_=i),
                     reads=[bank_t], writes=[vtm.t], tag=tagp + f"e{oc}")
            else:
                cs, sn = ropecs
                pe = bank[:, 0:512:2]
                po = bank[:, 1:512:2]
                css = cs.ap[:, oc * 256:(oc + 1) * 256]
                sns = sn.ap[:, oc * 256:(oc + 1) * 256]
                de = dstb.ap[:, oc * 512:(oc + 1) * 512][:, 0::2]
                do_ = dstb.ap[:, oc * 512:(oc + 1) * 512][:, 1::2]
                P.do("vector", lambda e, pe=pe, c=css: e.tensor_mul(out=r1.ap, in0=pe, in1=c),
                     reads=[bank_t, cs.t], writes=[r1.t], tag=tagp + f"ra{oc}")
                P.do("vector", lambda e, po=po, sx=sns: e.tensor_mul(out=r2.ap, in0=po, in1=sx),
                     reads=[bank_t, sn.t], writes=[r2.t], tag=tagp + f"rb{oc}")
                P.do("vector", lambda e, de=de: e.tensor_sub(out=de, in0=r1.ap, in1=r2.ap),
                     reads=[r1.t, r2.t], writes=[dstb.t], tag=tagp + f"rc{oc}")
                P.do("vector", lambda e, po=po, c=css: e.tensor_mul(out=r1.ap, in0=po, in1=c),
                     reads=[bank_t, cs.t], writes=[r1.t], tag=tagp + f"rd{oc}")
                P.do("vector", lambda e, pe=pe, sx=sns: e.tensor_mul(out=r2.ap, in0=pe, in1=sx),
                     reads=[bank_t, sn.t], writes=[r2.t], tag=tagp + f"re{oc}")
                P.do("vector", lambda e, do_=do_: e.tensor_add(out=do_, in0=r1.ap, in1=r2.ap),
                     reads=[r1.t, r2.t], writes=[dstb.t], tag=tagp + f"rf{oc}")

    for tt in range(TPB):
        gt = b * TPB + tt
        s = sl[tt % 2]
        P.dma("sync", s["xr"].ap, xin_ap(gt), writes=[s["xr"].t],
              reads=[xin_tiles[gt]], tag=name + f"xl{tt}")
        ln_tile(P, cm, s, s["xr"].ap, s["xr"].t, name + f"ln{tt}")
        transpose_to(P, ps, cm, s["xn"].ap, s["xn"].t, s["xnT"].ap, s["xnT"].t,
                     1024, name + f"xT{tt}")
        cs, sn = csl[tt % 2], snl[tt % 2]
        P.dma("sync", cs.ap, cosd[tt * 128:(tt + 1) * 128, :], writes=[cs.t],
              tag=name + f"cl{tt}")
        P.dma("sync", sn.ap, sind[tt * 128:(tt + 1) * 128, :], writes=[sn.t],
              tag=name + f"sl{tt}")
        proj("v", s, tt, None, None, name + f"v{tt}")
        proj("q", s, tt, s["qb"], (cs, sn), name + f"q{tt}")
        proj("k", s, tt, s["kb"], (cs, sn), name + f"k{tt}")
        for (bb_, fm) in ((s["qb"], qfm), (s["kb"], kfm)):
            dv = fm.ap.rearrange("p (c t) -> p c t", t=T)[:, :, tt * 128:(tt + 1) * 128]
            transpose_to(P, ps, cm, bb_.ap, bb_.t, None, fm.t, 1024,
                         name + ("q" if fm is qfm else "k") + f"F{tt}", dst_view=dv)

    for wbuf in (wq, wk, wv, bq, bk, bv):
        AB.freeb(wbuf)
    for bufl in (csl, snl):
        for bf in bufl:
            AF32.freeb(bf)

    attb = AB.alloc(name + "_attb", TPB * 512)
    mcol = AF32.alloc(name + "_mcol", 4)
    negm = AB.alloc(name + "_negm", 4)
    mtmp = AF32.alloc(name + "_mtmp", 4)
    mrow = AB.alloc(name + "_mrow", 512)
    srow = AB.alloc(name + "_srow", 512)
    rcol = AF32.alloc(name + "_rcol", 4)
    outs = [AF32.alloc(name + f"_o{i}", 1024, dma_slot=True) for i in range(2)]

    qv = qfm.ap.rearrange("p (c t) -> p c t", t=T)
    kv = kfm.ap.rearrange("p (c t) -> p c t", t=T)
    attv = attb.ap.rearrange("p (k t) -> p k t", t=512)

    for qc in range(4):
        for qs in range(4):
            qt0 = qc * 512 + qs * 128
            mc = mcol.ap[:, qs:qs + 1]
            for kc in range(4):
                bank, bank_t = ps.get()
                mms = [(bank, qv[:, ct, qt0:qt0 + 128],
                        kv[:, ct, kc * 512:(kc + 1) * 512], None) for ct in range(8)]
                mm_group(P, bank, bank_t, mms, reads=[qfm.t, kfm.t],
                         tag=name + f"x{qc}{qs}{kc}")
                if kc == 0:
                    P.do("vector", lambda e, o=mc, i=bank:
                         e.reduce_max(out=o, in_=i, axis=AX.X),
                         reads=[bank_t], writes=[mcol.t], tag=name + f"xm{qc}{qs}{kc}")
                else:
                    P.do("vector", lambda e, o=mtmp.ap[:, 0:1], i=bank:
                         e.reduce_max(out=o, in_=i, axis=AX.X),
                         reads=[bank_t], writes=[mtmp.t], tag=name + f"xm{qc}{qs}{kc}")
                    P.do("vector", lambda e, o=mc, i=mtmp.ap[:, 0:1]:
                         e.tensor_max(out=o, in0=o, in1=i),
                         reads=[mtmp.t, mcol.t], writes=[mcol.t], tag=name + f"xx{qc}{qs}{kc}")
        P.do("vector", lambda e: e.tensor_scalar(out=negm.ap, in0=mcol.ap,
                                                 scalar1=-1.0, scalar2=None, op0=ALU.mult),
             reads=[mcol.t], writes=[negm.t], tag=name + f"ng{qc}")
        bank, bank_t = ps.get()
        def fn(e, bank=bank):
            ins = None
            for qs in range(4):
                ins = e.matmul(out=bank[0:1, qs * 128:(qs + 1) * 128],
                               lhsT=negm.ap[:, qs:qs + 1], rhs=cm.ident,
                               start=True, stop=True)
            return ins
        P.do("tensor", fn, reads=[negm.t, cm.ident_t], writes=[bank_t],
             tag=name + f"mt{qc}")
        P.do("scalar", lambda e, o=mrow.ap[0:1, :], i=bank[0:1, 0:512]:
             e.copy(out=o, in_=i),
             reads=[bank_t], writes=[mrow.t], tag=name + f"me{qc}")

        smb, smb_t, smb_i = ps.take()
        for kt in range(TPB):
            bank, bank_t = ps.get()
            mms = [(bank, kv[:, ct, kt * 128:(kt + 1) * 128],
                    qv[:, ct, qc * 512:(qc + 1) * 512], None) for ct in range(8)]
            mms.append((bank, cm.ones_row[0:1, :], mrow.ap[0:1, :], None))
            mm_group(P, bank, bank_t, mms,
                     reads=[qfm.t, kfm.t, mrow.t, cm.ones_row_t],
                     tag=name + f"s{qc}{kt}")
            att = attv[:, kt, :]
            P.do("scalar", lambda e, o=att, i=bank:
                 e.activation(out=o, in_=i, func=AF.Exp, scale=0.03125),
                 reads=[bank_t], writes=[attb.t], tag=name + f"p{qc}{kt}")
            P.do("tensor", lambda e, o=smb[0:1, :], r=att, s1_=(kt == 0),
                 s2_=(kt == TPB - 1):
                 e.matmul(out=o, lhsT=cm.ones_col, rhs=r, start=s1_, stop=s2_,
                          skip_group_check=True),
                 reads=[attb.t, cm.ones_col_t], writes=[smb_t],
                 tag=name + f"u{qc}{kt}")
        P.do("scalar", lambda e, o=srow.ap[0:1, :], i=smb[0:1, :]: e.copy(out=o, in_=i),
             reads=[smb_t], writes=[srow.t], tag=name + f"w{qc}")
        ps.release(smb_i)
        bank, bank_t = ps.get()
        def fns(e, bank=bank):
            ins = None
            for qs in range(4):
                ins = e.matmul(out=bank[:, qs:qs + 1],
                               lhsT=srow.ap[0:1, qs * 128:(qs + 1) * 128],
                               rhs=cm.ones_row[0:1, 0:1], start=True, stop=True)
            return ins
        P.do("tensor", fns, reads=[srow.t, cm.ones_row_t], writes=[bank_t],
             tag=name + f"rt{qc}")
        P.do("vector", lambda e, o=rcol.ap[:, 0:4], i=bank[:, 0:4]: e.reciprocal(out=o, in_=i),
             reads=[bank_t], writes=[rcol.t], tag=name + f"rc{qc}")

        for qs in range(4):
            gt = b * TPB + qc * 4 + qs
            s = sl[qs % 2]
            P.dma("sync", s["xr"].ap, xin_ap(gt), writes=[s["xr"].t],
                  reads=[xin_tiles[gt]], tag=name + f"zl{qc}{qs}")
            ob = outs[qs % 2]
            for oc in range(2):
                bank, bank_t = ps.get()
                mms = [(bank, attv[:, kt, qs * 128:(qs + 1) * 128],
                        vtm.ap[:, kt * 1024 + oc * 512: kt * 1024 + (oc + 1) * 512],
                        None) for kt in range(TPB)]
                mm_group(P, bank, bank_t, mms, reads=[attb.t, vtm.t],
                         tag=name + f"a{qc}{qs}{oc}")
                dst = ob.ap[:, oc * 512:(oc + 1) * 512]
                P.do("vector", lambda e, o=dst, i=bank, rc=rcol.ap[:, qs:qs + 1],
                     x=s["xr"].ap[:, oc * 512:(oc + 1) * 512]:
                     e.scalar_tensor_tensor(out=o, in0=i, scalar=rc, in1=x,
                                            op0=ALU.mult, op1=ALU.add),
                     reads=[bank_t, rcol.t, s["xr"].t], writes=[ob.t],
                     tag=name + f"r{qc}{qs}{oc}")
            P.dma("scalar", xout_ap(gt), ob.ap, writes=[xout_tiles[gt]],
                  reads=[ob.t], tag=name + f"zs{qc}{qs}")

    for bb2 in (qfm, kfm, vtm, attb, negm, mrow, srow):
        AB.freeb(bb2)
    for s in sl:
        AF32.freeb(s["xr"])
        AF32.freeb(s["st"])
        for k in ("xn", "xnT", "qb", "kb"):
            AB.freeb(s[k])
    AF32.freeb(sq_sh)
    for bb2 in (mcol, mtmp, rcol, r1, r2):
        AF32.freeb(bb2)
    for o in outs:
        AF32.freeb(o)




def emit_l1_ffn(P, AB, AF32, AU8, ps, cm, nc, io, wdr):
    name = "l1f"
    xin_ap, xin_tiles = io["xin"]
    xout_ap, xout_tiles = io["xout"]

    wg = AB.alloc(name + "_wg", 8 * 8)
    wgs = AF32.alloc(name + "_wgs", 64, dma_slot=True)
    P.dma("sync", wgs.ap[:, 0:64].rearrange("p (c o) -> p c o", o=8),
          wdr["l1f_Wg"].rearrange("(c p) o -> p c o", p=128), writes=[wgs.t],
          tag=name + "wgl")
    P.do("vector", lambda e: e.tensor_copy(out=wg.ap, in_=wgs.ap[:, 0:64]),
         reads=[wgs.t], writes=[wg.t], tag=name + "wgc")
    bg = AB.alloc(name + "_bg", 32)
    bgs = AF32.alloc(name + "_bgs", 32, dma_slot=True)
    P.dma("sync", bgs.ap[0:1, 0:8], wdr["l1f_bg"], writes=[bgs.t], tag=name + "bgl")
    P.do("vector", lambda e: e.tensor_copy(out=bg.ap[0:1, 0:8], in_=bgs.ap[0:1, 0:8]),
         reads=[bgs.t], writes=[bg.t], tag=name + "bgc")
    b1c = AF32.alloc(name + "_b1c", 64, dma_slot=True)
    P.dma("sync", b1c.ap[:, 0:64], wdr["l1f_b1"], writes=[b1c.t],
          tag=name + "b1l")
    b2s = AF32.alloc(name + "_b2s", 1024, dma_slot=True)
    b2b = AB.alloc(name + "_b2b", 1024)
    P.dma("sync", b2s.ap[0:8, :], wdr["l1f_b2"], writes=[b2s.t], tag=name + "b2l")
    P.do("vector", lambda e: e.tensor_copy(out=b2b.ap[0:8, :], in_=b2s.ap[0:8, :]),
         reads=[b2s.t], writes=[b2b.t], tag=name + "b2c")
    AF32.freeb(wgs)
    AF32.freeb(bgs)
    AF32.freeb(b2s)

    sq_sh = AF32.alloc(name + "_sq", 1024)
    sl = []
    for i in range(2):
        sl.append({
            "xr": AF32.alloc(name + f"_xr{i}", 1024, dma_slot=True),
            "st": AF32.alloc(name + f"_st{i}", 32),
            "sq": sq_sh,
            "xn": AB.alloc(name + f"_xn{i}", 1024),
        })
    outs = [AF32.alloc(name + f"_o{i}", 1024, dma_slot=True) for i in range(2)]
    gsc = AF32.alloc(name + "_gsc", 64)
    TC = 2048
    NCH = NT // TC

    for ch in range(NCH):
        ntt = TC // 128
        xnT = AB.alloc(name + f"_xnT{ch}", 8 * TC)
        xnTv = xnT.ap.rearrange("p (c t) -> p c t", t=TC)
        oacc = AB.alloc(name + f"_oacc{ch}", ntt * 1024)
        oaccv = oacc.ap.rearrange("p (t o) -> p t o", o=1024)
        gcol = AF32.alloc(name + f"_gcol{ch}", ntt * 8)
        gbT = AB.alloc(name + f"_gbT{ch}", ntt * 128)

        for lt in range(ntt):
            gt = ch * ntt + lt
            s = sl[lt % 2]
            P.dma("sync", s["xr"].ap, xin_ap(gt), writes=[s["xr"].t],
                  reads=[xin_tiles[gt]], tag=name + f"xl{gt}")
            ln_tile(P, cm, s, s["xr"].ap, s["xr"].t, name + f"ln{gt}")
            dv = xnTv[:, :, lt * 128:(lt + 1) * 128]
            transpose_to(P, ps, cm, s["xn"].ap, s["xn"].t, None, xnT.t, 1024,
                         name + f"xT{gt}", dst_view=dv)
            bank, bank_t = ps.get()
            mms = [(bank[:, 0:8], cm.ones_row[0:1, :], bg.ap[0:1, 0:8], None)]
            for ct in range(8):
                mms.append((bank[:, 0:8], xnTv[:, ct, lt * 128:(lt + 1) * 128],
                            wg.ap[:, ct * 8:(ct + 1) * 8], None))
            mm_group(P, bank, bank_t, mms, reads=[xnT.t, wg.t, bg.t, cm.ones_row_t],
                     tag=name + f"gm{gt}")
            lg = gsc.ap[:, 0:8]
            ml = gsc.ap[:, 8:9]
            zs = gsc.ap[:, 16:24]
            ssum = gsc.ap[:, 9:10]
            wcol = gsc.ap[:, 10:11]
            eq = gsc.ap[:, 24:32]
            imax = gsc.ap[:, 11:12]
            P.do("vector", lambda e, o=lg, i=bank[:, 0:8]: e.tensor_copy(out=o, in_=i),
                 reads=[bank_t], writes=[gsc.t], tag=name + f"gc{gt}")
            P.do("vector", lambda e, o=ml, i=lg: e.reduce_max(out=o, in_=i, axis=AX.X),
                 reads=[gsc.t], writes=[gsc.t], tag=name + f"gx{gt}")
            P.do("vector", lambda e, o=ml, i=ml: e.tensor_scalar(
                 out=o, in0=i, scalar1=-1.0, scalar2=None, op0=ALU.mult),
                 reads=[gsc.t], writes=[gsc.t], tag=name + f"gn{gt}")
            P.do("scalar", lambda e, o=zs, i=lg, b_=ml: e.activation(
                 out=o, in_=i, func=AF.Exp, bias=b_),
                 reads=[gsc.t], writes=[gsc.t], tag=name + f"ge{gt}")
            P.do("vector", lambda e, o=ssum, i=zs: e.reduce_sum(out=o, in_=i, axis=AX.X),
                 reads=[gsc.t], writes=[gsc.t], tag=name + f"gs{gt}")
            P.do("vector", lambda e, o=wcol, i=ssum: e.reciprocal(out=o, in_=i),
                 reads=[gsc.t], writes=[gsc.t], tag=name + f"gr{gt}")
            P.do("vector", lambda e, o=eq, i=lg, m_=ml: e.tensor_scalar(
                 out=o, in0=i, scalar1=m_, scalar2=-1.0, op0=ALU.add, op1=ALU.mult),
                 reads=[gsc.t], writes=[gsc.t], tag=name + f"gq{gt}")
            P.do("vector", lambda e, o=eq, i=eq: e.tensor_scalar(
                 out=o, in0=i, scalar1=0.0, scalar2=None, op0=ALU.is_equal),
                 reads=[gsc.t], writes=[gsc.t], tag=name + f"gqq{gt}")
            P.do("vector", lambda e, o=eq, i=eq: e.tensor_mul(
                 out=o, in0=i, in1=cm.iota8r[:, 0:8]),
                 reads=[gsc.t, cm.iota8r_t], writes=[gsc.t], tag=name + f"gi{gt}")
            P.do("vector", lambda e, o=imax, i=eq: e.reduce_max(out=o, in_=i, axis=AX.X),
                 reads=[gsc.t], writes=[gsc.t], tag=name + f"gj{gt}")
            gslice = gcol.ap[:, lt * 8:(lt + 1) * 8]
            P.do("vector", lambda e, o=gslice, i=cm.iota8r[:, 0:8], m_=imax:
                 e.tensor_scalar(out=o, in0=i, scalar1=m_, scalar2=None, op0=ALU.is_equal),
                 reads=[gsc.t, cm.iota8r_t], writes=[gcol.t], tag=name + f"gf{gt}")
            P.do("vector", lambda e, o=gslice, i=gslice, m_=wcol:
                 e.tensor_scalar(out=o, in0=i, scalar1=m_, scalar2=None, op0=ALU.mult),
                 reads=[gcol.t, gsc.t], writes=[gcol.t], tag=name + f"gw{gt}")
            gb = gsc.ap[:, 0:8]
            gbb = AB
            P.do("vector", lambda e, o=s["xn"].ap[:, 0:8], i=gslice:
                 e.tensor_copy(out=o, in_=i),
                 reads=[gcol.t], writes=[s["xn"].t], tag=name + f"gb{gt}")
            bank2, bank2_t = ps.get()
            bb2 = bank2.bitcast(BF16)
            P.do("tensor", lambda e, o=bb2[0:8, 0:128], i=s["xn"].ap[:, 0:8]:
                 e.transpose(out=o, in_=i, identity=cm.ident),
                 reads=[s["xn"].t, cm.ident_t], writes=[bank2_t], tag=name + f"gt{gt}")
            P.do("scalar", lambda e, o=gbT.ap[0:8, lt * 128:(lt + 1) * 128],
                 i=bb2[0:8, 0:128]: e.copy(out=o, in_=i),
                 reads=[bank2_t], writes=[gbT.t], tag=name + f"gv{gt}")
            for oc in range(2):
                bank3, bank3_t = ps.get()
                P.do("tensor", lambda e, o=bank3, l=gbT.ap[0:8, lt * 128:(lt + 1) * 128],
                     r=b2b.ap[0:8, oc * 512:(oc + 1) * 512]:
                     e.matmul(out=o, lhsT=l, rhs=r, start=True, stop=True),
                     reads=[gbT.t, b2b.t], writes=[bank3_t], tag=name + f"bi{gt}{oc}")
                P.do("scalar", lambda e, o=oaccv[:, lt, oc * 512:(oc + 1) * 512],
                     i=bank3: e.copy(out=o, in_=i),
                     reads=[bank3_t], writes=[oacc.t], tag=name + f"bo{gt}{oc}")

        hbuf = AB.alloc(name + f"_hb{ch}", 8 * 512)
        hbv = hbuf.ap.rearrange("p (c t) -> p c t", t=512)
        w1b = [AB.alloc(name + f"_w1{ch}{i}", 8 * C) for i in range(2)]
        w2b = [AB.alloc(name + f"_w2{ch}{i}", 8 * C) for i in range(2)]
        wstg = [AF32.alloc(name + f"_ws{ch}{i}", C, dma_slot=True) for i in range(2)]
        load_weight_into(P, w1b[0], wstg[0], wdr["l1f_W1"][0], C, C,
                         name + f"_w1_{ch}_0")
        load_weight_into(P, w2b[0], wstg[1], wdr["l1f_W2"][0], C, C,
                         name + f"_w2_{ch}_0")
        for e_ in range(8):
            w1 = w1b[e_ % 2]
            w2 = w2b[e_ % 2]
            if e_ + 1 < 8:
                load_weight_into(P, w1b[(e_ + 1) % 2], wstg[0],
                                 wdr["l1f_W1"][e_ + 1], C, C,
                                 name + f"_w1_{ch}_{e_ + 1}")
                load_weight_into(P, w2b[(e_ + 1) % 2], wstg[1],
                                 wdr["l1f_W2"][e_ + 1], C, C,
                                 name + f"_w2_{ch}_{e_ + 1}")
            for ts in range(TC // 512):
                for ot in range(8):
                    bank, bank_t = ps.get()
                    mms = []
                    for ct in range(8):
                        mms.append((bank, w1.ap[:, ct * C + ot * 128: ct * C + ot * 128 + 128],
                                    xnTv[:, ct, ts * 512:(ts + 1) * 512], None))
                    mm_group(P, bank, bank_t, mms, reads=[w1.t, xnT.t],
                             tag=name + f"h{ch}{e_}{ts}{ot}")
                    P.do("scalar", lambda e, o=hbv[:, ot, :], i=bank,
                         b_=b1c.ap[:, e_ * 8 + ot: e_ * 8 + ot + 1]:
                         e.activation(out=o, in_=i, func=AF.Gelu, bias=b_),
                         reads=[bank_t, b1c.t], writes=[hbuf.t],
                         tag=name + f"hg{ch}{e_}{ts}{ot}")
                for tq in range(4):
                    lt = ts * 4 + tq
                    gt = ch * ntt + lt
                    for oc in range(2):
                        bank, bank_t = ps.get()
                        mms = []
                        for ht in range(8):
                            mms.append((bank, hbv[:, ht, tq * 128:(tq + 1) * 128],
                                        w2.ap[:, ht * C + oc * 512: ht * C + (oc + 1) * 512],
                                        None))
                        mm_group(P, bank, bank_t, mms, reads=[hbuf.t, w2.t],
                                 tag=name + f"o{ch}{e_}{lt}{oc}")
                        dst = oaccv[:, lt, oc * 512:(oc + 1) * 512]
                        P.do("vector", lambda e, o=dst, i=bank,
                             g_=gcol.ap[:, lt * 8 + e_: lt * 8 + e_ + 1]:
                             e.scalar_tensor_tensor(out=o, in0=i, scalar=g_, in1=o,
                                                    op0=ALU.mult, op1=ALU.add),
                             reads=[bank_t, gcol.t, oacc.t], writes=[oacc.t],
                             tag=name + f"oa{ch}{e_}{lt}{oc}")
        for i in range(2):
            AB.freeb(w1b[i])
            AB.freeb(w2b[i])
            AF32.freeb(wstg[i])
        AB.freeb(hbuf)

        for lt in range(ntt):
            gt = ch * ntt + lt
            s = sl[lt % 2]
            P.dma("sync", s["xr"].ap, xin_ap(gt), writes=[s["xr"].t],
                  reads=[xin_tiles[gt]], tag=name + f"z{gt}")
            ob = outs[lt % 2]
            P.do("vector", lambda e, o=ob.ap, a=oaccv[:, lt, :], x=s["xr"].ap:
                 e.tensor_add(out=o, in0=a, in1=x),
                 reads=[oacc.t, s["xr"].t], writes=[ob.t], tag=name + f"ra{gt}")
            P.dma("scalar", xout_ap(gt), ob.ap, writes=[xout_tiles[gt]],
                  reads=[ob.t], tag=name + f"zs{gt}")
        AB.freeb(xnT)
        AB.freeb(oacc)
        AB.freeb(gbT)
        AF32.freeb(gcol)

    for bb3 in (wg, bg, b2b):
        AB.freeb(bb3)
    AF32.freeb(b1c)
    AF32.freeb(sq_sh)
    AF32.freeb(gsc)
    for s in sl:
        AF32.freeb(s["xr"])
        AF32.freeb(s["st"])
        AB.freeb(s["xn"])
    for o in outs:
        AF32.freeb(o)




def emit_l2_attn(P, AB, AF32, AU8, ps, cm, nc, io, wdr):
    for b in range(BPC):
        _softmax_attn_batch(
            P, AB, AF32, ps, cm, nc, io, b, name=f"l2a{b}",
            wspec=dict(kind="fused", wqkv=("l2a_wqkv", "l2a_bqkv"),
                       wp=("l2a_wp", "l2a_bp")),
            wdr=wdr, rope=None, causal=False, alibi=True, scale=0.125)




def emit_l2_ffn(P, AB, AF32, AU8, ps, cm, nc, io, wdr):
    name = "l2f"
    xin_ap, xin_tiles = io["xin"]
    xout_ap, xout_tiles = io["xout"]

    wg = AB.alloc(name + "_wg", 64)
    wgs = AF32.alloc(name + "_wgs", 64, dma_slot=True)
    P.dma("sync", wgs.ap[:, 0:64].rearrange("p (c o) -> p c o", o=8),
          wdr["l2f_gateT"].rearrange("(c p) o -> p c o", p=128), writes=[wgs.t],
          tag=name + "wgl")
    P.do("vector", lambda e: e.tensor_copy(out=wg.ap[:, 0:64], in_=wgs.ap[:, 0:64]),
         reads=[wgs.t], writes=[wg.t], tag=name + "wgc")
    AF32.freeb(wgs)

    sq_sh = AF32.alloc(name + "_sq", 1024)
    sl = []
    for i in range(2):
        sl.append({
            "xr": AF32.alloc(name + f"_xr{i}", 1024, dma_slot=True),
            "st": AF32.alloc(name + f"_st{i}", 32),
            "sq": sq_sh,
            "xn": AB.alloc(name + f"_xn{i}", 1024),
        })
    outs = [AF32.alloc(name + f"_o{i}", 1024, dma_slot=True) for i in range(2)]
    gsc = AF32.alloc(name + "_gsc", 32)
    bcf = AF32.alloc(name + "_bcf", 512)
    gsf = AF32.alloc(name + "_gsf", 512)
    sgf = AF32.alloc(name + "_sgf", 512)
    usf = AB.alloc(name + "_usf", 512)
    TC = 1024
    NCH = NT // TC

    for ch in range(NCH):
        ntt = TC // 128
        xnT = AB.alloc(name + f"_xnT{ch}", 8 * TC)
        xnTv = xnT.ap.rearrange("p (c t) -> p c t", t=TC)
        srow = AB.alloc(name + f"_srow{ch}", ntt * 128)

        for lt in range(ntt):
            gt = ch * ntt + lt
            s = sl[lt % 2]
            P.dma("sync", s["xr"].ap, xin_ap(gt), writes=[s["xr"].t],
                  reads=[xin_tiles[gt]], tag=name + f"xl{gt}")
            ln_tile(P, cm, s, s["xr"].ap, s["xr"].t, name + f"ln{gt}")
            dv = xnTv[:, :, lt * 128:(lt + 1) * 128]
            transpose_to(P, ps, cm, s["xn"].ap, s["xn"].t, None, xnT.t, 1024,
                         name + f"xT{gt}", dst_view=dv)
        for lt in range(ntt):
            gt = ch * ntt + lt
            s = sl[lt % 2]
            bank, bank_t = ps.get()
            mms = []
            for ct in range(8):
                mms.append((bank[:, 0:8], xnTv[:, ct, lt * 128:(lt + 1) * 128],
                            wg.ap[:, ct * 8:(ct + 1) * 8], None))
            mm_group(P, bank, bank_t, mms, reads=[xnT.t, wg.t], tag=name + f"gm{gt}")
            lg = gsc.ap[:, 0:8]
            ml = gsc.ap[:, 8:9]
            rw = gsc.ap[:, 9:10]
            eq = gsc.ap[:, 16:24]
            imax = gsc.ap[:, 10:11]
            fsc = gsc.ap[:, 24:32]
            P.do("vector", lambda e, o=lg, i=bank[:, 0:8]: e.tensor_copy(out=o, in_=i),
                 reads=[bank_t], writes=[gsc.t], tag=name + f"gc{gt}")
            P.do("vector", lambda e, o=ml, i=lg: e.reduce_max(out=o, in_=i, axis=AX.X),
                 reads=[gsc.t], writes=[gsc.t], tag=name + f"gx{gt}")
            P.do("scalar", lambda e, o=rw, i=ml: e.activation(out=o, in_=i, func=AF.Sigmoid),
                 reads=[gsc.t], writes=[gsc.t], tag=name + f"gw{gt}")
            P.do("vector", lambda e, o=eq, i=lg, m_=ml: e.tensor_scalar(
                 out=o, in0=i, scalar1=m_, scalar2=None, op0=ALU.is_equal),
                 reads=[gsc.t], writes=[gsc.t], tag=name + f"gq{gt}")
            P.do("vector", lambda e, o=eq, i=eq: e.tensor_mul(
                 out=o, in0=i, in1=cm.iota8r[:, 0:8]),
                 reads=[gsc.t, cm.iota8r_t], writes=[gsc.t], tag=name + f"gi{gt}")
            P.do("vector", lambda e, o=imax, i=eq: e.reduce_max(out=o, in_=i, axis=AX.X),
                 reads=[gsc.t], writes=[gsc.t], tag=name + f"gj{gt}")
            P.do("vector", lambda e, o=fsc, i=cm.iota8r[:, 0:8], m_=imax:
                 e.tensor_scalar(out=o, in0=i, scalar1=m_, scalar2=None, op0=ALU.is_equal),
                 reads=[gsc.t, cm.iota8r_t], writes=[gsc.t], tag=name + f"gf{gt}")
            P.do("vector", lambda e, o=fsc, i=fsc, m_=rw:
                 e.tensor_scalar(out=o, in0=i, scalar1=m_, scalar2=None, op0=ALU.mult),
                 reads=[gsc.t], writes=[gsc.t], tag=name + f"gs{gt}")
            P.do("vector", lambda e, o=s["xn"].ap[:, 0:8], i=fsc: e.tensor_copy(out=o, in_=i),
                 reads=[gsc.t], writes=[s["xn"].t], tag=name + f"gb{gt}")
            bank2, bank2_t = ps.get()
            bb2 = bank2.bitcast(BF16)
            P.do("tensor", lambda e, o=bb2[0:8, 0:128], i=s["xn"].ap[:, 0:8]:
                 e.transpose(out=o, in_=i, identity=cm.ident),
                 reads=[s["xn"].t, cm.ident_t], writes=[bank2_t], tag=name + f"gv{gt}")
            P.do("scalar", lambda e, o=srow.ap[0:8, lt * 128:(lt + 1) * 128],
                 i=bb2[0:8, 0:128]: e.copy(out=o, in_=i),
                 reads=[bank2_t], writes=[srow.t], tag=name + f"gu{gt}")

        zb = AB.alloc(name + f"_zb{ch}", 32 * TC)
        zbv = zb.ap.rearrange("p (i t) -> p i t", t=TC)
        wgb = [AB.alloc(name + f"_wg{ch}{i}", 8 * 512) for i in range(2)]
        wub = [AB.alloc(name + f"_wu{ch}{i}", 8 * 512) for i in range(2)]
        wstg = [AF32.alloc(name + f"_ws{ch}{i}", 512, dma_slot=True) for i in range(2)]
        load_weight_into(P, wgb[0], wstg[0], wdr["l2f_wgT"][0], C, 512,
                         name + f"_wg{ch}0")
        load_weight_into(P, wub[0], wstg[1], wdr["l2f_wuT"][0], C, 512,
                         name + f"_wu{ch}0")
        for e_ in range(8):
            wgt = wgb[e_ % 2]
            wut = wub[e_ % 2]
            if e_ + 1 < 8:
                load_weight_into(P, wgb[(e_ + 1) % 2], wstg[0],
                                 wdr["l2f_wgT"][e_ + 1], C, 512,
                                 name + f"_wg{ch}{e_ + 1}")
                load_weight_into(P, wub[(e_ + 1) % 2], wstg[1],
                                 wdr["l2f_wuT"][e_ + 1], C, 512,
                                 name + f"_wu{ch}{e_ + 1}")
            for ts in range(TC // 512):
                bank, bank_t = ps.get()
                P.do("tensor", lambda e, o=bank, l=cm.esel[0:8, e_ * 128:(e_ + 1) * 128],
                     r=srow.ap[0:8, ts * 512:(ts + 1) * 512]:
                     e.matmul(out=o, lhsT=l, rhs=r, start=True, stop=True),
                     reads=[srow.t, cm.esel_t], writes=[bank_t],
                     tag=name + f"eb{ch}{e_}{ts}")
                P.do("scalar", lambda e, o=bcf.ap, i=bank: e.copy(out=o, in_=i),
                     reads=[bank_t], writes=[bcf.t], tag=name + f"ec{ch}{e_}{ts}")
                for it in range(4):
                    banku, banku_t = ps.get()
                    mms = [(banku, wut.ap[:, ct * 512 + it * 128: ct * 512 + it * 128 + 128],
                            xnTv[:, ct, ts * 512:(ts + 1) * 512], None) for ct in range(8)]
                    mm_group(P, banku, banku_t, mms, reads=[wut.t, xnT.t],
                             tag=name + f"um{ch}{e_}{ts}{it}")
                    P.do("vector", lambda e, o=usf.ap, i=banku, b_=bcf.ap:
                         e.tensor_mul(out=o, in0=i, in1=b_),
                         reads=[banku_t, bcf.t], writes=[usf.t],
                         tag=name + f"ue{ch}{e_}{ts}{it}")
                    bankg, bankg_t = ps.get()
                    mms = [(bankg, wgt.ap[:, ct * 512 + it * 128: ct * 512 + it * 128 + 128],
                            xnTv[:, ct, ts * 512:(ts + 1) * 512], None) for ct in range(8)]
                    mm_group(P, bankg, bankg_t, mms, reads=[wgt.t, xnT.t],
                             tag=name + f"gm2{ch}{e_}{ts}{it}")
                    P.do("vector", lambda e, o=gsf.ap, i=bankg, b_=bcf.ap:
                         e.tensor_mul(out=o, in0=i, in1=b_),
                         reads=[bankg_t, bcf.t], writes=[gsf.t],
                         tag=name + f"ge2{ch}{e_}{ts}{it}")
                    P.do("scalar", lambda e, o=sgf.ap, i=gsf.ap:
                         e.activation(out=o, in_=i, func=AF.Silu),
                         reads=[gsf.t], writes=[sgf.t], tag=name + f"gs2{ch}{e_}{ts}{it}")
                    dst = zbv[:, e_ * 4 + it, ts * 512:(ts + 1) * 512]
                    P.do("vector", lambda e, o=dst, a=sgf.ap, b_=usf.ap:
                         e.tensor_mul(out=o, in0=a, in1=b_),
                         reads=[sgf.t, usf.t], writes=[zb.t],
                         tag=name + f"zm{ch}{e_}{ts}{it}")
        for i in range(2):
            AB.freeb(wgb[i])
            AB.freeb(wub[i])
            AF32.freeb(wstg[i])
        AB.freeb(xnT)

        wdt = [load_weight_bf16(P, AB, AF32, wdr["l2f_wdT"][e_], 512, C,
                                name + f"_wd{ch}{e_}") for e_ in range(8)]
        outfm = AB.alloc(name + f"_ofm{ch}", 8 * TC)
        ofmv = outfm.ap.rearrange("p (o t) -> p o t", t=TC)
        for ot in range(8):
            for ts in range(TC // 512):
                bank, bank_t = ps.get()
                mms = []
                for e_ in range(8):
                    for it in range(4):
                        mms.append((bank,
                                    wdt[e_].ap[:, it * C + ot * 128: it * C + ot * 128 + 128],
                                    zbv[:, e_ * 4 + it, ts * 512:(ts + 1) * 512], None))
                mm_group(P, bank, bank_t, mms,
                         reads=[zb.t] + [w.t for w in wdt],
                         tag=name + f"dm{ch}{ot}{ts}")
                P.do("scalar", lambda e, o=ofmv[:, ot, ts * 512:(ts + 1) * 512],
                     i=bank: e.copy(out=o, in_=i),
                     reads=[bank_t], writes=[outfm.t], tag=name + f"de{ch}{ot}{ts}")
        for w in wdt:
            AB.freeb(w)
        AB.freeb(zb)

        for lt in range(ntt):
            gt = ch * ntt + lt
            s = sl[lt % 2]
            P.dma("sync", s["xr"].ap, xin_ap(gt), writes=[s["xr"].t],
                  reads=[xin_tiles[gt]], tag=name + f"z{gt}")
            bank, bank_t = ps.get()
            bb = bank.bitcast(BF16)
            def fnt(e, lt=lt, bb=bb):
                ins = None
                for ot in range(8):
                    ins = e.transpose(out=bb[:, ot * 128:(ot + 1) * 128],
                                      in_=ofmv[:, ot, lt * 128:(lt + 1) * 128],
                                      identity=cm.ident)
                return ins
            P.do("tensor", fnt, reads=[outfm.t, cm.ident_t], writes=[bank_t],
                 tag=name + f"tt{gt}")
            ob = outs[lt % 2]
            P.do("vector", lambda e, o=ob.ap, a=bb[:, 0:1024], x=s["xr"].ap:
                 e.tensor_add(out=o, in0=a, in1=x),
                 reads=[bank_t, s["xr"].t], writes=[ob.t], tag=name + f"ra{gt}")
            P.dma("scalar", xout_ap(gt), ob.ap, writes=[xout_tiles[gt]],
                  reads=[ob.t], tag=name + f"zs{gt}")
        AB.freeb(outfm)
        AB.freeb(srow)

    AB.freeb(wg)
    AF32.freeb(sq_sh)
    for bb3 in (gsc, bcf, gsf):
        AF32.freeb(bb3)
    AB.freeb(usf)
    AF32.freeb(sgf)
    for s in sl:
        AF32.freeb(s["xr"])
        AF32.freeb(s["st"])
        AB.freeb(s["xn"])
    for o in outs:
        AF32.freeb(o)




L3A_DEBUG = "ABC"


def emit_l3_attn(P, AB, AF32, AU8, ps, cm, nc, io, wdr):
    for b in range(BPC):
        _l3_attn_batch(P, AB, AF32, ps, cm, nc, io, b, wdr, name=f"l3a{b}")


def _l3_attn_batch(P, AB, AF32, ps, cm, nc, io, b, wdr, name):
    xin_ap, xin_tiles = io["xin"]
    xout_ap, xout_tiles = io["xout"]

    fktm = AB.alloc(name + "_fktm", TPB * 1024)
    fqfm = AB.alloc(name + "_fqfm", 8 * T)
    vtm = AB.alloc(name + "_vtm", TPB * 1024)

    wqkv = load_weight_bf16(P, AB, AF32, wdr["l3a_wqkvT"], C, 3 * C, name + "_wqkv")
    rfs = AF32.alloc(name + "_rfs", 128, dma_slot=True)
    rft = AB.alloc(name + "_rft", 128)
    P.dma("sync", rfs.ap[:, 0:128], wdr["l3a_rfT"], writes=[rfs.t], tag=name + "rfl")
    P.do("vector", lambda e: e.tensor_copy(out=rft.ap[:, 0:128], in_=rfs.ap[:, 0:128]),
         reads=[rfs.t], writes=[rft.t], tag=name + "rfc")
    AF32.freeb(rfs)

    sq_sh = AF32.alloc(name + "_sq", 1024)
    sl = []
    for i in range(2):
        sl.append({
            "xr": AF32.alloc(name + f"_xr{i}", 1024, dma_slot=True),
            "st": AF32.alloc(name + f"_st{i}", 32),
            "sq": sq_sh,
            "xn": AB.alloc(name + f"_xn{i}", 1024),
            "xnT": AB.alloc(name + f"_xnT{i}", 1024),
            "qf": AF32.alloc(name + f"_qf{i}", 1024),
            "nrm": AF32.alloc(name + f"_nrm{i}", 32),
            "qb": AB.alloc(name + f"_qb{i}", 1024),
            "qfm": AB.alloc(name + f"_qfm{i}", 1024),
            "fpre": AF32.alloc(name + f"_fp{i}", 1024),
        })

    fkv = fktm.ap.rearrange("p (t c) -> p t c", c=1024)
    fqv = fqfm.ap.rearrange("p (c t) -> p c t", t=T)

    def qkv_proj(s, which, tt, tagp):
        off = {"q": 0, "k": C, "v": 2 * C}[which]
        for oc in range(2):
            bank, bank_t = ps.get()
            mms = []
            for ct in range(8):
                mms.append((bank, s["xnT"].ap[:, ct * 128:(ct + 1) * 128],
                            wqkv.ap[:, ct * 3 * C + off + oc * 512:
                                    ct * 3 * C + off + (oc + 1) * 512], None))
            mm_group(P, bank, bank_t, mms, reads=[s["xnT"].t, wqkv.t],
                     tag=tagp + f"m{oc}")
            if which == "v":
                dst = vtm.ap[:, tt * 1024 + oc * 512: tt * 1024 + (oc + 1) * 512]
                P.do("scalar", lambda e, o=dst, i=bank: e.copy(out=o, in_=i),
                     reads=[bank_t], writes=[vtm.t], tag=tagp + f"e{oc}")
            else:
                dst = s["qf"].ap[:, oc * 512:(oc + 1) * 512]
                P.do("scalar", lambda e, o=dst, i=bank: e.copy(out=o, in_=i),
                     reads=[bank_t], writes=[s["qf"].t], tag=tagp + f"e{oc}")

    def feature_map(s, tt, is_q, tagp):
        P.do("scalar", lambda e: e.activation(out=s["sq"].ap, in_=s["qf"].ap,
                                              func=AF.Square),
             reads=[s["qf"].t], writes=[s["sq"].t], tag=tagp + "sq")
        n2 = s["nrm"].ap[:, 0:16]
        P.do("vector", lambda e: e.reduce_sum(
             out=n2.rearrange("p (h o) -> p h o", o=1),
             in_=s["sq"].ap.rearrange("p (h d) -> p h d", d=64), axis=AX.X),
             reads=[s["sq"].t], writes=[s["nrm"].t], tag=tagp + "rd")
        P.do("scalar", lambda e: e.activation(out=n2, in_=n2, func=AF.Ln),
             reads=[s["nrm"].t], writes=[s["nrm"].t], tag=tagp + "ln")
        P.do("scalar", lambda e: e.activation(out=n2, in_=n2, func=AF.Exp, scale=0.5),
             reads=[s["nrm"].t], writes=[s["nrm"].t], tag=tagp + "ex")
        P.do("vector", lambda e: e.tensor_copy(out=s["qb"].ap, in_=s["qf"].ap),
             reads=[s["qf"].t], writes=[s["qb"].t], tag=tagp + "cv")
        transpose_to(P, ps, cm, s["qb"].ap, s["qb"].t, s["qfm"].ap, s["qfm"].t,
                     1024, tagp + "tr")
        for half in range(2):
            bank, bank_t = ps.get()
            def fn(e, half=half, bank=bank, s=s):
                ins = None
                for pp in range(4):
                    ct = half * 4 + pp
                    ins = e.matmul(out=bank[:, pp * 128:(pp + 1) * 128],
                                   lhsT=s["qfm"].ap[:, ct * 128:(ct + 1) * 128],
                                   rhs=rft.ap[:, 0:128], start=True, stop=True)
                return ins
            P.do("tensor", fn, reads=[s["qfm"].t, rft.t], writes=[bank_t],
                 tag=tagp + f"pj{half}")
            dst = s["fpre"].ap[:, half * 512:(half + 1) * 512]
            nb = n2[:, half * 8:(half + 1) * 8].unsqueeze(2).broadcast_to((128, 8, 64))
            P.do("vector", lambda e, o=dst.rearrange("p (h d) -> p h d", d=64),
                 i=bank, nb=nb:
                 e.scalar_tensor_tensor(out=o,
                                        in0=i.rearrange("p (h d) -> p h d", d=64),
                                        scalar=0.125, in1=nb,
                                        op0=ALU.mult, op1=ALU.subtract),
                 reads=[bank_t, s["nrm"].t], writes=[s["fpre"].t], tag=tagp + f"fp{half}")
        dst = fkv[:, tt, :] if not is_q else s["qb"].ap
        dt_t = fktm.t if not is_q else s["qb"].t
        P.do("scalar", lambda e, o=dst, i=s["fpre"].ap:
             e.activation(out=o, in_=i, func=AF.Exp),
             reads=[s["fpre"].t], writes=[dt_t], tag=tagp + "phe")
        if is_q:
            dv = fqv[:, :, tt * 128:(tt + 1) * 128]
            transpose_to(P, ps, cm, s["qb"].ap, s["qb"].t, None, fqfm.t, 1024,
                         tagp + "qtr", dst_view=dv)

    for tt in range(TPB):
        gt = b * TPB + tt
        s = sl[tt % 2]
        P.dma("sync", s["xr"].ap, xin_ap(gt), writes=[s["xr"].t],
              reads=[xin_tiles[gt]], tag=name + f"xl{tt}")
        ln_tile(P, cm, s, s["xr"].ap, s["xr"].t, name + f"ln{tt}")
        transpose_to(P, ps, cm, s["xn"].ap, s["xn"].t, s["xnT"].ap, s["xnT"].t,
                     1024, name + f"xT{tt}")
        qkv_proj(s, "v", tt, name + f"v{tt}")
        qkv_proj(s, "q", tt, name + f"q{tt}")
        feature_map(s, tt, True, name + f"fq{tt}")
        qkv_proj(s, "k", tt, name + f"k{tt}")
        feature_map(s, tt, False, name + f"fk{tt}")

    AB.freeb(wqkv)
    AF32.freeb(sq_sh)
    for s in sl:
        for k2 in ("xn", "xnT", "qb", "qfm"):
            AB.freeb(s[k2])
        for k2 in ("qf", "nrm", "fpre"):
            AF32.freeb(s[k2])

    def _copy_through():
        outs0 = [AF32.alloc(name + f"_dbg{i}", 1024, dma_slot=True) for i in range(2)]
        for gt2 in range(b * TPB, (b + 1) * TPB):
            ob3 = outs0[gt2 % 2]
            P.dma("sync", ob3.ap, xin_ap(gt2), writes=[ob3.t],
                  reads=[xin_tiles[gt2]], tag=name + f"dbl{gt2}")
            P.dma("scalar", xout_ap(gt2), ob3.ap, writes=[xout_tiles[gt2]],
                  reads=[ob3.t], tag=name + f"dbs{gt2}")
        for o in outs0:
            AF32.freeb(o)

    if "B" not in L3A_DEBUG:
        _copy_through()
        for bb3 in (fktm, fqfm, vtm):
            AB.freeb(bb3)
        AB.freeb(rft)
        for s in sl:
            AF32.freeb(s["xr"])
            AF32.freeb(s["st"])
        return

    kvsb = AB.alloc(name + "_kvsb", 16 * 65 + 62)
    P.do("vector", lambda e: e.memset(kvsb.ap, 0.0), writes=[kvsb.t],
         tag=name + "kvz")
    banks = [ps.take() for _ in range(8)]
    for nt in range(TPB):
        for hp in range(8):
            bank, bank_t, _bi = banks[hp]
            def fn(e, nt=nt, hp=hp, bank=bank):
                ins = None
                for z in range(2):
                    h = 2 * hp + z
                    lh = fkv[:, nt, h * 64:(h + 1) * 64]
                    ins = e.matmul(out=bank[64 * z:64 * z + 64, 0:64],
                                   lhsT=lh,
                                   rhs=vtm.ap[:, nt * 1024 + h * 64: nt * 1024 + (h + 1) * 64],
                                   start=(nt == 0), stop=(nt == TPB - 1),
                                   tile_position=(0, 64 * z), skip_group_check=True)
                    ins = e.matmul(out=bank[64 * z:64 * z + 64, 64:65],
                                   lhsT=lh, rhs=cm.ones_col,
                                   start=(nt == 0), stop=(nt == TPB - 1),
                                   tile_position=(0, 64 * z), skip_group_check=True)
                return ins
            P.do("tensor", fn, reads=[fktm.t, vtm.t, cm.ones_col_t],
                 writes=[banks[hp][1]], tag=name + f"kv{nt}_{hp}")
    for hp in range(8):
        bank, bank_t, bi = banks[hp]
        P.do("scalar", lambda e, o=kvsb.ap[0:64, (2 * hp) * 65: (2 * hp) * 65 + 65],
             i=bank[0:64, 0:65]: e.copy(out=o, in_=i),
             reads=[bank_t], writes=[kvsb.t], tag=name + f"kve{hp}a")
        P.do("scalar", lambda e, o=kvsb.ap[64:128, (2 * hp + 1) * 65: (2 * hp + 1) * 65 + 65],
             i=bank[64:128, 0:65]: e.copy(out=o, in_=i),
             reads=[bank_t], writes=[kvsb.t], tag=name + f"kve{hp}b")
        ps.release(bi)

    if "C" not in L3A_DEBUG:
        _copy_through()
        for bb3 in (fktm, fqfm, vtm, kvsb, rft):
            AB.freeb(bb3)
        for s in sl:
            AF32.freeb(s["xr"])
            AF32.freeb(s["st"])
        return

    wo = load_weight_bf16(P, AB, AF32, wdr["l3a_woT"], C, C, name + "_wo")
    bo = load_bias_row(P, AB, AF32, wdr["l3a_bo"], C, name + "_bo")
    ybuf = AB.alloc(name + "_y", 8 * 512)
    drow = AB.alloc(name + "_dr", 1024)
    dcol = AF32.alloc(name + "_dc", 512)
    bcf2 = AF32.alloc(name + "_bc", 512)
    outs = [AF32.alloc(name + f"_o{i}", 1024, dma_slot=True) for i in range(2)]

    for qc in range(4):
        for hp in range(8):
            ob_, ob_t = ps.get()
            dbank = [ps.get() for _ in range(2)]
            rhsq = fqv[:, hp, qc * 512:(qc + 1) * 512]
            def fn(e, hp=hp, ob_=ob_, rhsq=rhsq):
                ins = None
                for z in range(2):
                    h = 2 * hp + z
                    ins = e.matmul(out=ob_[64 * z:64 * z + 64, :],
                                   lhsT=kvsb.ap[:, h * 65: h * 65 + 64],
                                   rhs=rhsq, start=True, stop=True,
                                   tile_position=(0, 64 * z), skip_group_check=True)
                return ins
            P.do("tensor", fn, reads=[kvsb.t, fqfm.t], writes=[ob_t],
                 tag=name + f"ov{qc}{hp}")
            for z in range(2):
                h = 2 * hp + z
                P.do("tensor", lambda e, o=dbank[z][0], h=h, rhsq=rhsq:
                     e.matmul(out=o[0:1, :], lhsT=kvsb.ap[:, h * 65 + 64: h * 65 + 65],
                              rhs=rhsq, start=True, stop=True, skip_group_check=True),
                     reads=[kvsb.t, fqfm.t], writes=[dbank[z][1]],
                     tag=name + f"dn{qc}{hp}{z}")
            for z in range(2):
                P.do("vector", lambda e, o=dcol.ap[0:1, :], i=dbank[z][0][0:1, :]:
                     e.tensor_scalar(out=o, in0=i, scalar1=1e-6, scalar2=None, op0=ALU.add),
                     reads=[dbank[z][1]], writes=[dcol.t], tag=name + f"da{qc}{hp}{z}")
                P.do("vector", lambda e, o=dcol.ap[0:1, :], i=dcol.ap[0:1, :]:
                     e.reciprocal(out=o, in_=i),
                     reads=[dcol.t], writes=[dcol.t], tag=name + f"dr{qc}{hp}{z}")
                P.do("vector", lambda e, o=drow.ap[0:1, z * 512:(z + 1) * 512],
                     i=dcol.ap[0:1, :]: e.tensor_copy(out=o, in_=i),
                     reads=[dcol.t], writes=[drow.t], tag=name + f"db{qc}{hp}{z}")
            bcb, bcb_t = ps.get()
            def fnb(e, bcb=bcb):
                e.matmul(out=bcb[0:64, :], lhsT=cm.ones_row[0:1, 0:64],
                         rhs=drow.ap[0:1, 0:512], start=True, stop=True,
                         tile_position=(0, 0))
                return e.matmul(out=bcb[64:128, :], lhsT=cm.ones_row[0:1, 0:64],
                                rhs=drow.ap[0:1, 512:1024], start=True, stop=True,
                                tile_position=(0, 64))
            P.do("tensor", fnb, reads=[drow.t, cm.ones_row_t], writes=[bcb_t],
                 tag=name + f"bb{qc}{hp}")
            P.do("scalar", lambda e, o=bcf2.ap, i=bcb: e.copy(out=o, in_=i),
                 reads=[bcb_t], writes=[bcf2.t], tag=name + f"bc{qc}{hp}")
            ydst = ybuf.ap[:, hp * 512:(hp + 1) * 512]
            P.do("vector", lambda e, o=ydst, a=ob_, d=bcf2.ap:
                 e.tensor_mul(out=o, in0=a, in1=d),
                 reads=[ob_t, bcf2.t], writes=[ybuf.t], tag=name + f"ym{qc}{hp}")
        for qs in range(4):
            gt = b * TPB + qc * 4 + qs
            s = sl[qs % 2]
            P.dma("sync", s["xr"].ap, xin_ap(gt), writes=[s["xr"].t],
                  reads=[xin_tiles[gt]], tag=name + f"zl{qc}{qs}")
            ob2 = outs[qs % 2]
            for oc in range(2):
                bank, bank_t = ps.get()
                mms = [(bank, cm.ones_row[0:1, :], bo.ap[0:1, oc * 512:(oc + 1) * 512], None)]
                for hp in range(8):
                    mms.append((bank, ybuf.ap[:, hp * 512 + qs * 128: hp * 512 + qs * 128 + 128],
                                wo.ap[:, hp * C + oc * 512: hp * C + (oc + 1) * 512], None))
                mm_group(P, bank, bank_t, mms,
                         reads=[ybuf.t, wo.t, bo.t, cm.ones_row_t],
                         tag=name + f"j{qc}{qs}{oc}")
                dst = ob2.ap[:, oc * 512:(oc + 1) * 512]
                P.do("vector", lambda e, o=dst, i=bank,
                     x=s["xr"].ap[:, oc * 512:(oc + 1) * 512]:
                     e.tensor_add(out=o, in0=i, in1=x),
                     reads=[bank_t, s["xr"].t], writes=[ob2.t], tag=name + f"r{qc}{qs}{oc}")
            P.dma("scalar", xout_ap(gt), ob2.ap, writes=[xout_tiles[gt]],
                  reads=[ob2.t], tag=name + f"zs{qc}{qs}")

    for bb3 in (fktm, fqfm, vtm, kvsb, rft, wo, bo, ybuf, drow):
        AB.freeb(bb3)
    AF32.freeb(dcol)
    AF32.freeb(bcf2)
    for s in sl:
        AF32.freeb(s["xr"])
        AF32.freeb(s["st"])
    for o in outs:
        AF32.freeb(o)




def emit_l3_ffn(P, AB, AF32, AU8, ps, cm, nc, io, wdr):
    name = "l3f"
    xin_ap, xin_tiles = io["xin"]
    xout_ap, xout_tiles = io["xout"]

    wg = AB.alloc(name + "_wg", 64)
    wgs = AF32.alloc(name + "_wgs", 64, dma_slot=True)
    P.dma("sync", wgs.ap[:, 0:64].rearrange("p (c o) -> p c o", o=8),
          wdr["l3f_gateT"].rearrange("(c p) o -> p c o", p=128), writes=[wgs.t],
          tag=name + "wgl")
    P.do("vector", lambda e: e.tensor_copy(out=wg.ap[:, 0:64], in_=wgs.ap[:, 0:64]),
         reads=[wgs.t], writes=[wg.t], tag=name + "wgc")
    AF32.freeb(wgs)

    sq_sh = AF32.alloc(name + "_sq", 1024)
    sl = []
    for i in range(2):
        sl.append({
            "xr": AF32.alloc(name + f"_xr{i}", 1024, dma_slot=True),
            "st": AF32.alloc(name + f"_st{i}", 32),
            "sq": sq_sh,
            "xn": AB.alloc(name + f"_xn{i}", 1024),
        })
    outs = [AF32.alloc(name + f"_o{i}", 1024, dma_slot=True) for i in range(2)]
    gsc = AF32.alloc(name + "_gsc", 64)
    usf = AB.alloc(name + "_usf", 512)
    gsf = AF32.alloc(name + "_gsf", 512)
    sgf = AF32.alloc(name + "_sgf", 512)
    TC = 1024
    NCH = NT // TC

    for ch in range(NCH):
        ntt = TC // 128
        xnT = AB.alloc(name + f"_xnT{ch}", 8 * TC)
        xnTv = xnT.ap.rearrange("p (c t) -> p c t", t=TC)
        oacc = AB.alloc(name + f"_oacc{ch}", ntt * 1024)
        oaccv = oacc.ap.rearrange("p (t o) -> p t o", o=1024)
        cw = AF32.alloc(name + f"_cw{ch}", ntt * 8)

        for lt in range(ntt):
            gt = ch * ntt + lt
            s = sl[lt % 2]
            P.dma("sync", s["xr"].ap, xin_ap(gt), writes=[s["xr"].t],
                  reads=[xin_tiles[gt]], tag=name + f"xl{gt}")
            ln_tile(P, cm, s, s["xr"].ap, s["xr"].t, name + f"ln{gt}")
            dv = xnTv[:, :, lt * 128:(lt + 1) * 128]
            transpose_to(P, ps, cm, s["xn"].ap, s["xn"].t, None, xnT.t, 1024,
                         name + f"xT{gt}", dst_view=dv)
        for lt in range(ntt):
            gt = ch * ntt + lt
            bank, bank_t = ps.get()
            mms = [(bank[:, 0:8], xnTv[:, ct, lt * 128:(lt + 1) * 128],
                    wg.ap[:, ct * 8:(ct + 1) * 8], None) for ct in range(8)]
            mm_group(P, bank, bank_t, mms, reads=[xnT.t, wg.t], tag=name + f"gm{gt}")
            lg = gsc.ap[:, 0:8]
            m1 = gsc.ap[:, 8:9]
            zz = gsc.ap[:, 16:24]
            ss = gsc.ap[:, 9:10]
            p1 = gsc.ap[:, 10:11]
            m2 = gsc.ap[:, 11:12]
            p2 = gsc.ap[:, 12:13]
            dnm = gsc.ap[:, 13:14]
            imax = gsc.ap[:, 14:15]
            f1 = gsc.ap[:, 24:32]
            f2 = gsc.ap[:, 32:40]
            eq = gsc.ap[:, 40:48]
            def V(fn, rd=(), tg=""):
                P.do("vector", fn, reads=[gsc.t] + list(rd), writes=[gsc.t],
                     tag=name + tg + str(gt))
            P.do("vector", lambda e, i=bank: e.tensor_copy(out=lg, in_=i[:, 0:8]),
                 reads=[bank_t], writes=[gsc.t], tag=name + f"gc{gt}")
            V(lambda e: e.reduce_max(out=m1, in_=lg, axis=AX.X), tg="g1")
            V(lambda e: e.tensor_scalar(out=imax, in0=m1, scalar1=-1.0, scalar2=None,
                                        op0=ALU.mult), tg="g1n")
            P.do("scalar", lambda e: e.activation(out=zz, in_=lg, func=AF.Exp,
                                                  bias=imax),
                 reads=[gsc.t], writes=[gsc.t], tag=name + f"g2{gt}")
            V(lambda e: e.reduce_sum(out=ss, in_=zz, axis=AX.X), tg="g3")
            V(lambda e: e.reciprocal(out=p1, in_=ss), tg="g4")
            V(lambda e: e.tensor_scalar(out=eq, in0=lg, scalar1=m1, scalar2=None,
                                        op0=ALU.is_equal), tg="g5")
            V(lambda e: e.tensor_mul(out=eq, in0=eq, in1=cm.iota8r[:, 0:8]),
              rd=[cm.iota8r_t], tg="g6")
            V(lambda e: e.reduce_max(out=imax, in_=eq, axis=AX.X), tg="g7")
            V(lambda e: e.tensor_scalar(out=f1, in0=cm.iota8r[:, 0:8], scalar1=imax,
                                        scalar2=None, op0=ALU.is_equal),
              rd=[cm.iota8r_t], tg="g8")
            V(lambda e: e.scalar_tensor_tensor(out=eq, in0=f1, scalar=-1.0, in1=zz,
                                               op0=ALU.mult, op1=ALU.mult), tg="g9")
            V(lambda e: e.tensor_add(out=eq, in0=zz, in1=eq), tg="gA")
            V(lambda e: e.reduce_max(out=m2, in_=eq, axis=AX.X), tg="gB")
            V(lambda e: e.tensor_scalar(out=eq, in0=eq, scalar1=m2, scalar2=None,
                                        op0=ALU.is_equal), tg="gC")
            V(lambda e: e.tensor_mul(out=eq, in0=eq, in1=cm.iota8r[:, 0:8]),
              rd=[cm.iota8r_t], tg="gD")
            V(lambda e: e.reduce_max(out=imax, in_=eq, axis=AX.X), tg="gE")
            V(lambda e: e.tensor_scalar(out=f2, in0=cm.iota8r[:, 0:8], scalar1=imax,
                                        scalar2=None, op0=ALU.is_equal),
              rd=[cm.iota8r_t], tg="gF")
            V(lambda e: e.tensor_scalar(out=p2, in0=m2, scalar1=p1, scalar2=None,
                                        op0=ALU.mult), tg="gG")
            V(lambda e: e.tensor_scalar(out=dnm, in0=p1, scalar1=p2, scalar2=1e-8,
                                        op0=ALU.add, op1=ALU.add), tg="gH")
            V(lambda e: e.reciprocal(out=dnm, in_=dnm), tg="gI")
            V(lambda e: e.tensor_scalar(out=p1, in0=p1, scalar1=dnm, scalar2=None,
                                        op0=ALU.mult), tg="gJ")
            V(lambda e: e.tensor_scalar(out=p2, in0=p2, scalar1=dnm, scalar2=None,
                                        op0=ALU.mult), tg="gK")
            cwsl = cw.ap[:, lt * 8:(lt + 1) * 8]
            P.do("vector", lambda e, o=cwsl: e.tensor_scalar(out=o, in0=f1, scalar1=p1,
                                                             scalar2=None, op0=ALU.mult),
                 reads=[gsc.t], writes=[cw.t], tag=name + f"gL{gt}")
            P.do("vector", lambda e, o=cwsl: e.scalar_tensor_tensor(out=o, in0=f2, scalar=p2,
                                                                    in1=o, op0=ALU.mult,
                                                                    op1=ALU.add),
                 reads=[gsc.t, cw.t], writes=[cw.t], tag=name + f"gM{gt}")

        zbuf = AB.alloc(name + f"_zb{ch}", 8 * TC)
        zbv = zbuf.ap.rearrange("p (i t) -> p i t", t=TC)
        gub = [AB.alloc(name + f"_gu{ch}{i}", 16 * C) for i in range(2)]
        dnb = [AB.alloc(name + f"_dn{ch}{i}", 8 * C) for i in range(2)]
        wstg = [AF32.alloc(name + f"_ws{ch}{i}", 1024, dma_slot=True) for i in range(2)]
        def gu_src(e2):
            return wdr["l3f_sh_guT"] if e2 < 0 else wdr["l3f_ex_guT"][e2]
        def dn_src(e2):
            return wdr["l3f_sh_dnT"] if e2 < 0 else wdr["l3f_ex_dnT"][e2]
        def load_gu(buf, src, tagn):
            for hf in range(2):
                for ct in range(8):
                    P.dma("sync", wstg[0].ap[:, 0:1024],
                          src[ct * 128:(ct + 1) * 128, hf * 1024:(hf + 1) * 1024],
                          writes=[wstg[0].t], tag=tagn + f"L{hf}{ct}")
                    dst = buf.ap[:, ct * 2048 + hf * 1024: ct * 2048 + (hf + 1) * 1024]
                    P.do("gpsimd",
                         lambda e, o=dst, i=wstg[0].ap[:, 0:1024]: e.tensor_copy(out=o, in_=i),
                         reads=[wstg[0].t], writes=[buf.t], tag=tagn + f"C{hf}{ct}")
        load_gu(gub[0], gu_src(-1), name + f"_gu{ch}s")
        load_weight_into(P, dnb[0], wstg[1], dn_src(-1), C, C, name + f"_dn{ch}s")
        for e_ in range(-1, 8):
            gu = gub[(e_ + 1) % 2]
            dn = dnb[(e_ + 1) % 2]
            if e_ + 1 < 8:
                load_gu(gub[e_ % 2], gu_src(e_ + 1), name + f"_gu{ch}{e_ + 1}")
                load_weight_into(P, dnb[e_ % 2], wstg[1], dn_src(e_ + 1), C, C,
                                 name + f"_dn{ch}{e_ + 1}")
            for ts in range(TC // 512):
                for it in range(8):
                    banku, banku_t = ps.get()
                    mms = [(banku, gu.ap[:, ct * 2048 + 1024 + it * 128:
                                         ct * 2048 + 1024 + it * 128 + 128],
                            xnTv[:, ct, ts * 512:(ts + 1) * 512], None) for ct in range(8)]
                    mm_group(P, banku, banku_t, mms, reads=[gu.t, xnT.t],
                             tag=name + f"u{ch}{e_}{ts}{it}")
                    P.do("vector", lambda e, o=usf.ap, i=banku: e.tensor_copy(out=o, in_=i),
                         reads=[banku_t], writes=[usf.t], tag=name + f"ue{ch}{e_}{ts}{it}")
                    bankg, bankg_t = ps.get()
                    mms = [(bankg, gu.ap[:, ct * 2048 + it * 128:
                                         ct * 2048 + it * 128 + 128],
                            xnTv[:, ct, ts * 512:(ts + 1) * 512], None) for ct in range(8)]
                    mm_group(P, bankg, bankg_t, mms, reads=[gu.t, xnT.t],
                             tag=name + f"g{ch}{e_}{ts}{it}")
                    P.do("scalar", lambda e, o=sgf.ap, i=bankg:
                         e.activation(out=o, in_=i, func=AF.Silu),
                         reads=[bankg_t], writes=[sgf.t], tag=name + f"gs{ch}{e_}{ts}{it}")
                    dst = zbv[:, it, ts * 512:(ts + 1) * 512]
                    P.do("vector", lambda e, o=dst, a=sgf.ap, b_=usf.ap:
                         e.tensor_mul(out=o, in0=a, in1=b_),
                         reads=[sgf.t, usf.t], writes=[zbuf.t],
                         tag=name + f"zm{ch}{e_}{ts}{it}")
            for lt in range(ntt):
                for oc in range(2):
                    bank, bank_t = ps.get()
                    mms = [(bank, zbv[:, itile, lt * 128:(lt + 1) * 128],
                            dn.ap[:, itile * C + oc * 512: itile * C + (oc + 1) * 512],
                            None) for itile in range(8)]
                    mm_group(P, bank, bank_t, mms, reads=[zbuf.t, dn.t],
                             tag=name + f"d{ch}{e_}{lt}{oc}")
                    dst = oaccv[:, lt, oc * 512:(oc + 1) * 512]
                    if e_ < 0:
                        P.do("vector", lambda e, o=dst, i=bank: e.tensor_copy(out=o, in_=i),
                             reads=[bank_t], writes=[oacc.t],
                             tag=name + f"da{ch}{e_}{lt}{oc}")
                    else:
                        P.do("vector", lambda e, o=dst, i=bank,
                             g_=cw.ap[:, lt * 8 + e_: lt * 8 + e_ + 1]:
                             e.scalar_tensor_tensor(out=o, in0=i, scalar=g_, in1=o,
                                                    op0=ALU.mult, op1=ALU.add),
                             reads=[bank_t, cw.t, oacc.t], writes=[oacc.t],
                             tag=name + f"da{ch}{e_}{lt}{oc}")
        for i in range(2):
            AB.freeb(gub[i])
            AB.freeb(dnb[i])
            AF32.freeb(wstg[i])
        AB.freeb(zbuf)
        AB.freeb(xnT)

        for lt in range(ntt):
            gt = ch * ntt + lt
            s = sl[lt % 2]
            P.dma("sync", s["xr"].ap, xin_ap(gt), writes=[s["xr"].t],
                  reads=[xin_tiles[gt]], tag=name + f"z{gt}")
            ob = outs[lt % 2]
            P.do("vector", lambda e, o=ob.ap, a=oaccv[:, lt, :], x=s["xr"].ap:
                 e.tensor_add(out=o, in0=a, in1=x),
                 reads=[oacc.t, s["xr"].t], writes=[ob.t], tag=name + f"ra{gt}")
            P.dma("scalar", xout_ap(gt), ob.ap, writes=[xout_tiles[gt]],
                  reads=[ob.t], tag=name + f"zs{gt}")
        AB.freeb(oacc)
        AF32.freeb(cw)

    AB.freeb(wg)
    AF32.freeb(sq_sh)
    AF32.freeb(gsc)
    AB.freeb(usf)
    AF32.freeb(gsf)
    AF32.freeb(sgf)
    for s in sl:
        AF32.freeb(s["xr"])
        AF32.freeb(s["st"])
        AB.freeb(s["xn"])
    for o in outs:
        AF32.freeb(o)



CONST_SPECS = {
    "ident": ([128, 128], BF16),
    "ones_row": ([1, 128], BF16),
    "ones_col": ([128, 1], BF16),
    "iota8": ([128, 8], F32),
    "iota8r": ([128, 8], F32),
    "causmask": ([128, 896], BF16),
    "esel": ([8, 1024], BF16),
    "rope0_cos": ([T, 512], F32),
    "rope0_sin": ([T, 512], F32),
    "rope1_cos": ([T, 512], F32),
    "rope1_sin": ([T, 512], F32),
    "alibi_srow": ([1, NHEAD * T], F32),
    "alibi_negtrow": ([1, NHEAD * T], F32),
    "alibi_band": ([NHEAD, 128, 896], F32),
    "alibi_scol": ([128, 256], F32),
}

WEIGHT_SPECS = {
    "l0a_wqT": ([C, C], F32), "l0a_wkT": ([C, C], F32),
    "l0a_wvT": ([C, C], F32), "l0a_wpT": ([C, C], F32),
    "l0a_bq": ([1, C], F32), "l0a_bk": ([1, C], F32),
    "l0a_bv": ([1, C], F32), "l0a_bp": ([1, C], F32),
    "l0f_k0t": ([C, 4096], U8), "l0f_k0b": ([C, 4096], U8),
    "l0f_k1": ([4096, 4096], U8), "l0f_k2": ([4096, C], U8),
    "l0f_s0": ([128, 1], F32), "l0f_s1": ([128, 1], F32), "l0f_s2": ([128, 1], F32),
    "l1a_wqT": ([C, C], F32), "l1a_wkT": ([C, C], F32), "l1a_wvT": ([C, C], F32),
    "l1a_bq": ([1, C], F32), "l1a_bk": ([1, C], F32), "l1a_bv": ([1, C], F32),
    "l1f_Wg": ([C, 8], F32), "l1f_bg": ([1, 8], F32),
    "l1f_W1": ([8, C, C], F32), "l1f_b1": ([128, 64], F32),
    "l1f_W2": ([8, C, C], F32), "l1f_b2": ([8, C], F32),
    "l2a_wqkv": ([C, 3 * C], F32), "l2a_bqkv": ([1, 3 * C], F32),
    "l2a_wp": ([C, C], F32), "l2a_bp": ([1, C], F32),
    "l2f_gateT": ([C, 8], F32),
    "l2f_wgT": ([8, C, 512], F32), "l2f_wuT": ([8, C, 512], F32),
    "l2f_wdT": ([8, 512, C], F32),
    "l3a_wqkvT": ([C, 3 * C], F32), "l3a_woT": ([C, C], F32),
    "l3a_bo": ([1, C], F32), "l3a_rfT": ([128, 128], F32),
    "l3f_gateT": ([C, 8], F32),
    "l3f_sh_guT": ([C, 2048], F32), "l3f_sh_dnT": ([C, C], F32),
    "l3f_ex_guT": ([8, C, 2048], F32), "l3f_ex_dnT": ([8, C, C], F32),
}

PART_WEIGHTS = {
    "l0a": ["l0a_wqT", "l0a_wkT", "l0a_wvT", "l0a_wpT",
            "l0a_bq", "l0a_bk", "l0a_bv", "l0a_bp"],
    "l0f": ["l0f_k0t", "l0f_k0b", "l0f_k1", "l0f_k2", "l0f_s0", "l0f_s1", "l0f_s2"],
    "l1a": ["l1a_wqT", "l1a_wkT", "l1a_wvT", "l1a_bq", "l1a_bk", "l1a_bv"],
    "l1f": ["l1f_Wg", "l1f_bg", "l1f_W1", "l1f_b1", "l1f_W2", "l1f_b2"],
    "l2a": ["l2a_wqkv", "l2a_bqkv", "l2a_wp", "l2a_bp"],
    "l2f": ["l2f_gateT", "l2f_wgT", "l2f_wuT", "l2f_wdT"],
    "l3a": ["l3a_wqkvT", "l3a_woT", "l3a_bo", "l3a_rfT"],
    "l3f": ["l3f_gateT", "l3f_sh_guT", "l3f_sh_dnT", "l3f_ex_guT", "l3f_ex_dnT"],
}

BF16_COLS = 85 * 1024
F32_COLS = 9 * 1024 + 512
U8_COLS = 11 * 1024


def build_program(part_ids):
    Arena._sem_pool = []
    Arena._sem_n = [0]
    Arena._sem_last = {}
    nc = bass.Bass("TRN2", target_bir_lowering=False, debug=False,
                   dynamic_dma_scratch_size=512)
    ctx = ExitStack()
    P = Prog(nc, ctx)

    xin_d = nc.dram_tensor("xin", [NT, C], F32, kind="ExternalInput").ap()
    xout_d = nc.dram_tensor("xout", [NT, C], F32, kind="ExternalOutput").ap()
    scr = [nc.dram_tensor(f"xscr{i}", [NT, C], F32, kind="Internal").ap()
           for i in range(2)]

    wdr = {}
    for pid in part_ids:
        for wn in PART_WEIGHTS[pid]:
            shape, dt = WEIGHT_SPECS[wn]
            wdr[wn] = nc.dram_tensor(wn, list(shape), dt, kind="ExternalInput").ap()
    for cn, (shape, dt) in CONST_SPECS.items():
        wdr[cn] = nc.dram_tensor("c_" + cn, list(shape), dt, kind="ExternalInput").ap()

    AB = Arena(P, nc, ctx, "arena_bf", BF16_COLS, BF16)
    AF32_ = Arena(P, nc, ctx, "arena_f32", F32_COLS, F32)
    AU8 = Arena(P, nc, ctx, "arena_u8", U8_COLS, U8)

    cm = Common()
    ps = PsumPool(P, nc, ctx)

    def resident(name, shape, dt):
        h = ctx.enter_context(nc.sbuf_tensor("sb_" + name, list(shape), dt))
        t = P.tile("c_" + name, dma_sem_key="c_" + name)
        P.dma("sync", h[:], wdr[name], writes=[t], tag="cl_" + name)
        return h[:], t

    cm.ident, cm.ident_t = resident("ident", [128, 128], BF16)
    cm.ones_row, cm.ones_row_t = resident("ones_row", [1, 128], BF16)
    cm.ones_col, cm.ones_col_t = resident("ones_col", [128, 1], BF16)
    cm.iota8, cm.iota8_t = resident("iota8", [128, 8], F32)
    cm.iota8r, cm.iota8r_t = resident("iota8r", [128, 8], F32)
    cm.causmask, cm.causmask_t = resident("causmask", [128, 896], BF16)
    cm.esel, cm.esel_t = resident("esel", [8, 1024], BF16)

    xres_tiles = {}

    def x_stage(si):
        if si == 0:
            d = xin_d
        elif si == len(part_ids):
            d = xout_d
        else:
            d = scr[(si - 1) % 2]
        key = id(d)
        if key not in xres_tiles:
            xres_tiles[key] = [P.tile(f"x{si}_{i}") for i in range(NTT)]
        return (lambda i, d=d: d[i * 128:(i + 1) * 128, :]), xres_tiles[key]

    emitters = {"l0a": emit_l0_attn, "l0f": emit_l0_ffn,
                "l1a": emit_l1_attn, "l1f": emit_l1_ffn,
                "l2a": emit_l2_attn, "l2f": emit_l2_ffn,
                "l3a": emit_l3_attn, "l3f": emit_l3_ffn}

    for si, pid in enumerate(part_ids):
        io = {"xin": x_stage(si), "xout": x_stage(si + 1)}
        emitters[pid](P, AB, AF32_, AU8, ps, cm, nc, io, wdr)

    P.finalize()
    with nc.Block() as block:
        @block.sync
        def _(eng):
            P.emit("sync", eng)

        @block.vector
        def _(eng):
            P.emit("vector", eng)

        @block.scalar
        def _(eng):
            P.emit("scalar", eng)

        @block.tensor
        def _(eng):
            P.emit("tensor", eng)

        @block.gpsimd
        def _(eng):
            P.emit("gpsimd", eng)
            P.emit_final_wait(eng)

    return nc, ctx, P


PART_GROUPS = [["l0a", "l0f", "l1a", "l1f", "l2a", "l2f", "l3a", "l3f"]]


def _run_groups(inputs, groups, x0):
    consts = prepare_consts()
    weights = prepare_weights(inputs)
    x = np.ascontiguousarray(
        np.asarray(x0, dtype=np.float32).reshape(NCORES, NT, C))
    for group in groups:
        nc, ctx, P = build_program(group)
        in_maps = []
        for cidx in range(NCORES):
            m = {"xin": x[cidx]}
            for pid in group:
                for wn in PART_WEIGHTS[pid]:
                    m[wn] = np.asarray(weights[wn])
            for cn in CONST_SPECS:
                m["c_" + cn] = np.asarray(consts[cn])
            in_maps.append(m)
        res = run_bass_kernel_spmd(nc, in_maps, core_ids=list(range(NCORES)))
        x = np.stack([np.asarray(res.results[cidx]["xout"]) for cidx in range(NCORES)])
        ctx.close()
    return x.reshape(B, T, C)


def kernel(**inputs):
    x0 = np.asarray(inputs["L0_attn_x"], dtype=np.float32)
    out = _run_groups(inputs, PART_GROUPS, x0)
    return out.astype(np.float32)


# version 07
# speedup vs baseline: 1.2153x; 1.2153x over previous
import numpy as np
import ml_dtypes
import concourse.bass as bass
import concourse.mybir as mybir
from concourse.bass_utils import run_bass_kernel_spmd
from contextlib import ExitStack

F32 = mybir.dt.float32
BF16 = mybir.dt.bfloat16
U8 = mybir.dt.uint8
AF = mybir.ActivationFunctionType
ALU = mybir.AluOpType
AX = mybir.AxisListType

NCORES = 8
B, T, C = 16, 2048, 1024
BPC = B // NCORES
NT = BPC * T
NTT = NT // 128
TPB = T // 128
NHEAD = 16
HD = 64


class Tile:
    __slots__ = ("name", "last_write", "readers", "war_deps", "dma_sem_key")

    def __init__(self, name, dma_sem_key=None):
        self.name = name
        self.last_write = None
        self.readers = []
        self.war_deps = ()
        self.dma_sem_key = dma_sem_key


class Step:
    __slots__ = ("idx", "role", "fn", "deps", "needs_inc", "ev", "inck",
                 "sem_key", "tag")

    def __init__(self, idx, role, fn, inck, sem_key, tag):
        self.idx = idx
        self.role = role
        self.fn = fn
        self.deps = ()
        self.needs_inc = inck == 16
        self.ev = None
        self.inck = inck
        self.sem_key = sem_key
        self.tag = tag


class FlushStep:
    __slots__ = ("sem_key", "ev", "needs_inc", "role")

    def __init__(self, sem_key, snap):
        self.sem_key = sem_key
        self.ev = (sem_key, snap)
        self.needs_inc = True
        self.role = "_flush"


class Prog:
    def __init__(self, nc, ctx):
        self.nc = nc
        self.ctx = ctx
        self.steps = []
        self.sem_handles = {}
        self.live_counts = {}

    def tile(self, name, dma_sem_key=None):
        return Tile(name, dma_sem_key)

    def _record(self, role, fn, reads, writes, inck, sem_key, tag):
        st = Step(len(self.steps), role, fn, inck, sem_key, tag)
        deps = set()
        for t in reads:
            if t.last_write is not None:
                deps.add(t.last_write)
        for t in writes:
            if t.last_write is not None:
                deps.add(t.last_write)
            for r in t.readers:
                deps.add(r)
            if t.war_deps:
                deps.update(t.war_deps)
                t.war_deps = ()
        deps.discard(st)
        st.deps = tuple(d for d in deps
                        if not (role == "tensor" and getattr(d, "role", "") == "tensor"))
        for t in writes:
            t.last_write = st
            t.readers = []
        for t in reads:
            t.readers.append(st)
        self.steps.append(st)
        self.live_counts[sem_key] = self.live_counts.get(sem_key, 0) + inck
        return st

    def do(self, role, fn, reads=(), writes=(), tag=""):
        return self._record(role, fn, reads, writes, 1, "e_" + role, tag)

    def dma(self, role, out_ap, in_ap, writes, reads=(), tag=""):
        key = None
        if writes and writes[0].dma_sem_key:
            key = writes[0].dma_sem_key
        elif reads and reads[0].dma_sem_key:
            key = reads[0].dma_sem_key
        else:
            key = "dma_store"
        fn = lambda eng, o=out_ap, i=in_ap: eng.dma_start(out=o, in_=i)
        return self._record(role, fn, reads, writes, 16, key, tag)

    def dram_flush(self, tiles):
        fl = FlushStep("dma_store", self.live_counts.get("dma_store", 0))
        for t in tiles:
            if t.last_write is not None:
                t.last_write = fl
                t.readers = []

    def finalize(self):
        for st in self.steps:
            for d in st.deps:
                d.needs_inc = True
        counts = {}
        for st in self.steps:
            if st.needs_inc:
                counts.setdefault(st.sem_key, 0)
                counts[st.sem_key] += st.inck
                st.ev = (st.sem_key, counts[st.sem_key])
        self.sem_totals = counts
        keys = set(counts)
        for st in self.steps:
            for d in st.deps:
                keys.add(d.ev[0] if d.ev else d.sem_key)
        for k in sorted(keys):
            self.sem_handles[k] = self.ctx.enter_context(self.nc.semaphore(k))

    def emit(self, role, eng):
        hw = {}
        for st in self.steps:
            if st.role != role:
                continue
            for d in st.deps:
                if d.ev is None:
                    continue
                key, val = d.ev
                if hw.get(key, 0) < val:
                    eng.wait_ge(self.sem_handles[key], val)
                    hw[key] = val
            ins = st.fn(eng)
            if st.needs_inc:
                ins.then_inc(self.sem_handles[st.sem_key], st.inck)

    def emit_final_wait(self, eng):
        for k, tot in sorted(self.sem_totals.items()):
            if tot and (k.startswith("dl") or k == "dma_store" or k.startswith("c_")):
                eng.wait_ge(self.sem_handles[k], tot)



class Buf:
    __slots__ = ("ap", "t", "off", "cols", "arena")

    def __getitem__(self, sl):
        return self.ap[sl]


class Arena:
    ALIGN = 32

    _sem_pool = []
    _sem_n = [0]
    _sem_last = {}

    def __init__(self, P, nc, ctx, name, cols, dtype):
        self.P = P
        self.h = ctx.enter_context(nc.sbuf_tensor(name, [128, cols], dtype))
        self.free = [(0, cols)]
        self.war_iv = []
        self.name = name
        self.live = {}

    def _collect_war(self, off, end):
        deps = set()
        keep = []
        for (a, b, dd) in self.war_iv:
            if a < end and off < b:
                deps.update(dd)
                if a < off:
                    keep.append((a, off, dd))
                if b > end:
                    keep.append((end, b, dd))
            else:
                keep.append((a, b, dd))
        self.war_iv = keep
        return deps

    def alloc(self, name, cols, dma_slot=False):
        cols = (cols + self.ALIGN - 1) & ~(self.ALIGN - 1)
        for i, (off, ln) in enumerate(self.free):
            if ln >= cols:
                if ln == cols:
                    self.free.pop(i)
                else:
                    self.free[i] = (off + cols, ln - cols)
                b = Buf()
                b.ap = self.h[:, off:off + cols]
                key = None
                sem_deps = set()
                if dma_slot:
                    if Arena._sem_pool:
                        key = Arena._sem_pool.pop()
                        old = Arena._sem_last.get(key)
                        if old is not None:
                            sem_deps.update(old.readers)
                            if old.last_write is not None and \
                                    not isinstance(old.last_write, FlushStep):
                                sem_deps.add(old.last_write)
                    else:
                        key = f"dl{Arena._sem_n[0]}"
                        Arena._sem_n[0] += 1
                b.t = self.P.tile(name, dma_sem_key=key)
                if key is not None:
                    Arena._sem_last[key] = b.t
                b.t.war_deps = tuple(self._collect_war(off, off + cols) | sem_deps)
                b.off = off
                b.cols = cols
                b.arena = self
                for o2, (n2, c2) in self.live.items():
                    assert not (off < o2 + c2 and o2 < off + cols), \
                        f"arena {self.name}: {name} [{off},{off+cols}) overlaps {n2} [{o2},{o2+c2})"
                self.live[off] = (name, cols)
                return b
        raise RuntimeError(f"arena {self.name} overflow: need {cols}, free={self.free}")

    def freeb(self, b):
        assert b.off in self.live, f"arena {self.name}: double free of {b.t.name} at {b.off}"
        del self.live[b.off]
        if b.t.dma_sem_key is not None:
            Arena._sem_pool.append(b.t.dma_sem_key)
        deps = set()
        if b.t.last_write is not None and not isinstance(b.t.last_write, FlushStep):
            deps.add(b.t.last_write)
        deps.update(b.t.readers)
        if deps:
            self.war_iv.append((b.off, b.off + b.cols, frozenset(deps)))
        self.free.append((b.off, b.cols))
        self.free.sort()
        merged = []
        for off, ln in self.free:
            if merged and merged[-1][0] + merged[-1][1] == off:
                merged[-1] = (merged[-1][0], merged[-1][1] + ln)
            else:
                merged.append([off, ln])
        self.free = [tuple(x) for x in merged]


class PsumPool:
    def __init__(self, P, nc, ctx):
        self.P = P
        self.banks = []
        self.busy = [False] * 8
        for i in range(8):
            h = ctx.enter_context(nc.psum_tensor(f"psb{i}", [128, 512], F32))
            self.banks.append((h[:], P.tile(f"psum{i}"), i))
        self.rr = 0

    def get(self):
        for _ in range(8):
            ap, t, i = self.banks[self.rr % 8]
            self.rr += 1
            if not self.busy[i]:
                return ap, t
        raise RuntimeError("all psum banks taken")

    def take(self):
        for ap, t, i in self.banks:
            if not self.busy[i]:
                self.busy[i] = True
                return ap, t, i
        raise RuntimeError("no free psum bank to take")

    def release(self, i):
        self.busy[i] = False



def _t(a):
    return np.ascontiguousarray(np.asarray(a).T)


def _rope_tables(head_dim_half, repeat):
    j = np.arange(head_dim_half, dtype=np.float32)
    theta = np.power(np.float32(10000.0),
                     np.float32(-2.0) * j / np.float32(2 * head_dim_half))
    t = np.arange(T, dtype=np.float32)
    ang = t[:, None] * theta[None, :]
    cos = np.cos(ang).astype(np.float32)
    sin = np.sin(ang).astype(np.float32)
    if repeat > 1:
        cos = np.tile(cos, (1, repeat))
        sin = np.tile(sin, (1, repeat))
    return np.ascontiguousarray(cos), np.ascontiguousarray(sin)


def _alibi_slopes(n):
    import math
    def pow2(m):
        start = 2 ** (-2 ** (-(math.log2(m) - 3)))
        return [start * start ** i for i in range(m)]
    if math.log2(n).is_integer():
        return pow2(n)
    c = 2 ** math.floor(math.log2(n))
    return pow2(c) + _alibi_slopes(2 * c)[0::2][:n - c]


def prepare_consts():
    c = {}
    c["ident"] = np.eye(128, dtype=ml_dtypes.bfloat16)
    c["ones_row"] = np.ones((1, 128), dtype=ml_dtypes.bfloat16)
    c["ones_col"] = np.ones((128, 1), dtype=ml_dtypes.bfloat16)
    c["iota8"] = np.tile(np.arange(8, dtype=np.float32), (128, 1))
    c["iota8r"] = np.tile(1000.0 - np.arange(8, dtype=np.float32), (128, 1))
    p = np.arange(128)[:, None]
    u = np.arange(896)[None, :]
    c["causmask"] = (p <= u - 384).astype(ml_dtypes.bfloat16)
    esel = np.zeros((8, 8 * 128), dtype=ml_dtypes.bfloat16)
    for e in range(8):
        esel[e, e * 128:(e + 1) * 128] = 1
    c["esel"] = esel
    c["rope0_cos"], c["rope0_sin"] = _rope_tables(HD // 2, NHEAD)
    c["rope1_cos"], c["rope1_sin"] = _rope_tables(C // 2, 1)
    slopes = np.asarray(_alibi_slopes(NHEAD), dtype=np.float64)
    idx = np.arange(T, dtype=np.float64)
    c["alibi_srow"] = np.ascontiguousarray(
        (8.0 * slopes[:, None] * idx[None, :]).reshape(1, -1).astype(np.float32))
    c["alibi_negtrow"] = np.ascontiguousarray(
        (-8.0 * slopes[:, None] * idx[None, :]).reshape(1, -1).astype(np.float32))
    band = np.minimum(p - (u - 384), 0).astype(np.float64)
    c["alibi_band"] = np.ascontiguousarray(
        (8.0 * slopes[:, None, None] * band[None]).astype(np.float32))
    pp = np.arange(128, dtype=np.float64)
    sc = np.zeros((128, 256), dtype=np.float64)
    for h in range(NHEAD):
        for kt in range(TPB):
            sc[:, h * 16 + kt] = slopes[h] * (kt * 128 + pp)
    c["alibi_scol"] = sc.astype(np.float32)
    return c


def prepare_weights(inp):
    w = {}
    for n in ("wq", "wk", "wv", "wp"):
        w[f"l0a_{n}T"] = _t(inp[f"L0_attn_{n}"])
    for n in ("bq", "bk", "bv", "bp"):
        w[f"l0a_{n}"] = np.asarray(inp[f"L0_attn_{n}"]).reshape(1, C)
    k0 = np.asarray(inp["L0_ffn_k0"]).view(np.uint8)
    w["l0f_k0t"] = np.ascontiguousarray(k0[:C])
    w["l0f_k0b"] = np.ascontiguousarray(k0[C:])
    w["l0f_k1"] = np.asarray(inp["L0_ffn_k1"]).view(np.uint8)
    w["l0f_k2"] = np.asarray(inp["L0_ffn_k2"]).view(np.uint8)
    for n in ("s0", "s1", "s2"):
        w[f"l0f_{n}"] = np.ascontiguousarray(
            np.broadcast_to(np.asarray(inp[f"L0_ffn_{n}"]).reshape(1, 1), (128, 1)))
    for n in ("wq", "wk", "wv"):
        w[f"l1a_{n}T"] = _t(inp[f"L1_attn_{n}_w"])
        w[f"l1a_b{n[1]}"] = np.asarray(inp[f"L1_attn_{n}_b"]).reshape(1, C)
    w["l1f_Wg"] = np.asarray(inp["L1_ffn_Wg"])
    w["l1f_bg"] = np.asarray(inp["L1_ffn_bg"]).reshape(1, 8)
    w["l1f_W1"] = np.asarray(inp["L1_ffn_W1"])
    w["l1f_b1"] = np.ascontiguousarray(
        np.asarray(inp["L1_ffn_b1"]).reshape(8, 8, 128).transpose(2, 0, 1).reshape(128, 64))
    w["l1f_W2"] = np.asarray(inp["L1_ffn_W2"])
    w["l1f_b2"] = np.asarray(inp["L1_ffn_b2"])
    w["l2a_wqkv"] = np.asarray(inp["L2_attn_W_qkv"])
    w["l2a_bqkv"] = np.asarray(inp["L2_attn_b_qkv"]).reshape(1, 3 * C)
    w["l2a_wp"] = np.asarray(inp["L2_attn_W_proj"])
    w["l2a_bp"] = np.asarray(inp["L2_attn_b_proj"]).reshape(1, C)
    w["l2f_gateT"] = _t(inp["L2_ffn_gate_w"])
    w["l2f_wgT"] = np.ascontiguousarray(np.transpose(inp["L2_ffn_w_gate"], (0, 2, 1)))
    w["l2f_wuT"] = np.ascontiguousarray(np.transpose(inp["L2_ffn_w_up"], (0, 2, 1)))
    w["l2f_wdT"] = np.ascontiguousarray(np.transpose(inp["L2_ffn_w_down"], (0, 2, 1)))
    w["l3a_wqkvT"] = _t(inp["L3_attn_w_qkv"])
    w["l3a_woT"] = _t(inp["L3_attn_w_out"])
    w["l3a_bo"] = np.asarray(inp["L3_attn_b_out"]).reshape(1, C)
    rft = _t(inp["L3_attn_random_features"]).astype(np.float32)
    blk = np.zeros((128, 128), dtype=np.float32)
    blk[0:64, 0:64] = rft
    blk[64:128, 64:128] = rft
    w["l3a_rfT"] = np.ascontiguousarray(blk)
    I0 = 938
    w["l3f_gateT"] = _t(inp["L3_ffn_gate_weight"])
    def pad_gu(gu):
        out = np.zeros((2048, C), dtype=np.float32)
        out[:I0] = gu[:I0]
        out[1024:1024 + I0] = gu[I0:]
        return _t(out)
    def pad_dn(dn):
        out = np.zeros((1024, C), dtype=np.float32)
        out[:I0] = np.asarray(dn).T
        return np.ascontiguousarray(out)
    w["l3f_sh_guT"] = pad_gu(np.asarray(inp["L3_ffn_shared_gate_up"]))
    w["l3f_sh_dnT"] = pad_dn(np.asarray(inp["L3_ffn_shared_down"]))
    egu = np.asarray(inp["L3_ffn_experts_gate_up"])
    edn = np.asarray(inp["L3_ffn_experts_down"])
    w["l3f_ex_guT"] = np.stack([pad_gu(egu[e]) for e in range(8)])
    w["l3f_ex_dnT"] = np.stack([pad_dn(edn[e]) for e in range(8)])
    return w



class Common:
    pass


def mm_group(P, psum_ap, psum_t, mms, reads, tag="", skip_gc=False):
    def fn(e):
        n = len(mms)
        ins = None
        for i, (out_ap, lh, rh, tp) in enumerate(mms):
            ins = e.matmul(out=out_ap, lhsT=lh, rhs=rh, start=(i == 0),
                           stop=(i == n - 1), tile_position=tp,
                           skip_group_check=skip_gc)
        return ins
    return P.do("tensor", fn, reads=reads, writes=[psum_t], tag=tag)


def ln_tile(P, cm, sl, xt_ap, xt_t, name):
    stat, sq, xn = sl["st"], sl["sq"], sl["xn"]
    mean = stat.ap[:, 0:1]
    msq = stat.ap[:, 1:2]
    var = stat.ap[:, 2:3]
    rstd = stat.ap[:, 3:4]
    P.do("vector", lambda e: e.reduce_sum(out=mean, in_=xt_ap, axis=AX.X),
         reads=[xt_t], writes=[stat.t], tag=name + "s")
    P.do("scalar", lambda e: e.activation(out=sq.ap, in_=xt_ap, func=AF.Square),
         reads=[xt_t], writes=[sq.t], tag=name + "q")
    P.do("vector", lambda e: e.reduce_sum(out=msq, in_=sq.ap, axis=AX.X),
         reads=[sq.t], writes=[stat.t], tag=name + "q2")
    P.do("vector", lambda e: e.tensor_scalar(out=stat.ap[:, 0:2], in0=stat.ap[:, 0:2],
                                             scalar1=1.0 / 1024, scalar2=None, op0=ALU.mult),
         reads=[stat.t], writes=[stat.t], tag=name + "d")
    P.do("vector", lambda e: e.tensor_scalar(out=var, in0=mean, scalar1=mean,
                                             scalar2=None, op0=ALU.mult),
         reads=[stat.t], writes=[stat.t], tag=name + "m2")
    P.do("vector", lambda e: e.scalar_tensor_tensor(out=var, in0=msq, scalar=1e-5,
                                                     in1=var, op0=ALU.add, op1=ALU.subtract),
         reads=[stat.t], writes=[stat.t], tag=name + "v")
    P.do("scalar", lambda e: e.activation(out=rstd, in_=var, func=AF.Ln),
         reads=[stat.t], writes=[stat.t], tag=name + "l")
    P.do("scalar", lambda e: e.activation(out=rstd, in_=rstd, func=AF.Exp, scale=-0.5),
         reads=[stat.t], writes=[stat.t], tag=name + "e")
    P.do("vector", lambda e: e.tensor_scalar(out=xn.ap, in0=xt_ap, scalar1=mean,
                                             scalar2=rstd, op0=ALU.subtract, op1=ALU.mult),
         reads=[xt_t, stat.t], writes=[xn.t], tag=name + "n")
    return xn


def transpose_to(P, ps, cm, src_ap, src_t, dst_ap, dst_t, ncols, tag,
                 dst_view=None):
    assert ncols <= 1024
    bank, bank_t = ps.get()
    bb = bank.bitcast(BF16)
    nb = ncols // 128
    def fn(e):
        ins = None
        for ct in range(nb):
            ins = e.transpose(out=bb[:, ct * 128:(ct + 1) * 128],
                              in_=src_ap[:, ct * 128:(ct + 1) * 128],
                              identity=cm.ident)
        return ins
    P.do("tensor", fn, reads=[src_t, cm.ident_t], writes=[bank_t], tag=tag + "t")
    if dst_view is not None:
        dv = dst_view
        sv = bb[:, 0:ncols].rearrange("p (c t) -> p c t", t=128)
    else:
        dv = dst_ap
        sv = bb[:, 0:ncols]
    P.do("scalar", lambda e: e.copy(out=dv, in_=sv),
         reads=[bank_t], writes=[dst_t], tag=tag + "e")


def load_weight_into(P, wb, stg, dram_ap, rows, cols, name, role="sync"):
    ntile = rows // 128
    for ct in range(ntile):
        P.dma(role, stg.ap[:, 0:cols], dram_ap[ct * 128:(ct + 1) * 128, :],
              writes=[stg.t], tag=name + f"L{ct}")
        dst = wb.ap[:, ct * cols:(ct + 1) * cols]
        P.do("vector",
             lambda e, o=dst, i=stg.ap[:, 0:cols]: e.tensor_copy(out=o, in_=i),
             reads=[stg.t], writes=[wb.t], tag=name + f"C{ct}")


def load_weight_bf16(P, AB, AF32, dram_ap, rows, cols, name, role="sync"):
    ntile = rows // 128
    wb = AB.alloc(name, ntile * cols)
    stg = AF32.alloc(name + "_stg", cols, dma_slot=True)
    for ct in range(ntile):
        P.dma(role, stg.ap, dram_ap[ct * 128:(ct + 1) * 128, :], writes=[stg.t],
              tag=name + f"L{ct}")
        dst = wb.ap[:, ct * cols:(ct + 1) * cols]
        P.do("vector",
             lambda e, o=dst, i=stg.ap: e.tensor_copy(out=o, in_=i),
             reads=[stg.t], writes=[wb.t], tag=name + f"C{ct}")
    AF32.freeb(stg)
    return wb


def load_bias_row(P, AB, AF32, dram_ap, cols, name):
    stg = AF32.alloc(name + "_stg", cols, dma_slot=True)
    bb = AB.alloc(name, cols)
    P.dma("sync", stg.ap[0:1, :], dram_ap, writes=[stg.t], tag=name + "L")
    P.do("vector", lambda e: e.tensor_copy(out=bb.ap[0:1, :], in_=stg.ap[0:1, :]),
         reads=[stg.t], writes=[bb.t], tag=name + "C")
    AF32.freeb(stg)
    return bb



def emit_l0_attn(P, AB, AF32, AU8, ps, cm, nc, io, wdr):
    for b in range(BPC):
        _softmax_attn_batch(
            P, AB, AF32, ps, cm, nc, io, b, name=f"l0a{b}",
            wspec=dict(kind="sep",
                       wq=("l0a_wqT", "l0a_bq"), wk=("l0a_wkT", "l0a_bk"),
                       wv=("l0a_wvT", "l0a_bv"), wp=("l0a_wpT", "l0a_bp")),
            wdr=wdr, rope=("rope0_cos", "rope0_sin"), causal=True,
            alibi=False, scale=0.125)


def _softmax_attn_batch(P, AB, AF32, ps, cm, nc, io, b, name, wspec, wdr,
                        rope, causal, alibi, scale):
    xin_ap, xin_tiles = io["xin"]
    xout_ap, xout_tiles = io["xout"]

    qfm = AB.alloc(name + "_qfm", 8 * T)
    kfm = AB.alloc(name + "_kfm", 8 * T)
    vtm = AB.alloc(name + "_vtm", TPB * 1024)

    if wspec["kind"] == "sep":
        wq = load_weight_bf16(P, AB, AF32, wdr[wspec["wq"][0]], C, C, name + "_wq")
        wk = load_weight_bf16(P, AB, AF32, wdr[wspec["wk"][0]], C, C, name + "_wk")
        wv = load_weight_bf16(P, AB, AF32, wdr[wspec["wv"][0]], C, C, name + "_wv")
        bq = load_bias_row(P, AB, AF32, wdr[wspec["wq"][1]], C, name + "_bq")
        bk = load_bias_row(P, AB, AF32, wdr[wspec["wk"][1]], C, name + "_bk")
        bv = load_bias_row(P, AB, AF32, wdr[wspec["wv"][1]], C, name + "_bv")
        def wslice(which, ct, oc):
            wb = {"q": wq, "k": wk, "v": wv}[which]
            return wb.ap[:, ct * C + oc * 512: ct * C + (oc + 1) * 512]
        def bslice(which, oc):
            bb = {"q": bq, "k": bk, "v": bv}[which]
            return bb.ap[0:1, oc * 512:(oc + 1) * 512]
        def wtile(which):
            return {"q": wq, "k": wk, "v": wv}[which].t
        def btile(which):
            return {"q": bq, "k": bk, "v": bv}[which].t
    else:
        wqkv = load_weight_bf16(P, AB, AF32, wdr[wspec["wqkv"][0]], C, 3 * C,
                                name + "_wqkv")
        bqkv = load_bias_row(P, AB, AF32, wdr[wspec["wqkv"][1]], 3 * C, name + "_bqkv")
        offmap = {"q": 0, "k": C, "v": 2 * C}
        def wslice(which, ct, oc):
            o0 = offmap[which] + oc * 512
            return wqkv.ap[:, ct * 3 * C + o0: ct * 3 * C + o0 + 512]
        def bslice(which, oc):
            o0 = offmap[which] + oc * 512
            return bqkv.ap[0:1, o0:o0 + 512]
        def wtile(which):
            return wqkv.t
        def btile(which):
            return bqkv.t

    sq_sh = AF32.alloc(name + "_sq", 1024)
    sl = []
    for i in range(2):
        sl.append({
            "xr": AF32.alloc(name + f"_xr{i}", 1024, dma_slot=True),
            "st": AF32.alloc(name + f"_st{i}", 32),
            "sq": sq_sh,
            "xn": AB.alloc(name + f"_xn{i}", 1024),
            "xnT": AB.alloc(name + f"_xnT{i}", 1024),
            "qb": AB.alloc(name + f"_qb{i}", 1024),
            "kb": AB.alloc(name + f"_kb{i}", 1024),
        })
    if rope is not None:
        r1 = AF32.alloc(name + "_r1", 256)
        r2 = AF32.alloc(name + "_r2", 256)
        csl = [AF32.alloc(name + f"_cs{i}", 512, dma_slot=True) for i in range(2)]
        snl = [AF32.alloc(name + f"_sn{i}", 512, dma_slot=True) for i in range(2)]
        cosd, sind = wdr[rope[0]], wdr[rope[1]]

    def proj(which, s, tt, dstb=None, ropecs=None, tagp=""):
        for oc in range(2):
            bank, bank_t = ps.get()
            mms = [(bank, cm.ones_row[0:1, :], bslice(which, oc), None)]
            for ct in range(8):
                mms.append((bank, s["xnT"].ap[:, ct * 128:(ct + 1) * 128],
                            wslice(which, ct, oc), None))
            mm_group(P, bank, bank_t, mms,
                     reads=[s["xnT"].t, wtile(which), btile(which), cm.ones_row_t],
                     tag=tagp + f"m{oc}")
            if dstb is None:
                dst = vtm.ap[:, tt * 1024 + oc * 512: tt * 1024 + (oc + 1) * 512]
                P.do("scalar", lambda e, o=dst, i=bank: e.copy(out=o, in_=i),
                     reads=[bank_t], writes=[vtm.t], tag=tagp + f"e{oc}")
            elif ropecs is None:
                dst = dstb.ap[:, oc * 512:(oc + 1) * 512]
                P.do("scalar", lambda e, o=dst, i=bank: e.copy(out=o, in_=i),
                     reads=[bank_t], writes=[dstb.t], tag=tagp + f"e{oc}")
            else:
                cs, sn = ropecs
                pe = bank[:, 0:512:2]
                po = bank[:, 1:512:2]
                css = cs.ap[:, oc * 256:(oc + 1) * 256]
                sns = sn.ap[:, oc * 256:(oc + 1) * 256]
                de = dstb.ap[:, oc * 512:(oc + 1) * 512][:, 0::2]
                do_ = dstb.ap[:, oc * 512:(oc + 1) * 512][:, 1::2]
                P.do("vector", lambda e, pe=pe, c=css: e.tensor_mul(out=r1.ap, in0=pe, in1=c),
                     reads=[bank_t, cs.t], writes=[r1.t], tag=tagp + f"ra{oc}")
                P.do("vector", lambda e, po=po, sx=sns: e.tensor_mul(out=r2.ap, in0=po, in1=sx),
                     reads=[bank_t, sn.t], writes=[r2.t], tag=tagp + f"rb{oc}")
                P.do("vector", lambda e, de=de: e.tensor_sub(out=de, in0=r1.ap, in1=r2.ap),
                     reads=[r1.t, r2.t], writes=[dstb.t], tag=tagp + f"rc{oc}")
                P.do("vector", lambda e, po=po, c=css: e.tensor_mul(out=r1.ap, in0=po, in1=c),
                     reads=[bank_t, cs.t], writes=[r1.t], tag=tagp + f"rd{oc}")
                P.do("vector", lambda e, pe=pe, sx=sns: e.tensor_mul(out=r2.ap, in0=pe, in1=sx),
                     reads=[bank_t, sn.t], writes=[r2.t], tag=tagp + f"re{oc}")
                P.do("vector", lambda e, do_=do_: e.tensor_add(out=do_, in0=r1.ap, in1=r2.ap),
                     reads=[r1.t, r2.t], writes=[dstb.t], tag=tagp + f"rf{oc}")

    for tt in range(TPB):
        gt = b * TPB + tt
        s = sl[tt % 2]
        P.dma("sync", s["xr"].ap, xin_ap(gt), writes=[s["xr"].t],
              reads=[xin_tiles[gt]], tag=name + f"xl{tt}")
        ln_tile(P, cm, s, s["xr"].ap, s["xr"].t, name + f"ln{tt}")
        transpose_to(P, ps, cm, s["xn"].ap, s["xn"].t, s["xnT"].ap, s["xnT"].t,
                     1024, name + f"xT{tt}")
        if rope is not None:
            cs, sn = csl[tt % 2], snl[tt % 2]
            P.dma("sync", cs.ap, cosd[tt * 128:(tt + 1) * 128, :], writes=[cs.t],
                  tag=name + f"cl{tt}")
            P.dma("sync", sn.ap, sind[tt * 128:(tt + 1) * 128, :], writes=[sn.t],
                  tag=name + f"sl{tt}")
        proj("v", s, tt, None, None, name + f"v{tt}")
        rc = (cs, sn) if rope is not None else None
        proj("q", s, tt, s["qb"], rc, name + f"q{tt}")
        proj("k", s, tt, s["kb"], rc, name + f"k{tt}")
        for (bb_, fm) in ((s["qb"], qfm), (s["kb"], kfm)):
            dv = fm.ap.rearrange("p (c t) -> p c t", t=T)[:, :, tt * 128:(tt + 1) * 128]
            transpose_to(P, ps, cm, bb_.ap, bb_.t, None, fm.t, 1024,
                         name + ("q" if fm is qfm else "k") + f"F{tt}", dst_view=dv)

    if wspec["kind"] == "sep":
        for wbuf in (wq, wk, wv):
            AB.freeb(wbuf)
        for bbuf in (bq, bk, bv):
            AB.freeb(bbuf)
    else:
        AB.freeb(wqkv)
        AB.freeb(bqkv)
    if rope is not None:
        for bufl in (csl, snl):
            for bf in bufl:
                AF32.freeb(bf)
        AF32.freeb(r1)
        AF32.freeb(r2)
    AF32.freeb(sq_sh)
    for s in sl:
        for k in ("xn", "xnT", "qb", "kb"):
            AB.freeb(s[k])

    wp = load_weight_bf16(P, AB, AF32, wdr[wspec["wp"][0]], C, C, name + "_wp")
    bp = load_bias_row(P, AB, AF32, wdr[wspec["wp"][1]], C, name + "_bp")

    ybuf = AB.alloc(name + "_y", 8 * 512)
    mcol = AF32.alloc(name + "_mcol", NHEAD)
    mtmp = AF32.alloc(name + "_mtmp", 16)
    mbias = AF32.alloc(name + "_mbias", NHEAD)
    mneg = AB.alloc(name + "_mneg", 16)
    mrowb = AB.alloc(name + "_mrowb", 16)
    attf = AB.alloc(name + "_attf", 2 * TPB * 512)
    attv = attf.ap.rearrange("p (z k t) -> p z k t", k=TPB, t=512)
    bcf = AF32.alloc(name + "_bcf", 512)
    srow = AB.alloc(name + "_srow", 1024)
    outs = [AF32.alloc(name + f"_o{i}", 1024, dma_slot=True) for i in range(2)]
    if alibi:
        bandb = [AF32.alloc(name + f"_band{z}", 896, dma_slot=True) for z in range(2)]
        srsl = [AF32.alloc(name + f"_sr{z}", 128, dma_slot=True) for z in range(2)]
        ntsl = [AF32.alloc(name + f"_nt{z}", 512, dma_slot=True) for z in range(2)]
        onesf = AF32.alloc(name + "_onesf", 512)
        P.do("vector", lambda e: e.memset(onesf.ap[0:1, :], 1.0),
             writes=[onesf.t], tag=name + "onf")

    qv = qfm.ap.rearrange("p (c t) -> p c t", t=T)
    kv = kfm.ap.rearrange("p (c t) -> p c t", t=T)

    for h in range(NHEAD):
        ct, half = h // 2, (h % 2) * 64
        qsl = qv[half:half + 64, ct, :]
        ksl = kv[half:half + 64, ct, :]
        first = True
        for qs in range(TPB):
            nkc = ((qs // 4) + 1) if causal else 4
            for kc in range(nkc):
                bank, bank_t = ps.get()
                mm_group(P, bank, bank_t,
                         [(bank, qsl[:, qs * 128:(qs + 1) * 128],
                           ksl[:, kc * 512:(kc + 1) * 512], None)],
                         reads=[qfm.t, kfm.t], tag=name + f"x{h}_{qs}_{kc}")
                if first:
                    P.do("vector", lambda e, o=mcol.ap[:, h:h + 1], i=bank:
                         e.reduce_max(out=o, in_=i, axis=AX.X),
                         reads=[bank_t], writes=[mcol.t], tag=name + f"xm{h}_{qs}_{kc}")
                    first = False
                else:
                    P.do("vector", lambda e, o=mtmp.ap[:, 0:1], i=bank:
                         e.reduce_max(out=o, in_=i, axis=AX.X),
                         reads=[bank_t], writes=[mtmp.t], tag=name + f"xm{h}_{qs}_{kc}")
                    P.do("vector", lambda e, o=mcol.ap[:, h:h + 1], i=mtmp.ap[:, 0:1]:
                         e.tensor_max(out=o, in0=o, in1=i),
                         reads=[mtmp.t, mcol.t], writes=[mcol.t],
                         tag=name + f"xx{h}_{qs}_{kc}")
    P.do("vector", lambda e: e.tensor_copy(out=mneg.ap[:, 0:16], in_=mcol.ap[:, 0:16]),
         reads=[mcol.t], writes=[mneg.t], tag=name + "mn")
    bankm, bankm_t = ps.get()
    bbm = bankm.bitcast(BF16)
    P.do("tensor", lambda e: e.transpose(out=bbm[0:16, 0:128], in_=mneg.ap[:, 0:16],
                                         identity=cm.ident),
         reads=[mneg.t, cm.ident_t], writes=[bankm_t], tag=name + "mt")
    P.do("vector", lambda e: e.reduce_max(out=mtmp.ap[0:16, 0:1],
                                          in_=bbm[0:16, 0:128], axis=AX.X),
         reads=[bankm_t], writes=[mtmp.t], tag=name + "mr")
    P.do("vector", lambda e, sc=-scale: e.tensor_scalar(
         out=mrowb.ap[0:16, 0:1], in0=mtmp.ap[0:16, 0:1], scalar1=sc,
         scalar2=None, op0=ALU.mult),
         reads=[mtmp.t], writes=[mrowb.t], tag=name + "ms")
    bankn, bankn_t = ps.get()
    bbn = bankn.bitcast(BF16)
    P.do("tensor", lambda e: e.transpose(out=bbn[0:16, 0:16],
                                         in_=mrowb.ap[0:16, 0:16],
                                         identity=cm.ident[0:16, 0:16]),
         reads=[mrowb.t, cm.ident_t], writes=[bankn_t], tag=name + "mt2")
    P.do("scalar", lambda e: e.copy(out=mrowb.ap[0:1, 0:16], in_=bbn[0:1, 0:16]),
         reads=[bankn_t], writes=[mrowb.t], tag=name + "mv")
    bankb, bankb_t = ps.get()
    def fnmb(e, bankb=bankb):
        ins = None
        for h in range(NHEAD):
            ins = e.matmul(out=bankb[:, h:h + 1], lhsT=cm.ones_row[0:1, :],
                           rhs=mrowb.ap[0:1, h:h + 1], start=True, stop=True)
        return ins
    P.do("tensor", fnmb, reads=[mrowb.t, cm.ones_row_t], writes=[bankb_t],
         tag=name + "mb")
    P.do("scalar", lambda e: e.copy(out=mbias.ap[:, 0:16], in_=bankb[:, 0:16]),
         reads=[bankb_t], writes=[mbias.t], tag=name + "mbe")

    for qc in range(4):
        ktmax = (4 * qc + 4) if causal else TPB
        for hp in range(8):
            if alibi:
                for z in range(2):
                    h = 2 * hp + z
                    P.dma("sync", bandb[z].ap[:, 0:896], wdr["alibi_band"][h],
                          writes=[bandb[z].t], tag=name + f"bl{qc}{hp}{z}")
                    P.dma("sync", ntsl[z].ap[0:1, 0:512],
                          wdr["alibi_negtrow"][0:1, h * T + qc * 512: h * T + (qc + 1) * 512],
                          writes=[ntsl[z].t], tag=name + f"nt{qc}{hp}{z}")
            for z in range(2):
                h = 2 * hp + z
                ct, half = h // 2, (h % 2) * 64
                ksl = kv[half:half + 64, ct, :]
                qsl = qv[half:half + 64, ct, :]
                for kt in range(ktmax):
                    bank, bank_t = ps.get()
                    mms = [(bank, ksl[:, kt * 128:(kt + 1) * 128],
                            qsl[:, qc * 512:(qc + 1) * 512], None)]
                    rd = [qfm.t, kfm.t]
                    past = kt * 128 + 127 < qc * 512
                    diag = (not past) and kt * 128 < qc * 512 + 512
                    if alibi and past:
                        P.dma("sync", srsl[z].ap[0:1, 0:128],
                              wdr["alibi_srow"][0:1, h * T + kt * 128: h * T + (kt + 1) * 128],
                              writes=[srsl[z].t], tag=name + f"sr{qc}{hp}{kt}{z}")
                        mms.append((bank, srsl[z].ap[0:1, 0:128],
                                    onesf.ap[0:1, 0:512], None))
                        mms.append((bank, onesf.ap[0:1, 0:128],
                                    ntsl[z].ap[0:1, 0:512], None))
                        rd += [srsl[z].t, ntsl[z].t, onesf.t]
                    mm_group(P, bank, bank_t, mms, reads=rd,
                             tag=name + f"s{qc}{hp}{kt}{z}", skip_gc=True)
                    att = attv[:, z, kt, :]
                    mb = mbias.ap[:, h:h + 1]
                    if alibi and diag:
                        delta = kt * 128 - qc * 512
                        off = 384 - delta
                        P.do("vector", lambda e, o=bcf.ap, i=bank,
                             m=bandb[z].ap[:, off:off + 512]:
                             e.tensor_add(out=o, in0=i, in1=m),
                             reads=[bank_t, bandb[z].t], writes=[bcf.t],
                             tag=name + f"ba{qc}{hp}{kt}{z}")
                        P.do("scalar", lambda e, o=att, i=bcf.ap, sc=scale, mb=mb:
                             e.activation(out=o, in_=i, func=AF.Exp, scale=sc, bias=mb),
                             reads=[bcf.t, mbias.t], writes=[attf.t],
                             tag=name + f"p{qc}{hp}{kt}{z}")
                    else:
                        P.do("scalar", lambda e, o=att, i=bank, sc=scale, mb=mb:
                             e.activation(out=o, in_=i, func=AF.Exp, scale=sc, bias=mb),
                             reads=[bank_t, mbias.t], writes=[attf.t],
                             tag=name + f"p{qc}{hp}{kt}{z}")
                    if causal and kt >= 4 * qc:
                        delta = kt * 128 - qc * 512
                        off = 384 - delta
                        msk = cm.causmask[:, off:off + 512]
                        P.do("vector", lambda e, o=att, m=msk:
                             e.tensor_mul(out=o, in0=o, in1=m),
                             reads=[attf.t, cm.causmask_t], writes=[attf.t],
                             tag=name + f"k{qc}{hp}{kt}{z}")
            av_ap, av_t, av_i = ps.take()
            sm = [ps.take() for _ in range(2)]
            for kt in range(ktmax):
                st_, sp_ = (kt == 0), (kt == ktmax - 1)
                def fna(e, kt=kt, st_=st_, sp_=sp_, av_ap=av_ap, hp=hp):
                    ins = None
                    for z in range(2):
                        h = 2 * hp + z
                        ins = e.matmul(out=av_ap[64 * z:64 * z + 64, :],
                                       lhsT=vtm.ap[:, kt * 1024 + h * 64: kt * 1024 + h * 64 + 64],
                                       rhs=attv[:, z, kt, :], start=st_, stop=sp_,
                                       tile_position=(0, 64 * z), skip_group_check=True)
                    return ins
                P.do("tensor", fna, reads=[vtm.t, attf.t], writes=[av_t],
                     tag=name + f"a{qc}{hp}{kt}")
                for z in range(2):
                    P.do("tensor", lambda e, o=sm[z][0][0:1, :], r=attv[:, z, kt, :],
                         s1=st_, s2=sp_:
                         e.matmul(out=o, lhsT=cm.ones_col, rhs=r, start=s1, stop=s2,
                                  skip_group_check=True),
                         reads=[attf.t, cm.ones_col_t], writes=[sm[z][1]],
                         tag=name + f"u{qc}{hp}{kt}{z}")
            for z in range(2):
                P.do("scalar", lambda e, o=srow.ap[0:1, z * 512:(z + 1) * 512],
                     i=sm[z][0][0:1, :]: e.copy(out=o, in_=i),
                     reads=[sm[z][1]], writes=[srow.t], tag=name + f"w{qc}{hp}{z}")
            bcb, bcb_t = ps.get()
            def fnb(e, bcb=bcb):
                e.matmul(out=bcb[0:64, :], lhsT=cm.ones_row[0:1, 0:64],
                         rhs=srow.ap[0:1, 0:512], start=True, stop=True,
                         tile_position=(0, 0))
                return e.matmul(out=bcb[64:128, :], lhsT=cm.ones_row[0:1, 0:64],
                                rhs=srow.ap[0:1, 512:1024], start=True, stop=True,
                                tile_position=(0, 64))
            P.do("tensor", fnb, reads=[srow.t, cm.ones_row_t], writes=[bcb_t],
                 tag=name + f"b{qc}{hp}")
            P.do("scalar", lambda e, o=bcf.ap, i=bcb: e.copy(out=o, in_=i),
                 reads=[bcb_t], writes=[bcf.t], tag=name + f"c{qc}{hp}")
            P.do("vector", lambda e, o=bcf.ap, i=bcf.ap: e.reciprocal(out=o, in_=i),
                 reads=[bcf.t], writes=[bcf.t], tag=name + f"v{qc}{hp}")
            ydst = ybuf.ap[:, hp * 512:(hp + 1) * 512]
            P.do("vector", lambda e, o=ydst, a=av_ap, d=bcf.ap:
                 e.tensor_mul(out=o, in0=a, in1=d),
                 reads=[av_t, bcf.t], writes=[ybuf.t], tag=name + f"y{qc}{hp}")
            ps.release(av_i)
            for _, smt, smi in sm:
                ps.release(smi)

        for qs in range(4):
            gt = b * TPB + qc * 4 + qs
            s = sl[qs % 2]
            P.dma("sync", s["xr"].ap, xin_ap(gt), writes=[s["xr"].t],
                  reads=[xin_tiles[gt]], tag=name + f"zl{qc}{qs}")
            ob = outs[qs % 2]
            for oc in range(2):
                bank, bank_t = ps.get()
                mms = [(bank, cm.ones_row[0:1, :], bp.ap[0:1, oc * 512:(oc + 1) * 512], None)]
                for hp in range(8):
                    mms.append((bank, ybuf.ap[:, hp * 512 + qs * 128: hp * 512 + qs * 128 + 128],
                                wp.ap[:, hp * C + oc * 512: hp * C + (oc + 1) * 512], None))
                mm_group(P, bank, bank_t, mms,
                         reads=[ybuf.t, wp.t, bp.t, cm.ones_row_t],
                         tag=name + f"j{qc}{qs}{oc}")
                dst = ob.ap[:, oc * 512:(oc + 1) * 512]
                P.do("vector", lambda e, o=dst, i=bank,
                     x=s["xr"].ap[:, oc * 512:(oc + 1) * 512]:
                     e.tensor_add(out=o, in0=i, in1=x),
                     reads=[bank_t, s["xr"].t], writes=[ob.t], tag=name + f"r{qc}{qs}{oc}")
            P.dma("scalar", xout_ap(gt), ob.ap, writes=[xout_tiles[gt]],
                  reads=[ob.t], tag=name + f"zs{qc}{qs}")

    for bb2 in (qfm, kfm, vtm, ybuf, srow, wp, bp, attf, mneg, mrowb):
        AB.freeb(bb2)
    for s in sl:
        AF32.freeb(s["xr"])
        AF32.freeb(s["st"])
    for bb2 in (mcol, mtmp, bcf, mbias):
        AF32.freeb(bb2)
    for o in outs:
        AF32.freeb(o)
    if alibi:
        for z in range(2):
            AF32.freeb(bandb[z])
            AF32.freeb(srsl[z])
            AF32.freeb(ntsl[z])
        AF32.freeb(onesf)




def emit_l0_ffn(P, AB, AF32, AU8, ps, cm, nc, io, wdr):
    name = "l0f"
    xin_ap, xin_tiles = io["xin"]
    xout_ap, xout_tiles = io["xout"]

    scol = AF32.alloc(name + "_sc", 8, dma_slot=True)
    sc2 = AF32.alloc(name + "_sc2", 8)
    for i, nsc in enumerate(("s0", "s1", "s2")):
        P.dma("sync", scol.ap[:, i:i + 1], wdr["l0f_" + nsc], writes=[scol.t],
              tag=name + f"scl{i}")
    P.do("vector", lambda e: e.tensor_scalar(out=sc2.ap[:, 0:3], in0=scol.ap[:, 0:3],
                                             scalar1=2.0, scalar2=None, op0=ALU.mult),
         reads=[scol.t], writes=[sc2.t], tag=name + "sc2")
    s0c = sc2.ap[:, 0:1]
    s1c = sc2.ap[:, 1:2]
    s2c = sc2.ap[:, 2:3]

    sq_sh = AF32.alloc(name + "_sq", 1024)
    sl = []
    for i in range(2):
        sl.append({
            "xr": AF32.alloc(name + f"_xr{i}", 1024, dma_slot=True),
            "st": AF32.alloc(name + f"_st{i}", 32),
            "sq": sq_sh,
            "xn": AB.alloc(name + f"_xn{i}", 1024),
        })
    outs = [AF32.alloc(name + f"_o{i}", 1024, dma_slot=True) for i in range(2)]

    TC = 1024
    NCH = NT // TC

    for ch in range(NCH):
        h0 = AB.alloc(name + f"_h0_{ch}", 32 * TC)
        xnT = AB.alloc(name + f"_xnT{ch}", 8 * TC)
        for lt in range(TC // 128):
            gt = ch * (TC // 128) + lt
            s = sl[lt % 2]
            P.dma("sync", s["xr"].ap, xin_ap(gt), writes=[s["xr"].t],
                  reads=[xin_tiles[gt]], tag=name + f"xl{gt}")
            ln_tile(P, cm, s, s["xr"].ap, s["xr"].t, name + f"ln{gt}")
            dv = xnT.ap.rearrange("p (c t) -> p c t", t=TC)[:, :, lt * 128:(lt + 1) * 128]
            transpose_to(P, ps, cm, s["xn"].ap, s["xn"].t, None, xnT.t, 1024,
                         name + f"xT{gt}", dst_view=dv)
        xnTv = xnT.ap.rearrange("p (c t) -> p c t", t=TC)

        dstg = AU8.alloc(name + "_dstg", 2 * 1024, dma_slot=True)
        dbf = AB.alloc(name + "_dbf", 8 * 128)
        h0v = h0.ap.rearrange("p (o t) -> p o t", t=TC)
        for o1 in range(32):
            P.dma("sync", dstg.ap[:, 0:1024].rearrange("p (c o) -> p c o", o=128),
                  wdr["l0f_k0t"].rearrange("(c p) o -> p c o", p=128)[:, :, o1 * 128:(o1 + 1) * 128],
                  writes=[dstg.t], tag=name + f"k0t{ch}_{o1}")
            P.dma("sync", dstg.ap[:, 1024:2048].rearrange("p (c o) -> p c o", o=128),
                  wdr["l0f_k0b"].rearrange("(c p) o -> p c o", p=128)[:, :, o1 * 128:(o1 + 1) * 128],
                  writes=[dstg.t], tag=name + f"k0b{ch}_{o1}")
            P.do("vector", lambda e, o=dbf.ap, a=dstg.ap[:, 0:1024], b=dstg.ap[:, 1024:2048]:
                 e.tensor_sub(out=o, in0=a, in1=b),
                 reads=[dstg.t], writes=[dbf.t], tag=name + f"dcv{ch}_{o1}")
            for ts in range(TC // 512):
                bank, bank_t = ps.get()
                mms = []
                for ct in range(8):
                    mms.append((bank, dbf.ap[:, ct * 128:(ct + 1) * 128],
                                xnTv[:, ct, ts * 512:(ts + 1) * 512], None))
                mm_group(P, bank, bank_t, mms, reads=[dbf.t, xnT.t],
                         tag=name + f"m1_{ch}_{o1}_{ts}")
                dst = h0v[:, o1, ts * 512:(ts + 1) * 512]
                P.do("vector", lambda e, o=dst, i=bank:
                     e.tensor_scalar(out=o, in0=i, scalar1=s0c, scalar2=0.0,
                                     op0=ALU.mult, op1=ALU.max),
                     reads=[bank_t, sc2.t], writes=[h0.t], tag=name + f"e1_{ch}_{o1}_{ts}")
        AU8.freeb(dstg)
        AB.freeb(dbf)
        AB.freeb(xnT)
        h1 = AB.alloc(name + f"_h1_{ch}", 32 * TC)

        kstg = AU8.alloc(name + "_kstg", 4 * 512, dma_slot=True)
        kbf = AB.alloc(name + "_kbf", 4 * 512)
        h1v = h1.ap.rearrange("p (o t) -> p o t", t=TC)
        k1r = wdr["l0f_k1"].rearrange("(c p) o -> p c o", p=128)
        for o2 in range(8):
            bank2 = [ps.get() for _ in range(8)]
            for cg in range(8):
                P.dma("sync", kstg.ap.rearrange("p (c o) -> p c o", o=512),
                      k1r[:, cg * 4:(cg + 1) * 4, o2 * 512:(o2 + 1) * 512],
                      writes=[kstg.t], tag=name + f"k1{ch}_{o2}_{cg}")
                P.do("vector", lambda e, o=kbf.ap, i=kstg.ap:
                     e.tensor_scalar(out=o, in0=i, scalar1=2.0, scalar2=1.0,
                                     op0=ALU.mult, op1=ALU.subtract),
                     reads=[kstg.t], writes=[kbf.t], tag=name + f"kc{ch}_{o2}_{cg}")
                for ci in range(4):
                    c = cg * 4 + ci
                    for ts in range(TC // 512):
                        for ot in range(4):
                            bank, bank_t = bank2[ts * 4 + ot]
                            P.do("tensor", lambda e, o=bank,
                                 l=kbf.ap[:, ci * 512 + ot * 128: ci * 512 + ot * 128 + 128],
                                 r=h0v[:, c, ts * 512:(ts + 1) * 512],
                                 s1_=(c == 0), s2_=(c == 31):
                                 e.matmul(out=o, lhsT=l, rhs=r, start=s1_, stop=s2_,
                                          skip_group_check=True),
                                 reads=[kbf.t, h0.t], writes=[bank_t],
                                 tag=name + f"m2_{ch}_{o2}_{ts}_{c}_{ot}")
            for ts in range(TC // 512):
                for ot in range(4):
                    bank, bank_t = bank2[ts * 4 + ot]
                    dst = h1v[:, o2 * 4 + ot, ts * 512:(ts + 1) * 512]
                    P.do("vector", lambda e, o=dst, i=bank:
                         e.tensor_scalar(out=o, in0=i, scalar1=s1c, scalar2=0.0,
                                         op0=ALU.mult, op1=ALU.max),
                         reads=[bank_t, sc2.t], writes=[h1.t],
                         tag=name + f"e2_{ch}_{o2}_{ts}_{ot}")
        AU8.freeb(kstg)
        AB.freeb(kbf)
        AB.freeb(h0)

        k2stg = AU8.alloc(name + "_k2stg", 8 * 1024, dma_slot=True)
        k2bf = AB.alloc(name + "_k2bf", 32 * 1024)
        k2r = wdr["l0f_k2"].rearrange("(c p) o -> p c o", p=128)
        for qtr in range(4):
            P.dma("sync", k2stg.ap.rearrange("p (c o) -> p c o", o=1024),
                  k2r[:, qtr * 8:(qtr + 1) * 8, :],
                  writes=[k2stg.t], tag=name + f"k2l{ch}_{qtr}")
            P.do("vector", lambda e, o=k2bf.ap[:, qtr * 8192:(qtr + 1) * 8192],
                 i=k2stg.ap:
                 e.tensor_scalar(out=o, in0=i, scalar1=2.0, scalar2=1.0,
                                 op0=ALU.mult, op1=ALU.subtract),
                 reads=[k2stg.t], writes=[k2bf.t], tag=name + f"k2c{ch}_{qtr}")
        k2v = k2bf.ap.rearrange("p (c o) -> p c o", o=1024)
        for lt in range(TC // 128):
            gt = ch * (TC // 128) + lt
            s = sl[lt % 2]
            P.dma("sync", s["xr"].ap, xin_ap(gt), writes=[s["xr"].t],
                  reads=[xin_tiles[gt]], tag=name + f"x3{gt}")
            ob = outs[lt % 2]
            for oc in range(2):
                bank, bank_t = ps.get()
                mms = []
                for c in range(32):
                    mms.append((bank, h1v[:, c, lt * 128:(lt + 1) * 128],
                                k2v[:, c, oc * 512:(oc + 1) * 512], None))
                mm_group(P, bank, bank_t, mms, reads=[h1.t, k2bf.t],
                         tag=name + f"m3_{gt}_{oc}")
                dst = ob.ap[:, oc * 512:(oc + 1) * 512]
                P.do("vector", lambda e, o=dst, i=bank,
                     x=s["xr"].ap[:, oc * 512:(oc + 1) * 512]:
                     e.scalar_tensor_tensor(out=o, in0=i, scalar=s2c, in1=x,
                                            op0=ALU.mult, op1=ALU.add),
                     reads=[bank_t, sc2.t, s["xr"].t], writes=[ob.t],
                     tag=name + f"e3_{gt}_{oc}")
            P.dma("scalar", xout_ap(gt), ob.ap, writes=[xout_tiles[gt]],
                  reads=[ob.t], tag=name + f"zs{gt}")
        AU8.freeb(k2stg)
        AB.freeb(k2bf)
        AB.freeb(h1)

    AF32.freeb(sq_sh)
    for s in sl:
        AF32.freeb(s["xr"])
        AF32.freeb(s["st"])
        AB.freeb(s["xn"])
    for o in outs:
        AF32.freeb(o)
    AF32.freeb(scol)
    AF32.freeb(sc2)




def emit_l1_attn(P, AB, AF32, AU8, ps, cm, nc, io, wdr):
    for b in range(BPC):
        _l1_attn_batch(P, AB, AF32, ps, cm, nc, io, b, wdr, name=f"l1a{b}")


def _l1_attn_batch(P, AB, AF32, ps, cm, nc, io, b, wdr, name):
    xin_ap, xin_tiles = io["xin"]
    xout_ap, xout_tiles = io["xout"]

    qfm = AB.alloc(name + "_qfm", 8 * T)
    kfm = AB.alloc(name + "_kfm", 8 * T)
    vtm = AB.alloc(name + "_vtm", TPB * 1024)

    wq = load_weight_bf16(P, AB, AF32, wdr["l1a_wqT"], C, C, name + "_wq")
    wk = load_weight_bf16(P, AB, AF32, wdr["l1a_wkT"], C, C, name + "_wk")
    wv = load_weight_bf16(P, AB, AF32, wdr["l1a_wvT"], C, C, name + "_wv")
    bq = load_bias_row(P, AB, AF32, wdr["l1a_bq"], C, name + "_bq")
    bk = load_bias_row(P, AB, AF32, wdr["l1a_bk"], C, name + "_bk")
    bv = load_bias_row(P, AB, AF32, wdr["l1a_bv"], C, name + "_bv")
    wmap = {"q": (wq, bq), "k": (wk, bk), "v": (wv, bv)}

    sq_sh = AF32.alloc(name + "_sq", 1024)
    sl = []
    for i in range(2):
        sl.append({
            "xr": AF32.alloc(name + f"_xr{i}", 1024, dma_slot=True),
            "st": AF32.alloc(name + f"_st{i}", 32),
            "sq": sq_sh,
            "xn": AB.alloc(name + f"_xn{i}", 1024),
            "xnT": AB.alloc(name + f"_xnT{i}", 1024),
            "qb": AB.alloc(name + f"_qb{i}", 1024),
            "kb": AB.alloc(name + f"_kb{i}", 1024),
        })
    r1 = AF32.alloc(name + "_r1", 256)
    r2 = AF32.alloc(name + "_r2", 256)
    csl = [AF32.alloc(name + f"_cs{i}", 512, dma_slot=True) for i in range(2)]
    snl = [AF32.alloc(name + f"_sn{i}", 512, dma_slot=True) for i in range(2)]
    cosd, sind = wdr["rope1_cos"], wdr["rope1_sin"]

    def proj(which, s, tt, dstb, ropecs, tagp):
        wb, bb = wmap[which]
        for oc in range(2):
            bank, bank_t = ps.get()
            mms = [(bank, cm.ones_row[0:1, :], bb.ap[0:1, oc * 512:(oc + 1) * 512], None)]
            for ct in range(8):
                mms.append((bank, s["xnT"].ap[:, ct * 128:(ct + 1) * 128],
                            wb.ap[:, ct * C + oc * 512: ct * C + (oc + 1) * 512], None))
            mm_group(P, bank, bank_t, mms,
                     reads=[s["xnT"].t, wb.t, bb.t, cm.ones_row_t], tag=tagp + f"m{oc}")
            if dstb is None:
                dst = vtm.ap[:, tt * 1024 + oc * 512: tt * 1024 + (oc + 1) * 512]
                P.do("scalar", lambda e, o=dst, i=bank: e.copy(out=o, in_=i),
                     reads=[bank_t], writes=[vtm.t], tag=tagp + f"e{oc}")
            else:
                cs, sn = ropecs
                pe = bank[:, 0:512:2]
                po = bank[:, 1:512:2]
                css = cs.ap[:, oc * 256:(oc + 1) * 256]
                sns = sn.ap[:, oc * 256:(oc + 1) * 256]
                de = dstb.ap[:, oc * 512:(oc + 1) * 512][:, 0::2]
                do_ = dstb.ap[:, oc * 512:(oc + 1) * 512][:, 1::2]
                P.do("vector", lambda e, pe=pe, c=css: e.tensor_mul(out=r1.ap, in0=pe, in1=c),
                     reads=[bank_t, cs.t], writes=[r1.t], tag=tagp + f"ra{oc}")
                P.do("vector", lambda e, po=po, sx=sns: e.tensor_mul(out=r2.ap, in0=po, in1=sx),
                     reads=[bank_t, sn.t], writes=[r2.t], tag=tagp + f"rb{oc}")
                P.do("vector", lambda e, de=de: e.tensor_sub(out=de, in0=r1.ap, in1=r2.ap),
                     reads=[r1.t, r2.t], writes=[dstb.t], tag=tagp + f"rc{oc}")
                P.do("vector", lambda e, po=po, c=css: e.tensor_mul(out=r1.ap, in0=po, in1=c),
                     reads=[bank_t, cs.t], writes=[r1.t], tag=tagp + f"rd{oc}")
                P.do("vector", lambda e, pe=pe, sx=sns: e.tensor_mul(out=r2.ap, in0=pe, in1=sx),
                     reads=[bank_t, sn.t], writes=[r2.t], tag=tagp + f"re{oc}")
                P.do("vector", lambda e, do_=do_: e.tensor_add(out=do_, in0=r1.ap, in1=r2.ap),
                     reads=[r1.t, r2.t], writes=[dstb.t], tag=tagp + f"rf{oc}")

    for tt in range(TPB):
        gt = b * TPB + tt
        s = sl[tt % 2]
        P.dma("sync", s["xr"].ap, xin_ap(gt), writes=[s["xr"].t],
              reads=[xin_tiles[gt]], tag=name + f"xl{tt}")
        ln_tile(P, cm, s, s["xr"].ap, s["xr"].t, name + f"ln{tt}")
        transpose_to(P, ps, cm, s["xn"].ap, s["xn"].t, s["xnT"].ap, s["xnT"].t,
                     1024, name + f"xT{tt}")
        cs, sn = csl[tt % 2], snl[tt % 2]
        P.dma("sync", cs.ap, cosd[tt * 128:(tt + 1) * 128, :], writes=[cs.t],
              tag=name + f"cl{tt}")
        P.dma("sync", sn.ap, sind[tt * 128:(tt + 1) * 128, :], writes=[sn.t],
              tag=name + f"sl{tt}")
        proj("v", s, tt, None, None, name + f"v{tt}")
        proj("q", s, tt, s["qb"], (cs, sn), name + f"q{tt}")
        proj("k", s, tt, s["kb"], (cs, sn), name + f"k{tt}")
        for (bb_, fm) in ((s["qb"], qfm), (s["kb"], kfm)):
            dv = fm.ap.rearrange("p (c t) -> p c t", t=T)[:, :, tt * 128:(tt + 1) * 128]
            transpose_to(P, ps, cm, bb_.ap, bb_.t, None, fm.t, 1024,
                         name + ("q" if fm is qfm else "k") + f"F{tt}", dst_view=dv)

    for wbuf in (wq, wk, wv, bq, bk, bv):
        AB.freeb(wbuf)
    for bufl in (csl, snl):
        for bf in bufl:
            AF32.freeb(bf)

    attb = AB.alloc(name + "_attb", TPB * 512)
    mcol = AF32.alloc(name + "_mcol", 4)
    negm = AB.alloc(name + "_negm", 4)
    mtmp = AF32.alloc(name + "_mtmp", 4)
    mrow = AB.alloc(name + "_mrow", 512)
    srow = AB.alloc(name + "_srow", 512)
    rcol = AF32.alloc(name + "_rcol", 4)
    outs = [AF32.alloc(name + f"_o{i}", 1024, dma_slot=True) for i in range(2)]

    qv = qfm.ap.rearrange("p (c t) -> p c t", t=T)
    kv = kfm.ap.rearrange("p (c t) -> p c t", t=T)
    attv = attb.ap.rearrange("p (k t) -> p k t", t=512)

    for qc in range(4):
        for qs in range(4):
            qt0 = qc * 512 + qs * 128
            mc = mcol.ap[:, qs:qs + 1]
            for kc in range(4):
                bank, bank_t = ps.get()
                mms = [(bank, qv[:, ct, qt0:qt0 + 128],
                        kv[:, ct, kc * 512:(kc + 1) * 512], None) for ct in range(8)]
                mm_group(P, bank, bank_t, mms, reads=[qfm.t, kfm.t],
                         tag=name + f"x{qc}{qs}{kc}")
                if kc == 0:
                    P.do("vector", lambda e, o=mc, i=bank:
                         e.reduce_max(out=o, in_=i, axis=AX.X),
                         reads=[bank_t], writes=[mcol.t], tag=name + f"xm{qc}{qs}{kc}")
                else:
                    P.do("vector", lambda e, o=mtmp.ap[:, 0:1], i=bank:
                         e.reduce_max(out=o, in_=i, axis=AX.X),
                         reads=[bank_t], writes=[mtmp.t], tag=name + f"xm{qc}{qs}{kc}")
                    P.do("vector", lambda e, o=mc, i=mtmp.ap[:, 0:1]:
                         e.tensor_max(out=o, in0=o, in1=i),
                         reads=[mtmp.t, mcol.t], writes=[mcol.t], tag=name + f"xx{qc}{qs}{kc}")
        P.do("vector", lambda e: e.tensor_scalar(out=negm.ap, in0=mcol.ap,
                                                 scalar1=-1.0, scalar2=None, op0=ALU.mult),
             reads=[mcol.t], writes=[negm.t], tag=name + f"ng{qc}")
        bank, bank_t = ps.get()
        def fn(e, bank=bank):
            ins = None
            for qs in range(4):
                ins = e.matmul(out=bank[0:1, qs * 128:(qs + 1) * 128],
                               lhsT=negm.ap[:, qs:qs + 1], rhs=cm.ident,
                               start=True, stop=True)
            return ins
        P.do("tensor", fn, reads=[negm.t, cm.ident_t], writes=[bank_t],
             tag=name + f"mt{qc}")
        P.do("scalar", lambda e, o=mrow.ap[0:1, :], i=bank[0:1, 0:512]:
             e.copy(out=o, in_=i),
             reads=[bank_t], writes=[mrow.t], tag=name + f"me{qc}")

        smb, smb_t, smb_i = ps.take()
        for kt in range(TPB):
            bank, bank_t = ps.get()
            mms = [(bank, kv[:, ct, kt * 128:(kt + 1) * 128],
                    qv[:, ct, qc * 512:(qc + 1) * 512], None) for ct in range(8)]
            mms.append((bank, cm.ones_row[0:1, :], mrow.ap[0:1, :], None))
            mm_group(P, bank, bank_t, mms,
                     reads=[qfm.t, kfm.t, mrow.t, cm.ones_row_t],
                     tag=name + f"s{qc}{kt}")
            att = attv[:, kt, :]
            P.do("scalar", lambda e, o=att, i=bank:
                 e.activation(out=o, in_=i, func=AF.Exp, scale=0.03125),
                 reads=[bank_t], writes=[attb.t], tag=name + f"p{qc}{kt}")
            P.do("tensor", lambda e, o=smb[0:1, :], r=att, s1_=(kt == 0),
                 s2_=(kt == TPB - 1):
                 e.matmul(out=o, lhsT=cm.ones_col, rhs=r, start=s1_, stop=s2_,
                          skip_group_check=True),
                 reads=[attb.t, cm.ones_col_t], writes=[smb_t],
                 tag=name + f"u{qc}{kt}")
        P.do("scalar", lambda e, o=srow.ap[0:1, :], i=smb[0:1, :]: e.copy(out=o, in_=i),
             reads=[smb_t], writes=[srow.t], tag=name + f"w{qc}")
        ps.release(smb_i)
        bank, bank_t = ps.get()
        def fns(e, bank=bank):
            ins = None
            for qs in range(4):
                ins = e.matmul(out=bank[:, qs:qs + 1],
                               lhsT=srow.ap[0:1, qs * 128:(qs + 1) * 128],
                               rhs=cm.ones_row[0:1, 0:1], start=True, stop=True)
            return ins
        P.do("tensor", fns, reads=[srow.t, cm.ones_row_t], writes=[bank_t],
             tag=name + f"rt{qc}")
        P.do("vector", lambda e, o=rcol.ap[:, 0:4], i=bank[:, 0:4]: e.reciprocal(out=o, in_=i),
             reads=[bank_t], writes=[rcol.t], tag=name + f"rc{qc}")

        for qs in range(4):
            gt = b * TPB + qc * 4 + qs
            s = sl[qs % 2]
            P.dma("sync", s["xr"].ap, xin_ap(gt), writes=[s["xr"].t],
                  reads=[xin_tiles[gt]], tag=name + f"zl{qc}{qs}")
            ob = outs[qs % 2]
            for oc in range(2):
                bank, bank_t = ps.get()
                mms = [(bank, attv[:, kt, qs * 128:(qs + 1) * 128],
                        vtm.ap[:, kt * 1024 + oc * 512: kt * 1024 + (oc + 1) * 512],
                        None) for kt in range(TPB)]
                mm_group(P, bank, bank_t, mms, reads=[attb.t, vtm.t],
                         tag=name + f"a{qc}{qs}{oc}")
                dst = ob.ap[:, oc * 512:(oc + 1) * 512]
                P.do("vector", lambda e, o=dst, i=bank, rc=rcol.ap[:, qs:qs + 1],
                     x=s["xr"].ap[:, oc * 512:(oc + 1) * 512]:
                     e.scalar_tensor_tensor(out=o, in0=i, scalar=rc, in1=x,
                                            op0=ALU.mult, op1=ALU.add),
                     reads=[bank_t, rcol.t, s["xr"].t], writes=[ob.t],
                     tag=name + f"r{qc}{qs}{oc}")
            P.dma("scalar", xout_ap(gt), ob.ap, writes=[xout_tiles[gt]],
                  reads=[ob.t], tag=name + f"zs{qc}{qs}")

    for bb2 in (qfm, kfm, vtm, attb, negm, mrow, srow):
        AB.freeb(bb2)
    for s in sl:
        AF32.freeb(s["xr"])
        AF32.freeb(s["st"])
        for k in ("xn", "xnT", "qb", "kb"):
            AB.freeb(s[k])
    AF32.freeb(sq_sh)
    for bb2 in (mcol, mtmp, rcol, r1, r2):
        AF32.freeb(bb2)
    for o in outs:
        AF32.freeb(o)




def emit_l1_ffn(P, AB, AF32, AU8, ps, cm, nc, io, wdr):
    name = "l1f"
    xin_ap, xin_tiles = io["xin"]
    xout_ap, xout_tiles = io["xout"]

    wg = AB.alloc(name + "_wg", 8 * 8)
    wgs = AF32.alloc(name + "_wgs", 64, dma_slot=True)
    P.dma("sync", wgs.ap[:, 0:64].rearrange("p (c o) -> p c o", o=8),
          wdr["l1f_Wg"].rearrange("(c p) o -> p c o", p=128), writes=[wgs.t],
          tag=name + "wgl")
    P.do("vector", lambda e: e.tensor_copy(out=wg.ap, in_=wgs.ap[:, 0:64]),
         reads=[wgs.t], writes=[wg.t], tag=name + "wgc")
    bg = AB.alloc(name + "_bg", 32)
    bgs = AF32.alloc(name + "_bgs", 32, dma_slot=True)
    P.dma("sync", bgs.ap[0:1, 0:8], wdr["l1f_bg"], writes=[bgs.t], tag=name + "bgl")
    P.do("vector", lambda e: e.tensor_copy(out=bg.ap[0:1, 0:8], in_=bgs.ap[0:1, 0:8]),
         reads=[bgs.t], writes=[bg.t], tag=name + "bgc")
    b1c = AF32.alloc(name + "_b1c", 64, dma_slot=True)
    P.dma("sync", b1c.ap[:, 0:64], wdr["l1f_b1"], writes=[b1c.t],
          tag=name + "b1l")
    b2s = AF32.alloc(name + "_b2s", 1024, dma_slot=True)
    b2b = AB.alloc(name + "_b2b", 1024)
    P.dma("sync", b2s.ap[0:8, :], wdr["l1f_b2"], writes=[b2s.t], tag=name + "b2l")
    P.do("vector", lambda e: e.tensor_copy(out=b2b.ap[0:8, :], in_=b2s.ap[0:8, :]),
         reads=[b2s.t], writes=[b2b.t], tag=name + "b2c")
    AF32.freeb(wgs)
    AF32.freeb(bgs)
    AF32.freeb(b2s)

    sq_sh = AF32.alloc(name + "_sq", 1024)
    sl = []
    for i in range(2):
        sl.append({
            "xr": AF32.alloc(name + f"_xr{i}", 1024, dma_slot=True),
            "st": AF32.alloc(name + f"_st{i}", 32),
            "sq": sq_sh,
            "xn": AB.alloc(name + f"_xn{i}", 1024),
        })
    outs = [AF32.alloc(name + f"_o{i}", 1024, dma_slot=True) for i in range(2)]
    gsc = AF32.alloc(name + "_gsc", 64)
    TC = 2048
    NCH = NT // TC

    for ch in range(NCH):
        ntt = TC // 128
        xnT = AB.alloc(name + f"_xnT{ch}", 8 * TC)
        xnTv = xnT.ap.rearrange("p (c t) -> p c t", t=TC)
        oacc = AB.alloc(name + f"_oacc{ch}", ntt * 1024)
        oaccv = oacc.ap.rearrange("p (t o) -> p t o", o=1024)
        gcol = AF32.alloc(name + f"_gcol{ch}", ntt * 8)
        gbT = AB.alloc(name + f"_gbT{ch}", ntt * 128)

        for lt in range(ntt):
            gt = ch * ntt + lt
            s = sl[lt % 2]
            P.dma("sync", s["xr"].ap, xin_ap(gt), writes=[s["xr"].t],
                  reads=[xin_tiles[gt]], tag=name + f"xl{gt}")
            ln_tile(P, cm, s, s["xr"].ap, s["xr"].t, name + f"ln{gt}")
            dv = xnTv[:, :, lt * 128:(lt + 1) * 128]
            transpose_to(P, ps, cm, s["xn"].ap, s["xn"].t, None, xnT.t, 1024,
                         name + f"xT{gt}", dst_view=dv)
            bank, bank_t = ps.get()
            mms = [(bank[:, 0:8], cm.ones_row[0:1, :], bg.ap[0:1, 0:8], None)]
            for ct in range(8):
                mms.append((bank[:, 0:8], xnTv[:, ct, lt * 128:(lt + 1) * 128],
                            wg.ap[:, ct * 8:(ct + 1) * 8], None))
            mm_group(P, bank, bank_t, mms, reads=[xnT.t, wg.t, bg.t, cm.ones_row_t],
                     tag=name + f"gm{gt}")
            lg = gsc.ap[:, 0:8]
            ml = gsc.ap[:, 8:9]
            zs = gsc.ap[:, 16:24]
            ssum = gsc.ap[:, 9:10]
            wcol = gsc.ap[:, 10:11]
            eq = gsc.ap[:, 24:32]
            imax = gsc.ap[:, 11:12]
            P.do("vector", lambda e, o=lg, i=bank[:, 0:8]: e.tensor_copy(out=o, in_=i),
                 reads=[bank_t], writes=[gsc.t], tag=name + f"gc{gt}")
            P.do("vector", lambda e, o=ml, i=lg: e.reduce_max(out=o, in_=i, axis=AX.X),
                 reads=[gsc.t], writes=[gsc.t], tag=name + f"gx{gt}")
            P.do("vector", lambda e, o=ml, i=ml: e.tensor_scalar(
                 out=o, in0=i, scalar1=-1.0, scalar2=None, op0=ALU.mult),
                 reads=[gsc.t], writes=[gsc.t], tag=name + f"gn{gt}")
            P.do("scalar", lambda e, o=zs, i=lg, b_=ml: e.activation(
                 out=o, in_=i, func=AF.Exp, bias=b_),
                 reads=[gsc.t], writes=[gsc.t], tag=name + f"ge{gt}")
            P.do("vector", lambda e, o=ssum, i=zs: e.reduce_sum(out=o, in_=i, axis=AX.X),
                 reads=[gsc.t], writes=[gsc.t], tag=name + f"gs{gt}")
            P.do("vector", lambda e, o=wcol, i=ssum: e.reciprocal(out=o, in_=i),
                 reads=[gsc.t], writes=[gsc.t], tag=name + f"gr{gt}")
            P.do("vector", lambda e, o=eq, i=lg, m_=ml: e.tensor_scalar(
                 out=o, in0=i, scalar1=m_, scalar2=-1.0, op0=ALU.add, op1=ALU.mult),
                 reads=[gsc.t], writes=[gsc.t], tag=name + f"gq{gt}")
            P.do("vector", lambda e, o=eq, i=eq: e.tensor_scalar(
                 out=o, in0=i, scalar1=0.0, scalar2=None, op0=ALU.is_equal),
                 reads=[gsc.t], writes=[gsc.t], tag=name + f"gqq{gt}")
            P.do("vector", lambda e, o=eq, i=eq: e.tensor_mul(
                 out=o, in0=i, in1=cm.iota8r[:, 0:8]),
                 reads=[gsc.t, cm.iota8r_t], writes=[gsc.t], tag=name + f"gi{gt}")
            P.do("vector", lambda e, o=imax, i=eq: e.reduce_max(out=o, in_=i, axis=AX.X),
                 reads=[gsc.t], writes=[gsc.t], tag=name + f"gj{gt}")
            gslice = gcol.ap[:, lt * 8:(lt + 1) * 8]
            P.do("vector", lambda e, o=gslice, i=cm.iota8r[:, 0:8], m_=imax:
                 e.tensor_scalar(out=o, in0=i, scalar1=m_, scalar2=None, op0=ALU.is_equal),
                 reads=[gsc.t, cm.iota8r_t], writes=[gcol.t], tag=name + f"gf{gt}")
            P.do("vector", lambda e, o=gslice, i=gslice, m_=wcol:
                 e.tensor_scalar(out=o, in0=i, scalar1=m_, scalar2=None, op0=ALU.mult),
                 reads=[gcol.t, gsc.t], writes=[gcol.t], tag=name + f"gw{gt}")
            gb = gsc.ap[:, 0:8]
            gbb = AB
            P.do("vector", lambda e, o=s["xn"].ap[:, 0:8], i=gslice:
                 e.tensor_copy(out=o, in_=i),
                 reads=[gcol.t], writes=[s["xn"].t], tag=name + f"gb{gt}")
            bank2, bank2_t = ps.get()
            bb2 = bank2.bitcast(BF16)
            P.do("tensor", lambda e, o=bb2[0:8, 0:128], i=s["xn"].ap[:, 0:8]:
                 e.transpose(out=o, in_=i, identity=cm.ident),
                 reads=[s["xn"].t, cm.ident_t], writes=[bank2_t], tag=name + f"gt{gt}")
            P.do("scalar", lambda e, o=gbT.ap[0:8, lt * 128:(lt + 1) * 128],
                 i=bb2[0:8, 0:128]: e.copy(out=o, in_=i),
                 reads=[bank2_t], writes=[gbT.t], tag=name + f"gv{gt}")
            for oc in range(2):
                bank3, bank3_t = ps.get()
                P.do("tensor", lambda e, o=bank3, l=gbT.ap[0:8, lt * 128:(lt + 1) * 128],
                     r=b2b.ap[0:8, oc * 512:(oc + 1) * 512]:
                     e.matmul(out=o, lhsT=l, rhs=r, start=True, stop=True),
                     reads=[gbT.t, b2b.t], writes=[bank3_t], tag=name + f"bi{gt}{oc}")
                P.do("scalar", lambda e, o=oaccv[:, lt, oc * 512:(oc + 1) * 512],
                     i=bank3: e.copy(out=o, in_=i),
                     reads=[bank3_t], writes=[oacc.t], tag=name + f"bo{gt}{oc}")

        hbuf = AB.alloc(name + f"_hb{ch}", 8 * 512)
        hbv = hbuf.ap.rearrange("p (c t) -> p c t", t=512)
        w1b = [AB.alloc(name + f"_w1{ch}{i}", 8 * C) for i in range(2)]
        w2b = [AB.alloc(name + f"_w2{ch}{i}", 8 * C) for i in range(2)]
        wstg = [AF32.alloc(name + f"_ws{ch}{i}", C, dma_slot=True) for i in range(2)]
        load_weight_into(P, w1b[0], wstg[0], wdr["l1f_W1"][0], C, C,
                         name + f"_w1_{ch}_0")
        load_weight_into(P, w2b[0], wstg[1], wdr["l1f_W2"][0], C, C,
                         name + f"_w2_{ch}_0")
        for e_ in range(8):
            w1 = w1b[e_ % 2]
            w2 = w2b[e_ % 2]
            if e_ + 1 < 8:
                load_weight_into(P, w1b[(e_ + 1) % 2], wstg[0],
                                 wdr["l1f_W1"][e_ + 1], C, C,
                                 name + f"_w1_{ch}_{e_ + 1}")
                load_weight_into(P, w2b[(e_ + 1) % 2], wstg[1],
                                 wdr["l1f_W2"][e_ + 1], C, C,
                                 name + f"_w2_{ch}_{e_ + 1}")
            for ts in range(TC // 512):
                for ot in range(8):
                    bank, bank_t = ps.get()
                    mms = []
                    for ct in range(8):
                        mms.append((bank, w1.ap[:, ct * C + ot * 128: ct * C + ot * 128 + 128],
                                    xnTv[:, ct, ts * 512:(ts + 1) * 512], None))
                    mm_group(P, bank, bank_t, mms, reads=[w1.t, xnT.t],
                             tag=name + f"h{ch}{e_}{ts}{ot}")
                    P.do("scalar", lambda e, o=hbv[:, ot, :], i=bank,
                         b_=b1c.ap[:, e_ * 8 + ot: e_ * 8 + ot + 1]:
                         e.activation(out=o, in_=i, func=AF.Gelu, bias=b_),
                         reads=[bank_t, b1c.t], writes=[hbuf.t],
                         tag=name + f"hg{ch}{e_}{ts}{ot}")
                for tq in range(4):
                    lt = ts * 4 + tq
                    gt = ch * ntt + lt
                    for oc in range(2):
                        bank, bank_t = ps.get()
                        mms = []
                        for ht in range(8):
                            mms.append((bank, hbv[:, ht, tq * 128:(tq + 1) * 128],
                                        w2.ap[:, ht * C + oc * 512: ht * C + (oc + 1) * 512],
                                        None))
                        mm_group(P, bank, bank_t, mms, reads=[hbuf.t, w2.t],
                                 tag=name + f"o{ch}{e_}{lt}{oc}")
                        dst = oaccv[:, lt, oc * 512:(oc + 1) * 512]
                        P.do("vector", lambda e, o=dst, i=bank,
                             g_=gcol.ap[:, lt * 8 + e_: lt * 8 + e_ + 1]:
                             e.scalar_tensor_tensor(out=o, in0=i, scalar=g_, in1=o,
                                                    op0=ALU.mult, op1=ALU.add),
                             reads=[bank_t, gcol.t, oacc.t], writes=[oacc.t],
                             tag=name + f"oa{ch}{e_}{lt}{oc}")
        for i in range(2):
            AB.freeb(w1b[i])
            AB.freeb(w2b[i])
            AF32.freeb(wstg[i])
        AB.freeb(hbuf)

        for lt in range(ntt):
            gt = ch * ntt + lt
            s = sl[lt % 2]
            P.dma("sync", s["xr"].ap, xin_ap(gt), writes=[s["xr"].t],
                  reads=[xin_tiles[gt]], tag=name + f"z{gt}")
            ob = outs[lt % 2]
            P.do("vector", lambda e, o=ob.ap, a=oaccv[:, lt, :], x=s["xr"].ap:
                 e.tensor_add(out=o, in0=a, in1=x),
                 reads=[oacc.t, s["xr"].t], writes=[ob.t], tag=name + f"ra{gt}")
            P.dma("scalar", xout_ap(gt), ob.ap, writes=[xout_tiles[gt]],
                  reads=[ob.t], tag=name + f"zs{gt}")
        AB.freeb(xnT)
        AB.freeb(oacc)
        AB.freeb(gbT)
        AF32.freeb(gcol)

    for bb3 in (wg, bg, b2b):
        AB.freeb(bb3)
    AF32.freeb(b1c)
    AF32.freeb(sq_sh)
    AF32.freeb(gsc)
    for s in sl:
        AF32.freeb(s["xr"])
        AF32.freeb(s["st"])
        AB.freeb(s["xn"])
    for o in outs:
        AF32.freeb(o)




def emit_l2_attn(P, AB, AF32, AU8, ps, cm, nc, io, wdr):
    for b in range(BPC):
        _softmax_attn_batch(
            P, AB, AF32, ps, cm, nc, io, b, name=f"l2a{b}",
            wspec=dict(kind="fused", wqkv=("l2a_wqkv", "l2a_bqkv"),
                       wp=("l2a_wp", "l2a_bp")),
            wdr=wdr, rope=None, causal=False, alibi=True, scale=0.125)




def emit_l2_ffn(P, AB, AF32, AU8, ps, cm, nc, io, wdr):
    name = "l2f"
    xin_ap, xin_tiles = io["xin"]
    xout_ap, xout_tiles = io["xout"]

    wg = AB.alloc(name + "_wg", 64)
    wgs = AF32.alloc(name + "_wgs", 64, dma_slot=True)
    P.dma("sync", wgs.ap[:, 0:64].rearrange("p (c o) -> p c o", o=8),
          wdr["l2f_gateT"].rearrange("(c p) o -> p c o", p=128), writes=[wgs.t],
          tag=name + "wgl")
    P.do("vector", lambda e: e.tensor_copy(out=wg.ap[:, 0:64], in_=wgs.ap[:, 0:64]),
         reads=[wgs.t], writes=[wg.t], tag=name + "wgc")
    AF32.freeb(wgs)

    sq_sh = AF32.alloc(name + "_sq", 1024)
    sl = []
    for i in range(2):
        sl.append({
            "xr": AF32.alloc(name + f"_xr{i}", 1024, dma_slot=True),
            "st": AF32.alloc(name + f"_st{i}", 32),
            "sq": sq_sh,
            "xn": AB.alloc(name + f"_xn{i}", 1024),
        })
    outs = [AF32.alloc(name + f"_o{i}", 1024, dma_slot=True) for i in range(2)]
    gsc = AF32.alloc(name + "_gsc", 32)
    bcf = AF32.alloc(name + "_bcf", 512)
    gsf = AF32.alloc(name + "_gsf", 512)
    sgf = AF32.alloc(name + "_sgf", 512)
    usf = AB.alloc(name + "_usf", 512)
    TC = 1024
    NCH = NT // TC

    for ch in range(NCH):
        ntt = TC // 128
        xnT = AB.alloc(name + f"_xnT{ch}", 8 * TC)
        xnTv = xnT.ap.rearrange("p (c t) -> p c t", t=TC)
        srow = AB.alloc(name + f"_srow{ch}", ntt * 128)

        for lt in range(ntt):
            gt = ch * ntt + lt
            s = sl[lt % 2]
            P.dma("sync", s["xr"].ap, xin_ap(gt), writes=[s["xr"].t],
                  reads=[xin_tiles[gt]], tag=name + f"xl{gt}")
            ln_tile(P, cm, s, s["xr"].ap, s["xr"].t, name + f"ln{gt}")
            dv = xnTv[:, :, lt * 128:(lt + 1) * 128]
            transpose_to(P, ps, cm, s["xn"].ap, s["xn"].t, None, xnT.t, 1024,
                         name + f"xT{gt}", dst_view=dv)
        for lt in range(ntt):
            gt = ch * ntt + lt
            s = sl[lt % 2]
            bank, bank_t = ps.get()
            mms = []
            for ct in range(8):
                mms.append((bank[:, 0:8], xnTv[:, ct, lt * 128:(lt + 1) * 128],
                            wg.ap[:, ct * 8:(ct + 1) * 8], None))
            mm_group(P, bank, bank_t, mms, reads=[xnT.t, wg.t], tag=name + f"gm{gt}")
            lg = gsc.ap[:, 0:8]
            ml = gsc.ap[:, 8:9]
            rw = gsc.ap[:, 9:10]
            eq = gsc.ap[:, 16:24]
            imax = gsc.ap[:, 10:11]
            fsc = gsc.ap[:, 24:32]
            P.do("vector", lambda e, o=lg, i=bank[:, 0:8]: e.tensor_copy(out=o, in_=i),
                 reads=[bank_t], writes=[gsc.t], tag=name + f"gc{gt}")
            P.do("vector", lambda e, o=ml, i=lg: e.reduce_max(out=o, in_=i, axis=AX.X),
                 reads=[gsc.t], writes=[gsc.t], tag=name + f"gx{gt}")
            P.do("scalar", lambda e, o=rw, i=ml: e.activation(out=o, in_=i, func=AF.Sigmoid),
                 reads=[gsc.t], writes=[gsc.t], tag=name + f"gw{gt}")
            P.do("vector", lambda e, o=eq, i=lg, m_=ml: e.tensor_scalar(
                 out=o, in0=i, scalar1=m_, scalar2=None, op0=ALU.is_equal),
                 reads=[gsc.t], writes=[gsc.t], tag=name + f"gq{gt}")
            P.do("vector", lambda e, o=eq, i=eq: e.tensor_mul(
                 out=o, in0=i, in1=cm.iota8r[:, 0:8]),
                 reads=[gsc.t, cm.iota8r_t], writes=[gsc.t], tag=name + f"gi{gt}")
            P.do("vector", lambda e, o=imax, i=eq: e.reduce_max(out=o, in_=i, axis=AX.X),
                 reads=[gsc.t], writes=[gsc.t], tag=name + f"gj{gt}")
            P.do("vector", lambda e, o=fsc, i=cm.iota8r[:, 0:8], m_=imax:
                 e.tensor_scalar(out=o, in0=i, scalar1=m_, scalar2=None, op0=ALU.is_equal),
                 reads=[gsc.t, cm.iota8r_t], writes=[gsc.t], tag=name + f"gf{gt}")
            P.do("vector", lambda e, o=fsc, i=fsc, m_=rw:
                 e.tensor_scalar(out=o, in0=i, scalar1=m_, scalar2=None, op0=ALU.mult),
                 reads=[gsc.t], writes=[gsc.t], tag=name + f"gs{gt}")
            P.do("vector", lambda e, o=s["xn"].ap[:, 0:8], i=fsc: e.tensor_copy(out=o, in_=i),
                 reads=[gsc.t], writes=[s["xn"].t], tag=name + f"gb{gt}")
            bank2, bank2_t = ps.get()
            bb2 = bank2.bitcast(BF16)
            P.do("tensor", lambda e, o=bb2[0:8, 0:128], i=s["xn"].ap[:, 0:8]:
                 e.transpose(out=o, in_=i, identity=cm.ident),
                 reads=[s["xn"].t, cm.ident_t], writes=[bank2_t], tag=name + f"gv{gt}")
            P.do("scalar", lambda e, o=srow.ap[0:8, lt * 128:(lt + 1) * 128],
                 i=bb2[0:8, 0:128]: e.copy(out=o, in_=i),
                 reads=[bank2_t], writes=[srow.t], tag=name + f"gu{gt}")

        zb = AB.alloc(name + f"_zb{ch}", 32 * TC)
        zbv = zb.ap.rearrange("p (i t) -> p i t", t=TC)
        wgb = [AB.alloc(name + f"_wg{ch}{i}", 8 * 512) for i in range(2)]
        wub = [AB.alloc(name + f"_wu{ch}{i}", 8 * 512) for i in range(2)]
        wstg = [AF32.alloc(name + f"_ws{ch}{i}", 512, dma_slot=True) for i in range(2)]
        load_weight_into(P, wgb[0], wstg[0], wdr["l2f_wgT"][0], C, 512,
                         name + f"_wg{ch}0")
        load_weight_into(P, wub[0], wstg[1], wdr["l2f_wuT"][0], C, 512,
                         name + f"_wu{ch}0")
        for e_ in range(8):
            wgt = wgb[e_ % 2]
            wut = wub[e_ % 2]
            if e_ + 1 < 8:
                load_weight_into(P, wgb[(e_ + 1) % 2], wstg[0],
                                 wdr["l2f_wgT"][e_ + 1], C, 512,
                                 name + f"_wg{ch}{e_ + 1}")
                load_weight_into(P, wub[(e_ + 1) % 2], wstg[1],
                                 wdr["l2f_wuT"][e_ + 1], C, 512,
                                 name + f"_wu{ch}{e_ + 1}")
            for ts in range(TC // 512):
                bank, bank_t = ps.get()
                P.do("tensor", lambda e, o=bank, l=cm.esel[0:8, e_ * 128:(e_ + 1) * 128],
                     r=srow.ap[0:8, ts * 512:(ts + 1) * 512]:
                     e.matmul(out=o, lhsT=l, rhs=r, start=True, stop=True),
                     reads=[srow.t, cm.esel_t], writes=[bank_t],
                     tag=name + f"eb{ch}{e_}{ts}")
                P.do("scalar", lambda e, o=bcf.ap, i=bank: e.copy(out=o, in_=i),
                     reads=[bank_t], writes=[bcf.t], tag=name + f"ec{ch}{e_}{ts}")
                for it in range(4):
                    banku, banku_t = ps.get()
                    mms = [(banku, wut.ap[:, ct * 512 + it * 128: ct * 512 + it * 128 + 128],
                            xnTv[:, ct, ts * 512:(ts + 1) * 512], None) for ct in range(8)]
                    mm_group(P, banku, banku_t, mms, reads=[wut.t, xnT.t],
                             tag=name + f"um{ch}{e_}{ts}{it}")
                    P.do("vector", lambda e, o=usf.ap, i=banku, b_=bcf.ap:
                         e.tensor_mul(out=o, in0=i, in1=b_),
                         reads=[banku_t, bcf.t], writes=[usf.t],
                         tag=name + f"ue{ch}{e_}{ts}{it}")
                    bankg, bankg_t = ps.get()
                    mms = [(bankg, wgt.ap[:, ct * 512 + it * 128: ct * 512 + it * 128 + 128],
                            xnTv[:, ct, ts * 512:(ts + 1) * 512], None) for ct in range(8)]
                    mm_group(P, bankg, bankg_t, mms, reads=[wgt.t, xnT.t],
                             tag=name + f"gm2{ch}{e_}{ts}{it}")
                    P.do("vector", lambda e, o=gsf.ap, i=bankg, b_=bcf.ap:
                         e.tensor_mul(out=o, in0=i, in1=b_),
                         reads=[bankg_t, bcf.t], writes=[gsf.t],
                         tag=name + f"ge2{ch}{e_}{ts}{it}")
                    P.do("scalar", lambda e, o=sgf.ap, i=gsf.ap:
                         e.activation(out=o, in_=i, func=AF.Silu),
                         reads=[gsf.t], writes=[sgf.t], tag=name + f"gs2{ch}{e_}{ts}{it}")
                    dst = zbv[:, e_ * 4 + it, ts * 512:(ts + 1) * 512]
                    P.do("vector", lambda e, o=dst, a=sgf.ap, b_=usf.ap:
                         e.tensor_mul(out=o, in0=a, in1=b_),
                         reads=[sgf.t, usf.t], writes=[zb.t],
                         tag=name + f"zm{ch}{e_}{ts}{it}")
        for i in range(2):
            AB.freeb(wgb[i])
            AB.freeb(wub[i])
            AF32.freeb(wstg[i])
        AB.freeb(xnT)

        wdt = [load_weight_bf16(P, AB, AF32, wdr["l2f_wdT"][e_], 512, C,
                                name + f"_wd{ch}{e_}") for e_ in range(8)]
        outfm = AB.alloc(name + f"_ofm{ch}", 8 * TC)
        ofmv = outfm.ap.rearrange("p (o t) -> p o t", t=TC)
        for ot in range(8):
            for ts in range(TC // 512):
                bank, bank_t = ps.get()
                mms = []
                for e_ in range(8):
                    for it in range(4):
                        mms.append((bank,
                                    wdt[e_].ap[:, it * C + ot * 128: it * C + ot * 128 + 128],
                                    zbv[:, e_ * 4 + it, ts * 512:(ts + 1) * 512], None))
                mm_group(P, bank, bank_t, mms,
                         reads=[zb.t] + [w.t for w in wdt],
                         tag=name + f"dm{ch}{ot}{ts}")
                P.do("scalar", lambda e, o=ofmv[:, ot, ts * 512:(ts + 1) * 512],
                     i=bank: e.copy(out=o, in_=i),
                     reads=[bank_t], writes=[outfm.t], tag=name + f"de{ch}{ot}{ts}")
        for w in wdt:
            AB.freeb(w)
        AB.freeb(zb)

        for lt in range(ntt):
            gt = ch * ntt + lt
            s = sl[lt % 2]
            P.dma("sync", s["xr"].ap, xin_ap(gt), writes=[s["xr"].t],
                  reads=[xin_tiles[gt]], tag=name + f"z{gt}")
            bank, bank_t = ps.get()
            bb = bank.bitcast(BF16)
            def fnt(e, lt=lt, bb=bb):
                ins = None
                for ot in range(8):
                    ins = e.transpose(out=bb[:, ot * 128:(ot + 1) * 128],
                                      in_=ofmv[:, ot, lt * 128:(lt + 1) * 128],
                                      identity=cm.ident)
                return ins
            P.do("tensor", fnt, reads=[outfm.t, cm.ident_t], writes=[bank_t],
                 tag=name + f"tt{gt}")
            ob = outs[lt % 2]
            P.do("vector", lambda e, o=ob.ap, a=bb[:, 0:1024], x=s["xr"].ap:
                 e.tensor_add(out=o, in0=a, in1=x),
                 reads=[bank_t, s["xr"].t], writes=[ob.t], tag=name + f"ra{gt}")
            P.dma("scalar", xout_ap(gt), ob.ap, writes=[xout_tiles[gt]],
                  reads=[ob.t], tag=name + f"zs{gt}")
        AB.freeb(outfm)
        AB.freeb(srow)

    AB.freeb(wg)
    AF32.freeb(sq_sh)
    for bb3 in (gsc, bcf, gsf):
        AF32.freeb(bb3)
    AB.freeb(usf)
    AF32.freeb(sgf)
    for s in sl:
        AF32.freeb(s["xr"])
        AF32.freeb(s["st"])
        AB.freeb(s["xn"])
    for o in outs:
        AF32.freeb(o)




L3A_DEBUG = "ABC"


def emit_l3_attn(P, AB, AF32, AU8, ps, cm, nc, io, wdr):
    for b in range(BPC):
        _l3_attn_batch(P, AB, AF32, ps, cm, nc, io, b, wdr, name=f"l3a{b}")


def _l3_attn_batch(P, AB, AF32, ps, cm, nc, io, b, wdr, name):
    xin_ap, xin_tiles = io["xin"]
    xout_ap, xout_tiles = io["xout"]

    fktm = AB.alloc(name + "_fktm", TPB * 1024)
    fqfm = AB.alloc(name + "_fqfm", 8 * T)
    vtm = AB.alloc(name + "_vtm", TPB * 1024)

    wqkv = load_weight_bf16(P, AB, AF32, wdr["l3a_wqkvT"], C, 3 * C, name + "_wqkv")
    rfs = AF32.alloc(name + "_rfs", 128, dma_slot=True)
    rft = AB.alloc(name + "_rft", 128)
    P.dma("sync", rfs.ap[:, 0:128], wdr["l3a_rfT"], writes=[rfs.t], tag=name + "rfl")
    P.do("vector", lambda e: e.tensor_copy(out=rft.ap[:, 0:128], in_=rfs.ap[:, 0:128]),
         reads=[rfs.t], writes=[rft.t], tag=name + "rfc")
    AF32.freeb(rfs)

    sq_sh = AF32.alloc(name + "_sq", 1024)
    sl = []
    for i in range(2):
        sl.append({
            "xr": AF32.alloc(name + f"_xr{i}", 1024, dma_slot=True),
            "st": AF32.alloc(name + f"_st{i}", 32),
            "sq": sq_sh,
            "xn": AB.alloc(name + f"_xn{i}", 1024),
            "xnT": AB.alloc(name + f"_xnT{i}", 1024),
            "qf": AF32.alloc(name + f"_qf{i}", 1024),
            "nrm": AF32.alloc(name + f"_nrm{i}", 32),
            "qb": AB.alloc(name + f"_qb{i}", 1024),
            "qfm": AB.alloc(name + f"_qfm{i}", 1024),
            "fpre": AF32.alloc(name + f"_fp{i}", 1024),
        })

    fkv = fktm.ap.rearrange("p (t c) -> p t c", c=1024)
    fqv = fqfm.ap.rearrange("p (c t) -> p c t", t=T)

    def qkv_proj(s, which, tt, tagp):
        off = {"q": 0, "k": C, "v": 2 * C}[which]
        for oc in range(2):
            bank, bank_t = ps.get()
            mms = []
            for ct in range(8):
                mms.append((bank, s["xnT"].ap[:, ct * 128:(ct + 1) * 128],
                            wqkv.ap[:, ct * 3 * C + off + oc * 512:
                                    ct * 3 * C + off + (oc + 1) * 512], None))
            mm_group(P, bank, bank_t, mms, reads=[s["xnT"].t, wqkv.t],
                     tag=tagp + f"m{oc}")
            if which == "v":
                dst = vtm.ap[:, tt * 1024 + oc * 512: tt * 1024 + (oc + 1) * 512]
                P.do("scalar", lambda e, o=dst, i=bank: e.copy(out=o, in_=i),
                     reads=[bank_t], writes=[vtm.t], tag=tagp + f"e{oc}")
            else:
                dst = s["qf"].ap[:, oc * 512:(oc + 1) * 512]
                P.do("scalar", lambda e, o=dst, i=bank: e.copy(out=o, in_=i),
                     reads=[bank_t], writes=[s["qf"].t], tag=tagp + f"e{oc}")

    def feature_map(s, tt, is_q, tagp):
        P.do("scalar", lambda e: e.activation(out=s["sq"].ap, in_=s["qf"].ap,
                                              func=AF.Square),
             reads=[s["qf"].t], writes=[s["sq"].t], tag=tagp + "sq")
        n2 = s["nrm"].ap[:, 0:16]
        P.do("vector", lambda e: e.reduce_sum(
             out=n2.rearrange("p (h o) -> p h o", o=1),
             in_=s["sq"].ap.rearrange("p (h d) -> p h d", d=64), axis=AX.X),
             reads=[s["sq"].t], writes=[s["nrm"].t], tag=tagp + "rd")
        P.do("scalar", lambda e: e.activation(out=n2, in_=n2, func=AF.Ln),
             reads=[s["nrm"].t], writes=[s["nrm"].t], tag=tagp + "ln")
        P.do("scalar", lambda e: e.activation(out=n2, in_=n2, func=AF.Exp, scale=0.5),
             reads=[s["nrm"].t], writes=[s["nrm"].t], tag=tagp + "ex")
        P.do("vector", lambda e: e.tensor_copy(out=s["qb"].ap, in_=s["qf"].ap),
             reads=[s["qf"].t], writes=[s["qb"].t], tag=tagp + "cv")
        transpose_to(P, ps, cm, s["qb"].ap, s["qb"].t, s["qfm"].ap, s["qfm"].t,
                     1024, tagp + "tr")
        for half in range(2):
            bank, bank_t = ps.get()
            def fn(e, half=half, bank=bank, s=s):
                ins = None
                for pp in range(4):
                    ct = half * 4 + pp
                    ins = e.matmul(out=bank[:, pp * 128:(pp + 1) * 128],
                                   lhsT=s["qfm"].ap[:, ct * 128:(ct + 1) * 128],
                                   rhs=rft.ap[:, 0:128], start=True, stop=True)
                return ins
            P.do("tensor", fn, reads=[s["qfm"].t, rft.t], writes=[bank_t],
                 tag=tagp + f"pj{half}")
            dst = s["fpre"].ap[:, half * 512:(half + 1) * 512]
            nb = n2[:, half * 8:(half + 1) * 8].unsqueeze(2).broadcast_to((128, 8, 64))
            P.do("vector", lambda e, o=dst.rearrange("p (h d) -> p h d", d=64),
                 i=bank, nb=nb:
                 e.scalar_tensor_tensor(out=o,
                                        in0=i.rearrange("p (h d) -> p h d", d=64),
                                        scalar=0.125, in1=nb,
                                        op0=ALU.mult, op1=ALU.subtract),
                 reads=[bank_t, s["nrm"].t], writes=[s["fpre"].t], tag=tagp + f"fp{half}")
        dst = fkv[:, tt, :] if not is_q else s["qb"].ap
        dt_t = fktm.t if not is_q else s["qb"].t
        P.do("scalar", lambda e, o=dst, i=s["fpre"].ap:
             e.activation(out=o, in_=i, func=AF.Exp),
             reads=[s["fpre"].t], writes=[dt_t], tag=tagp + "phe")
        if is_q:
            dv = fqv[:, :, tt * 128:(tt + 1) * 128]
            transpose_to(P, ps, cm, s["qb"].ap, s["qb"].t, None, fqfm.t, 1024,
                         tagp + "qtr", dst_view=dv)

    for tt in range(TPB):
        gt = b * TPB + tt
        s = sl[tt % 2]
        P.dma("sync", s["xr"].ap, xin_ap(gt), writes=[s["xr"].t],
              reads=[xin_tiles[gt]], tag=name + f"xl{tt}")
        ln_tile(P, cm, s, s["xr"].ap, s["xr"].t, name + f"ln{tt}")
        transpose_to(P, ps, cm, s["xn"].ap, s["xn"].t, s["xnT"].ap, s["xnT"].t,
                     1024, name + f"xT{tt}")
        qkv_proj(s, "v", tt, name + f"v{tt}")
        qkv_proj(s, "q", tt, name + f"q{tt}")
        feature_map(s, tt, True, name + f"fq{tt}")
        qkv_proj(s, "k", tt, name + f"k{tt}")
        feature_map(s, tt, False, name + f"fk{tt}")

    AB.freeb(wqkv)
    AF32.freeb(sq_sh)
    for s in sl:
        for k2 in ("xn", "xnT", "qb", "qfm"):
            AB.freeb(s[k2])
        for k2 in ("qf", "nrm", "fpre"):
            AF32.freeb(s[k2])

    def _copy_through():
        outs0 = [AF32.alloc(name + f"_dbg{i}", 1024, dma_slot=True) for i in range(2)]
        for gt2 in range(b * TPB, (b + 1) * TPB):
            ob3 = outs0[gt2 % 2]
            P.dma("sync", ob3.ap, xin_ap(gt2), writes=[ob3.t],
                  reads=[xin_tiles[gt2]], tag=name + f"dbl{gt2}")
            P.dma("scalar", xout_ap(gt2), ob3.ap, writes=[xout_tiles[gt2]],
                  reads=[ob3.t], tag=name + f"dbs{gt2}")
        for o in outs0:
            AF32.freeb(o)

    if "B" not in L3A_DEBUG:
        _copy_through()
        for bb3 in (fktm, fqfm, vtm):
            AB.freeb(bb3)
        AB.freeb(rft)
        for s in sl:
            AF32.freeb(s["xr"])
            AF32.freeb(s["st"])
        return

    kvsb = AB.alloc(name + "_kvsb", 16 * 65 + 62)
    P.do("vector", lambda e: e.memset(kvsb.ap, 0.0), writes=[kvsb.t],
         tag=name + "kvz")
    banks = [ps.take() for _ in range(8)]
    for nt in range(TPB):
        for hp in range(8):
            bank, bank_t, _bi = banks[hp]
            def fn(e, nt=nt, hp=hp, bank=bank):
                ins = None
                for z in range(2):
                    h = 2 * hp + z
                    lh = fkv[:, nt, h * 64:(h + 1) * 64]
                    ins = e.matmul(out=bank[64 * z:64 * z + 64, 0:64],
                                   lhsT=lh,
                                   rhs=vtm.ap[:, nt * 1024 + h * 64: nt * 1024 + (h + 1) * 64],
                                   start=(nt == 0), stop=(nt == TPB - 1),
                                   tile_position=(0, 64 * z), skip_group_check=True)
                    ins = e.matmul(out=bank[64 * z:64 * z + 64, 64:65],
                                   lhsT=lh, rhs=cm.ones_col,
                                   start=(nt == 0), stop=(nt == TPB - 1),
                                   tile_position=(0, 64 * z), skip_group_check=True)
                return ins
            P.do("tensor", fn, reads=[fktm.t, vtm.t, cm.ones_col_t],
                 writes=[banks[hp][1]], tag=name + f"kv{nt}_{hp}")
    for hp in range(8):
        bank, bank_t, bi = banks[hp]
        P.do("scalar", lambda e, o=kvsb.ap[0:64, (2 * hp) * 65: (2 * hp) * 65 + 65],
             i=bank[0:64, 0:65]: e.copy(out=o, in_=i),
             reads=[bank_t], writes=[kvsb.t], tag=name + f"kve{hp}a")
        P.do("scalar", lambda e, o=kvsb.ap[64:128, (2 * hp + 1) * 65: (2 * hp + 1) * 65 + 65],
             i=bank[64:128, 0:65]: e.copy(out=o, in_=i),
             reads=[bank_t], writes=[kvsb.t], tag=name + f"kve{hp}b")
        ps.release(bi)

    if "C" not in L3A_DEBUG:
        _copy_through()
        for bb3 in (fktm, fqfm, vtm, kvsb, rft):
            AB.freeb(bb3)
        for s in sl:
            AF32.freeb(s["xr"])
            AF32.freeb(s["st"])
        return

    wo = load_weight_bf16(P, AB, AF32, wdr["l3a_woT"], C, C, name + "_wo")
    bo = load_bias_row(P, AB, AF32, wdr["l3a_bo"], C, name + "_bo")
    ybuf = AB.alloc(name + "_y", 8 * 512)
    drow = AB.alloc(name + "_dr", 1024)
    dcol = AF32.alloc(name + "_dc", 512)
    bcf2 = AF32.alloc(name + "_bc", 512)
    outs = [AF32.alloc(name + f"_o{i}", 1024, dma_slot=True) for i in range(2)]

    for qc in range(4):
        for hp in range(8):
            ob_, ob_t = ps.get()
            dbank = [ps.get() for _ in range(2)]
            rhsq = fqv[:, hp, qc * 512:(qc + 1) * 512]
            def fn(e, hp=hp, ob_=ob_, rhsq=rhsq):
                ins = None
                for z in range(2):
                    h = 2 * hp + z
                    ins = e.matmul(out=ob_[64 * z:64 * z + 64, :],
                                   lhsT=kvsb.ap[:, h * 65: h * 65 + 64],
                                   rhs=rhsq, start=True, stop=True,
                                   tile_position=(0, 64 * z), skip_group_check=True)
                return ins
            P.do("tensor", fn, reads=[kvsb.t, fqfm.t], writes=[ob_t],
                 tag=name + f"ov{qc}{hp}")
            for z in range(2):
                h = 2 * hp + z
                P.do("tensor", lambda e, o=dbank[z][0], h=h, rhsq=rhsq:
                     e.matmul(out=o[0:1, :], lhsT=kvsb.ap[:, h * 65 + 64: h * 65 + 65],
                              rhs=rhsq, start=True, stop=True, skip_group_check=True),
                     reads=[kvsb.t, fqfm.t], writes=[dbank[z][1]],
                     tag=name + f"dn{qc}{hp}{z}")
            for z in range(2):
                P.do("vector", lambda e, o=dcol.ap[0:1, :], i=dbank[z][0][0:1, :]:
                     e.tensor_scalar(out=o, in0=i, scalar1=1e-6, scalar2=None, op0=ALU.add),
                     reads=[dbank[z][1]], writes=[dcol.t], tag=name + f"da{qc}{hp}{z}")
                P.do("vector", lambda e, o=dcol.ap[0:1, :], i=dcol.ap[0:1, :]:
                     e.reciprocal(out=o, in_=i),
                     reads=[dcol.t], writes=[dcol.t], tag=name + f"dr{qc}{hp}{z}")
                P.do("vector", lambda e, o=drow.ap[0:1, z * 512:(z + 1) * 512],
                     i=dcol.ap[0:1, :]: e.tensor_copy(out=o, in_=i),
                     reads=[dcol.t], writes=[drow.t], tag=name + f"db{qc}{hp}{z}")
            bcb, bcb_t = ps.get()
            def fnb(e, bcb=bcb):
                e.matmul(out=bcb[0:64, :], lhsT=cm.ones_row[0:1, 0:64],
                         rhs=drow.ap[0:1, 0:512], start=True, stop=True,
                         tile_position=(0, 0))
                return e.matmul(out=bcb[64:128, :], lhsT=cm.ones_row[0:1, 0:64],
                                rhs=drow.ap[0:1, 512:1024], start=True, stop=True,
                                tile_position=(0, 64))
            P.do("tensor", fnb, reads=[drow.t, cm.ones_row_t], writes=[bcb_t],
                 tag=name + f"bb{qc}{hp}")
            P.do("scalar", lambda e, o=bcf2.ap, i=bcb: e.copy(out=o, in_=i),
                 reads=[bcb_t], writes=[bcf2.t], tag=name + f"bc{qc}{hp}")
            ydst = ybuf.ap[:, hp * 512:(hp + 1) * 512]
            P.do("vector", lambda e, o=ydst, a=ob_, d=bcf2.ap:
                 e.tensor_mul(out=o, in0=a, in1=d),
                 reads=[ob_t, bcf2.t], writes=[ybuf.t], tag=name + f"ym{qc}{hp}")
        for qs in range(4):
            gt = b * TPB + qc * 4 + qs
            s = sl[qs % 2]
            P.dma("sync", s["xr"].ap, xin_ap(gt), writes=[s["xr"].t],
                  reads=[xin_tiles[gt]], tag=name + f"zl{qc}{qs}")
            ob2 = outs[qs % 2]
            for oc in range(2):
                bank, bank_t = ps.get()
                mms = [(bank, cm.ones_row[0:1, :], bo.ap[0:1, oc * 512:(oc + 1) * 512], None)]
                for hp in range(8):
                    mms.append((bank, ybuf.ap[:, hp * 512 + qs * 128: hp * 512 + qs * 128 + 128],
                                wo.ap[:, hp * C + oc * 512: hp * C + (oc + 1) * 512], None))
                mm_group(P, bank, bank_t, mms,
                         reads=[ybuf.t, wo.t, bo.t, cm.ones_row_t],
                         tag=name + f"j{qc}{qs}{oc}")
                dst = ob2.ap[:, oc * 512:(oc + 1) * 512]
                P.do("vector", lambda e, o=dst, i=bank,
                     x=s["xr"].ap[:, oc * 512:(oc + 1) * 512]:
                     e.tensor_add(out=o, in0=i, in1=x),
                     reads=[bank_t, s["xr"].t], writes=[ob2.t], tag=name + f"r{qc}{qs}{oc}")
            P.dma("scalar", xout_ap(gt), ob2.ap, writes=[xout_tiles[gt]],
                  reads=[ob2.t], tag=name + f"zs{qc}{qs}")

    for bb3 in (fktm, fqfm, vtm, kvsb, rft, wo, bo, ybuf, drow):
        AB.freeb(bb3)
    AF32.freeb(dcol)
    AF32.freeb(bcf2)
    for s in sl:
        AF32.freeb(s["xr"])
        AF32.freeb(s["st"])
    for o in outs:
        AF32.freeb(o)




def emit_l3_ffn(P, AB, AF32, AU8, ps, cm, nc, io, wdr):
    name = "l3f"
    xin_ap, xin_tiles = io["xin"]
    xout_ap, xout_tiles = io["xout"]

    wg = AB.alloc(name + "_wg", 64)
    wgs = AF32.alloc(name + "_wgs", 64, dma_slot=True)
    P.dma("sync", wgs.ap[:, 0:64].rearrange("p (c o) -> p c o", o=8),
          wdr["l3f_gateT"].rearrange("(c p) o -> p c o", p=128), writes=[wgs.t],
          tag=name + "wgl")
    P.do("vector", lambda e: e.tensor_copy(out=wg.ap[:, 0:64], in_=wgs.ap[:, 0:64]),
         reads=[wgs.t], writes=[wg.t], tag=name + "wgc")
    AF32.freeb(wgs)

    sq_sh = AF32.alloc(name + "_sq", 1024)
    sl = []
    for i in range(2):
        sl.append({
            "xr": AF32.alloc(name + f"_xr{i}", 1024, dma_slot=True),
            "st": AF32.alloc(name + f"_st{i}", 32),
            "sq": sq_sh,
            "xn": AB.alloc(name + f"_xn{i}", 1024),
        })
    outs = [AF32.alloc(name + f"_o{i}", 1024, dma_slot=True) for i in range(2)]
    gsc = AF32.alloc(name + "_gsc", 64)
    usf = AB.alloc(name + "_usf", 512)
    gsf = AF32.alloc(name + "_gsf", 512)
    sgf = AF32.alloc(name + "_sgf", 512)
    TC = 1024
    NCH = NT // TC

    for ch in range(NCH):
        ntt = TC // 128
        xnT = AB.alloc(name + f"_xnT{ch}", 8 * TC)
        xnTv = xnT.ap.rearrange("p (c t) -> p c t", t=TC)
        oacc = AB.alloc(name + f"_oacc{ch}", ntt * 1024)
        oaccv = oacc.ap.rearrange("p (t o) -> p t o", o=1024)
        cw = AF32.alloc(name + f"_cw{ch}", ntt * 8)

        for lt in range(ntt):
            gt = ch * ntt + lt
            s = sl[lt % 2]
            P.dma("sync", s["xr"].ap, xin_ap(gt), writes=[s["xr"].t],
                  reads=[xin_tiles[gt]], tag=name + f"xl{gt}")
            ln_tile(P, cm, s, s["xr"].ap, s["xr"].t, name + f"ln{gt}")
            dv = xnTv[:, :, lt * 128:(lt + 1) * 128]
            transpose_to(P, ps, cm, s["xn"].ap, s["xn"].t, None, xnT.t, 1024,
                         name + f"xT{gt}", dst_view=dv)
        for lt in range(ntt):
            gt = ch * ntt + lt
            bank, bank_t = ps.get()
            mms = [(bank[:, 0:8], xnTv[:, ct, lt * 128:(lt + 1) * 128],
                    wg.ap[:, ct * 8:(ct + 1) * 8], None) for ct in range(8)]
            mm_group(P, bank, bank_t, mms, reads=[xnT.t, wg.t], tag=name + f"gm{gt}")
            lg = gsc.ap[:, 0:8]
            m1 = gsc.ap[:, 8:9]
            zz = gsc.ap[:, 16:24]
            ss = gsc.ap[:, 9:10]
            p1 = gsc.ap[:, 10:11]
            m2 = gsc.ap[:, 11:12]
            p2 = gsc.ap[:, 12:13]
            dnm = gsc.ap[:, 13:14]
            imax = gsc.ap[:, 14:15]
            f1 = gsc.ap[:, 24:32]
            f2 = gsc.ap[:, 32:40]
            eq = gsc.ap[:, 40:48]
            def V(fn, rd=(), tg=""):
                P.do("vector", fn, reads=[gsc.t] + list(rd), writes=[gsc.t],
                     tag=name + tg + str(gt))
            P.do("vector", lambda e, i=bank: e.tensor_copy(out=lg, in_=i[:, 0:8]),
                 reads=[bank_t], writes=[gsc.t], tag=name + f"gc{gt}")
            V(lambda e: e.reduce_max(out=m1, in_=lg, axis=AX.X), tg="g1")
            V(lambda e: e.tensor_scalar(out=imax, in0=m1, scalar1=-1.0, scalar2=None,
                                        op0=ALU.mult), tg="g1n")
            P.do("scalar", lambda e: e.activation(out=zz, in_=lg, func=AF.Exp,
                                                  bias=imax),
                 reads=[gsc.t], writes=[gsc.t], tag=name + f"g2{gt}")
            V(lambda e: e.reduce_sum(out=ss, in_=zz, axis=AX.X), tg="g3")
            V(lambda e: e.reciprocal(out=p1, in_=ss), tg="g4")
            V(lambda e: e.tensor_scalar(out=eq, in0=lg, scalar1=m1, scalar2=None,
                                        op0=ALU.is_equal), tg="g5")
            V(lambda e: e.tensor_mul(out=eq, in0=eq, in1=cm.iota8r[:, 0:8]),
              rd=[cm.iota8r_t], tg="g6")
            V(lambda e: e.reduce_max(out=imax, in_=eq, axis=AX.X), tg="g7")
            V(lambda e: e.tensor_scalar(out=f1, in0=cm.iota8r[:, 0:8], scalar1=imax,
                                        scalar2=None, op0=ALU.is_equal),
              rd=[cm.iota8r_t], tg="g8")
            V(lambda e: e.scalar_tensor_tensor(out=eq, in0=f1, scalar=-1.0, in1=zz,
                                               op0=ALU.mult, op1=ALU.mult), tg="g9")
            V(lambda e: e.tensor_add(out=eq, in0=zz, in1=eq), tg="gA")
            V(lambda e: e.reduce_max(out=m2, in_=eq, axis=AX.X), tg="gB")
            V(lambda e: e.tensor_scalar(out=eq, in0=eq, scalar1=m2, scalar2=None,
                                        op0=ALU.is_equal), tg="gC")
            V(lambda e: e.tensor_mul(out=eq, in0=eq, in1=cm.iota8r[:, 0:8]),
              rd=[cm.iota8r_t], tg="gD")
            V(lambda e: e.reduce_max(out=imax, in_=eq, axis=AX.X), tg="gE")
            V(lambda e: e.tensor_scalar(out=f2, in0=cm.iota8r[:, 0:8], scalar1=imax,
                                        scalar2=None, op0=ALU.is_equal),
              rd=[cm.iota8r_t], tg="gF")
            V(lambda e: e.tensor_scalar(out=p2, in0=m2, scalar1=p1, scalar2=None,
                                        op0=ALU.mult), tg="gG")
            V(lambda e: e.tensor_scalar(out=dnm, in0=p1, scalar1=p2, scalar2=1e-8,
                                        op0=ALU.add, op1=ALU.add), tg="gH")
            V(lambda e: e.reciprocal(out=dnm, in_=dnm), tg="gI")
            V(lambda e: e.tensor_scalar(out=p1, in0=p1, scalar1=dnm, scalar2=None,
                                        op0=ALU.mult), tg="gJ")
            V(lambda e: e.tensor_scalar(out=p2, in0=p2, scalar1=dnm, scalar2=None,
                                        op0=ALU.mult), tg="gK")
            cwsl = cw.ap[:, lt * 8:(lt + 1) * 8]
            P.do("vector", lambda e, o=cwsl: e.tensor_scalar(out=o, in0=f1, scalar1=p1,
                                                             scalar2=None, op0=ALU.mult),
                 reads=[gsc.t], writes=[cw.t], tag=name + f"gL{gt}")
            P.do("vector", lambda e, o=cwsl: e.scalar_tensor_tensor(out=o, in0=f2, scalar=p2,
                                                                    in1=o, op0=ALU.mult,
                                                                    op1=ALU.add),
                 reads=[gsc.t, cw.t], writes=[cw.t], tag=name + f"gM{gt}")

        zbuf = AB.alloc(name + f"_zb{ch}", 8 * TC)
        zbv = zbuf.ap.rearrange("p (i t) -> p i t", t=TC)
        gub = [AB.alloc(name + f"_gu{ch}{i}", 16 * C) for i in range(2)]
        dnb = [AB.alloc(name + f"_dn{ch}{i}", 8 * C) for i in range(2)]
        wstg = [AF32.alloc(name + f"_ws{ch}{i}", 1024, dma_slot=True) for i in range(2)]
        def gu_src(e2):
            return wdr["l3f_sh_guT"] if e2 < 0 else wdr["l3f_ex_guT"][e2]
        def dn_src(e2):
            return wdr["l3f_sh_dnT"] if e2 < 0 else wdr["l3f_ex_dnT"][e2]
        def load_gu(buf, src, tagn):
            for hf in range(2):
                for ct in range(8):
                    P.dma("sync", wstg[0].ap[:, 0:1024],
                          src[ct * 128:(ct + 1) * 128, hf * 1024:(hf + 1) * 1024],
                          writes=[wstg[0].t], tag=tagn + f"L{hf}{ct}")
                    dst = buf.ap[:, ct * 2048 + hf * 1024: ct * 2048 + (hf + 1) * 1024]
                    P.do("vector",
                         lambda e, o=dst, i=wstg[0].ap[:, 0:1024]: e.tensor_copy(out=o, in_=i),
                         reads=[wstg[0].t], writes=[buf.t], tag=tagn + f"C{hf}{ct}")
        load_gu(gub[0], gu_src(-1), name + f"_gu{ch}s")
        load_weight_into(P, dnb[0], wstg[1], dn_src(-1), C, C, name + f"_dn{ch}s")
        for e_ in range(-1, 8):
            gu = gub[(e_ + 1) % 2]
            dn = dnb[(e_ + 1) % 2]
            if e_ + 1 < 8:
                load_gu(gub[e_ % 2], gu_src(e_ + 1), name + f"_gu{ch}{e_ + 1}")
                load_weight_into(P, dnb[e_ % 2], wstg[1], dn_src(e_ + 1), C, C,
                                 name + f"_dn{ch}{e_ + 1}")
            for ts in range(TC // 512):
                for it in range(8):
                    banku, banku_t = ps.get()
                    mms = [(banku, gu.ap[:, ct * 2048 + 1024 + it * 128:
                                         ct * 2048 + 1024 + it * 128 + 128],
                            xnTv[:, ct, ts * 512:(ts + 1) * 512], None) for ct in range(8)]
                    mm_group(P, banku, banku_t, mms, reads=[gu.t, xnT.t],
                             tag=name + f"u{ch}{e_}{ts}{it}")
                    P.do("vector", lambda e, o=usf.ap, i=banku: e.tensor_copy(out=o, in_=i),
                         reads=[banku_t], writes=[usf.t], tag=name + f"ue{ch}{e_}{ts}{it}")
                    bankg, bankg_t = ps.get()
                    mms = [(bankg, gu.ap[:, ct * 2048 + it * 128:
                                         ct * 2048 + it * 128 + 128],
                            xnTv[:, ct, ts * 512:(ts + 1) * 512], None) for ct in range(8)]
                    mm_group(P, bankg, bankg_t, mms, reads=[gu.t, xnT.t],
                             tag=name + f"g{ch}{e_}{ts}{it}")
                    P.do("scalar", lambda e, o=sgf.ap, i=bankg:
                         e.activation(out=o, in_=i, func=AF.Silu),
                         reads=[bankg_t], writes=[sgf.t], tag=name + f"gs{ch}{e_}{ts}{it}")
                    dst = zbv[:, it, ts * 512:(ts + 1) * 512]
                    P.do("vector", lambda e, o=dst, a=sgf.ap, b_=usf.ap:
                         e.tensor_mul(out=o, in0=a, in1=b_),
                         reads=[sgf.t, usf.t], writes=[zbuf.t],
                         tag=name + f"zm{ch}{e_}{ts}{it}")
            for lt in range(ntt):
                for oc in range(2):
                    bank, bank_t = ps.get()
                    mms = [(bank, zbv[:, itile, lt * 128:(lt + 1) * 128],
                            dn.ap[:, itile * C + oc * 512: itile * C + (oc + 1) * 512],
                            None) for itile in range(8)]
                    mm_group(P, bank, bank_t, mms, reads=[zbuf.t, dn.t],
                             tag=name + f"d{ch}{e_}{lt}{oc}")
                    dst = oaccv[:, lt, oc * 512:(oc + 1) * 512]
                    if e_ < 0:
                        P.do("vector", lambda e, o=dst, i=bank: e.tensor_copy(out=o, in_=i),
                             reads=[bank_t], writes=[oacc.t],
                             tag=name + f"da{ch}{e_}{lt}{oc}")
                    else:
                        P.do("vector", lambda e, o=dst, i=bank,
                             g_=cw.ap[:, lt * 8 + e_: lt * 8 + e_ + 1]:
                             e.scalar_tensor_tensor(out=o, in0=i, scalar=g_, in1=o,
                                                    op0=ALU.mult, op1=ALU.add),
                             reads=[bank_t, cw.t, oacc.t], writes=[oacc.t],
                             tag=name + f"da{ch}{e_}{lt}{oc}")
        for i in range(2):
            AB.freeb(gub[i])
            AB.freeb(dnb[i])
            AF32.freeb(wstg[i])
        AB.freeb(zbuf)
        AB.freeb(xnT)

        for lt in range(ntt):
            gt = ch * ntt + lt
            s = sl[lt % 2]
            P.dma("sync", s["xr"].ap, xin_ap(gt), writes=[s["xr"].t],
                  reads=[xin_tiles[gt]], tag=name + f"z{gt}")
            ob = outs[lt % 2]
            P.do("vector", lambda e, o=ob.ap, a=oaccv[:, lt, :], x=s["xr"].ap:
                 e.tensor_add(out=o, in0=a, in1=x),
                 reads=[oacc.t, s["xr"].t], writes=[ob.t], tag=name + f"ra{gt}")
            P.dma("scalar", xout_ap(gt), ob.ap, writes=[xout_tiles[gt]],
                  reads=[ob.t], tag=name + f"zs{gt}")
        AB.freeb(oacc)
        AF32.freeb(cw)

    AB.freeb(wg)
    AF32.freeb(sq_sh)
    AF32.freeb(gsc)
    AB.freeb(usf)
    AF32.freeb(gsf)
    AF32.freeb(sgf)
    for s in sl:
        AF32.freeb(s["xr"])
        AF32.freeb(s["st"])
        AB.freeb(s["xn"])
    for o in outs:
        AF32.freeb(o)



CONST_SPECS = {
    "ident": ([128, 128], BF16),
    "ones_row": ([1, 128], BF16),
    "ones_col": ([128, 1], BF16),
    "iota8": ([128, 8], F32),
    "iota8r": ([128, 8], F32),
    "causmask": ([128, 896], BF16),
    "esel": ([8, 1024], BF16),
    "rope0_cos": ([T, 512], F32),
    "rope0_sin": ([T, 512], F32),
    "rope1_cos": ([T, 512], F32),
    "rope1_sin": ([T, 512], F32),
    "alibi_srow": ([1, NHEAD * T], F32),
    "alibi_negtrow": ([1, NHEAD * T], F32),
    "alibi_band": ([NHEAD, 128, 896], F32),
    "alibi_scol": ([128, 256], F32),
}

WEIGHT_SPECS = {
    "l0a_wqT": ([C, C], F32), "l0a_wkT": ([C, C], F32),
    "l0a_wvT": ([C, C], F32), "l0a_wpT": ([C, C], F32),
    "l0a_bq": ([1, C], F32), "l0a_bk": ([1, C], F32),
    "l0a_bv": ([1, C], F32), "l0a_bp": ([1, C], F32),
    "l0f_k0t": ([C, 4096], U8), "l0f_k0b": ([C, 4096], U8),
    "l0f_k1": ([4096, 4096], U8), "l0f_k2": ([4096, C], U8),
    "l0f_s0": ([128, 1], F32), "l0f_s1": ([128, 1], F32), "l0f_s2": ([128, 1], F32),
    "l1a_wqT": ([C, C], F32), "l1a_wkT": ([C, C], F32), "l1a_wvT": ([C, C], F32),
    "l1a_bq": ([1, C], F32), "l1a_bk": ([1, C], F32), "l1a_bv": ([1, C], F32),
    "l1f_Wg": ([C, 8], F32), "l1f_bg": ([1, 8], F32),
    "l1f_W1": ([8, C, C], F32), "l1f_b1": ([128, 64], F32),
    "l1f_W2": ([8, C, C], F32), "l1f_b2": ([8, C], F32),
    "l2a_wqkv": ([C, 3 * C], F32), "l2a_bqkv": ([1, 3 * C], F32),
    "l2a_wp": ([C, C], F32), "l2a_bp": ([1, C], F32),
    "l2f_gateT": ([C, 8], F32),
    "l2f_wgT": ([8, C, 512], F32), "l2f_wuT": ([8, C, 512], F32),
    "l2f_wdT": ([8, 512, C], F32),
    "l3a_wqkvT": ([C, 3 * C], F32), "l3a_woT": ([C, C], F32),
    "l3a_bo": ([1, C], F32), "l3a_rfT": ([128, 128], F32),
    "l3f_gateT": ([C, 8], F32),
    "l3f_sh_guT": ([C, 2048], F32), "l3f_sh_dnT": ([C, C], F32),
    "l3f_ex_guT": ([8, C, 2048], F32), "l3f_ex_dnT": ([8, C, C], F32),
}

PART_WEIGHTS = {
    "l0a": ["l0a_wqT", "l0a_wkT", "l0a_wvT", "l0a_wpT",
            "l0a_bq", "l0a_bk", "l0a_bv", "l0a_bp"],
    "l0f": ["l0f_k0t", "l0f_k0b", "l0f_k1", "l0f_k2", "l0f_s0", "l0f_s1", "l0f_s2"],
    "l1a": ["l1a_wqT", "l1a_wkT", "l1a_wvT", "l1a_bq", "l1a_bk", "l1a_bv"],
    "l1f": ["l1f_Wg", "l1f_bg", "l1f_W1", "l1f_b1", "l1f_W2", "l1f_b2"],
    "l2a": ["l2a_wqkv", "l2a_bqkv", "l2a_wp", "l2a_bp"],
    "l2f": ["l2f_gateT", "l2f_wgT", "l2f_wuT", "l2f_wdT"],
    "l3a": ["l3a_wqkvT", "l3a_woT", "l3a_bo", "l3a_rfT"],
    "l3f": ["l3f_gateT", "l3f_sh_guT", "l3f_sh_dnT", "l3f_ex_guT", "l3f_ex_dnT"],
}

BF16_COLS = 85 * 1024
F32_COLS = 9 * 1024 + 512
U8_COLS = 11 * 1024


def build_program(part_ids):
    Arena._sem_pool = []
    Arena._sem_n = [0]
    Arena._sem_last = {}
    nc = bass.Bass("TRN2", target_bir_lowering=False, debug=False,
                   dynamic_dma_scratch_size=512)
    ctx = ExitStack()
    P = Prog(nc, ctx)

    xin_d = nc.dram_tensor("xin", [NT, C], F32, kind="ExternalInput").ap()
    xout_d = nc.dram_tensor("xout", [NT, C], F32, kind="ExternalOutput").ap()
    scr = [nc.dram_tensor(f"xscr{i}", [NT, C], F32, kind="Internal").ap()
           for i in range(2)]

    wdr = {}
    for pid in part_ids:
        for wn in PART_WEIGHTS[pid]:
            shape, dt = WEIGHT_SPECS[wn]
            wdr[wn] = nc.dram_tensor(wn, list(shape), dt, kind="ExternalInput").ap()
    for cn, (shape, dt) in CONST_SPECS.items():
        wdr[cn] = nc.dram_tensor("c_" + cn, list(shape), dt, kind="ExternalInput").ap()

    AB = Arena(P, nc, ctx, "arena_bf", BF16_COLS, BF16)
    AF32_ = Arena(P, nc, ctx, "arena_f32", F32_COLS, F32)
    AU8 = Arena(P, nc, ctx, "arena_u8", U8_COLS, U8)

    cm = Common()
    ps = PsumPool(P, nc, ctx)

    def resident(name, shape, dt):
        h = ctx.enter_context(nc.sbuf_tensor("sb_" + name, list(shape), dt))
        t = P.tile("c_" + name, dma_sem_key="c_" + name)
        P.dma("sync", h[:], wdr[name], writes=[t], tag="cl_" + name)
        return h[:], t

    cm.ident, cm.ident_t = resident("ident", [128, 128], BF16)
    cm.ones_row, cm.ones_row_t = resident("ones_row", [1, 128], BF16)
    cm.ones_col, cm.ones_col_t = resident("ones_col", [128, 1], BF16)
    cm.iota8, cm.iota8_t = resident("iota8", [128, 8], F32)
    cm.iota8r, cm.iota8r_t = resident("iota8r", [128, 8], F32)
    cm.causmask, cm.causmask_t = resident("causmask", [128, 896], BF16)
    cm.esel, cm.esel_t = resident("esel", [8, 1024], BF16)

    xres_tiles = {}

    def x_stage(si):
        if si == 0:
            d = xin_d
        elif si == len(part_ids):
            d = xout_d
        else:
            d = scr[(si - 1) % 2]
        key = id(d)
        if key not in xres_tiles:
            xres_tiles[key] = [P.tile(f"x{si}_{i}") for i in range(NTT)]
        return (lambda i, d=d: d[i * 128:(i + 1) * 128, :]), xres_tiles[key]

    emitters = {"l0a": emit_l0_attn, "l0f": emit_l0_ffn,
                "l1a": emit_l1_attn, "l1f": emit_l1_ffn,
                "l2a": emit_l2_attn, "l2f": emit_l2_ffn,
                "l3a": emit_l3_attn, "l3f": emit_l3_ffn}

    for si, pid in enumerate(part_ids):
        io = {"xin": x_stage(si), "xout": x_stage(si + 1)}
        emitters[pid](P, AB, AF32_, AU8, ps, cm, nc, io, wdr)

    P.finalize()
    with nc.Block() as block:
        @block.sync
        def _(eng):
            P.emit("sync", eng)

        @block.vector
        def _(eng):
            P.emit("vector", eng)

        @block.scalar
        def _(eng):
            P.emit("scalar", eng)

        @block.tensor
        def _(eng):
            P.emit("tensor", eng)

        @block.gpsimd
        def _(eng):
            P.emit("gpsimd", eng)
            P.emit_final_wait(eng)

    return nc, ctx, P


PART_GROUPS = [["l0a", "l0f", "l1a", "l1f", "l2a", "l2f", "l3a", "l3f"]]


def _run_groups(inputs, groups, x0):
    consts = prepare_consts()
    weights = prepare_weights(inputs)
    x = np.ascontiguousarray(
        np.asarray(x0, dtype=np.float32).reshape(NCORES, NT, C))
    for group in groups:
        nc, ctx, P = build_program(group)
        in_maps = []
        for cidx in range(NCORES):
            m = {"xin": x[cidx]}
            for pid in group:
                for wn in PART_WEIGHTS[pid]:
                    m[wn] = np.asarray(weights[wn])
            for cn in CONST_SPECS:
                m["c_" + cn] = np.asarray(consts[cn])
            in_maps.append(m)
        res = run_bass_kernel_spmd(nc, in_maps, core_ids=list(range(NCORES)))
        x = np.stack([np.asarray(res.results[cidx]["xout"]) for cidx in range(NCORES)])
        ctx.close()
    return x.reshape(B, T, C)


def kernel(**inputs):
    x0 = np.asarray(inputs["L0_attn_x"], dtype=np.float32)
    out = _run_groups(inputs, PART_GROUPS, x0)
    return out.astype(np.float32)


# version 08
# speedup vs baseline: 1.3112x; 1.0789x over previous
import numpy as np
import ml_dtypes
import concourse.bass as bass
import concourse.mybir as mybir
from concourse.bass_utils import run_bass_kernel_spmd
from contextlib import ExitStack

F32 = mybir.dt.float32
BF16 = mybir.dt.bfloat16
U8 = mybir.dt.uint8
AF = mybir.ActivationFunctionType
ALU = mybir.AluOpType
AX = mybir.AxisListType

NCORES = 8
B, T, C = 16, 2048, 1024
BPC = B // NCORES
NT = BPC * T
NTT = NT // 128
TPB = T // 128
NHEAD = 16
HD = 64


class Tile:
    __slots__ = ("name", "last_write", "readers", "war_deps", "dma_sem_key")

    def __init__(self, name, dma_sem_key=None):
        self.name = name
        self.last_write = None
        self.readers = []
        self.war_deps = ()
        self.dma_sem_key = dma_sem_key


class Step:
    __slots__ = ("idx", "role", "fn", "deps", "needs_inc", "ev", "inck",
                 "sem_key", "tag")

    def __init__(self, idx, role, fn, inck, sem_key, tag):
        self.idx = idx
        self.role = role
        self.fn = fn
        self.deps = ()
        self.needs_inc = inck == 16
        self.ev = None
        self.inck = inck
        self.sem_key = sem_key
        self.tag = tag


class FlushStep:
    __slots__ = ("sem_key", "ev", "needs_inc", "role")

    def __init__(self, sem_key, snap):
        self.sem_key = sem_key
        self.ev = (sem_key, snap)
        self.needs_inc = True
        self.role = "_flush"


class Prog:
    def __init__(self, nc, ctx):
        self.nc = nc
        self.ctx = ctx
        self.steps = []
        self.sem_handles = {}
        self.live_counts = {}

    def tile(self, name, dma_sem_key=None):
        return Tile(name, dma_sem_key)

    def _record(self, role, fn, reads, writes, inck, sem_key, tag):
        st = Step(len(self.steps), role, fn, inck, sem_key, tag)
        deps = set()
        for t in reads:
            if t.last_write is not None:
                deps.add(t.last_write)
        for t in writes:
            if t.last_write is not None:
                deps.add(t.last_write)
            for r in t.readers:
                deps.add(r)
            if t.war_deps:
                deps.update(t.war_deps)
                t.war_deps = ()
        deps.discard(st)
        st.deps = tuple(d for d in deps
                        if not (role == "tensor" and getattr(d, "role", "") == "tensor"))
        for t in writes:
            t.last_write = st
            t.readers = []
        for t in reads:
            t.readers.append(st)
        self.steps.append(st)
        self.live_counts[sem_key] = self.live_counts.get(sem_key, 0) + inck
        return st

    def do(self, role, fn, reads=(), writes=(), tag=""):
        return self._record(role, fn, reads, writes, 1, "e_" + role, tag)

    def dma(self, role, out_ap, in_ap, writes, reads=(), tag=""):
        key = None
        if writes and writes[0].dma_sem_key:
            key = writes[0].dma_sem_key
        elif reads and reads[0].dma_sem_key:
            key = reads[0].dma_sem_key
        else:
            key = "dma_store"
        fn = lambda eng, o=out_ap, i=in_ap: eng.dma_start(out=o, in_=i)
        return self._record(role, fn, reads, writes, 16, key, tag)

    def dram_flush(self, tiles):
        fl = FlushStep("dma_store", self.live_counts.get("dma_store", 0))
        for t in tiles:
            if t.last_write is not None:
                t.last_write = fl
                t.readers = []

    def finalize(self):
        for st in self.steps:
            for d in st.deps:
                d.needs_inc = True
        counts = {}
        for st in self.steps:
            if st.needs_inc:
                counts.setdefault(st.sem_key, 0)
                counts[st.sem_key] += st.inck
                st.ev = (st.sem_key, counts[st.sem_key])
        self.sem_totals = counts
        keys = set(counts)
        for st in self.steps:
            for d in st.deps:
                keys.add(d.ev[0] if d.ev else d.sem_key)
        for k in sorted(keys):
            self.sem_handles[k] = self.ctx.enter_context(self.nc.semaphore(k))

    def emit(self, role, eng):
        hw = {}
        for st in self.steps:
            if st.role != role:
                continue
            for d in st.deps:
                if d.ev is None:
                    continue
                key, val = d.ev
                if hw.get(key, 0) < val:
                    eng.wait_ge(self.sem_handles[key], val)
                    hw[key] = val
            ins = st.fn(eng)
            if st.needs_inc:
                ins.then_inc(self.sem_handles[st.sem_key], st.inck)

    def emit_final_wait(self, eng):
        for k, tot in sorted(self.sem_totals.items()):
            if tot and (k.startswith("dl") or k == "dma_store" or k.startswith("c_")):
                eng.wait_ge(self.sem_handles[k], tot)



class Buf:
    __slots__ = ("ap", "t", "off", "cols", "arena")

    def __getitem__(self, sl):
        return self.ap[sl]


class Arena:
    ALIGN = 32

    _sem_pool = []
    _sem_n = [0]
    _sem_last = {}

    def __init__(self, P, nc, ctx, name, cols, dtype):
        self.P = P
        self.h = ctx.enter_context(nc.sbuf_tensor(name, [128, cols], dtype))
        self.free = [(0, cols)]
        self.war_iv = []
        self.name = name
        self.live = {}

    def _collect_war(self, off, end):
        deps = set()
        keep = []
        for (a, b, dd) in self.war_iv:
            if a < end and off < b:
                deps.update(dd)
                if a < off:
                    keep.append((a, off, dd))
                if b > end:
                    keep.append((end, b, dd))
            else:
                keep.append((a, b, dd))
        self.war_iv = keep
        return deps

    def alloc(self, name, cols, dma_slot=False):
        cols = (cols + self.ALIGN - 1) & ~(self.ALIGN - 1)
        for i, (off, ln) in enumerate(self.free):
            if ln >= cols:
                if ln == cols:
                    self.free.pop(i)
                else:
                    self.free[i] = (off + cols, ln - cols)
                b = Buf()
                b.ap = self.h[:, off:off + cols]
                key = None
                sem_deps = set()
                if dma_slot:
                    if Arena._sem_pool:
                        key = Arena._sem_pool.pop()
                        old = Arena._sem_last.get(key)
                        if old is not None:
                            sem_deps.update(old.readers)
                            if old.last_write is not None and \
                                    not isinstance(old.last_write, FlushStep):
                                sem_deps.add(old.last_write)
                    else:
                        key = f"dl{Arena._sem_n[0]}"
                        Arena._sem_n[0] += 1
                b.t = self.P.tile(name, dma_sem_key=key)
                if key is not None:
                    Arena._sem_last[key] = b.t
                b.t.war_deps = tuple(self._collect_war(off, off + cols) | sem_deps)
                b.off = off
                b.cols = cols
                b.arena = self
                for o2, (n2, c2) in self.live.items():
                    assert not (off < o2 + c2 and o2 < off + cols), \
                        f"arena {self.name}: {name} [{off},{off+cols}) overlaps {n2} [{o2},{o2+c2})"
                self.live[off] = (name, cols)
                return b
        raise RuntimeError(f"arena {self.name} overflow: need {cols}, free={self.free}")

    def freeb(self, b):
        assert b.off in self.live, f"arena {self.name}: double free of {b.t.name} at {b.off}"
        del self.live[b.off]
        if b.t.dma_sem_key is not None:
            Arena._sem_pool.append(b.t.dma_sem_key)
        deps = set()
        if b.t.last_write is not None and not isinstance(b.t.last_write, FlushStep):
            deps.add(b.t.last_write)
        deps.update(b.t.readers)
        if deps:
            self.war_iv.append((b.off, b.off + b.cols, frozenset(deps)))
        self.free.append((b.off, b.cols))
        self.free.sort()
        merged = []
        for off, ln in self.free:
            if merged and merged[-1][0] + merged[-1][1] == off:
                merged[-1] = (merged[-1][0], merged[-1][1] + ln)
            else:
                merged.append([off, ln])
        self.free = [tuple(x) for x in merged]


class PsumPool:
    def __init__(self, P, nc, ctx):
        self.P = P
        self.banks = []
        self.busy = [False] * 8
        for i in range(8):
            h = ctx.enter_context(nc.psum_tensor(f"psb{i}", [128, 512], F32))
            self.banks.append((h[:], P.tile(f"psum{i}"), i))
        self.rr = 0

    def get(self):
        for _ in range(8):
            ap, t, i = self.banks[self.rr % 8]
            self.rr += 1
            if not self.busy[i]:
                return ap, t
        raise RuntimeError("all psum banks taken")

    def take(self):
        for ap, t, i in self.banks:
            if not self.busy[i]:
                self.busy[i] = True
                return ap, t, i
        raise RuntimeError("no free psum bank to take")

    def release(self, i):
        self.busy[i] = False



def _t(a):
    return np.ascontiguousarray(np.asarray(a).T)


def _rope_tables(head_dim_half, repeat):
    j = np.arange(head_dim_half, dtype=np.float32)
    theta = np.power(np.float32(10000.0),
                     np.float32(-2.0) * j / np.float32(2 * head_dim_half))
    t = np.arange(T, dtype=np.float32)
    ang = t[:, None] * theta[None, :]
    cos = np.cos(ang).astype(np.float32)
    sin = np.sin(ang).astype(np.float32)
    if repeat > 1:
        cos = np.tile(cos, (1, repeat))
        sin = np.tile(sin, (1, repeat))
    return np.ascontiguousarray(cos), np.ascontiguousarray(sin)


def _alibi_slopes(n):
    import math
    def pow2(m):
        start = 2 ** (-2 ** (-(math.log2(m) - 3)))
        return [start * start ** i for i in range(m)]
    if math.log2(n).is_integer():
        return pow2(n)
    c = 2 ** math.floor(math.log2(n))
    return pow2(c) + _alibi_slopes(2 * c)[0::2][:n - c]


def prepare_consts():
    c = {}
    c["ident"] = np.eye(128, dtype=ml_dtypes.bfloat16)
    c["ones_row"] = np.ones((1, 128), dtype=ml_dtypes.bfloat16)
    c["ones_col"] = np.ones((128, 1), dtype=ml_dtypes.bfloat16)
    c["iota8"] = np.tile(np.arange(8, dtype=np.float32), (128, 1))
    c["iota8r"] = np.tile(1000.0 - np.arange(8, dtype=np.float32), (128, 1))
    p = np.arange(128)[:, None]
    u = np.arange(896)[None, :]
    c["causmask"] = (p <= u - 384).astype(ml_dtypes.bfloat16)
    esel = np.zeros((8, 8 * 128), dtype=ml_dtypes.bfloat16)
    for e in range(8):
        esel[e, e * 128:(e + 1) * 128] = 1
    c["esel"] = esel
    c["rope0_cos"], c["rope0_sin"] = _rope_tables(HD // 2, NHEAD)
    c["rope1_cos"], c["rope1_sin"] = _rope_tables(C // 2, 1)
    slopes = np.asarray(_alibi_slopes(NHEAD), dtype=np.float64)
    idx = np.arange(T, dtype=np.float64)
    c["alibi_srow"] = np.ascontiguousarray(
        (8.0 * slopes[:, None] * idx[None, :]).reshape(1, -1).astype(np.float32))
    c["alibi_negtrow"] = np.ascontiguousarray(
        (-8.0 * slopes[:, None] * idx[None, :]).reshape(1, -1).astype(np.float32))
    band = np.minimum(p - (u - 384), 0).astype(np.float64)
    c["alibi_band"] = np.ascontiguousarray(
        (8.0 * slopes[:, None, None] * band[None]).astype(np.float32))
    pp = np.arange(128, dtype=np.float64)
    sc = np.zeros((128, 256), dtype=np.float64)
    for h in range(NHEAD):
        for kt in range(TPB):
            sc[:, h * 16 + kt] = slopes[h] * (kt * 128 + pp)
    c["alibi_scol"] = sc.astype(np.float32)
    return c


def prepare_weights(inp):
    w = {}
    for n in ("wq", "wk", "wv", "wp"):
        w[f"l0a_{n}T"] = _t(inp[f"L0_attn_{n}"])
    for n in ("bq", "bk", "bv", "bp"):
        w[f"l0a_{n}"] = np.asarray(inp[f"L0_attn_{n}"]).reshape(1, C)
    k0 = np.asarray(inp["L0_ffn_k0"]).view(np.uint8)
    w["l0f_k0t"] = np.ascontiguousarray(k0[:C])
    w["l0f_k0b"] = np.ascontiguousarray(k0[C:])
    w["l0f_k1"] = np.asarray(inp["L0_ffn_k1"]).view(np.uint8)
    w["l0f_k2"] = np.asarray(inp["L0_ffn_k2"]).view(np.uint8)
    for n in ("s0", "s1", "s2"):
        w[f"l0f_{n}"] = np.ascontiguousarray(
            np.broadcast_to(np.asarray(inp[f"L0_ffn_{n}"]).reshape(1, 1), (128, 1)))
    for n in ("wq", "wk", "wv"):
        w[f"l1a_{n}T"] = _t(inp[f"L1_attn_{n}_w"])
        w[f"l1a_b{n[1]}"] = np.asarray(inp[f"L1_attn_{n}_b"]).reshape(1, C)
    w["l1f_Wg"] = np.asarray(inp["L1_ffn_Wg"])
    w["l1f_bg"] = np.asarray(inp["L1_ffn_bg"]).reshape(1, 8)
    w["l1f_W1"] = np.asarray(inp["L1_ffn_W1"])
    w["l1f_b1"] = np.ascontiguousarray(
        np.asarray(inp["L1_ffn_b1"]).reshape(8, 8, 128).transpose(2, 0, 1).reshape(128, 64))
    w["l1f_W2"] = np.asarray(inp["L1_ffn_W2"])
    w["l1f_b2"] = np.asarray(inp["L1_ffn_b2"])
    w["l2a_wqkv"] = np.asarray(inp["L2_attn_W_qkv"])
    w["l2a_bqkv"] = np.asarray(inp["L2_attn_b_qkv"]).reshape(1, 3 * C)
    w["l2a_wp"] = np.asarray(inp["L2_attn_W_proj"])
    w["l2a_bp"] = np.asarray(inp["L2_attn_b_proj"]).reshape(1, C)
    w["l2f_gateT"] = _t(inp["L2_ffn_gate_w"])
    w["l2f_wgT"] = np.ascontiguousarray(np.transpose(inp["L2_ffn_w_gate"], (0, 2, 1)))
    w["l2f_wuT"] = np.ascontiguousarray(np.transpose(inp["L2_ffn_w_up"], (0, 2, 1)))
    w["l2f_wdT"] = np.ascontiguousarray(np.transpose(inp["L2_ffn_w_down"], (0, 2, 1)))
    w["l3a_wqkvT"] = _t(inp["L3_attn_w_qkv"])
    w["l3a_woT"] = _t(inp["L3_attn_w_out"])
    w["l3a_bo"] = np.asarray(inp["L3_attn_b_out"]).reshape(1, C)
    rft = _t(inp["L3_attn_random_features"]).astype(np.float32)
    blk = np.zeros((128, 128), dtype=np.float32)
    blk[0:64, 0:64] = rft
    blk[64:128, 64:128] = rft
    w["l3a_rfT"] = np.ascontiguousarray(blk)
    I0 = 938
    w["l3f_gateT"] = _t(inp["L3_ffn_gate_weight"])
    def pad_gu(gu):
        out = np.zeros((2048, C), dtype=np.float32)
        out[:I0] = gu[:I0]
        out[1024:1024 + I0] = gu[I0:]
        return _t(out)
    def pad_dn(dn):
        out = np.zeros((1024, C), dtype=np.float32)
        out[:I0] = np.asarray(dn).T
        return np.ascontiguousarray(out)
    w["l3f_sh_guT"] = pad_gu(np.asarray(inp["L3_ffn_shared_gate_up"]))
    w["l3f_sh_dnT"] = pad_dn(np.asarray(inp["L3_ffn_shared_down"]))
    egu = np.asarray(inp["L3_ffn_experts_gate_up"])
    edn = np.asarray(inp["L3_ffn_experts_down"])
    w["l3f_ex_guT"] = np.stack([pad_gu(egu[e]) for e in range(8)])
    w["l3f_ex_dnT"] = np.stack([pad_dn(edn[e]) for e in range(8)])
    return w



class Common:
    pass


def mm_group(P, psum_ap, psum_t, mms, reads, tag="", skip_gc=False):
    def fn(e):
        n = len(mms)
        ins = None
        for i, (out_ap, lh, rh, tp) in enumerate(mms):
            ins = e.matmul(out=out_ap, lhsT=lh, rhs=rh, start=(i == 0),
                           stop=(i == n - 1), tile_position=tp,
                           skip_group_check=skip_gc)
        return ins
    return P.do("tensor", fn, reads=reads, writes=[psum_t], tag=tag)


def ln_tile(P, cm, sl, xt_ap, xt_t, name):
    stat, sq, xn = sl["st"], sl["sq"], sl["xn"]
    mean = stat.ap[:, 0:1]
    msq = stat.ap[:, 1:2]
    var = stat.ap[:, 2:3]
    rstd = stat.ap[:, 3:4]
    P.do("vector", lambda e: e.reduce_sum(out=mean, in_=xt_ap, axis=AX.X),
         reads=[xt_t], writes=[stat.t], tag=name + "s")
    P.do("scalar", lambda e: e.activation(out=sq.ap, in_=xt_ap, func=AF.Square),
         reads=[xt_t], writes=[sq.t], tag=name + "q")
    P.do("vector", lambda e: e.reduce_sum(out=msq, in_=sq.ap, axis=AX.X),
         reads=[sq.t], writes=[stat.t], tag=name + "q2")
    P.do("vector", lambda e: e.tensor_scalar(out=stat.ap[:, 0:2], in0=stat.ap[:, 0:2],
                                             scalar1=1.0 / 1024, scalar2=None, op0=ALU.mult),
         reads=[stat.t], writes=[stat.t], tag=name + "d")
    P.do("vector", lambda e: e.tensor_scalar(out=var, in0=mean, scalar1=mean,
                                             scalar2=None, op0=ALU.mult),
         reads=[stat.t], writes=[stat.t], tag=name + "m2")
    P.do("vector", lambda e: e.scalar_tensor_tensor(out=var, in0=msq, scalar=1e-5,
                                                     in1=var, op0=ALU.add, op1=ALU.subtract),
         reads=[stat.t], writes=[stat.t], tag=name + "v")
    P.do("scalar", lambda e: e.activation(out=rstd, in_=var, func=AF.Ln),
         reads=[stat.t], writes=[stat.t], tag=name + "l")
    P.do("scalar", lambda e: e.activation(out=rstd, in_=rstd, func=AF.Exp, scale=-0.5),
         reads=[stat.t], writes=[stat.t], tag=name + "e")
    P.do("vector", lambda e: e.tensor_scalar(out=xn.ap, in0=xt_ap, scalar1=mean,
                                             scalar2=rstd, op0=ALU.subtract, op1=ALU.mult),
         reads=[xt_t, stat.t], writes=[xn.t], tag=name + "n")
    return xn


def transpose_to(P, ps, cm, src_ap, src_t, dst_ap, dst_t, ncols, tag,
                 dst_view=None):
    assert ncols <= 1024
    bank, bank_t = ps.get()
    bb = bank.bitcast(BF16)
    nb = ncols // 128
    def fn(e):
        ins = None
        for ct in range(nb):
            ins = e.transpose(out=bb[:, ct * 128:(ct + 1) * 128],
                              in_=src_ap[:, ct * 128:(ct + 1) * 128],
                              identity=cm.ident)
        return ins
    P.do("tensor", fn, reads=[src_t, cm.ident_t], writes=[bank_t], tag=tag + "t")
    if dst_view is not None:
        dv = dst_view
        sv = bb[:, 0:ncols].rearrange("p (c t) -> p c t", t=128)
    else:
        dv = dst_ap
        sv = bb[:, 0:ncols]
    P.do("scalar", lambda e: e.copy(out=dv, in_=sv),
         reads=[bank_t], writes=[dst_t], tag=tag + "e")


def load_weight_into(P, wb, stgs, dram_ap, rows, cols, name, role="sync"):
    ntile = rows // 128
    ns = len(stgs)
    for ct in range(ntile):
        stg = stgs[ct % ns]
        P.dma(role, stg.ap[:, 0:cols], dram_ap[ct * 128:(ct + 1) * 128, :],
              writes=[stg.t], tag=name + f"L{ct}")
        dst = wb.ap[:, ct * cols:(ct + 1) * cols]
        P.do("vector",
             lambda e, o=dst, i=stg.ap[:, 0:cols]: e.tensor_copy(out=o, in_=i),
             reads=[stg.t], writes=[wb.t], tag=name + f"C{ct}")


def load_weight_bf16(P, AB, AF32, dram_ap, rows, cols, name, role="sync", ns=None):
    ntile = rows // 128
    wb = AB.alloc(name, ntile * cols)
    if ns is None:
        ns = 2 if cols >= 2048 else 4
    stgs = [AF32.alloc(name + f"_stg{i}", cols, dma_slot=True) for i in range(ns)]
    load_weight_into(P, wb, stgs, dram_ap, rows, cols, name, role)
    for st in stgs:
        AF32.freeb(st)
    return wb


def load_bias_row(P, AB, AF32, dram_ap, cols, name):
    stg = AF32.alloc(name + "_stg", cols, dma_slot=True)
    bb = AB.alloc(name, cols)
    P.dma("sync", stg.ap[0:1, :], dram_ap, writes=[stg.t], tag=name + "L")
    P.do("vector", lambda e: e.tensor_copy(out=bb.ap[0:1, :], in_=stg.ap[0:1, :]),
         reads=[stg.t], writes=[bb.t], tag=name + "C")
    AF32.freeb(stg)
    return bb



def emit_l0_attn(P, AB, AF32, AU8, ps, cm, nc, io, wdr):
    for b in range(BPC):
        _softmax_attn_batch(
            P, AB, AF32, ps, cm, nc, io, b, name=f"l0a{b}",
            wspec=dict(kind="sep",
                       wq=("l0a_wqT", "l0a_bq"), wk=("l0a_wkT", "l0a_bk"),
                       wv=("l0a_wvT", "l0a_bv"), wp=("l0a_wpT", "l0a_bp")),
            wdr=wdr, rope=("rope0_cos", "rope0_sin"), causal=True,
            alibi=False, scale=0.125)


def _softmax_attn_batch(P, AB, AF32, ps, cm, nc, io, b, name, wspec, wdr,
                        rope, causal, alibi, scale):
    xin_ap, xin_tiles = io["xin"]
    xout_ap, xout_tiles = io["xout"]

    qfm = AB.alloc(name + "_qfm", 8 * T)
    kfm = AB.alloc(name + "_kfm", 8 * T)
    vtm = AB.alloc(name + "_vtm", TPB * 1024)

    if wspec["kind"] == "sep":
        wq = load_weight_bf16(P, AB, AF32, wdr[wspec["wq"][0]], C, C, name + "_wq")
        wk = load_weight_bf16(P, AB, AF32, wdr[wspec["wk"][0]], C, C, name + "_wk")
        wv = load_weight_bf16(P, AB, AF32, wdr[wspec["wv"][0]], C, C, name + "_wv")
        bq = load_bias_row(P, AB, AF32, wdr[wspec["wq"][1]], C, name + "_bq")
        bk = load_bias_row(P, AB, AF32, wdr[wspec["wk"][1]], C, name + "_bk")
        bv = load_bias_row(P, AB, AF32, wdr[wspec["wv"][1]], C, name + "_bv")
        def wslice(which, ct, oc):
            wb = {"q": wq, "k": wk, "v": wv}[which]
            return wb.ap[:, ct * C + oc * 512: ct * C + (oc + 1) * 512]
        def bslice(which, oc):
            bb = {"q": bq, "k": bk, "v": bv}[which]
            return bb.ap[0:1, oc * 512:(oc + 1) * 512]
        def wtile(which):
            return {"q": wq, "k": wk, "v": wv}[which].t
        def btile(which):
            return {"q": bq, "k": bk, "v": bv}[which].t
    else:
        wqkv = load_weight_bf16(P, AB, AF32, wdr[wspec["wqkv"][0]], C, 3 * C,
                                name + "_wqkv")
        bqkv = load_bias_row(P, AB, AF32, wdr[wspec["wqkv"][1]], 3 * C, name + "_bqkv")
        offmap = {"q": 0, "k": C, "v": 2 * C}
        def wslice(which, ct, oc):
            o0 = offmap[which] + oc * 512
            return wqkv.ap[:, ct * 3 * C + o0: ct * 3 * C + o0 + 512]
        def bslice(which, oc):
            o0 = offmap[which] + oc * 512
            return bqkv.ap[0:1, o0:o0 + 512]
        def wtile(which):
            return wqkv.t
        def btile(which):
            return bqkv.t

    sq_sh = AF32.alloc(name + "_sq", 1024)
    sl = []
    for i in range(2):
        sl.append({
            "xr": AF32.alloc(name + f"_xr{i}", 1024, dma_slot=True),
            "st": AF32.alloc(name + f"_st{i}", 32),
            "sq": sq_sh,
            "xn": AB.alloc(name + f"_xn{i}", 1024),
            "xnT": AB.alloc(name + f"_xnT{i}", 1024),
            "qb": AB.alloc(name + f"_qb{i}", 1024),
            "kb": AB.alloc(name + f"_kb{i}", 1024),
        })
    if rope is not None:
        r1 = AF32.alloc(name + "_r1", 256)
        r2 = AF32.alloc(name + "_r2", 256)
        csl = [AF32.alloc(name + f"_cs{i}", 512, dma_slot=True) for i in range(2)]
        snl = [AF32.alloc(name + f"_sn{i}", 512, dma_slot=True) for i in range(2)]
        cosd, sind = wdr[rope[0]], wdr[rope[1]]

    def proj(which, s, tt, dstb=None, ropecs=None, tagp=""):
        for oc in range(2):
            bank, bank_t = ps.get()
            mms = [(bank, cm.ones_row[0:1, :], bslice(which, oc), None)]
            for ct in range(8):
                mms.append((bank, s["xnT"].ap[:, ct * 128:(ct + 1) * 128],
                            wslice(which, ct, oc), None))
            mm_group(P, bank, bank_t, mms,
                     reads=[s["xnT"].t, wtile(which), btile(which), cm.ones_row_t],
                     tag=tagp + f"m{oc}")
            if dstb is None:
                dst = vtm.ap[:, tt * 1024 + oc * 512: tt * 1024 + (oc + 1) * 512]
                P.do("scalar", lambda e, o=dst, i=bank: e.copy(out=o, in_=i),
                     reads=[bank_t], writes=[vtm.t], tag=tagp + f"e{oc}")
            elif ropecs is None:
                dst = dstb.ap[:, oc * 512:(oc + 1) * 512]
                P.do("scalar", lambda e, o=dst, i=bank: e.copy(out=o, in_=i),
                     reads=[bank_t], writes=[dstb.t], tag=tagp + f"e{oc}")
            else:
                cs, sn = ropecs
                pe = bank[:, 0:512:2]
                po = bank[:, 1:512:2]
                css = cs.ap[:, oc * 256:(oc + 1) * 256]
                sns = sn.ap[:, oc * 256:(oc + 1) * 256]
                de = dstb.ap[:, oc * 512:(oc + 1) * 512][:, 0::2]
                do_ = dstb.ap[:, oc * 512:(oc + 1) * 512][:, 1::2]
                P.do("vector", lambda e, pe=pe, c=css: e.tensor_mul(out=r1.ap, in0=pe, in1=c),
                     reads=[bank_t, cs.t], writes=[r1.t], tag=tagp + f"ra{oc}")
                P.do("vector", lambda e, po=po, sx=sns: e.tensor_mul(out=r2.ap, in0=po, in1=sx),
                     reads=[bank_t, sn.t], writes=[r2.t], tag=tagp + f"rb{oc}")
                P.do("vector", lambda e, de=de: e.tensor_sub(out=de, in0=r1.ap, in1=r2.ap),
                     reads=[r1.t, r2.t], writes=[dstb.t], tag=tagp + f"rc{oc}")
                P.do("vector", lambda e, po=po, c=css: e.tensor_mul(out=r1.ap, in0=po, in1=c),
                     reads=[bank_t, cs.t], writes=[r1.t], tag=tagp + f"rd{oc}")
                P.do("vector", lambda e, pe=pe, sx=sns: e.tensor_mul(out=r2.ap, in0=pe, in1=sx),
                     reads=[bank_t, sn.t], writes=[r2.t], tag=tagp + f"re{oc}")
                P.do("vector", lambda e, do_=do_: e.tensor_add(out=do_, in0=r1.ap, in1=r2.ap),
                     reads=[r1.t, r2.t], writes=[dstb.t], tag=tagp + f"rf{oc}")

    for tt in range(TPB):
        gt = b * TPB + tt
        s = sl[tt % 2]
        P.dma("sync", s["xr"].ap, xin_ap(gt), writes=[s["xr"].t],
              reads=[xin_tiles[gt]], tag=name + f"xl{tt}")
        ln_tile(P, cm, s, s["xr"].ap, s["xr"].t, name + f"ln{tt}")
        transpose_to(P, ps, cm, s["xn"].ap, s["xn"].t, s["xnT"].ap, s["xnT"].t,
                     1024, name + f"xT{tt}")
        if rope is not None:
            cs, sn = csl[tt % 2], snl[tt % 2]
            P.dma("sync", cs.ap, cosd[tt * 128:(tt + 1) * 128, :], writes=[cs.t],
                  tag=name + f"cl{tt}")
            P.dma("sync", sn.ap, sind[tt * 128:(tt + 1) * 128, :], writes=[sn.t],
                  tag=name + f"sl{tt}")
        proj("v", s, tt, None, None, name + f"v{tt}")
        rc = (cs, sn) if rope is not None else None
        proj("q", s, tt, s["qb"], rc, name + f"q{tt}")
        proj("k", s, tt, s["kb"], rc, name + f"k{tt}")
        for (bb_, fm) in ((s["qb"], qfm), (s["kb"], kfm)):
            dv = fm.ap.rearrange("p (c t) -> p c t", t=T)[:, :, tt * 128:(tt + 1) * 128]
            transpose_to(P, ps, cm, bb_.ap, bb_.t, None, fm.t, 1024,
                         name + ("q" if fm is qfm else "k") + f"F{tt}", dst_view=dv)

    if wspec["kind"] == "sep":
        for wbuf in (wq, wk, wv):
            AB.freeb(wbuf)
        for bbuf in (bq, bk, bv):
            AB.freeb(bbuf)
    else:
        AB.freeb(wqkv)
        AB.freeb(bqkv)
    if rope is not None:
        for bufl in (csl, snl):
            for bf in bufl:
                AF32.freeb(bf)
        AF32.freeb(r1)
        AF32.freeb(r2)
    AF32.freeb(sq_sh)
    for s in sl:
        for k in ("xn", "xnT", "qb", "kb"):
            AB.freeb(s[k])

    wp = load_weight_bf16(P, AB, AF32, wdr[wspec["wp"][0]], C, C, name + "_wp")
    bp = load_bias_row(P, AB, AF32, wdr[wspec["wp"][1]], C, name + "_bp")

    ybuf = AB.alloc(name + "_y", 8 * 512)
    mcol = AF32.alloc(name + "_mcol", NHEAD)
    mtmp = AF32.alloc(name + "_mtmp", 16)
    mbias = AF32.alloc(name + "_mbias", NHEAD)
    mneg = AB.alloc(name + "_mneg", 16)
    mrowb = AB.alloc(name + "_mrowb", 16)
    attf = AB.alloc(name + "_attf", 2 * TPB * 512)
    attv = attf.ap.rearrange("p (z k t) -> p z k t", k=TPB, t=512)
    bcf = AF32.alloc(name + "_bcf", 512)
    srow = AB.alloc(name + "_srow", 1024)
    outs = [AF32.alloc(name + f"_o{i}", 1024, dma_slot=True) for i in range(2)]
    if alibi:
        bandb = [AF32.alloc(name + f"_band{z}", 896, dma_slot=True) for z in range(2)]
        srsl = [AF32.alloc(name + f"_sr{z}", 128, dma_slot=True) for z in range(2)]
        ntsl = [AF32.alloc(name + f"_nt{z}", 512, dma_slot=True) for z in range(2)]
        onesf = AF32.alloc(name + "_onesf", 512)
        P.do("vector", lambda e: e.memset(onesf.ap[0:1, :], 1.0),
             writes=[onesf.t], tag=name + "onf")

    qv = qfm.ap.rearrange("p (c t) -> p c t", t=T)
    kv = kfm.ap.rearrange("p (c t) -> p c t", t=T)

    for h in range(NHEAD):
        ct, half = h // 2, (h % 2) * 64
        qsl = qv[half:half + 64, ct, :]
        ksl = kv[half:half + 64, ct, :]
        first = True
        for qs in range(TPB):
            nkc = ((qs // 4) + 1) if causal else 4
            for kc in range(nkc):
                bank, bank_t = ps.get()
                mm_group(P, bank, bank_t,
                         [(bank, qsl[:, qs * 128:(qs + 1) * 128],
                           ksl[:, kc * 512:(kc + 1) * 512], None)],
                         reads=[qfm.t, kfm.t], tag=name + f"x{h}_{qs}_{kc}")
                if first:
                    P.do("vector", lambda e, o=mcol.ap[:, h:h + 1], i=bank:
                         e.reduce_max(out=o, in_=i, axis=AX.X),
                         reads=[bank_t], writes=[mcol.t], tag=name + f"xm{h}_{qs}_{kc}")
                    first = False
                else:
                    P.do("vector", lambda e, o=mtmp.ap[:, 0:1], i=bank:
                         e.reduce_max(out=o, in_=i, axis=AX.X),
                         reads=[bank_t], writes=[mtmp.t], tag=name + f"xm{h}_{qs}_{kc}")
                    P.do("vector", lambda e, o=mcol.ap[:, h:h + 1], i=mtmp.ap[:, 0:1]:
                         e.tensor_max(out=o, in0=o, in1=i),
                         reads=[mtmp.t, mcol.t], writes=[mcol.t],
                         tag=name + f"xx{h}_{qs}_{kc}")
    P.do("vector", lambda e: e.tensor_copy(out=mneg.ap[:, 0:16], in_=mcol.ap[:, 0:16]),
         reads=[mcol.t], writes=[mneg.t], tag=name + "mn")
    bankm, bankm_t = ps.get()
    bbm = bankm.bitcast(BF16)
    P.do("tensor", lambda e: e.transpose(out=bbm[0:16, 0:128], in_=mneg.ap[:, 0:16],
                                         identity=cm.ident),
         reads=[mneg.t, cm.ident_t], writes=[bankm_t], tag=name + "mt")
    P.do("vector", lambda e: e.reduce_max(out=mtmp.ap[0:16, 0:1],
                                          in_=bbm[0:16, 0:128], axis=AX.X),
         reads=[bankm_t], writes=[mtmp.t], tag=name + "mr")
    P.do("vector", lambda e, sc=-scale: e.tensor_scalar(
         out=mrowb.ap[0:16, 0:1], in0=mtmp.ap[0:16, 0:1], scalar1=sc,
         scalar2=None, op0=ALU.mult),
         reads=[mtmp.t], writes=[mrowb.t], tag=name + "ms")
    bankn, bankn_t = ps.get()
    bbn = bankn.bitcast(BF16)
    P.do("tensor", lambda e: e.transpose(out=bbn[0:16, 0:16],
                                         in_=mrowb.ap[0:16, 0:16],
                                         identity=cm.ident[0:16, 0:16]),
         reads=[mrowb.t, cm.ident_t], writes=[bankn_t], tag=name + "mt2")
    P.do("scalar", lambda e: e.copy(out=mrowb.ap[0:1, 0:16], in_=bbn[0:1, 0:16]),
         reads=[bankn_t], writes=[mrowb.t], tag=name + "mv")
    bankb, bankb_t = ps.get()
    def fnmb(e, bankb=bankb):
        ins = None
        for h in range(NHEAD):
            ins = e.matmul(out=bankb[:, h:h + 1], lhsT=cm.ones_row[0:1, :],
                           rhs=mrowb.ap[0:1, h:h + 1], start=True, stop=True)
        return ins
    P.do("tensor", fnmb, reads=[mrowb.t, cm.ones_row_t], writes=[bankb_t],
         tag=name + "mb")
    P.do("scalar", lambda e: e.copy(out=mbias.ap[:, 0:16], in_=bankb[:, 0:16]),
         reads=[bankb_t], writes=[mbias.t], tag=name + "mbe")

    for qc in range(4):
        ktmax = (4 * qc + 4) if causal else TPB
        for hp in range(8):
            if alibi:
                for z in range(2):
                    h = 2 * hp + z
                    P.dma("sync", bandb[z].ap[:, 0:896], wdr["alibi_band"][h],
                          writes=[bandb[z].t], tag=name + f"bl{qc}{hp}{z}")
                    P.dma("sync", ntsl[z].ap[0:1, 0:512],
                          wdr["alibi_negtrow"][0:1, h * T + qc * 512: h * T + (qc + 1) * 512],
                          writes=[ntsl[z].t], tag=name + f"nt{qc}{hp}{z}")
            for z in range(2):
                h = 2 * hp + z
                ct, half = h // 2, (h % 2) * 64
                ksl = kv[half:half + 64, ct, :]
                qsl = qv[half:half + 64, ct, :]
                for kt in range(ktmax):
                    bank, bank_t = ps.get()
                    mms = [(bank, ksl[:, kt * 128:(kt + 1) * 128],
                            qsl[:, qc * 512:(qc + 1) * 512], None)]
                    rd = [qfm.t, kfm.t]
                    past = kt * 128 + 127 < qc * 512
                    diag = (not past) and kt * 128 < qc * 512 + 512
                    if alibi and past:
                        P.dma("sync", srsl[z].ap[0:1, 0:128],
                              wdr["alibi_srow"][0:1, h * T + kt * 128: h * T + (kt + 1) * 128],
                              writes=[srsl[z].t], tag=name + f"sr{qc}{hp}{kt}{z}")
                        mms.append((bank, srsl[z].ap[0:1, 0:128],
                                    onesf.ap[0:1, 0:512], None))
                        mms.append((bank, onesf.ap[0:1, 0:128],
                                    ntsl[z].ap[0:1, 0:512], None))
                        rd += [srsl[z].t, ntsl[z].t, onesf.t]
                    mm_group(P, bank, bank_t, mms, reads=rd,
                             tag=name + f"s{qc}{hp}{kt}{z}", skip_gc=True)
                    att = attv[:, z, kt, :]
                    mb = mbias.ap[:, h:h + 1]
                    if alibi and diag:
                        delta = kt * 128 - qc * 512
                        off = 384 - delta
                        P.do("vector", lambda e, o=bcf.ap, i=bank,
                             m=bandb[z].ap[:, off:off + 512]:
                             e.tensor_add(out=o, in0=i, in1=m),
                             reads=[bank_t, bandb[z].t], writes=[bcf.t],
                             tag=name + f"ba{qc}{hp}{kt}{z}")
                        P.do("scalar", lambda e, o=att, i=bcf.ap, sc=scale, mb=mb:
                             e.activation(out=o, in_=i, func=AF.Exp, scale=sc, bias=mb),
                             reads=[bcf.t, mbias.t], writes=[attf.t],
                             tag=name + f"p{qc}{hp}{kt}{z}")
                    else:
                        P.do("scalar", lambda e, o=att, i=bank, sc=scale, mb=mb:
                             e.activation(out=o, in_=i, func=AF.Exp, scale=sc, bias=mb),
                             reads=[bank_t, mbias.t], writes=[attf.t],
                             tag=name + f"p{qc}{hp}{kt}{z}")
                    if causal and kt >= 4 * qc:
                        delta = kt * 128 - qc * 512
                        off = 384 - delta
                        msk = cm.causmask[:, off:off + 512]
                        P.do("vector", lambda e, o=att, m=msk:
                             e.tensor_mul(out=o, in0=o, in1=m),
                             reads=[attf.t, cm.causmask_t], writes=[attf.t],
                             tag=name + f"k{qc}{hp}{kt}{z}")
            av_ap, av_t, av_i = ps.take()
            sm = [ps.take() for _ in range(2)]
            for kt in range(ktmax):
                st_, sp_ = (kt == 0), (kt == ktmax - 1)
                def fna(e, kt=kt, st_=st_, sp_=sp_, av_ap=av_ap, hp=hp):
                    ins = None
                    for z in range(2):
                        h = 2 * hp + z
                        ins = e.matmul(out=av_ap[64 * z:64 * z + 64, :],
                                       lhsT=vtm.ap[:, kt * 1024 + h * 64: kt * 1024 + h * 64 + 64],
                                       rhs=attv[:, z, kt, :], start=st_, stop=sp_,
                                       tile_position=(0, 64 * z), skip_group_check=True)
                    return ins
                P.do("tensor", fna, reads=[vtm.t, attf.t], writes=[av_t],
                     tag=name + f"a{qc}{hp}{kt}")
                for z in range(2):
                    P.do("tensor", lambda e, o=sm[z][0][0:1, :], r=attv[:, z, kt, :],
                         s1=st_, s2=sp_:
                         e.matmul(out=o, lhsT=cm.ones_col, rhs=r, start=s1, stop=s2,
                                  skip_group_check=True),
                         reads=[attf.t, cm.ones_col_t], writes=[sm[z][1]],
                         tag=name + f"u{qc}{hp}{kt}{z}")
            for z in range(2):
                P.do("scalar", lambda e, o=srow.ap[0:1, z * 512:(z + 1) * 512],
                     i=sm[z][0][0:1, :]: e.copy(out=o, in_=i),
                     reads=[sm[z][1]], writes=[srow.t], tag=name + f"w{qc}{hp}{z}")
            bcb, bcb_t = ps.get()
            def fnb(e, bcb=bcb):
                e.matmul(out=bcb[0:64, :], lhsT=cm.ones_row[0:1, 0:64],
                         rhs=srow.ap[0:1, 0:512], start=True, stop=True,
                         tile_position=(0, 0))
                return e.matmul(out=bcb[64:128, :], lhsT=cm.ones_row[0:1, 0:64],
                                rhs=srow.ap[0:1, 512:1024], start=True, stop=True,
                                tile_position=(0, 64))
            P.do("tensor", fnb, reads=[srow.t, cm.ones_row_t], writes=[bcb_t],
                 tag=name + f"b{qc}{hp}")
            P.do("scalar", lambda e, o=bcf.ap, i=bcb: e.copy(out=o, in_=i),
                 reads=[bcb_t], writes=[bcf.t], tag=name + f"c{qc}{hp}")
            P.do("vector", lambda e, o=bcf.ap, i=bcf.ap: e.reciprocal(out=o, in_=i),
                 reads=[bcf.t], writes=[bcf.t], tag=name + f"v{qc}{hp}")
            ydst = ybuf.ap[:, hp * 512:(hp + 1) * 512]
            P.do("vector", lambda e, o=ydst, a=av_ap, d=bcf.ap:
                 e.tensor_mul(out=o, in0=a, in1=d),
                 reads=[av_t, bcf.t], writes=[ybuf.t], tag=name + f"y{qc}{hp}")
            ps.release(av_i)
            for _, smt, smi in sm:
                ps.release(smi)

        for qs in range(4):
            gt = b * TPB + qc * 4 + qs
            s = sl[qs % 2]
            P.dma("sync", s["xr"].ap, xin_ap(gt), writes=[s["xr"].t],
                  reads=[xin_tiles[gt]], tag=name + f"zl{qc}{qs}")
            ob = outs[qs % 2]
            for oc in range(2):
                bank, bank_t = ps.get()
                mms = [(bank, cm.ones_row[0:1, :], bp.ap[0:1, oc * 512:(oc + 1) * 512], None)]
                for hp in range(8):
                    mms.append((bank, ybuf.ap[:, hp * 512 + qs * 128: hp * 512 + qs * 128 + 128],
                                wp.ap[:, hp * C + oc * 512: hp * C + (oc + 1) * 512], None))
                mm_group(P, bank, bank_t, mms,
                         reads=[ybuf.t, wp.t, bp.t, cm.ones_row_t],
                         tag=name + f"j{qc}{qs}{oc}")
                dst = ob.ap[:, oc * 512:(oc + 1) * 512]
                P.do("vector", lambda e, o=dst, i=bank,
                     x=s["xr"].ap[:, oc * 512:(oc + 1) * 512]:
                     e.tensor_add(out=o, in0=i, in1=x),
                     reads=[bank_t, s["xr"].t], writes=[ob.t], tag=name + f"r{qc}{qs}{oc}")
            P.dma("scalar", xout_ap(gt), ob.ap, writes=[xout_tiles[gt]],
                  reads=[ob.t], tag=name + f"zs{qc}{qs}")

    for bb2 in (qfm, kfm, vtm, ybuf, srow, wp, bp, attf, mneg, mrowb):
        AB.freeb(bb2)
    for s in sl:
        AF32.freeb(s["xr"])
        AF32.freeb(s["st"])
    for bb2 in (mcol, mtmp, bcf, mbias):
        AF32.freeb(bb2)
    for o in outs:
        AF32.freeb(o)
    if alibi:
        for z in range(2):
            AF32.freeb(bandb[z])
            AF32.freeb(srsl[z])
            AF32.freeb(ntsl[z])
        AF32.freeb(onesf)




def emit_l0_ffn(P, AB, AF32, AU8, ps, cm, nc, io, wdr):
    name = "l0f"
    xin_ap, xin_tiles = io["xin"]
    xout_ap, xout_tiles = io["xout"]

    scol = AF32.alloc(name + "_sc", 8, dma_slot=True)
    sc2 = AF32.alloc(name + "_sc2", 8)
    for i, nsc in enumerate(("s0", "s1", "s2")):
        P.dma("sync", scol.ap[:, i:i + 1], wdr["l0f_" + nsc], writes=[scol.t],
              tag=name + f"scl{i}")
    P.do("vector", lambda e: e.tensor_scalar(out=sc2.ap[:, 0:3], in0=scol.ap[:, 0:3],
                                             scalar1=2.0, scalar2=None, op0=ALU.mult),
         reads=[scol.t], writes=[sc2.t], tag=name + "sc2")
    s0c = sc2.ap[:, 0:1]
    s1c = sc2.ap[:, 1:2]
    s2c = sc2.ap[:, 2:3]

    sq_sh = AF32.alloc(name + "_sq", 1024)
    sl = []
    for i in range(2):
        sl.append({
            "xr": AF32.alloc(name + f"_xr{i}", 1024, dma_slot=True),
            "st": AF32.alloc(name + f"_st{i}", 32),
            "sq": sq_sh,
            "xn": AB.alloc(name + f"_xn{i}", 1024),
        })
    outs = [AF32.alloc(name + f"_o{i}", 1024, dma_slot=True) for i in range(2)]

    TC = 1024
    NCH = NT // TC

    for ch in range(NCH):
        h0 = AB.alloc(name + f"_h0_{ch}", 32 * TC)
        xnT = AB.alloc(name + f"_xnT{ch}", 8 * TC)
        for lt in range(TC // 128):
            gt = ch * (TC // 128) + lt
            s = sl[lt % 2]
            P.dma("sync", s["xr"].ap, xin_ap(gt), writes=[s["xr"].t],
                  reads=[xin_tiles[gt]], tag=name + f"xl{gt}")
            ln_tile(P, cm, s, s["xr"].ap, s["xr"].t, name + f"ln{gt}")
            dv = xnT.ap.rearrange("p (c t) -> p c t", t=TC)[:, :, lt * 128:(lt + 1) * 128]
            transpose_to(P, ps, cm, s["xn"].ap, s["xn"].t, None, xnT.t, 1024,
                         name + f"xT{gt}", dst_view=dv)
        xnTv = xnT.ap.rearrange("p (c t) -> p c t", t=TC)

        dstg = AU8.alloc(name + "_dstg", 2 * 1024, dma_slot=True)
        dbf = AB.alloc(name + "_dbf", 8 * 128)
        h0v = h0.ap.rearrange("p (o t) -> p o t", t=TC)
        for o1 in range(32):
            P.dma("sync", dstg.ap[:, 0:1024].rearrange("p (c o) -> p c o", o=128),
                  wdr["l0f_k0t"].rearrange("(c p) o -> p c o", p=128)[:, :, o1 * 128:(o1 + 1) * 128],
                  writes=[dstg.t], tag=name + f"k0t{ch}_{o1}")
            P.dma("sync", dstg.ap[:, 1024:2048].rearrange("p (c o) -> p c o", o=128),
                  wdr["l0f_k0b"].rearrange("(c p) o -> p c o", p=128)[:, :, o1 * 128:(o1 + 1) * 128],
                  writes=[dstg.t], tag=name + f"k0b{ch}_{o1}")
            P.do("vector", lambda e, o=dbf.ap, a=dstg.ap[:, 0:1024], b=dstg.ap[:, 1024:2048]:
                 e.tensor_sub(out=o, in0=a, in1=b),
                 reads=[dstg.t], writes=[dbf.t], tag=name + f"dcv{ch}_{o1}")
            for ts in range(TC // 512):
                bank, bank_t = ps.get()
                mms = []
                for ct in range(8):
                    mms.append((bank, dbf.ap[:, ct * 128:(ct + 1) * 128],
                                xnTv[:, ct, ts * 512:(ts + 1) * 512], None))
                mm_group(P, bank, bank_t, mms, reads=[dbf.t, xnT.t],
                         tag=name + f"m1_{ch}_{o1}_{ts}")
                dst = h0v[:, o1, ts * 512:(ts + 1) * 512]
                P.do("vector", lambda e, o=dst, i=bank:
                     e.tensor_scalar(out=o, in0=i, scalar1=s0c, scalar2=0.0,
                                     op0=ALU.mult, op1=ALU.max),
                     reads=[bank_t, sc2.t], writes=[h0.t], tag=name + f"e1_{ch}_{o1}_{ts}")
        AU8.freeb(dstg)
        AB.freeb(dbf)
        AB.freeb(xnT)
        h1 = AB.alloc(name + f"_h1_{ch}", 32 * TC)

        kstg = AU8.alloc(name + "_kstg", 4 * 512, dma_slot=True)
        kbf = AB.alloc(name + "_kbf", 4 * 512)
        h1v = h1.ap.rearrange("p (o t) -> p o t", t=TC)
        k1r = wdr["l0f_k1"].rearrange("(c p) o -> p c o", p=128)
        for o2 in range(8):
            bank2 = [ps.get() for _ in range(8)]
            for cg in range(8):
                P.dma("sync", kstg.ap.rearrange("p (c o) -> p c o", o=512),
                      k1r[:, cg * 4:(cg + 1) * 4, o2 * 512:(o2 + 1) * 512],
                      writes=[kstg.t], tag=name + f"k1{ch}_{o2}_{cg}")
                P.do("vector", lambda e, o=kbf.ap, i=kstg.ap:
                     e.tensor_scalar(out=o, in0=i, scalar1=2.0, scalar2=1.0,
                                     op0=ALU.mult, op1=ALU.subtract),
                     reads=[kstg.t], writes=[kbf.t], tag=name + f"kc{ch}_{o2}_{cg}")
                for ci in range(4):
                    c = cg * 4 + ci
                    for ts in range(TC // 512):
                        for ot in range(4):
                            bank, bank_t = bank2[ts * 4 + ot]
                            P.do("tensor", lambda e, o=bank,
                                 l=kbf.ap[:, ci * 512 + ot * 128: ci * 512 + ot * 128 + 128],
                                 r=h0v[:, c, ts * 512:(ts + 1) * 512],
                                 s1_=(c == 0), s2_=(c == 31):
                                 e.matmul(out=o, lhsT=l, rhs=r, start=s1_, stop=s2_,
                                          skip_group_check=True),
                                 reads=[kbf.t, h0.t], writes=[bank_t],
                                 tag=name + f"m2_{ch}_{o2}_{ts}_{c}_{ot}")
            for ts in range(TC // 512):
                for ot in range(4):
                    bank, bank_t = bank2[ts * 4 + ot]
                    dst = h1v[:, o2 * 4 + ot, ts * 512:(ts + 1) * 512]
                    P.do("vector", lambda e, o=dst, i=bank:
                         e.tensor_scalar(out=o, in0=i, scalar1=s1c, scalar2=0.0,
                                         op0=ALU.mult, op1=ALU.max),
                         reads=[bank_t, sc2.t], writes=[h1.t],
                         tag=name + f"e2_{ch}_{o2}_{ts}_{ot}")
        AU8.freeb(kstg)
        AB.freeb(kbf)
        AB.freeb(h0)

        k2stg = AU8.alloc(name + "_k2stg", 8 * 1024, dma_slot=True)
        k2bf = AB.alloc(name + "_k2bf", 32 * 1024)
        k2r = wdr["l0f_k2"].rearrange("(c p) o -> p c o", p=128)
        for qtr in range(4):
            P.dma("sync", k2stg.ap.rearrange("p (c o) -> p c o", o=1024),
                  k2r[:, qtr * 8:(qtr + 1) * 8, :],
                  writes=[k2stg.t], tag=name + f"k2l{ch}_{qtr}")
            P.do("vector", lambda e, o=k2bf.ap[:, qtr * 8192:(qtr + 1) * 8192],
                 i=k2stg.ap:
                 e.tensor_scalar(out=o, in0=i, scalar1=2.0, scalar2=1.0,
                                 op0=ALU.mult, op1=ALU.subtract),
                 reads=[k2stg.t], writes=[k2bf.t], tag=name + f"k2c{ch}_{qtr}")
        k2v = k2bf.ap.rearrange("p (c o) -> p c o", o=1024)
        for lt in range(TC // 128):
            gt = ch * (TC // 128) + lt
            s = sl[lt % 2]
            P.dma("sync", s["xr"].ap, xin_ap(gt), writes=[s["xr"].t],
                  reads=[xin_tiles[gt]], tag=name + f"x3{gt}")
            ob = outs[lt % 2]
            for oc in range(2):
                bank, bank_t = ps.get()
                mms = []
                for c in range(32):
                    mms.append((bank, h1v[:, c, lt * 128:(lt + 1) * 128],
                                k2v[:, c, oc * 512:(oc + 1) * 512], None))
                mm_group(P, bank, bank_t, mms, reads=[h1.t, k2bf.t],
                         tag=name + f"m3_{gt}_{oc}")
                dst = ob.ap[:, oc * 512:(oc + 1) * 512]
                P.do("vector", lambda e, o=dst, i=bank,
                     x=s["xr"].ap[:, oc * 512:(oc + 1) * 512]:
                     e.scalar_tensor_tensor(out=o, in0=i, scalar=s2c, in1=x,
                                            op0=ALU.mult, op1=ALU.add),
                     reads=[bank_t, sc2.t, s["xr"].t], writes=[ob.t],
                     tag=name + f"e3_{gt}_{oc}")
            P.dma("scalar", xout_ap(gt), ob.ap, writes=[xout_tiles[gt]],
                  reads=[ob.t], tag=name + f"zs{gt}")
        AU8.freeb(k2stg)
        AB.freeb(k2bf)
        AB.freeb(h1)

    AF32.freeb(sq_sh)
    for s in sl:
        AF32.freeb(s["xr"])
        AF32.freeb(s["st"])
        AB.freeb(s["xn"])
    for o in outs:
        AF32.freeb(o)
    AF32.freeb(scol)
    AF32.freeb(sc2)




def emit_l1_attn(P, AB, AF32, AU8, ps, cm, nc, io, wdr):
    for b in range(BPC):
        _l1_attn_batch(P, AB, AF32, ps, cm, nc, io, b, wdr, name=f"l1a{b}")


def _l1_attn_batch(P, AB, AF32, ps, cm, nc, io, b, wdr, name):
    xin_ap, xin_tiles = io["xin"]
    xout_ap, xout_tiles = io["xout"]

    qfm = AB.alloc(name + "_qfm", 8 * T)
    kfm = AB.alloc(name + "_kfm", 8 * T)
    vtm = AB.alloc(name + "_vtm", TPB * 1024)

    wq = load_weight_bf16(P, AB, AF32, wdr["l1a_wqT"], C, C, name + "_wq")
    wk = load_weight_bf16(P, AB, AF32, wdr["l1a_wkT"], C, C, name + "_wk")
    wv = load_weight_bf16(P, AB, AF32, wdr["l1a_wvT"], C, C, name + "_wv")
    bq = load_bias_row(P, AB, AF32, wdr["l1a_bq"], C, name + "_bq")
    bk = load_bias_row(P, AB, AF32, wdr["l1a_bk"], C, name + "_bk")
    bv = load_bias_row(P, AB, AF32, wdr["l1a_bv"], C, name + "_bv")
    wmap = {"q": (wq, bq), "k": (wk, bk), "v": (wv, bv)}

    sq_sh = AF32.alloc(name + "_sq", 1024)
    sl = []
    for i in range(2):
        sl.append({
            "xr": AF32.alloc(name + f"_xr{i}", 1024, dma_slot=True),
            "st": AF32.alloc(name + f"_st{i}", 32),
            "sq": sq_sh,
            "xn": AB.alloc(name + f"_xn{i}", 1024),
            "xnT": AB.alloc(name + f"_xnT{i}", 1024),
            "qb": AB.alloc(name + f"_qb{i}", 1024),
            "kb": AB.alloc(name + f"_kb{i}", 1024),
        })
    r1 = AF32.alloc(name + "_r1", 256)
    r2 = AF32.alloc(name + "_r2", 256)
    csl = [AF32.alloc(name + f"_cs{i}", 512, dma_slot=True) for i in range(2)]
    snl = [AF32.alloc(name + f"_sn{i}", 512, dma_slot=True) for i in range(2)]
    cosd, sind = wdr["rope1_cos"], wdr["rope1_sin"]

    def proj(which, s, tt, dstb, ropecs, tagp):
        wb, bb = wmap[which]
        for oc in range(2):
            bank, bank_t = ps.get()
            mms = [(bank, cm.ones_row[0:1, :], bb.ap[0:1, oc * 512:(oc + 1) * 512], None)]
            for ct in range(8):
                mms.append((bank, s["xnT"].ap[:, ct * 128:(ct + 1) * 128],
                            wb.ap[:, ct * C + oc * 512: ct * C + (oc + 1) * 512], None))
            mm_group(P, bank, bank_t, mms,
                     reads=[s["xnT"].t, wb.t, bb.t, cm.ones_row_t], tag=tagp + f"m{oc}")
            if dstb is None:
                dst = vtm.ap[:, tt * 1024 + oc * 512: tt * 1024 + (oc + 1) * 512]
                P.do("scalar", lambda e, o=dst, i=bank: e.copy(out=o, in_=i),
                     reads=[bank_t], writes=[vtm.t], tag=tagp + f"e{oc}")
            else:
                cs, sn = ropecs
                pe = bank[:, 0:512:2]
                po = bank[:, 1:512:2]
                css = cs.ap[:, oc * 256:(oc + 1) * 256]
                sns = sn.ap[:, oc * 256:(oc + 1) * 256]
                de = dstb.ap[:, oc * 512:(oc + 1) * 512][:, 0::2]
                do_ = dstb.ap[:, oc * 512:(oc + 1) * 512][:, 1::2]
                P.do("vector", lambda e, pe=pe, c=css: e.tensor_mul(out=r1.ap, in0=pe, in1=c),
                     reads=[bank_t, cs.t], writes=[r1.t], tag=tagp + f"ra{oc}")
                P.do("vector", lambda e, po=po, sx=sns: e.tensor_mul(out=r2.ap, in0=po, in1=sx),
                     reads=[bank_t, sn.t], writes=[r2.t], tag=tagp + f"rb{oc}")
                P.do("vector", lambda e, de=de: e.tensor_sub(out=de, in0=r1.ap, in1=r2.ap),
                     reads=[r1.t, r2.t], writes=[dstb.t], tag=tagp + f"rc{oc}")
                P.do("vector", lambda e, po=po, c=css: e.tensor_mul(out=r1.ap, in0=po, in1=c),
                     reads=[bank_t, cs.t], writes=[r1.t], tag=tagp + f"rd{oc}")
                P.do("vector", lambda e, pe=pe, sx=sns: e.tensor_mul(out=r2.ap, in0=pe, in1=sx),
                     reads=[bank_t, sn.t], writes=[r2.t], tag=tagp + f"re{oc}")
                P.do("vector", lambda e, do_=do_: e.tensor_add(out=do_, in0=r1.ap, in1=r2.ap),
                     reads=[r1.t, r2.t], writes=[dstb.t], tag=tagp + f"rf{oc}")

    for tt in range(TPB):
        gt = b * TPB + tt
        s = sl[tt % 2]
        P.dma("sync", s["xr"].ap, xin_ap(gt), writes=[s["xr"].t],
              reads=[xin_tiles[gt]], tag=name + f"xl{tt}")
        ln_tile(P, cm, s, s["xr"].ap, s["xr"].t, name + f"ln{tt}")
        transpose_to(P, ps, cm, s["xn"].ap, s["xn"].t, s["xnT"].ap, s["xnT"].t,
                     1024, name + f"xT{tt}")
        cs, sn = csl[tt % 2], snl[tt % 2]
        P.dma("sync", cs.ap, cosd[tt * 128:(tt + 1) * 128, :], writes=[cs.t],
              tag=name + f"cl{tt}")
        P.dma("sync", sn.ap, sind[tt * 128:(tt + 1) * 128, :], writes=[sn.t],
              tag=name + f"sl{tt}")
        proj("v", s, tt, None, None, name + f"v{tt}")
        proj("q", s, tt, s["qb"], (cs, sn), name + f"q{tt}")
        proj("k", s, tt, s["kb"], (cs, sn), name + f"k{tt}")
        for (bb_, fm) in ((s["qb"], qfm), (s["kb"], kfm)):
            dv = fm.ap.rearrange("p (c t) -> p c t", t=T)[:, :, tt * 128:(tt + 1) * 128]
            transpose_to(P, ps, cm, bb_.ap, bb_.t, None, fm.t, 1024,
                         name + ("q" if fm is qfm else "k") + f"F{tt}", dst_view=dv)

    for wbuf in (wq, wk, wv, bq, bk, bv):
        AB.freeb(wbuf)
    for bufl in (csl, snl):
        for bf in bufl:
            AF32.freeb(bf)

    attb = AB.alloc(name + "_attb", TPB * 512)
    mcol = AF32.alloc(name + "_mcol", 4)
    negm = AB.alloc(name + "_negm", 4)
    mtmp = AF32.alloc(name + "_mtmp", 4)
    mrow = AB.alloc(name + "_mrow", 512)
    srow = AB.alloc(name + "_srow", 512)
    rcol = AF32.alloc(name + "_rcol", 4)
    outs = [AF32.alloc(name + f"_o{i}", 1024, dma_slot=True) for i in range(2)]

    qv = qfm.ap.rearrange("p (c t) -> p c t", t=T)
    kv = kfm.ap.rearrange("p (c t) -> p c t", t=T)
    attv = attb.ap.rearrange("p (k t) -> p k t", t=512)

    for qc in range(4):
        for qs in range(4):
            qt0 = qc * 512 + qs * 128
            mc = mcol.ap[:, qs:qs + 1]
            for kc in range(4):
                bank, bank_t = ps.get()
                mms = [(bank, qv[:, ct, qt0:qt0 + 128],
                        kv[:, ct, kc * 512:(kc + 1) * 512], None) for ct in range(8)]
                mm_group(P, bank, bank_t, mms, reads=[qfm.t, kfm.t],
                         tag=name + f"x{qc}{qs}{kc}")
                if kc == 0:
                    P.do("vector", lambda e, o=mc, i=bank:
                         e.reduce_max(out=o, in_=i, axis=AX.X),
                         reads=[bank_t], writes=[mcol.t], tag=name + f"xm{qc}{qs}{kc}")
                else:
                    P.do("vector", lambda e, o=mtmp.ap[:, 0:1], i=bank:
                         e.reduce_max(out=o, in_=i, axis=AX.X),
                         reads=[bank_t], writes=[mtmp.t], tag=name + f"xm{qc}{qs}{kc}")
                    P.do("vector", lambda e, o=mc, i=mtmp.ap[:, 0:1]:
                         e.tensor_max(out=o, in0=o, in1=i),
                         reads=[mtmp.t, mcol.t], writes=[mcol.t], tag=name + f"xx{qc}{qs}{kc}")
        P.do("vector", lambda e: e.tensor_scalar(out=negm.ap, in0=mcol.ap,
                                                 scalar1=-1.0, scalar2=None, op0=ALU.mult),
             reads=[mcol.t], writes=[negm.t], tag=name + f"ng{qc}")
        bank, bank_t = ps.get()
        def fn(e, bank=bank):
            ins = None
            for qs in range(4):
                ins = e.matmul(out=bank[0:1, qs * 128:(qs + 1) * 128],
                               lhsT=negm.ap[:, qs:qs + 1], rhs=cm.ident,
                               start=True, stop=True)
            return ins
        P.do("tensor", fn, reads=[negm.t, cm.ident_t], writes=[bank_t],
             tag=name + f"mt{qc}")
        P.do("scalar", lambda e, o=mrow.ap[0:1, :], i=bank[0:1, 0:512]:
             e.copy(out=o, in_=i),
             reads=[bank_t], writes=[mrow.t], tag=name + f"me{qc}")

        smb, smb_t, smb_i = ps.take()
        for kt in range(TPB):
            bank, bank_t = ps.get()
            mms = [(bank, kv[:, ct, kt * 128:(kt + 1) * 128],
                    qv[:, ct, qc * 512:(qc + 1) * 512], None) for ct in range(8)]
            mms.append((bank, cm.ones_row[0:1, :], mrow.ap[0:1, :], None))
            mm_group(P, bank, bank_t, mms,
                     reads=[qfm.t, kfm.t, mrow.t, cm.ones_row_t],
                     tag=name + f"s{qc}{kt}")
            att = attv[:, kt, :]
            P.do("scalar", lambda e, o=att, i=bank:
                 e.activation(out=o, in_=i, func=AF.Exp, scale=0.03125),
                 reads=[bank_t], writes=[attb.t], tag=name + f"p{qc}{kt}")
            P.do("tensor", lambda e, o=smb[0:1, :], r=att, s1_=(kt == 0),
                 s2_=(kt == TPB - 1):
                 e.matmul(out=o, lhsT=cm.ones_col, rhs=r, start=s1_, stop=s2_,
                          skip_group_check=True),
                 reads=[attb.t, cm.ones_col_t], writes=[smb_t],
                 tag=name + f"u{qc}{kt}")
        P.do("scalar", lambda e, o=srow.ap[0:1, :], i=smb[0:1, :]: e.copy(out=o, in_=i),
             reads=[smb_t], writes=[srow.t], tag=name + f"w{qc}")
        ps.release(smb_i)
        bank, bank_t = ps.get()
        def fns(e, bank=bank):
            ins = None
            for qs in range(4):
                ins = e.matmul(out=bank[:, qs:qs + 1],
                               lhsT=srow.ap[0:1, qs * 128:(qs + 1) * 128],
                               rhs=cm.ones_row[0:1, 0:1], start=True, stop=True)
            return ins
        P.do("tensor", fns, reads=[srow.t, cm.ones_row_t], writes=[bank_t],
             tag=name + f"rt{qc}")
        P.do("vector", lambda e, o=rcol.ap[:, 0:4], i=bank[:, 0:4]: e.reciprocal(out=o, in_=i),
             reads=[bank_t], writes=[rcol.t], tag=name + f"rc{qc}")

        for qs in range(4):
            gt = b * TPB + qc * 4 + qs
            s = sl[qs % 2]
            P.dma("sync", s["xr"].ap, xin_ap(gt), writes=[s["xr"].t],
                  reads=[xin_tiles[gt]], tag=name + f"zl{qc}{qs}")
            ob = outs[qs % 2]
            for oc in range(2):
                bank, bank_t = ps.get()
                mms = [(bank, attv[:, kt, qs * 128:(qs + 1) * 128],
                        vtm.ap[:, kt * 1024 + oc * 512: kt * 1024 + (oc + 1) * 512],
                        None) for kt in range(TPB)]
                mm_group(P, bank, bank_t, mms, reads=[attb.t, vtm.t],
                         tag=name + f"a{qc}{qs}{oc}")
                dst = ob.ap[:, oc * 512:(oc + 1) * 512]
                P.do("vector", lambda e, o=dst, i=bank, rc=rcol.ap[:, qs:qs + 1],
                     x=s["xr"].ap[:, oc * 512:(oc + 1) * 512]:
                     e.scalar_tensor_tensor(out=o, in0=i, scalar=rc, in1=x,
                                            op0=ALU.mult, op1=ALU.add),
                     reads=[bank_t, rcol.t, s["xr"].t], writes=[ob.t],
                     tag=name + f"r{qc}{qs}{oc}")
            P.dma("scalar", xout_ap(gt), ob.ap, writes=[xout_tiles[gt]],
                  reads=[ob.t], tag=name + f"zs{qc}{qs}")

    for bb2 in (qfm, kfm, vtm, attb, negm, mrow, srow):
        AB.freeb(bb2)
    for s in sl:
        AF32.freeb(s["xr"])
        AF32.freeb(s["st"])
        for k in ("xn", "xnT", "qb", "kb"):
            AB.freeb(s[k])
    AF32.freeb(sq_sh)
    for bb2 in (mcol, mtmp, rcol, r1, r2):
        AF32.freeb(bb2)
    for o in outs:
        AF32.freeb(o)




def emit_l1_ffn(P, AB, AF32, AU8, ps, cm, nc, io, wdr):
    name = "l1f"
    xin_ap, xin_tiles = io["xin"]
    xout_ap, xout_tiles = io["xout"]

    wg = AB.alloc(name + "_wg", 8 * 8)
    wgs = AF32.alloc(name + "_wgs", 64, dma_slot=True)
    P.dma("sync", wgs.ap[:, 0:64].rearrange("p (c o) -> p c o", o=8),
          wdr["l1f_Wg"].rearrange("(c p) o -> p c o", p=128), writes=[wgs.t],
          tag=name + "wgl")
    P.do("vector", lambda e: e.tensor_copy(out=wg.ap, in_=wgs.ap[:, 0:64]),
         reads=[wgs.t], writes=[wg.t], tag=name + "wgc")
    bg = AB.alloc(name + "_bg", 32)
    bgs = AF32.alloc(name + "_bgs", 32, dma_slot=True)
    P.dma("sync", bgs.ap[0:1, 0:8], wdr["l1f_bg"], writes=[bgs.t], tag=name + "bgl")
    P.do("vector", lambda e: e.tensor_copy(out=bg.ap[0:1, 0:8], in_=bgs.ap[0:1, 0:8]),
         reads=[bgs.t], writes=[bg.t], tag=name + "bgc")
    b1c = AF32.alloc(name + "_b1c", 64, dma_slot=True)
    P.dma("sync", b1c.ap[:, 0:64], wdr["l1f_b1"], writes=[b1c.t],
          tag=name + "b1l")
    b2s = AF32.alloc(name + "_b2s", 1024, dma_slot=True)
    b2b = AB.alloc(name + "_b2b", 1024)
    P.dma("sync", b2s.ap[0:8, :], wdr["l1f_b2"], writes=[b2s.t], tag=name + "b2l")
    P.do("vector", lambda e: e.tensor_copy(out=b2b.ap[0:8, :], in_=b2s.ap[0:8, :]),
         reads=[b2s.t], writes=[b2b.t], tag=name + "b2c")
    AF32.freeb(wgs)
    AF32.freeb(bgs)
    AF32.freeb(b2s)

    sq_sh = AF32.alloc(name + "_sq", 1024)
    sl = []
    for i in range(2):
        sl.append({
            "xr": AF32.alloc(name + f"_xr{i}", 1024, dma_slot=True),
            "st": AF32.alloc(name + f"_st{i}", 32),
            "sq": sq_sh,
            "xn": AB.alloc(name + f"_xn{i}", 1024),
        })
    outs = [AF32.alloc(name + f"_o{i}", 1024, dma_slot=True) for i in range(2)]
    gsc = AF32.alloc(name + "_gsc", 64)
    TC = 2048
    NCH = NT // TC

    for ch in range(NCH):
        ntt = TC // 128
        xnT = AB.alloc(name + f"_xnT{ch}", 8 * TC)
        xnTv = xnT.ap.rearrange("p (c t) -> p c t", t=TC)
        oacc = AB.alloc(name + f"_oacc{ch}", ntt * 1024)
        oaccv = oacc.ap.rearrange("p (t o) -> p t o", o=1024)
        gcol = AF32.alloc(name + f"_gcol{ch}", ntt * 8)
        gbT = AB.alloc(name + f"_gbT{ch}", ntt * 128)

        for lt in range(ntt):
            gt = ch * ntt + lt
            s = sl[lt % 2]
            P.dma("sync", s["xr"].ap, xin_ap(gt), writes=[s["xr"].t],
                  reads=[xin_tiles[gt]], tag=name + f"xl{gt}")
            ln_tile(P, cm, s, s["xr"].ap, s["xr"].t, name + f"ln{gt}")
            dv = xnTv[:, :, lt * 128:(lt + 1) * 128]
            transpose_to(P, ps, cm, s["xn"].ap, s["xn"].t, None, xnT.t, 1024,
                         name + f"xT{gt}", dst_view=dv)
            bank, bank_t = ps.get()
            mms = [(bank[:, 0:8], cm.ones_row[0:1, :], bg.ap[0:1, 0:8], None)]
            for ct in range(8):
                mms.append((bank[:, 0:8], xnTv[:, ct, lt * 128:(lt + 1) * 128],
                            wg.ap[:, ct * 8:(ct + 1) * 8], None))
            mm_group(P, bank, bank_t, mms, reads=[xnT.t, wg.t, bg.t, cm.ones_row_t],
                     tag=name + f"gm{gt}")
            lg = gsc.ap[:, 0:8]
            ml = gsc.ap[:, 8:9]
            zs = gsc.ap[:, 16:24]
            ssum = gsc.ap[:, 9:10]
            wcol = gsc.ap[:, 10:11]
            eq = gsc.ap[:, 24:32]
            imax = gsc.ap[:, 11:12]
            P.do("vector", lambda e, o=lg, i=bank[:, 0:8]: e.tensor_copy(out=o, in_=i),
                 reads=[bank_t], writes=[gsc.t], tag=name + f"gc{gt}")
            P.do("vector", lambda e, o=ml, i=lg: e.reduce_max(out=o, in_=i, axis=AX.X),
                 reads=[gsc.t], writes=[gsc.t], tag=name + f"gx{gt}")
            P.do("vector", lambda e, o=ml, i=ml: e.tensor_scalar(
                 out=o, in0=i, scalar1=-1.0, scalar2=None, op0=ALU.mult),
                 reads=[gsc.t], writes=[gsc.t], tag=name + f"gn{gt}")
            P.do("scalar", lambda e, o=zs, i=lg, b_=ml: e.activation(
                 out=o, in_=i, func=AF.Exp, bias=b_),
                 reads=[gsc.t], writes=[gsc.t], tag=name + f"ge{gt}")
            P.do("vector", lambda e, o=ssum, i=zs: e.reduce_sum(out=o, in_=i, axis=AX.X),
                 reads=[gsc.t], writes=[gsc.t], tag=name + f"gs{gt}")
            P.do("vector", lambda e, o=wcol, i=ssum: e.reciprocal(out=o, in_=i),
                 reads=[gsc.t], writes=[gsc.t], tag=name + f"gr{gt}")
            P.do("vector", lambda e, o=eq, i=lg, m_=ml: e.tensor_scalar(
                 out=o, in0=i, scalar1=m_, scalar2=-1.0, op0=ALU.add, op1=ALU.mult),
                 reads=[gsc.t], writes=[gsc.t], tag=name + f"gq{gt}")
            P.do("vector", lambda e, o=eq, i=eq: e.tensor_scalar(
                 out=o, in0=i, scalar1=0.0, scalar2=None, op0=ALU.is_equal),
                 reads=[gsc.t], writes=[gsc.t], tag=name + f"gqq{gt}")
            P.do("vector", lambda e, o=eq, i=eq: e.tensor_mul(
                 out=o, in0=i, in1=cm.iota8r[:, 0:8]),
                 reads=[gsc.t, cm.iota8r_t], writes=[gsc.t], tag=name + f"gi{gt}")
            P.do("vector", lambda e, o=imax, i=eq: e.reduce_max(out=o, in_=i, axis=AX.X),
                 reads=[gsc.t], writes=[gsc.t], tag=name + f"gj{gt}")
            gslice = gcol.ap[:, lt * 8:(lt + 1) * 8]
            P.do("vector", lambda e, o=gslice, i=cm.iota8r[:, 0:8], m_=imax:
                 e.tensor_scalar(out=o, in0=i, scalar1=m_, scalar2=None, op0=ALU.is_equal),
                 reads=[gsc.t, cm.iota8r_t], writes=[gcol.t], tag=name + f"gf{gt}")
            P.do("vector", lambda e, o=gslice, i=gslice, m_=wcol:
                 e.tensor_scalar(out=o, in0=i, scalar1=m_, scalar2=None, op0=ALU.mult),
                 reads=[gcol.t, gsc.t], writes=[gcol.t], tag=name + f"gw{gt}")
            gb = gsc.ap[:, 0:8]
            gbb = AB
            P.do("vector", lambda e, o=s["xn"].ap[:, 0:8], i=gslice:
                 e.tensor_copy(out=o, in_=i),
                 reads=[gcol.t], writes=[s["xn"].t], tag=name + f"gb{gt}")
            bank2, bank2_t = ps.get()
            bb2 = bank2.bitcast(BF16)
            P.do("tensor", lambda e, o=bb2[0:8, 0:128], i=s["xn"].ap[:, 0:8]:
                 e.transpose(out=o, in_=i, identity=cm.ident),
                 reads=[s["xn"].t, cm.ident_t], writes=[bank2_t], tag=name + f"gt{gt}")
            P.do("scalar", lambda e, o=gbT.ap[0:8, lt * 128:(lt + 1) * 128],
                 i=bb2[0:8, 0:128]: e.copy(out=o, in_=i),
                 reads=[bank2_t], writes=[gbT.t], tag=name + f"gv{gt}")
            for oc in range(2):
                bank3, bank3_t = ps.get()
                P.do("tensor", lambda e, o=bank3, l=gbT.ap[0:8, lt * 128:(lt + 1) * 128],
                     r=b2b.ap[0:8, oc * 512:(oc + 1) * 512]:
                     e.matmul(out=o, lhsT=l, rhs=r, start=True, stop=True),
                     reads=[gbT.t, b2b.t], writes=[bank3_t], tag=name + f"bi{gt}{oc}")
                P.do("scalar", lambda e, o=oaccv[:, lt, oc * 512:(oc + 1) * 512],
                     i=bank3: e.copy(out=o, in_=i),
                     reads=[bank3_t], writes=[oacc.t], tag=name + f"bo{gt}{oc}")

        hbuf = AB.alloc(name + f"_hb{ch}", 8 * 512)
        hbv = hbuf.ap.rearrange("p (c t) -> p c t", t=512)
        w1b = [AB.alloc(name + f"_w1{ch}{i}", 8 * C) for i in range(2)]
        w2b = [AB.alloc(name + f"_w2{ch}{i}", 8 * C) for i in range(2)]
        wstg = [AF32.alloc(name + f"_ws{ch}{i}", C, dma_slot=True) for i in range(4)]
        load_weight_into(P, w1b[0], wstg[0:2], wdr["l1f_W1"][0], C, C,
                         name + f"_w1_{ch}_0")
        load_weight_into(P, w2b[0], wstg[2:4], wdr["l1f_W2"][0], C, C,
                         name + f"_w2_{ch}_0")
        for e_ in range(8):
            w1 = w1b[e_ % 2]
            w2 = w2b[e_ % 2]
            if e_ + 1 < 8:
                load_weight_into(P, w1b[(e_ + 1) % 2], wstg[0:2],
                                 wdr["l1f_W1"][e_ + 1], C, C,
                                 name + f"_w1_{ch}_{e_ + 1}")
                load_weight_into(P, w2b[(e_ + 1) % 2], wstg[2:4],
                                 wdr["l1f_W2"][e_ + 1], C, C,
                                 name + f"_w2_{ch}_{e_ + 1}")
            for ts in range(TC // 512):
                for ot in range(8):
                    bank, bank_t = ps.get()
                    mms = []
                    for ct in range(8):
                        mms.append((bank, w1.ap[:, ct * C + ot * 128: ct * C + ot * 128 + 128],
                                    xnTv[:, ct, ts * 512:(ts + 1) * 512], None))
                    mm_group(P, bank, bank_t, mms, reads=[w1.t, xnT.t],
                             tag=name + f"h{ch}{e_}{ts}{ot}")
                    P.do("scalar", lambda e, o=hbv[:, ot, :], i=bank,
                         b_=b1c.ap[:, e_ * 8 + ot: e_ * 8 + ot + 1]:
                         e.activation(out=o, in_=i, func=AF.Gelu, bias=b_),
                         reads=[bank_t, b1c.t], writes=[hbuf.t],
                         tag=name + f"hg{ch}{e_}{ts}{ot}")
                for tq in range(4):
                    lt = ts * 4 + tq
                    gt = ch * ntt + lt
                    for oc in range(2):
                        bank, bank_t = ps.get()
                        mms = []
                        for ht in range(8):
                            mms.append((bank, hbv[:, ht, tq * 128:(tq + 1) * 128],
                                        w2.ap[:, ht * C + oc * 512: ht * C + (oc + 1) * 512],
                                        None))
                        mm_group(P, bank, bank_t, mms, reads=[hbuf.t, w2.t],
                                 tag=name + f"o{ch}{e_}{lt}{oc}")
                        dst = oaccv[:, lt, oc * 512:(oc + 1) * 512]
                        P.do("vector", lambda e, o=dst, i=bank,
                             g_=gcol.ap[:, lt * 8 + e_: lt * 8 + e_ + 1]:
                             e.scalar_tensor_tensor(out=o, in0=i, scalar=g_, in1=o,
                                                    op0=ALU.mult, op1=ALU.add),
                             reads=[bank_t, gcol.t, oacc.t], writes=[oacc.t],
                             tag=name + f"oa{ch}{e_}{lt}{oc}")
        for i in range(2):
            AB.freeb(w1b[i])
            AB.freeb(w2b[i])
        for st2 in wstg:
            AF32.freeb(st2)
        AB.freeb(hbuf)

        for lt in range(ntt):
            gt = ch * ntt + lt
            s = sl[lt % 2]
            P.dma("sync", s["xr"].ap, xin_ap(gt), writes=[s["xr"].t],
                  reads=[xin_tiles[gt]], tag=name + f"z{gt}")
            ob = outs[lt % 2]
            P.do("vector", lambda e, o=ob.ap, a=oaccv[:, lt, :], x=s["xr"].ap:
                 e.tensor_add(out=o, in0=a, in1=x),
                 reads=[oacc.t, s["xr"].t], writes=[ob.t], tag=name + f"ra{gt}")
            P.dma("scalar", xout_ap(gt), ob.ap, writes=[xout_tiles[gt]],
                  reads=[ob.t], tag=name + f"zs{gt}")
        AB.freeb(xnT)
        AB.freeb(oacc)
        AB.freeb(gbT)
        AF32.freeb(gcol)

    for bb3 in (wg, bg, b2b):
        AB.freeb(bb3)
    AF32.freeb(b1c)
    AF32.freeb(sq_sh)
    AF32.freeb(gsc)
    for s in sl:
        AF32.freeb(s["xr"])
        AF32.freeb(s["st"])
        AB.freeb(s["xn"])
    for o in outs:
        AF32.freeb(o)




def emit_l2_attn(P, AB, AF32, AU8, ps, cm, nc, io, wdr):
    for b in range(BPC):
        _softmax_attn_batch(
            P, AB, AF32, ps, cm, nc, io, b, name=f"l2a{b}",
            wspec=dict(kind="fused", wqkv=("l2a_wqkv", "l2a_bqkv"),
                       wp=("l2a_wp", "l2a_bp")),
            wdr=wdr, rope=None, causal=False, alibi=True, scale=0.125)




def emit_l2_ffn(P, AB, AF32, AU8, ps, cm, nc, io, wdr):
    name = "l2f"
    xin_ap, xin_tiles = io["xin"]
    xout_ap, xout_tiles = io["xout"]

    wg = AB.alloc(name + "_wg", 64)
    wgs = AF32.alloc(name + "_wgs", 64, dma_slot=True)
    P.dma("sync", wgs.ap[:, 0:64].rearrange("p (c o) -> p c o", o=8),
          wdr["l2f_gateT"].rearrange("(c p) o -> p c o", p=128), writes=[wgs.t],
          tag=name + "wgl")
    P.do("vector", lambda e: e.tensor_copy(out=wg.ap[:, 0:64], in_=wgs.ap[:, 0:64]),
         reads=[wgs.t], writes=[wg.t], tag=name + "wgc")
    AF32.freeb(wgs)

    sq_sh = AF32.alloc(name + "_sq", 1024)
    sl = []
    for i in range(2):
        sl.append({
            "xr": AF32.alloc(name + f"_xr{i}", 1024, dma_slot=True),
            "st": AF32.alloc(name + f"_st{i}", 32),
            "sq": sq_sh,
            "xn": AB.alloc(name + f"_xn{i}", 1024),
        })
    outs = [AF32.alloc(name + f"_o{i}", 1024, dma_slot=True) for i in range(2)]
    gsc = AF32.alloc(name + "_gsc", 32)
    bcf = AF32.alloc(name + "_bcf", 512)
    gsf = AF32.alloc(name + "_gsf", 512)
    sgf = AF32.alloc(name + "_sgf", 512)
    usf = AB.alloc(name + "_usf", 512)
    TC = 1024
    NCH = NT // TC

    for ch in range(NCH):
        ntt = TC // 128
        xnT = AB.alloc(name + f"_xnT{ch}", 8 * TC)
        xnTv = xnT.ap.rearrange("p (c t) -> p c t", t=TC)
        srow = AB.alloc(name + f"_srow{ch}", ntt * 128)

        for lt in range(ntt):
            gt = ch * ntt + lt
            s = sl[lt % 2]
            P.dma("sync", s["xr"].ap, xin_ap(gt), writes=[s["xr"].t],
                  reads=[xin_tiles[gt]], tag=name + f"xl{gt}")
            ln_tile(P, cm, s, s["xr"].ap, s["xr"].t, name + f"ln{gt}")
            dv = xnTv[:, :, lt * 128:(lt + 1) * 128]
            transpose_to(P, ps, cm, s["xn"].ap, s["xn"].t, None, xnT.t, 1024,
                         name + f"xT{gt}", dst_view=dv)
        for lt in range(ntt):
            gt = ch * ntt + lt
            s = sl[lt % 2]
            bank, bank_t = ps.get()
            mms = []
            for ct in range(8):
                mms.append((bank[:, 0:8], xnTv[:, ct, lt * 128:(lt + 1) * 128],
                            wg.ap[:, ct * 8:(ct + 1) * 8], None))
            mm_group(P, bank, bank_t, mms, reads=[xnT.t, wg.t], tag=name + f"gm{gt}")
            lg = gsc.ap[:, 0:8]
            ml = gsc.ap[:, 8:9]
            rw = gsc.ap[:, 9:10]
            eq = gsc.ap[:, 16:24]
            imax = gsc.ap[:, 10:11]
            fsc = gsc.ap[:, 24:32]
            P.do("vector", lambda e, o=lg, i=bank[:, 0:8]: e.tensor_copy(out=o, in_=i),
                 reads=[bank_t], writes=[gsc.t], tag=name + f"gc{gt}")
            P.do("vector", lambda e, o=ml, i=lg: e.reduce_max(out=o, in_=i, axis=AX.X),
                 reads=[gsc.t], writes=[gsc.t], tag=name + f"gx{gt}")
            P.do("scalar", lambda e, o=rw, i=ml: e.activation(out=o, in_=i, func=AF.Sigmoid),
                 reads=[gsc.t], writes=[gsc.t], tag=name + f"gw{gt}")
            P.do("vector", lambda e, o=eq, i=lg, m_=ml: e.tensor_scalar(
                 out=o, in0=i, scalar1=m_, scalar2=None, op0=ALU.is_equal),
                 reads=[gsc.t], writes=[gsc.t], tag=name + f"gq{gt}")
            P.do("vector", lambda e, o=eq, i=eq: e.tensor_mul(
                 out=o, in0=i, in1=cm.iota8r[:, 0:8]),
                 reads=[gsc.t, cm.iota8r_t], writes=[gsc.t], tag=name + f"gi{gt}")
            P.do("vector", lambda e, o=imax, i=eq: e.reduce_max(out=o, in_=i, axis=AX.X),
                 reads=[gsc.t], writes=[gsc.t], tag=name + f"gj{gt}")
            P.do("vector", lambda e, o=fsc, i=cm.iota8r[:, 0:8], m_=imax:
                 e.tensor_scalar(out=o, in0=i, scalar1=m_, scalar2=None, op0=ALU.is_equal),
                 reads=[gsc.t, cm.iota8r_t], writes=[gsc.t], tag=name + f"gf{gt}")
            P.do("vector", lambda e, o=fsc, i=fsc, m_=rw:
                 e.tensor_scalar(out=o, in0=i, scalar1=m_, scalar2=None, op0=ALU.mult),
                 reads=[gsc.t], writes=[gsc.t], tag=name + f"gs{gt}")
            P.do("vector", lambda e, o=s["xn"].ap[:, 0:8], i=fsc: e.tensor_copy(out=o, in_=i),
                 reads=[gsc.t], writes=[s["xn"].t], tag=name + f"gb{gt}")
            bank2, bank2_t = ps.get()
            bb2 = bank2.bitcast(BF16)
            P.do("tensor", lambda e, o=bb2[0:8, 0:128], i=s["xn"].ap[:, 0:8]:
                 e.transpose(out=o, in_=i, identity=cm.ident),
                 reads=[s["xn"].t, cm.ident_t], writes=[bank2_t], tag=name + f"gv{gt}")
            P.do("scalar", lambda e, o=srow.ap[0:8, lt * 128:(lt + 1) * 128],
                 i=bb2[0:8, 0:128]: e.copy(out=o, in_=i),
                 reads=[bank2_t], writes=[srow.t], tag=name + f"gu{gt}")

        zb = AB.alloc(name + f"_zb{ch}", 32 * TC)
        zbv = zb.ap.rearrange("p (i t) -> p i t", t=TC)
        wgb = [AB.alloc(name + f"_wg{ch}{i}", 8 * 512) for i in range(2)]
        wub = [AB.alloc(name + f"_wu{ch}{i}", 8 * 512) for i in range(2)]
        wstg = [AF32.alloc(name + f"_ws{ch}{i}", 512, dma_slot=True) for i in range(2)]
        load_weight_into(P, wgb[0], [wstg[0]], wdr["l2f_wgT"][0], C, 512,
                         name + f"_wg{ch}0")
        load_weight_into(P, wub[0], [wstg[1]], wdr["l2f_wuT"][0], C, 512,
                         name + f"_wu{ch}0")
        for e_ in range(8):
            wgt = wgb[e_ % 2]
            wut = wub[e_ % 2]
            if e_ + 1 < 8:
                load_weight_into(P, wgb[(e_ + 1) % 2], [wstg[0]],
                                 wdr["l2f_wgT"][e_ + 1], C, 512,
                                 name + f"_wg{ch}{e_ + 1}")
                load_weight_into(P, wub[(e_ + 1) % 2], [wstg[1]],
                                 wdr["l2f_wuT"][e_ + 1], C, 512,
                                 name + f"_wu{ch}{e_ + 1}")
            for ts in range(TC // 512):
                bank, bank_t = ps.get()
                P.do("tensor", lambda e, o=bank, l=cm.esel[0:8, e_ * 128:(e_ + 1) * 128],
                     r=srow.ap[0:8, ts * 512:(ts + 1) * 512]:
                     e.matmul(out=o, lhsT=l, rhs=r, start=True, stop=True),
                     reads=[srow.t, cm.esel_t], writes=[bank_t],
                     tag=name + f"eb{ch}{e_}{ts}")
                P.do("scalar", lambda e, o=bcf.ap, i=bank: e.copy(out=o, in_=i),
                     reads=[bank_t], writes=[bcf.t], tag=name + f"ec{ch}{e_}{ts}")
                for it in range(4):
                    banku, banku_t = ps.get()
                    mms = [(banku, wut.ap[:, ct * 512 + it * 128: ct * 512 + it * 128 + 128],
                            xnTv[:, ct, ts * 512:(ts + 1) * 512], None) for ct in range(8)]
                    mm_group(P, banku, banku_t, mms, reads=[wut.t, xnT.t],
                             tag=name + f"um{ch}{e_}{ts}{it}")
                    P.do("vector", lambda e, o=usf.ap, i=banku, b_=bcf.ap:
                         e.tensor_mul(out=o, in0=i, in1=b_),
                         reads=[banku_t, bcf.t], writes=[usf.t],
                         tag=name + f"ue{ch}{e_}{ts}{it}")
                    bankg, bankg_t = ps.get()
                    mms = [(bankg, wgt.ap[:, ct * 512 + it * 128: ct * 512 + it * 128 + 128],
                            xnTv[:, ct, ts * 512:(ts + 1) * 512], None) for ct in range(8)]
                    mm_group(P, bankg, bankg_t, mms, reads=[wgt.t, xnT.t],
                             tag=name + f"gm2{ch}{e_}{ts}{it}")
                    P.do("vector", lambda e, o=gsf.ap, i=bankg, b_=bcf.ap:
                         e.tensor_mul(out=o, in0=i, in1=b_),
                         reads=[bankg_t, bcf.t], writes=[gsf.t],
                         tag=name + f"ge2{ch}{e_}{ts}{it}")
                    P.do("scalar", lambda e, o=sgf.ap, i=gsf.ap:
                         e.activation(out=o, in_=i, func=AF.Silu),
                         reads=[gsf.t], writes=[sgf.t], tag=name + f"gs2{ch}{e_}{ts}{it}")
                    dst = zbv[:, e_ * 4 + it, ts * 512:(ts + 1) * 512]
                    P.do("vector", lambda e, o=dst, a=sgf.ap, b_=usf.ap:
                         e.tensor_mul(out=o, in0=a, in1=b_),
                         reads=[sgf.t, usf.t], writes=[zb.t],
                         tag=name + f"zm{ch}{e_}{ts}{it}")
        for i in range(2):
            AB.freeb(wgb[i])
            AB.freeb(wub[i])
            AF32.freeb(wstg[i])
        AB.freeb(xnT)

        wdt = [load_weight_bf16(P, AB, AF32, wdr["l2f_wdT"][e_], 512, C,
                                name + f"_wd{ch}{e_}", ns=2) for e_ in range(8)]
        outfm = AB.alloc(name + f"_ofm{ch}", 8 * TC)
        ofmv = outfm.ap.rearrange("p (o t) -> p o t", t=TC)
        for ot in range(8):
            for ts in range(TC // 512):
                bank, bank_t = ps.get()
                mms = []
                for e_ in range(8):
                    for it in range(4):
                        mms.append((bank,
                                    wdt[e_].ap[:, it * C + ot * 128: it * C + ot * 128 + 128],
                                    zbv[:, e_ * 4 + it, ts * 512:(ts + 1) * 512], None))
                mm_group(P, bank, bank_t, mms,
                         reads=[zb.t] + [w.t for w in wdt],
                         tag=name + f"dm{ch}{ot}{ts}")
                P.do("scalar", lambda e, o=ofmv[:, ot, ts * 512:(ts + 1) * 512],
                     i=bank: e.copy(out=o, in_=i),
                     reads=[bank_t], writes=[outfm.t], tag=name + f"de{ch}{ot}{ts}")
        for w in wdt:
            AB.freeb(w)
        AB.freeb(zb)

        for lt in range(ntt):
            gt = ch * ntt + lt
            s = sl[lt % 2]
            P.dma("sync", s["xr"].ap, xin_ap(gt), writes=[s["xr"].t],
                  reads=[xin_tiles[gt]], tag=name + f"z{gt}")
            bank, bank_t = ps.get()
            bb = bank.bitcast(BF16)
            def fnt(e, lt=lt, bb=bb):
                ins = None
                for ot in range(8):
                    ins = e.transpose(out=bb[:, ot * 128:(ot + 1) * 128],
                                      in_=ofmv[:, ot, lt * 128:(lt + 1) * 128],
                                      identity=cm.ident)
                return ins
            P.do("tensor", fnt, reads=[outfm.t, cm.ident_t], writes=[bank_t],
                 tag=name + f"tt{gt}")
            ob = outs[lt % 2]
            P.do("vector", lambda e, o=ob.ap, a=bb[:, 0:1024], x=s["xr"].ap:
                 e.tensor_add(out=o, in0=a, in1=x),
                 reads=[bank_t, s["xr"].t], writes=[ob.t], tag=name + f"ra{gt}")
            P.dma("scalar", xout_ap(gt), ob.ap, writes=[xout_tiles[gt]],
                  reads=[ob.t], tag=name + f"zs{gt}")
        AB.freeb(outfm)
        AB.freeb(srow)

    AB.freeb(wg)
    AF32.freeb(sq_sh)
    for bb3 in (gsc, bcf, gsf):
        AF32.freeb(bb3)
    AB.freeb(usf)
    AF32.freeb(sgf)
    for s in sl:
        AF32.freeb(s["xr"])
        AF32.freeb(s["st"])
        AB.freeb(s["xn"])
    for o in outs:
        AF32.freeb(o)




L3A_DEBUG = "ABC"


def emit_l3_attn(P, AB, AF32, AU8, ps, cm, nc, io, wdr):
    for b in range(BPC):
        _l3_attn_batch(P, AB, AF32, ps, cm, nc, io, b, wdr, name=f"l3a{b}")


def _l3_attn_batch(P, AB, AF32, ps, cm, nc, io, b, wdr, name):
    xin_ap, xin_tiles = io["xin"]
    xout_ap, xout_tiles = io["xout"]

    fktm = AB.alloc(name + "_fktm", TPB * 1024)
    fqfm = AB.alloc(name + "_fqfm", 8 * T)
    vtm = AB.alloc(name + "_vtm", TPB * 1024)

    wqkv = load_weight_bf16(P, AB, AF32, wdr["l3a_wqkvT"], C, 3 * C, name + "_wqkv")
    rfs = AF32.alloc(name + "_rfs", 128, dma_slot=True)
    rft = AB.alloc(name + "_rft", 128)
    P.dma("sync", rfs.ap[:, 0:128], wdr["l3a_rfT"], writes=[rfs.t], tag=name + "rfl")
    P.do("vector", lambda e: e.tensor_copy(out=rft.ap[:, 0:128], in_=rfs.ap[:, 0:128]),
         reads=[rfs.t], writes=[rft.t], tag=name + "rfc")
    AF32.freeb(rfs)

    sq_sh = AF32.alloc(name + "_sq", 1024)
    sl = []
    for i in range(2):
        sl.append({
            "xr": AF32.alloc(name + f"_xr{i}", 1024, dma_slot=True),
            "st": AF32.alloc(name + f"_st{i}", 32),
            "sq": sq_sh,
            "xn": AB.alloc(name + f"_xn{i}", 1024),
            "xnT": AB.alloc(name + f"_xnT{i}", 1024),
            "qf": AF32.alloc(name + f"_qf{i}", 1024),
            "nrm": AF32.alloc(name + f"_nrm{i}", 32),
            "qb": AB.alloc(name + f"_qb{i}", 1024),
            "qfm": AB.alloc(name + f"_qfm{i}", 1024),
            "fpre": AF32.alloc(name + f"_fp{i}", 1024),
        })

    fkv = fktm.ap.rearrange("p (t c) -> p t c", c=1024)
    fqv = fqfm.ap.rearrange("p (c t) -> p c t", t=T)

    def qkv_proj(s, which, tt, tagp):
        off = {"q": 0, "k": C, "v": 2 * C}[which]
        for oc in range(2):
            bank, bank_t = ps.get()
            mms = []
            for ct in range(8):
                mms.append((bank, s["xnT"].ap[:, ct * 128:(ct + 1) * 128],
                            wqkv.ap[:, ct * 3 * C + off + oc * 512:
                                    ct * 3 * C + off + (oc + 1) * 512], None))
            mm_group(P, bank, bank_t, mms, reads=[s["xnT"].t, wqkv.t],
                     tag=tagp + f"m{oc}")
            if which == "v":
                dst = vtm.ap[:, tt * 1024 + oc * 512: tt * 1024 + (oc + 1) * 512]
                P.do("scalar", lambda e, o=dst, i=bank: e.copy(out=o, in_=i),
                     reads=[bank_t], writes=[vtm.t], tag=tagp + f"e{oc}")
            else:
                dst = s["qf"].ap[:, oc * 512:(oc + 1) * 512]
                P.do("scalar", lambda e, o=dst, i=bank: e.copy(out=o, in_=i),
                     reads=[bank_t], writes=[s["qf"].t], tag=tagp + f"e{oc}")

    def feature_map(s, tt, is_q, tagp):
        P.do("scalar", lambda e: e.activation(out=s["sq"].ap, in_=s["qf"].ap,
                                              func=AF.Square),
             reads=[s["qf"].t], writes=[s["sq"].t], tag=tagp + "sq")
        n2 = s["nrm"].ap[:, 0:16]
        P.do("vector", lambda e: e.reduce_sum(
             out=n2.rearrange("p (h o) -> p h o", o=1),
             in_=s["sq"].ap.rearrange("p (h d) -> p h d", d=64), axis=AX.X),
             reads=[s["sq"].t], writes=[s["nrm"].t], tag=tagp + "rd")
        P.do("scalar", lambda e: e.activation(out=n2, in_=n2, func=AF.Ln),
             reads=[s["nrm"].t], writes=[s["nrm"].t], tag=tagp + "ln")
        P.do("scalar", lambda e: e.activation(out=n2, in_=n2, func=AF.Exp, scale=0.5),
             reads=[s["nrm"].t], writes=[s["nrm"].t], tag=tagp + "ex")
        P.do("vector", lambda e: e.tensor_copy(out=s["qb"].ap, in_=s["qf"].ap),
             reads=[s["qf"].t], writes=[s["qb"].t], tag=tagp + "cv")
        transpose_to(P, ps, cm, s["qb"].ap, s["qb"].t, s["qfm"].ap, s["qfm"].t,
                     1024, tagp + "tr")
        for half in range(2):
            bank, bank_t = ps.get()
            def fn(e, half=half, bank=bank, s=s):
                ins = None
                for pp in range(4):
                    ct = half * 4 + pp
                    ins = e.matmul(out=bank[:, pp * 128:(pp + 1) * 128],
                                   lhsT=s["qfm"].ap[:, ct * 128:(ct + 1) * 128],
                                   rhs=rft.ap[:, 0:128], start=True, stop=True)
                return ins
            P.do("tensor", fn, reads=[s["qfm"].t, rft.t], writes=[bank_t],
                 tag=tagp + f"pj{half}")
            dst = s["fpre"].ap[:, half * 512:(half + 1) * 512]
            nb = n2[:, half * 8:(half + 1) * 8].unsqueeze(2).broadcast_to((128, 8, 64))
            P.do("vector", lambda e, o=dst.rearrange("p (h d) -> p h d", d=64),
                 i=bank, nb=nb:
                 e.scalar_tensor_tensor(out=o,
                                        in0=i.rearrange("p (h d) -> p h d", d=64),
                                        scalar=0.125, in1=nb,
                                        op0=ALU.mult, op1=ALU.subtract),
                 reads=[bank_t, s["nrm"].t], writes=[s["fpre"].t], tag=tagp + f"fp{half}")
        dst = fkv[:, tt, :] if not is_q else s["qb"].ap
        dt_t = fktm.t if not is_q else s["qb"].t
        P.do("scalar", lambda e, o=dst, i=s["fpre"].ap:
             e.activation(out=o, in_=i, func=AF.Exp),
             reads=[s["fpre"].t], writes=[dt_t], tag=tagp + "phe")
        if is_q:
            dv = fqv[:, :, tt * 128:(tt + 1) * 128]
            transpose_to(P, ps, cm, s["qb"].ap, s["qb"].t, None, fqfm.t, 1024,
                         tagp + "qtr", dst_view=dv)

    for tt in range(TPB):
        gt = b * TPB + tt
        s = sl[tt % 2]
        P.dma("sync", s["xr"].ap, xin_ap(gt), writes=[s["xr"].t],
              reads=[xin_tiles[gt]], tag=name + f"xl{tt}")
        ln_tile(P, cm, s, s["xr"].ap, s["xr"].t, name + f"ln{tt}")
        transpose_to(P, ps, cm, s["xn"].ap, s["xn"].t, s["xnT"].ap, s["xnT"].t,
                     1024, name + f"xT{tt}")
        qkv_proj(s, "v", tt, name + f"v{tt}")
        qkv_proj(s, "q", tt, name + f"q{tt}")
        feature_map(s, tt, True, name + f"fq{tt}")
        qkv_proj(s, "k", tt, name + f"k{tt}")
        feature_map(s, tt, False, name + f"fk{tt}")

    AB.freeb(wqkv)
    AF32.freeb(sq_sh)
    for s in sl:
        for k2 in ("xn", "xnT", "qb", "qfm"):
            AB.freeb(s[k2])
        for k2 in ("qf", "nrm", "fpre"):
            AF32.freeb(s[k2])

    def _copy_through():
        outs0 = [AF32.alloc(name + f"_dbg{i}", 1024, dma_slot=True) for i in range(2)]
        for gt2 in range(b * TPB, (b + 1) * TPB):
            ob3 = outs0[gt2 % 2]
            P.dma("sync", ob3.ap, xin_ap(gt2), writes=[ob3.t],
                  reads=[xin_tiles[gt2]], tag=name + f"dbl{gt2}")
            P.dma("scalar", xout_ap(gt2), ob3.ap, writes=[xout_tiles[gt2]],
                  reads=[ob3.t], tag=name + f"dbs{gt2}")
        for o in outs0:
            AF32.freeb(o)

    if "B" not in L3A_DEBUG:
        _copy_through()
        for bb3 in (fktm, fqfm, vtm):
            AB.freeb(bb3)
        AB.freeb(rft)
        for s in sl:
            AF32.freeb(s["xr"])
            AF32.freeb(s["st"])
        return

    kvsb = AB.alloc(name + "_kvsb", 16 * 65 + 62)
    P.do("vector", lambda e: e.memset(kvsb.ap, 0.0), writes=[kvsb.t],
         tag=name + "kvz")
    banks = [ps.take() for _ in range(8)]
    for nt in range(TPB):
        for hp in range(8):
            bank, bank_t, _bi = banks[hp]
            def fn(e, nt=nt, hp=hp, bank=bank):
                ins = None
                for z in range(2):
                    h = 2 * hp + z
                    lh = fkv[:, nt, h * 64:(h + 1) * 64]
                    ins = e.matmul(out=bank[64 * z:64 * z + 64, 0:64],
                                   lhsT=lh,
                                   rhs=vtm.ap[:, nt * 1024 + h * 64: nt * 1024 + (h + 1) * 64],
                                   start=(nt == 0), stop=(nt == TPB - 1),
                                   tile_position=(0, 64 * z), skip_group_check=True)
                    ins = e.matmul(out=bank[64 * z:64 * z + 64, 64:65],
                                   lhsT=lh, rhs=cm.ones_col,
                                   start=(nt == 0), stop=(nt == TPB - 1),
                                   tile_position=(0, 64 * z), skip_group_check=True)
                return ins
            P.do("tensor", fn, reads=[fktm.t, vtm.t, cm.ones_col_t],
                 writes=[banks[hp][1]], tag=name + f"kv{nt}_{hp}")
    for hp in range(8):
        bank, bank_t, bi = banks[hp]
        P.do("scalar", lambda e, o=kvsb.ap[0:64, (2 * hp) * 65: (2 * hp) * 65 + 65],
             i=bank[0:64, 0:65]: e.copy(out=o, in_=i),
             reads=[bank_t], writes=[kvsb.t], tag=name + f"kve{hp}a")
        P.do("scalar", lambda e, o=kvsb.ap[64:128, (2 * hp + 1) * 65: (2 * hp + 1) * 65 + 65],
             i=bank[64:128, 0:65]: e.copy(out=o, in_=i),
             reads=[bank_t], writes=[kvsb.t], tag=name + f"kve{hp}b")
        ps.release(bi)

    if "C" not in L3A_DEBUG:
        _copy_through()
        for bb3 in (fktm, fqfm, vtm, kvsb, rft):
            AB.freeb(bb3)
        for s in sl:
            AF32.freeb(s["xr"])
            AF32.freeb(s["st"])
        return

    wo = load_weight_bf16(P, AB, AF32, wdr["l3a_woT"], C, C, name + "_wo")
    bo = load_bias_row(P, AB, AF32, wdr["l3a_bo"], C, name + "_bo")
    ybuf = AB.alloc(name + "_y", 8 * 512)
    drow = AB.alloc(name + "_dr", 1024)
    dcol = AF32.alloc(name + "_dc", 512)
    bcf2 = AF32.alloc(name + "_bc", 512)
    outs = [AF32.alloc(name + f"_o{i}", 1024, dma_slot=True) for i in range(2)]

    for qc in range(4):
        for hp in range(8):
            ob_, ob_t = ps.get()
            dbank = [ps.get() for _ in range(2)]
            rhsq = fqv[:, hp, qc * 512:(qc + 1) * 512]
            def fn(e, hp=hp, ob_=ob_, rhsq=rhsq):
                ins = None
                for z in range(2):
                    h = 2 * hp + z
                    ins = e.matmul(out=ob_[64 * z:64 * z + 64, :],
                                   lhsT=kvsb.ap[:, h * 65: h * 65 + 64],
                                   rhs=rhsq, start=True, stop=True,
                                   tile_position=(0, 64 * z), skip_group_check=True)
                return ins
            P.do("tensor", fn, reads=[kvsb.t, fqfm.t], writes=[ob_t],
                 tag=name + f"ov{qc}{hp}")
            for z in range(2):
                h = 2 * hp + z
                P.do("tensor", lambda e, o=dbank[z][0], h=h, rhsq=rhsq:
                     e.matmul(out=o[0:1, :], lhsT=kvsb.ap[:, h * 65 + 64: h * 65 + 65],
                              rhs=rhsq, start=True, stop=True, skip_group_check=True),
                     reads=[kvsb.t, fqfm.t], writes=[dbank[z][1]],
                     tag=name + f"dn{qc}{hp}{z}")
            for z in range(2):
                P.do("vector", lambda e, o=dcol.ap[0:1, :], i=dbank[z][0][0:1, :]:
                     e.tensor_scalar(out=o, in0=i, scalar1=1e-6, scalar2=None, op0=ALU.add),
                     reads=[dbank[z][1]], writes=[dcol.t], tag=name + f"da{qc}{hp}{z}")
                P.do("vector", lambda e, o=dcol.ap[0:1, :], i=dcol.ap[0:1, :]:
                     e.reciprocal(out=o, in_=i),
                     reads=[dcol.t], writes=[dcol.t], tag=name + f"dr{qc}{hp}{z}")
                P.do("vector", lambda e, o=drow.ap[0:1, z * 512:(z + 1) * 512],
                     i=dcol.ap[0:1, :]: e.tensor_copy(out=o, in_=i),
                     reads=[dcol.t], writes=[drow.t], tag=name + f"db{qc}{hp}{z}")
            bcb, bcb_t = ps.get()
            def fnb(e, bcb=bcb):
                e.matmul(out=bcb[0:64, :], lhsT=cm.ones_row[0:1, 0:64],
                         rhs=drow.ap[0:1, 0:512], start=True, stop=True,
                         tile_position=(0, 0))
                return e.matmul(out=bcb[64:128, :], lhsT=cm.ones_row[0:1, 0:64],
                                rhs=drow.ap[0:1, 512:1024], start=True, stop=True,
                                tile_position=(0, 64))
            P.do("tensor", fnb, reads=[drow.t, cm.ones_row_t], writes=[bcb_t],
                 tag=name + f"bb{qc}{hp}")
            P.do("scalar", lambda e, o=bcf2.ap, i=bcb: e.copy(out=o, in_=i),
                 reads=[bcb_t], writes=[bcf2.t], tag=name + f"bc{qc}{hp}")
            ydst = ybuf.ap[:, hp * 512:(hp + 1) * 512]
            P.do("vector", lambda e, o=ydst, a=ob_, d=bcf2.ap:
                 e.tensor_mul(out=o, in0=a, in1=d),
                 reads=[ob_t, bcf2.t], writes=[ybuf.t], tag=name + f"ym{qc}{hp}")
        for qs in range(4):
            gt = b * TPB + qc * 4 + qs
            s = sl[qs % 2]
            P.dma("sync", s["xr"].ap, xin_ap(gt), writes=[s["xr"].t],
                  reads=[xin_tiles[gt]], tag=name + f"zl{qc}{qs}")
            ob2 = outs[qs % 2]
            for oc in range(2):
                bank, bank_t = ps.get()
                mms = [(bank, cm.ones_row[0:1, :], bo.ap[0:1, oc * 512:(oc + 1) * 512], None)]
                for hp in range(8):
                    mms.append((bank, ybuf.ap[:, hp * 512 + qs * 128: hp * 512 + qs * 128 + 128],
                                wo.ap[:, hp * C + oc * 512: hp * C + (oc + 1) * 512], None))
                mm_group(P, bank, bank_t, mms,
                         reads=[ybuf.t, wo.t, bo.t, cm.ones_row_t],
                         tag=name + f"j{qc}{qs}{oc}")
                dst = ob2.ap[:, oc * 512:(oc + 1) * 512]
                P.do("vector", lambda e, o=dst, i=bank,
                     x=s["xr"].ap[:, oc * 512:(oc + 1) * 512]:
                     e.tensor_add(out=o, in0=i, in1=x),
                     reads=[bank_t, s["xr"].t], writes=[ob2.t], tag=name + f"r{qc}{qs}{oc}")
            P.dma("scalar", xout_ap(gt), ob2.ap, writes=[xout_tiles[gt]],
                  reads=[ob2.t], tag=name + f"zs{qc}{qs}")

    for bb3 in (fktm, fqfm, vtm, kvsb, rft, wo, bo, ybuf, drow):
        AB.freeb(bb3)
    AF32.freeb(dcol)
    AF32.freeb(bcf2)
    for s in sl:
        AF32.freeb(s["xr"])
        AF32.freeb(s["st"])
    for o in outs:
        AF32.freeb(o)




def emit_l3_ffn(P, AB, AF32, AU8, ps, cm, nc, io, wdr):
    name = "l3f"
    xin_ap, xin_tiles = io["xin"]
    xout_ap, xout_tiles = io["xout"]

    wg = AB.alloc(name + "_wg", 64)
    wgs = AF32.alloc(name + "_wgs", 64, dma_slot=True)
    P.dma("sync", wgs.ap[:, 0:64].rearrange("p (c o) -> p c o", o=8),
          wdr["l3f_gateT"].rearrange("(c p) o -> p c o", p=128), writes=[wgs.t],
          tag=name + "wgl")
    P.do("vector", lambda e: e.tensor_copy(out=wg.ap[:, 0:64], in_=wgs.ap[:, 0:64]),
         reads=[wgs.t], writes=[wg.t], tag=name + "wgc")
    AF32.freeb(wgs)

    sq_sh = AF32.alloc(name + "_sq", 1024)
    sl = []
    for i in range(2):
        sl.append({
            "xr": AF32.alloc(name + f"_xr{i}", 1024, dma_slot=True),
            "st": AF32.alloc(name + f"_st{i}", 32),
            "sq": sq_sh,
            "xn": AB.alloc(name + f"_xn{i}", 1024),
        })
    outs = [AF32.alloc(name + f"_o{i}", 1024, dma_slot=True) for i in range(2)]
    gsc = AF32.alloc(name + "_gsc", 64)
    usf = AB.alloc(name + "_usf", 512)
    gsf = AF32.alloc(name + "_gsf", 512)
    sgf = AF32.alloc(name + "_sgf", 512)
    TC = 1024
    NCH = NT // TC

    for ch in range(NCH):
        ntt = TC // 128
        xnT = AB.alloc(name + f"_xnT{ch}", 8 * TC)
        xnTv = xnT.ap.rearrange("p (c t) -> p c t", t=TC)
        oacc = AB.alloc(name + f"_oacc{ch}", ntt * 1024)
        oaccv = oacc.ap.rearrange("p (t o) -> p t o", o=1024)
        cw = AF32.alloc(name + f"_cw{ch}", ntt * 8)

        for lt in range(ntt):
            gt = ch * ntt + lt
            s = sl[lt % 2]
            P.dma("sync", s["xr"].ap, xin_ap(gt), writes=[s["xr"].t],
                  reads=[xin_tiles[gt]], tag=name + f"xl{gt}")
            ln_tile(P, cm, s, s["xr"].ap, s["xr"].t, name + f"ln{gt}")
            dv = xnTv[:, :, lt * 128:(lt + 1) * 128]
            transpose_to(P, ps, cm, s["xn"].ap, s["xn"].t, None, xnT.t, 1024,
                         name + f"xT{gt}", dst_view=dv)
        for lt in range(ntt):
            gt = ch * ntt + lt
            bank, bank_t = ps.get()
            mms = [(bank[:, 0:8], xnTv[:, ct, lt * 128:(lt + 1) * 128],
                    wg.ap[:, ct * 8:(ct + 1) * 8], None) for ct in range(8)]
            mm_group(P, bank, bank_t, mms, reads=[xnT.t, wg.t], tag=name + f"gm{gt}")
            lg = gsc.ap[:, 0:8]
            m1 = gsc.ap[:, 8:9]
            zz = gsc.ap[:, 16:24]
            ss = gsc.ap[:, 9:10]
            p1 = gsc.ap[:, 10:11]
            m2 = gsc.ap[:, 11:12]
            p2 = gsc.ap[:, 12:13]
            dnm = gsc.ap[:, 13:14]
            imax = gsc.ap[:, 14:15]
            f1 = gsc.ap[:, 24:32]
            f2 = gsc.ap[:, 32:40]
            eq = gsc.ap[:, 40:48]
            def V(fn, rd=(), tg=""):
                P.do("vector", fn, reads=[gsc.t] + list(rd), writes=[gsc.t],
                     tag=name + tg + str(gt))
            P.do("vector", lambda e, i=bank: e.tensor_copy(out=lg, in_=i[:, 0:8]),
                 reads=[bank_t], writes=[gsc.t], tag=name + f"gc{gt}")
            V(lambda e: e.reduce_max(out=m1, in_=lg, axis=AX.X), tg="g1")
            V(lambda e: e.tensor_scalar(out=imax, in0=m1, scalar1=-1.0, scalar2=None,
                                        op0=ALU.mult), tg="g1n")
            P.do("scalar", lambda e: e.activation(out=zz, in_=lg, func=AF.Exp,
                                                  bias=imax),
                 reads=[gsc.t], writes=[gsc.t], tag=name + f"g2{gt}")
            V(lambda e: e.reduce_sum(out=ss, in_=zz, axis=AX.X), tg="g3")
            V(lambda e: e.reciprocal(out=p1, in_=ss), tg="g4")
            V(lambda e: e.tensor_scalar(out=eq, in0=lg, scalar1=m1, scalar2=None,
                                        op0=ALU.is_equal), tg="g5")
            V(lambda e: e.tensor_mul(out=eq, in0=eq, in1=cm.iota8r[:, 0:8]),
              rd=[cm.iota8r_t], tg="g6")
            V(lambda e: e.reduce_max(out=imax, in_=eq, axis=AX.X), tg="g7")
            V(lambda e: e.tensor_scalar(out=f1, in0=cm.iota8r[:, 0:8], scalar1=imax,
                                        scalar2=None, op0=ALU.is_equal),
              rd=[cm.iota8r_t], tg="g8")
            V(lambda e: e.scalar_tensor_tensor(out=eq, in0=f1, scalar=-1.0, in1=zz,
                                               op0=ALU.mult, op1=ALU.mult), tg="g9")
            V(lambda e: e.tensor_add(out=eq, in0=zz, in1=eq), tg="gA")
            V(lambda e: e.reduce_max(out=m2, in_=eq, axis=AX.X), tg="gB")
            V(lambda e: e.tensor_scalar(out=eq, in0=eq, scalar1=m2, scalar2=None,
                                        op0=ALU.is_equal), tg="gC")
            V(lambda e: e.tensor_mul(out=eq, in0=eq, in1=cm.iota8r[:, 0:8]),
              rd=[cm.iota8r_t], tg="gD")
            V(lambda e: e.reduce_max(out=imax, in_=eq, axis=AX.X), tg="gE")
            V(lambda e: e.tensor_scalar(out=f2, in0=cm.iota8r[:, 0:8], scalar1=imax,
                                        scalar2=None, op0=ALU.is_equal),
              rd=[cm.iota8r_t], tg="gF")
            V(lambda e: e.tensor_scalar(out=p2, in0=m2, scalar1=p1, scalar2=None,
                                        op0=ALU.mult), tg="gG")
            V(lambda e: e.tensor_scalar(out=dnm, in0=p1, scalar1=p2, scalar2=1e-8,
                                        op0=ALU.add, op1=ALU.add), tg="gH")
            V(lambda e: e.reciprocal(out=dnm, in_=dnm), tg="gI")
            V(lambda e: e.tensor_scalar(out=p1, in0=p1, scalar1=dnm, scalar2=None,
                                        op0=ALU.mult), tg="gJ")
            V(lambda e: e.tensor_scalar(out=p2, in0=p2, scalar1=dnm, scalar2=None,
                                        op0=ALU.mult), tg="gK")
            cwsl = cw.ap[:, lt * 8:(lt + 1) * 8]
            P.do("vector", lambda e, o=cwsl: e.tensor_scalar(out=o, in0=f1, scalar1=p1,
                                                             scalar2=None, op0=ALU.mult),
                 reads=[gsc.t], writes=[cw.t], tag=name + f"gL{gt}")
            P.do("vector", lambda e, o=cwsl: e.scalar_tensor_tensor(out=o, in0=f2, scalar=p2,
                                                                    in1=o, op0=ALU.mult,
                                                                    op1=ALU.add),
                 reads=[gsc.t, cw.t], writes=[cw.t], tag=name + f"gM{gt}")

        zbuf = AB.alloc(name + f"_zb{ch}", 8 * TC)
        zbv = zbuf.ap.rearrange("p (i t) -> p i t", t=TC)
        gub = [AB.alloc(name + f"_gu{ch}{i}", 16 * C) for i in range(2)]
        dnb = [AB.alloc(name + f"_dn{ch}{i}", 8 * C) for i in range(2)]
        wstg = [AF32.alloc(name + f"_ws{ch}{i}", 1024, dma_slot=True) for i in range(3)]
        def gu_src(e2):
            return wdr["l3f_sh_guT"] if e2 < 0 else wdr["l3f_ex_guT"][e2]
        def dn_src(e2):
            return wdr["l3f_sh_dnT"] if e2 < 0 else wdr["l3f_ex_dnT"][e2]
        def load_gu(buf, src, tagn):
            for hf in range(2):
                for ct in range(8):
                    sg = wstg[(hf * 8 + ct) % 2]
                    P.dma("sync", sg.ap[:, 0:1024],
                          src[ct * 128:(ct + 1) * 128, hf * 1024:(hf + 1) * 1024],
                          writes=[sg.t], tag=tagn + f"L{hf}{ct}")
                    dst = buf.ap[:, ct * 2048 + hf * 1024: ct * 2048 + (hf + 1) * 1024]
                    P.do("vector",
                         lambda e, o=dst, i=sg.ap[:, 0:1024]: e.tensor_copy(out=o, in_=i),
                         reads=[sg.t], writes=[buf.t], tag=tagn + f"C{hf}{ct}")
        load_gu(gub[0], gu_src(-1), name + f"_gu{ch}s")
        load_weight_into(P, dnb[0], [wstg[2]], dn_src(-1), C, C, name + f"_dn{ch}s")
        for e_ in range(-1, 8):
            gu = gub[(e_ + 1) % 2]
            dn = dnb[(e_ + 1) % 2]
            if e_ + 1 < 8:
                load_gu(gub[e_ % 2], gu_src(e_ + 1), name + f"_gu{ch}{e_ + 1}")
                load_weight_into(P, dnb[e_ % 2], [wstg[2]], dn_src(e_ + 1), C, C,
                                 name + f"_dn{ch}{e_ + 1}")
            for ts in range(TC // 512):
                for it in range(8):
                    banku, banku_t = ps.get()
                    mms = [(banku, gu.ap[:, ct * 2048 + 1024 + it * 128:
                                         ct * 2048 + 1024 + it * 128 + 128],
                            xnTv[:, ct, ts * 512:(ts + 1) * 512], None) for ct in range(8)]
                    mm_group(P, banku, banku_t, mms, reads=[gu.t, xnT.t],
                             tag=name + f"u{ch}{e_}{ts}{it}")
                    P.do("vector", lambda e, o=usf.ap, i=banku: e.tensor_copy(out=o, in_=i),
                         reads=[banku_t], writes=[usf.t], tag=name + f"ue{ch}{e_}{ts}{it}")
                    bankg, bankg_t = ps.get()
                    mms = [(bankg, gu.ap[:, ct * 2048 + it * 128:
                                         ct * 2048 + it * 128 + 128],
                            xnTv[:, ct, ts * 512:(ts + 1) * 512], None) for ct in range(8)]
                    mm_group(P, bankg, bankg_t, mms, reads=[gu.t, xnT.t],
                             tag=name + f"g{ch}{e_}{ts}{it}")
                    P.do("scalar", lambda e, o=sgf.ap, i=bankg:
                         e.activation(out=o, in_=i, func=AF.Silu),
                         reads=[bankg_t], writes=[sgf.t], tag=name + f"gs{ch}{e_}{ts}{it}")
                    dst = zbv[:, it, ts * 512:(ts + 1) * 512]
                    P.do("vector", lambda e, o=dst, a=sgf.ap, b_=usf.ap:
                         e.tensor_mul(out=o, in0=a, in1=b_),
                         reads=[sgf.t, usf.t], writes=[zbuf.t],
                         tag=name + f"zm{ch}{e_}{ts}{it}")
            for lt in range(ntt):
                for oc in range(2):
                    bank, bank_t = ps.get()
                    mms = [(bank, zbv[:, itile, lt * 128:(lt + 1) * 128],
                            dn.ap[:, itile * C + oc * 512: itile * C + (oc + 1) * 512],
                            None) for itile in range(8)]
                    mm_group(P, bank, bank_t, mms, reads=[zbuf.t, dn.t],
                             tag=name + f"d{ch}{e_}{lt}{oc}")
                    dst = oaccv[:, lt, oc * 512:(oc + 1) * 512]
                    if e_ < 0:
                        P.do("vector", lambda e, o=dst, i=bank: e.tensor_copy(out=o, in_=i),
                             reads=[bank_t], writes=[oacc.t],
                             tag=name + f"da{ch}{e_}{lt}{oc}")
                    else:
                        P.do("vector", lambda e, o=dst, i=bank,
                             g_=cw.ap[:, lt * 8 + e_: lt * 8 + e_ + 1]:
                             e.scalar_tensor_tensor(out=o, in0=i, scalar=g_, in1=o,
                                                    op0=ALU.mult, op1=ALU.add),
                             reads=[bank_t, cw.t, oacc.t], writes=[oacc.t],
                             tag=name + f"da{ch}{e_}{lt}{oc}")
        for i in range(2):
            AB.freeb(gub[i])
            AB.freeb(dnb[i])
        for st2 in wstg:
            AF32.freeb(st2)
        AB.freeb(zbuf)
        AB.freeb(xnT)

        for lt in range(ntt):
            gt = ch * ntt + lt
            s = sl[lt % 2]
            P.dma("sync", s["xr"].ap, xin_ap(gt), writes=[s["xr"].t],
                  reads=[xin_tiles[gt]], tag=name + f"z{gt}")
            ob = outs[lt % 2]
            P.do("vector", lambda e, o=ob.ap, a=oaccv[:, lt, :], x=s["xr"].ap:
                 e.tensor_add(out=o, in0=a, in1=x),
                 reads=[oacc.t, s["xr"].t], writes=[ob.t], tag=name + f"ra{gt}")
            P.dma("scalar", xout_ap(gt), ob.ap, writes=[xout_tiles[gt]],
                  reads=[ob.t], tag=name + f"zs{gt}")
        AB.freeb(oacc)
        AF32.freeb(cw)

    AB.freeb(wg)
    AF32.freeb(sq_sh)
    AF32.freeb(gsc)
    AB.freeb(usf)
    AF32.freeb(gsf)
    AF32.freeb(sgf)
    for s in sl:
        AF32.freeb(s["xr"])
        AF32.freeb(s["st"])
        AB.freeb(s["xn"])
    for o in outs:
        AF32.freeb(o)



CONST_SPECS = {
    "ident": ([128, 128], BF16),
    "ones_row": ([1, 128], BF16),
    "ones_col": ([128, 1], BF16),
    "iota8": ([128, 8], F32),
    "iota8r": ([128, 8], F32),
    "causmask": ([128, 896], BF16),
    "esel": ([8, 1024], BF16),
    "rope0_cos": ([T, 512], F32),
    "rope0_sin": ([T, 512], F32),
    "rope1_cos": ([T, 512], F32),
    "rope1_sin": ([T, 512], F32),
    "alibi_srow": ([1, NHEAD * T], F32),
    "alibi_negtrow": ([1, NHEAD * T], F32),
    "alibi_band": ([NHEAD, 128, 896], F32),
    "alibi_scol": ([128, 256], F32),
}

WEIGHT_SPECS = {
    "l0a_wqT": ([C, C], F32), "l0a_wkT": ([C, C], F32),
    "l0a_wvT": ([C, C], F32), "l0a_wpT": ([C, C], F32),
    "l0a_bq": ([1, C], F32), "l0a_bk": ([1, C], F32),
    "l0a_bv": ([1, C], F32), "l0a_bp": ([1, C], F32),
    "l0f_k0t": ([C, 4096], U8), "l0f_k0b": ([C, 4096], U8),
    "l0f_k1": ([4096, 4096], U8), "l0f_k2": ([4096, C], U8),
    "l0f_s0": ([128, 1], F32), "l0f_s1": ([128, 1], F32), "l0f_s2": ([128, 1], F32),
    "l1a_wqT": ([C, C], F32), "l1a_wkT": ([C, C], F32), "l1a_wvT": ([C, C], F32),
    "l1a_bq": ([1, C], F32), "l1a_bk": ([1, C], F32), "l1a_bv": ([1, C], F32),
    "l1f_Wg": ([C, 8], F32), "l1f_bg": ([1, 8], F32),
    "l1f_W1": ([8, C, C], F32), "l1f_b1": ([128, 64], F32),
    "l1f_W2": ([8, C, C], F32), "l1f_b2": ([8, C], F32),
    "l2a_wqkv": ([C, 3 * C], F32), "l2a_bqkv": ([1, 3 * C], F32),
    "l2a_wp": ([C, C], F32), "l2a_bp": ([1, C], F32),
    "l2f_gateT": ([C, 8], F32),
    "l2f_wgT": ([8, C, 512], F32), "l2f_wuT": ([8, C, 512], F32),
    "l2f_wdT": ([8, 512, C], F32),
    "l3a_wqkvT": ([C, 3 * C], F32), "l3a_woT": ([C, C], F32),
    "l3a_bo": ([1, C], F32), "l3a_rfT": ([128, 128], F32),
    "l3f_gateT": ([C, 8], F32),
    "l3f_sh_guT": ([C, 2048], F32), "l3f_sh_dnT": ([C, C], F32),
    "l3f_ex_guT": ([8, C, 2048], F32), "l3f_ex_dnT": ([8, C, C], F32),
}

PART_WEIGHTS = {
    "l0a": ["l0a_wqT", "l0a_wkT", "l0a_wvT", "l0a_wpT",
            "l0a_bq", "l0a_bk", "l0a_bv", "l0a_bp"],
    "l0f": ["l0f_k0t", "l0f_k0b", "l0f_k1", "l0f_k2", "l0f_s0", "l0f_s1", "l0f_s2"],
    "l1a": ["l1a_wqT", "l1a_wkT", "l1a_wvT", "l1a_bq", "l1a_bk", "l1a_bv"],
    "l1f": ["l1f_Wg", "l1f_bg", "l1f_W1", "l1f_b1", "l1f_W2", "l1f_b2"],
    "l2a": ["l2a_wqkv", "l2a_bqkv", "l2a_wp", "l2a_bp"],
    "l2f": ["l2f_gateT", "l2f_wgT", "l2f_wuT", "l2f_wdT"],
    "l3a": ["l3a_wqkvT", "l3a_woT", "l3a_bo", "l3a_rfT"],
    "l3f": ["l3f_gateT", "l3f_sh_guT", "l3f_sh_dnT", "l3f_ex_guT", "l3f_ex_dnT"],
}

BF16_COLS = 85 * 1024
F32_COLS = 9 * 1024 + 512
U8_COLS = 11 * 1024


def build_program(part_ids):
    Arena._sem_pool = []
    Arena._sem_n = [0]
    Arena._sem_last = {}
    nc = bass.Bass("TRN2", target_bir_lowering=False, debug=False,
                   dynamic_dma_scratch_size=512)
    ctx = ExitStack()
    P = Prog(nc, ctx)

    xin_d = nc.dram_tensor("xin", [NT, C], F32, kind="ExternalInput").ap()
    xout_d = nc.dram_tensor("xout", [NT, C], F32, kind="ExternalOutput").ap()
    scr = [nc.dram_tensor(f"xscr{i}", [NT, C], F32, kind="Internal").ap()
           for i in range(2)]

    wdr = {}
    for pid in part_ids:
        for wn in PART_WEIGHTS[pid]:
            shape, dt = WEIGHT_SPECS[wn]
            wdr[wn] = nc.dram_tensor(wn, list(shape), dt, kind="ExternalInput").ap()
    for cn, (shape, dt) in CONST_SPECS.items():
        wdr[cn] = nc.dram_tensor("c_" + cn, list(shape), dt, kind="ExternalInput").ap()

    AB = Arena(P, nc, ctx, "arena_bf", BF16_COLS, BF16)
    AF32_ = Arena(P, nc, ctx, "arena_f32", F32_COLS, F32)
    AU8 = Arena(P, nc, ctx, "arena_u8", U8_COLS, U8)

    cm = Common()
    ps = PsumPool(P, nc, ctx)

    def resident(name, shape, dt):
        h = ctx.enter_context(nc.sbuf_tensor("sb_" + name, list(shape), dt))
        t = P.tile("c_" + name, dma_sem_key="c_" + name)
        P.dma("sync", h[:], wdr[name], writes=[t], tag="cl_" + name)
        return h[:], t

    cm.ident, cm.ident_t = resident("ident", [128, 128], BF16)
    cm.ones_row, cm.ones_row_t = resident("ones_row", [1, 128], BF16)
    cm.ones_col, cm.ones_col_t = resident("ones_col", [128, 1], BF16)
    cm.iota8, cm.iota8_t = resident("iota8", [128, 8], F32)
    cm.iota8r, cm.iota8r_t = resident("iota8r", [128, 8], F32)
    cm.causmask, cm.causmask_t = resident("causmask", [128, 896], BF16)
    cm.esel, cm.esel_t = resident("esel", [8, 1024], BF16)

    xres_tiles = {}

    def x_stage(si):
        if si == 0:
            d = xin_d
        elif si == len(part_ids):
            d = xout_d
        else:
            d = scr[(si - 1) % 2]
        key = id(d)
        if key not in xres_tiles:
            xres_tiles[key] = [P.tile(f"x{si}_{i}") for i in range(NTT)]
        return (lambda i, d=d: d[i * 128:(i + 1) * 128, :]), xres_tiles[key]

    emitters = {"l0a": emit_l0_attn, "l0f": emit_l0_ffn,
                "l1a": emit_l1_attn, "l1f": emit_l1_ffn,
                "l2a": emit_l2_attn, "l2f": emit_l2_ffn,
                "l3a": emit_l3_attn, "l3f": emit_l3_ffn}

    for si, pid in enumerate(part_ids):
        io = {"xin": x_stage(si), "xout": x_stage(si + 1)}
        emitters[pid](P, AB, AF32_, AU8, ps, cm, nc, io, wdr)

    P.finalize()
    with nc.Block() as block:
        @block.sync
        def _(eng):
            P.emit("sync", eng)

        @block.vector
        def _(eng):
            P.emit("vector", eng)

        @block.scalar
        def _(eng):
            P.emit("scalar", eng)

        @block.tensor
        def _(eng):
            P.emit("tensor", eng)

        @block.gpsimd
        def _(eng):
            P.emit("gpsimd", eng)
            P.emit_final_wait(eng)

    return nc, ctx, P


PART_GROUPS = [["l0a", "l0f", "l1a", "l1f", "l2a", "l2f", "l3a", "l3f"]]


def _run_groups(inputs, groups, x0):
    consts = prepare_consts()
    weights = prepare_weights(inputs)
    x = np.ascontiguousarray(
        np.asarray(x0, dtype=np.float32).reshape(NCORES, NT, C))
    for group in groups:
        nc, ctx, P = build_program(group)
        in_maps = []
        for cidx in range(NCORES):
            m = {"xin": x[cidx]}
            for pid in group:
                for wn in PART_WEIGHTS[pid]:
                    m[wn] = np.asarray(weights[wn])
            for cn in CONST_SPECS:
                m["c_" + cn] = np.asarray(consts[cn])
            in_maps.append(m)
        res = run_bass_kernel_spmd(nc, in_maps, core_ids=list(range(NCORES)))
        x = np.stack([np.asarray(res.results[cidx]["xout"]) for cidx in range(NCORES)])
        ctx.close()
    return x.reshape(B, T, C)


def kernel(**inputs):
    x0 = np.asarray(inputs["L0_attn_x"], dtype=np.float32)
    out = _run_groups(inputs, PART_GROUPS, x0)
    return out.astype(np.float32)
